# version 04
# speedup vs baseline: 1.9410x; 1.0899x over previous
import numpy as np
import ml_dtypes
from contextlib import ExitStack

import concourse.bass as bass
import concourse.mybir as mybir
from concourse.bass_types import AP
from concourse.bass_utils import run_bass_kernel_spmd

F32 = mybir.dt.float32
BF16 = mybir.dt.bfloat16
I32 = mybir.dt.int32
AF = mybir.ActivationFunctionType
ALU = mybir.AluOpType
AX = mybir.AxisListType

B, TFULL, C = 4, 4096, 1024
T = 2048
NCH = C // 128
TT = 512
NTT = T // TT
EPS = 1e-5
KEYS = 2 * T
BIGNEG = -1.0e30
BF = ml_dtypes.bfloat16


class Ctx:
    def __init__(self, nc, st):
        self.nc = nc
        self.st = st
        self.sems = {}
        self.g = nc.gpsimd
        self.sy = nc.sync
        self.t = nc.tensor
        self.v = nc.vector
        self.s = nc.scalar
        self.sbuf_bytes = 0

    def sem(self, name):
        if name not in self.sems:
            h = self.st.enter_context(self.nc.semaphore(name))
            self.sems[name] = [h, 0]
        return self.sems[name]

    def inc(self, name, instr, k):
        rec = self.sem(name)
        instr.then_inc(rec[0], k)
        rec[1] += k
        return rec[1]

    def val(self, name):
        return self.sem(name)[1]

    def wait(self, eng, name, val=None):
        rec = self.sem(name)
        eng.wait_ge(rec[0], rec[1] if val is None else val)

    def sbuf(self, name, shape, dtype):
        nb = int(np.prod(shape[1:])) * {F32: 4, BF16: 2, I32: 4}[dtype]
        self.sbuf_bytes += nb
        assert self.sbuf_bytes < 205 * 1024, f"SBUF over budget at {name}: {self.sbuf_bytes}"
        pre = getattr(self, "_prefix", "")
        return self.st.enter_context(self.nc.sbuf_tensor(pre + name, list(shape), dtype))

    def psum(self, name, shape, dtype=F32):
        pre = getattr(self, "_prefix", "")
        return self.st.enter_context(self.nc.psum_tensor(pre + name, list(shape), dtype))

    def uniq(self, name):
        return getattr(self, "_prefix", "") + name

    def scope(self, prefix):
        import contextlib

        self._scope_n = getattr(self, "_scope_n", 0) + 1
        _n = self._scope_n

        @contextlib.contextmanager
        def _sc():
            saved_st, saved_bytes, saved_pre = self.st, self.sbuf_bytes, getattr(self, "_prefix", "")
            with ExitStack() as st2:
                self.st = st2
                self._prefix = saved_pre + prefix + str(_n) + "_"
                try:
                    yield self
                finally:
                    self.st = saved_st
                    self.sbuf_bytes = saved_bytes
                    self._prefix = saved_pre
        return _sc()

    def tv(self, i):
        return self.inc("t", i, 1)

    def vv(self, i):
        return self.inc("v", i, 1)

    def sv(self, i):
        return self.inc("s", i, 1)

    def hard_barrier(self):
        for eng in (self.t, self.v, self.s, self.g, self.sy):
            for nm, rec in list(self.sems.items()):
                if rec[1] > 0:
                    eng.wait_ge(rec[0], rec[1])

    def wall(self, eng):
        for nm in ("t", "v", "s", "ld"):
            if nm in self.sems and self.sems[nm][1] > 0:
                self.wait(eng, nm)


def load_weight(ctx, dst, src_ap, rows=128):
    i = ctx.g.dma_start(out=dst[:, :, :],
                        in_=src_ap.rearrange("(r p) w -> p r w", p=rows))
    return ctx.inc("ld", i, 16)


def emit_ln_stats(ctx, xsrc_bf, rows, rows_bf, sq_scratch, ones128b, ps_pool,
                  double_ln=False, nch=NCH, nfeat=None, eps11=None):
    t, s, v = ctx.t, ctx.s, ctx.v
    nfeat = nfeat if nfeat is not None else nch * 128.0
    ssum, ssq, svar, rs = rows["ssum"], rows["ssq"], rows["svar"], rows["rs"]
    get = xsrc_bf if callable(xsrc_bf) else (lambda c, tsl: xsrc_bf[:, c, tsl])
    for tt in range(NTT):
        tsl = slice(tt * TT, (tt + 1) * TT)
        ps_sum, ps_sq = ps_pool
        for c in range(nch):
            ctx.wait(s, "t")
            i = s.square(sq_scratch[:, 0:TT], get(c, tsl)); ctx.sv(i)
            ctx.wait(t, "s")
            t.matmul(ps_sum[:, :], ones128b[:, 0:1], get(c, tsl),
                     start=(c == 0), stop=(c == nch - 1))
            i = t.matmul(ps_sq[:, :], ones128b[:, 0:1], sq_scratch[:, 0:TT],
                         start=(c == 0), stop=(c == nch - 1))
            ctx.tv(i)
        ctx.wait(v, "t")
        i = v.tensor_scalar_mul(ssum[0:1, tsl], ps_sum[:, :], 1.0 / nfeat); ctx.vv(i)
        i = v.tensor_scalar_mul(ssq[0:1, tsl], ps_sq[:, :], 1.0 / nfeat); ctx.vv(i)
    ctx.wait(v, "v")
    i = v.tensor_mul(rs[0:1, :], ssum[0:1, :], ssum[0:1, :]); ctx.vv(i)
    ctx.wait(v, "v")
    i = v.tensor_sub(svar[0:1, :], ssq[0:1, :], rs[0:1, :]); ctx.vv(i)
    ctx.wait(s, "v")
    i = s.activation(rs[0:1, :], svar[0:1, :], AF.Sqrt, bias=eps11[0:1, 0:1]); ctx.sv(i)
    ctx.wait(v, "s")
    i = v.reciprocal(rs[0:1, :], rs[0:1, :]); ctx.vv(i)
    ctx.wait(v, "v")
    i = v.tensor_copy(rows_bf[0][0:1, :], ssum[0:1, :]); ctx.vv(i)
    i = v.tensor_copy(rows_bf[1][0:1, :], rs[0:1, :]); ctx.vv(i)
    if double_ln:
        ctx.wait(v, "v")
        i = v.tensor_mul(ssq[0:1, :], svar[0:1, :], rs[0:1, :]); ctx.vv(i)
        ctx.wait(v, "v")
        i = v.tensor_mul(ssq[0:1, :], ssq[0:1, :], rs[0:1, :]); ctx.vv(i)
        ctx.wait(s, "v")
        i = s.activation(ssq[0:1, :], ssq[0:1, :], AF.Sqrt, bias=eps11[0:1, 0:1]); ctx.sv(i)
        ctx.wait(v, "s")
        i = v.reciprocal(ssq[0:1, :], ssq[0:1, :]); ctx.vv(i)
        ctx.wait(v, "v")
        i = v.tensor_mul(rs[0:1, :], rs[0:1, :], ssq[0:1, :]); ctx.vv(i)
        ctx.wait(v, "v")
        i = v.tensor_copy(rows_bf[2][0:1, :], rs[0:1, :]); ctx.vv(i)
    ctx.wait(t, "v")


def emit_ln_apply(ctx, xsrc_f32, rows_bf, row_idx, out_bf, ones1, ps3, tmp32,
                  nch=NCH, gain_bias=None):
    t, v, s = ctx.t, ctx.v, ctx.s
    ps_mu, ps_r, _ = ps3
    get = xsrc_f32 if callable(xsrc_f32) else (lambda c, tsl: xsrc_f32[:, c, tsl])
    for tt in range(NTT):
        tsl = slice(tt * TT, (tt + 1) * TT)
        ctx.wait(t, "v")
        t.matmul(ps_mu[:, :], ones1[0:1, :], rows_bf[0][0:1, tsl], start=True, stop=True)
        i = t.matmul(ps_r[:, :], ones1[0:1, :], rows_bf[row_idx][0:1, tsl],
                     start=True, stop=True)
        ctx.tv(i)
        ctx.wait(v, "t")
        for c in range(nch):
            ctx.wait(v, "v")
            i = v.tensor_sub(tmp32[:, 0:TT], get(c, tsl), ps_mu[:, :]); ctx.vv(i)
            ctx.wait(v, "v")
            if gain_bias is None:
                i = v.tensor_mul(out_bf[:, c, tsl], tmp32[:, 0:TT], ps_r[:, :]); ctx.vv(i)
            else:
                i = v.tensor_mul(tmp32[:, 0:TT], tmp32[:, 0:TT], ps_r[:, :]); ctx.vv(i)
                gcols, bcols = gain_bias
                ctx.wait(v, "v")
                i = v.tensor_scalar(out_bf[:, c, tsl], tmp32[:, 0:TT],
                                    gcols[:, c:c + 1], bcols[:, c:c + 1],
                                    ALU.mult, ALU.add)
                ctx.vv(i)


def fold_max128_to_scalar(ctx, col128, row128, out11):
    v, sy = ctx.v, ctx.sy
    ctx.wait(sy, "v")
    i = sy.dma_start(out=row128[0:1, :], in_=col128[:, 0:1])
    ctx.inc("foldst", i, 16)
    ctx.wait(v, "foldst")
    i = v.reduce_max(out11[0:1, 0:1], row128[0:1, :], axis=AX.X); ctx.vv(i)
    ctx.wait(v, "v")


def emit_weight_maxrownorm2(ctx, WT_sb, nw, sqw, ones128b, ps1, rown, out11,
                            nch=NCH):
    t, s, v = ctx.t, ctx.s, ctx.v
    for nt in range(nw // TT):
        for cc in range(nch):
            ctx.wait(s, "t")
            i = s.square(sqw[:, 0:TT], WT_sb[:, cc, nt * TT:(nt + 1) * TT]); ctx.sv(i)
            ctx.wait(t, "s")
            i = t.matmul(ps1[:, :], ones128b[:, 0:1], sqw[:, 0:TT],
                         start=(cc == 0), stop=(cc == nch - 1))
            ctx.tv(i)
        ctx.wait(v, "t")
        i = v.tensor_copy(rown[0:1, nt * TT:(nt + 1) * TT], ps1[:, :]); ctx.vv(i)
    ctx.wait(v, "v")
    i = v.reduce_max(out11[0:1, 0:1], rown[0:1, 0:nw], axis=AX.X); ctx.vv(i)
    ctx.wait(v, "v")


def emit_natural_maxrownorm2(ctx, W_sb, nch_w, width, sq, col128, scratch32,
                             out11):
    t, s, v = ctx.t, ctx.s, ctx.v
    for ncw in range(nch_w):
        ctx.wait(s, "v")
        i = s.square(sq[:, 0:width], W_sb[:, ncw, 0:width]); ctx.sv(i)
        ctx.wait(v, "s")
        i = v.reduce_sum(col128[:, ncw:ncw + 1], sq[:, 0:width], axis=AX.X); ctx.vv(i)
    ctx.wait(v, "v")
    i = v.reduce_max(col128[:, nch_w:nch_w + 1], col128[:, 0:nch_w], axis=AX.X); ctx.vv(i)
    fold_max128_to_scalar(ctx, col128[:, nch_w:nch_w + 1], scratch32, out11)


def emit_pattention(ctx, b_, KT_sb, V_sb, rhs_sb, out_mode, out_sb, nw,
                    biascol, scale, tag, qnorm=None, mrow=None, stream=None):
    t, s, v, g = ctx.t, ctx.s, ctx.v, ctx.g
    nwch = nw // 128
    ps_sc2, ps_den, ps_bc, ps_o2 = b_["ps_sc2"], b_["ps_den"], b_["ps_bc"], b_["ps_o2"]
    PT, drec, drecb, bcdb = b_["PT"], b_["drec"], b_["drecb"], b_["bcdb"]
    ones1, ones128b, sq = b_["ones1"], b_["ones128b"], b_["sq"]
    if stream is not None:
        kt_dram, v_dram = stream

    for tt in range(NTT):
        tsl = slice(tt * TT, (tt + 1) * TT)
        s_ex = [None] * nwch
        ld_ct = [None] * nwch
        t_qk = [None] * nwch

        def emit_sc(nc_):
            if stream is not None:
                db = nc_ % 2
                if nc_ >= 2 and t_qk[nc_ - 2] is not None:
                    ctx.wait(g, "t", t_qk[nc_ - 2])
                i = g.dma_start(
                    out=KT_sb[:, :, db * 128:db * 128 + 128],
                    in_=kt_dram[:, nc_ * 128:(nc_ + 1) * 128].rearrange(
                        "(r p) w -> p r w", p=128))
                ld_ct[nc_] = ctx.inc("lds", i, 16)
                ctx.wait(t, "lds", ld_ct[nc_])
                ktv = KT_sb[:, :, db * 128:db * 128 + 128]
            if nc_ >= 2 and s_ex[nc_ - 2] is not None:
                ctx.wait(t, "s", s_ex[nc_ - 2])
            for cc in range(NCH):
                lhsT = (KT_sb[:, cc, nc_ * 128:(nc_ + 1) * 128] if stream is None
                        else ktv[:, cc, :])
                i = t.matmul(ps_sc2[nc_ % 2][:, :], lhsT,
                             rhs_sb[:, cc, tsl], start=(cc == 0),
                             stop=(cc == NCH - 1 and mrow is None))
            if mrow is not None:
                i = t.matmul(ps_sc2[nc_ % 2][:, :],
                             mrow[1][0:1, nc_ * 128:(nc_ + 1) * 128],
                             mrow[0][0:1, tsl], start=False, stop=True)
            t_qk[nc_] = ctx.tv(i)
            ctx.wait(s, "t", t_qk[nc_])
            i = s.activation(PT[:, nc_, :], ps_sc2[nc_ % 2][:, :], AF.Exp,
                             bias=biascol[:, 0:1], scale=scale)
            s_ex[nc_] = ctx.sv(i)

        for nc_ in range(nwch):
            emit_sc(nc_)
        ctx.wait(t, "s")
        ctx.wait(t, "v")
        for nc_ in range(nwch):
            i = t.matmul(ps_den[:, :], ones128b[:, 0:1], PT[:, nc_, :],
                         start=(nc_ == 0), stop=(nc_ == nwch - 1))
        ctx.tv(i)
        ctx.wait(v, "t")
        i = v.reciprocal(drec[:, :], ps_den[:, :]); ctx.vv(i)
        ctx.wait(v, "v")
        i = v.tensor_copy(drecb[:, :], drec[:, :]); ctx.vv(i)
        ctx.wait(t, "v")
        i = t.matmul(ps_bc[:, :], ones1[0:1, :], drecb[0:1, :], start=True, stop=True)
        ctx.tv(i)
        ctx.wait(v, "t")
        i = v.tensor_copy(bcdb[:, :], ps_bc[:, :]); ctx.vv(i)
        ctx.wait(v, "v")
        for nc_ in range(nwch):
            i = v.tensor_mul(PT[:, nc_, :], PT[:, nc_, :], bcdb[:, :])
        ctx.vv(i)
        ctx.wait(t, "v")
        if out_mode == "feat":
            ev_ct = [None] * NCH
            mm_ct = [None] * NCH
            for oc in range(NCH):
                if oc >= 2:
                    ctx.wait(t, "v", ev_ct[oc - 2])
                ctx.wait(t, "t_qn", ctx.val("t_qn"))
                for nc_ in range(nwch):
                    i = t.matmul(ps_o2[oc % 2][:, :],
                                 V_sb[:, nc_, oc * 128:(oc + 1) * 128],
                                 PT[:, nc_, :], start=(nc_ == 0), stop=(nc_ == nwch - 1))
                mm_ct[oc] = ctx.tv(i)
                ctx.wait(v, "t", mm_ct[oc])
                i = v.tensor_copy(out_sb[:, oc, tsl], ps_o2[oc % 2][:, :])
                ev_ct[oc] = ctx.vv(i)
                if qnorm is not None:
                    qsel_sb, ps_qn, m8_sb, mk2col = qnorm
                    ctx.wait(s, "v", ev_ct[oc])
                    i = s.square(sq[:, 0:TT], out_sb[:, oc, tsl]); ctx.sv(i)
                    ctx.wait(t, "s")
                    i = t.matmul(ps_qn[:, :], qsel_sb[:, oc * 16:(oc + 1) * 16],
                                 sq[:, 0:TT], start=(oc == 0), stop=(oc == NCH - 1))
                    ctx.inc("t_qn", i, 1)
            if qnorm is not None:
                qsel_sb, ps_qn, m8_sb, mk2col = qnorm
                ctx.wait(s, "t_qn")
                i = s.activation(m8_sb[0:16, tsl], ps_qn[:, :], AF.Sqrt,
                                 scale=mk2col[:, 0:1])
                ctx.sv(i)
        else:
            tokdram = b_.get("tok_dram")
            for tc in range(TT // 128):
                tgl = tt * (TT // 128) + tc
                for ch in range(2):
                    ctx.wait(t, "v")
                    for nc_ in range(nwch):
                        i = t.matmul(ps_o2[0][:, :], PT[:, nc_, tc * 128:(tc + 1) * 128],
                                     V_sb[:, nc_, ch * 512:(ch + 1) * 512],
                                     start=(nc_ == 0), stop=(nc_ == nwch - 1))
                    ctx.tv(i)
                    ctx.wait(v, "t")
                    ctx.wait(v, "sttok", ctx.val("sttok"))
                    i = v.tensor_copy(out_sb[:, 0, ch * 512:(ch + 1) * 512], ps_o2[0][:, :])
                    ctx.vv(i)
                if tokdram is not None:
                    ctx.wait(ctx.sy, "v")
                    i = ctx.sy.dma_start(out=tokdram[tgl * 128:(tgl + 1) * 128, :],
                                         in_=out_sb[:, 0, :])
                    ctx.inc("sttok", i, 16)


def phase_u1(ctx, d):
    nc = ctx.nc
    xT = d["xT"]
    outs = d
    w_in = d
    with ctx.scope("u1"):
        g, sy, t, v, s = ctx.g, ctx.sy, ctx.t, ctx.v, ctx.s

        XB = ctx.sbuf("XB", [128, NCH, T], BF16)
        HT = ctx.sbuf("HT", [128, NCH, T], BF16)
        OUTF = ctx.sbuf("OUTF", [128, NCH, T], BF16)
        VTOK = ctx.sbuf("VTOK", [128, 1, C], BF16)
        WKT = ctx.sbuf("WKT", [128, NCH, 1024], BF16)
        WV = ctx.sbuf("WV", [128, NCH, 1024], BF16)
        PT = ctx.sbuf("PT", [128, NCH, TT], BF16)
        SQ = ctx.sbuf("SQ", [128, 1024], BF16)
        TMP32 = ctx.sbuf("TMP32", [128, TT], F32)
        M8 = ctx.sbuf("M8", [16, T], F32)
        QSEL = ctx.sbuf("QSEL", [128, NCH * 16], BF16)
        ones128b = ctx.sbuf("ones128b", [128, 1], BF16)
        ones1 = ctx.sbuf("ones1", [1, 128], BF16)
        rows = {k: ctx.sbuf(k, [1, T], F32) for k in ("ssum", "ssq", "svar", "rs")}
        ROWSB = [ctx.sbuf(f"ROWSB{i_}", [1, T], BF16) for i_ in range(3)]
        DREC = ctx.sbuf("DREC", [1, TT], F32)
        DRECB = ctx.sbuf("DRECB", [1, TT], BF16)
        BCDB = ctx.sbuf("BCDB", [128, TT], BF16)
        ROWN = ctx.sbuf("ROWN", [1, 1024], F32)
        MK11 = ctx.sbuf("MK11", [1, 1], F32)
        MK2B = ctx.sbuf("MK2B", [1, 1], BF16)
        MK2C = ctx.sbuf("MK2C", [16, 1], F32)
        BIAS = ctx.sbuf("BIAS", [128, 1], F32)
        SC32 = ctx.sbuf("SC32", [1, 128], F32)
        COL9 = ctx.sbuf("COL9", [128, NCH + 1], F32)
        KV11 = ctx.sbuf("KV11", [1, 1], F32)
        EPS11 = ctx.sbuf("EPS11", [1, 1], F32)

        ps_a = ctx.psum("ps_a", [1, TT])
        ps_b = ctx.psum("ps_b", [1, TT])
        ps_sc2 = [ctx.psum(f"ps_sc{i_}", [128, TT]) for i_ in range(2)]
        ps_sc = ps_sc2[0]
        ps_bc = ctx.psum("ps_bc", [128, TT])
        ps_o2 = [ctx.psum(f"ps_o{i_}", [128, TT]) for i_ in range(2)]
        ps_o = ps_o2[0]
        ps_qn = ctx.psum("ps_qn", [16, TT])
        ps_c16 = ps_qn[0:16, 0:1]

        b_ = dict(ps_sc2=ps_sc2, ps_den=ps_a, ps_bc=ps_bc, ps_o2=ps_o2, PT=PT,
                  drec=DREC, drecb=DRECB, bcdb=BCDB, sq=SQ, ones1=ones1,
                  ones128b=ones128b)

        i = g.dma_start(out=XB[:, :, :], in_=xT.rearrange("(r p) w -> p r w", p=128))
        ctx.inc("ld", i, 16)
        i = g.dma_start(out=QSEL[:, :], in_=w_in["qsel"][:, :]); ctx.inc("ld", i, 16)
        i = v.memset(ones128b[:, :], 1.0); ctx.vv(i)
        i = v.memset(ones1[:, :], 1.0); ctx.vv(i)
        i = v.memset(EPS11[:, :], EPS); ctx.vv(i)

        ctx.wait(s, "ld"); ctx.wait(t, "ld"); ctx.wait(v, "ld")
        emit_ln_stats(ctx, XB, rows, ROWSB, SQ, ones128b, (ps_a, ps_b), double_ln=True,
                      eps11=EPS11)
        emit_ln_apply(ctx, XB, ROWSB, 2, HT, ones1, (ps_sc, ps_bc, ps_o), TMP32)
        emit_ln_apply(ctx, XB, ROWSB, 1, XB, ones1, (ps_sc, ps_bc, ps_o), TMP32)
        ctx.wait(sy, "v")
        i = sy.dma_start(out=outs["o_xin"].rearrange("(r p) w -> p r w", p=128), in_=XB[:, :, :])
        ctx.inc("stx", i, 16)
        i = sy.dma_start(out=outs["o_h"].rearrange("(r p) w -> p r w", p=128), in_=HT[:, :, :])
        ctx.inc("stx", i, 16)

        def patt(kt_name, v_name, mode, out_sb, out_dram, qn, kvnorm_out=None):
            ctx.wait(g, "t"); ctx.wait(g, "v"); ctx.wait(g, "s")
            load_weight(ctx, WKT, w_in[kt_name])
            load_weight(ctx, WV, w_in[v_name])
            ctx.wait(t, "ld"); ctx.wait(s, "ld"); ctx.wait(v, "ld")
            if kvnorm_out is not None:
                emit_natural_maxrownorm2(ctx, WV, NCH, 1024, SQ, COL9, SC32,
                                         kvnorm_out)
            emit_weight_maxrownorm2(ctx, WKT, 1024, SQ, ones128b, ps_a, ROWN, MK11)
            ctx.wait(s, "v")
            i = s.activation(MK11[0:1, 0:1], MK11[0:1, 0:1], AF.Sqrt, scale=1.0202)
            ctx.sv(i)
            ctx.wait(v, "s")
            i = v.tensor_scalar_mul(MK11[0:1, 0:1], MK11[0:1, 0:1], -1.0); ctx.vv(i)
            ctx.wait(v, "v")
            i = v.tensor_copy(MK2B[0:1, 0:1], MK11[0:1, 0:1]); ctx.vv(i)
            ctx.wait(t, "v")
            i = t.matmul(ps_bc[0:128, 0:1], ones1[0:1, :], MK2B[0:1, 0:1],
                         start=True, stop=True)
            ctx.tv(i)
            ctx.wait(v, "t")
            i = v.tensor_copy(BIAS[:, 0:1], ps_bc[0:128, 0:1]); ctx.vv(i)
            if qn:
                ctx.wait(v, "v")
                i = v.tensor_scalar_mul(KV11[0:1, 0:1], KV11[0:1, 0:1], 1.0202); ctx.vv(i)
                ctx.wait(v, "v")
                i = v.tensor_copy(MK2B[0:1, 0:1], KV11[0:1, 0:1]); ctx.vv(i)
                ctx.wait(t, "v")
                i = t.matmul(ps_c16, ones1[0:1, 0:16], MK2B[0:1, 0:1],
                             start=True, stop=True)
                ctx.tv(i)
                ctx.wait(v, "t")
                i = v.tensor_copy(MK2C[:, 0:1], ps_c16); ctx.vv(i)
            ctx.wait(t, "v"); ctx.wait(s, "v")
            qnorm = (QSEL, ps_qn, M8, MK2C) if qn else None
            b_["tok_dram"] = out_dram if mode == "tok" else None
            emit_pattention(ctx, b_, WKT, WV, HT, mode, out_sb, 1024,
                            BIAS, 1.0 / 32.0, kt_name, qnorm=qnorm)
            ctx.wait(sy, "v"); ctx.wait(sy, "s")
            if mode == "feat":
                i = sy.dma_start(out=out_dram.rearrange("(r p) w -> p r w", p=128),
                                 in_=out_sb[:, :, :])
                ctx.inc("stx", i, 16)
            if qn:
                i = sy.dma_start(out=outs["o_m8"][:, :], in_=M8[0:16, :])
                ctx.inc("stx", i, 16)

        patt("kKT", "kV", "feat", OUTF, outs["o_k"], False, kvnorm_out=KV11)
        patt("vKT", "vV", "tok", VTOK, outs["o_v"], False)
        patt("qKT", "qV", "feat", OUTF, outs["o_q"], True)

        ctx.wait(sy, "stx")

def build_u1():
    nc = bass.Bass("TRN2", target_bir_lowering=False)
    xT = nc.dram_tensor("xT", [C, T], F32, kind="ExternalInput").ap()
    w_in = {}
    for nm, shp in [("qKT", [C, 1024]), ("kKT", [C, 1024]), ("vKT", [C, 1024]),
                    ("qV", [1024, C]), ("kV", [1024, C]), ("vV", [1024, C]),
                    ("qsel", [128, NCH * 16])]:
        w_in[nm] = nc.dram_tensor(nm, shp, F32, kind="ExternalInput").ap()
    outs = {}
    for nm, shp, dt in [("o_xin", [C, T], BF16), ("o_h", [C, T], BF16),
                        ("o_q", [C, T], BF16), ("o_k", [C, T], BF16),
                        ("o_v", [T, C], BF16), ("o_m8", [16, T], F32)]:
        outs[nm] = nc.dram_tensor(nm, shp, dt, kind="ExternalOutput").ap()

    d = {"xT": xT}
    d.update(outs)
    d.update(w_in)
    with ExitStack() as st:
        ctx = Ctx(nc, st)
        phase_u1(ctx, d)
    return nc
def emit_flash(ctx, fb, nq_heads, kv_of_head, kdram, vdram, inddram, qsrc_sb,
               m8_sb, out_sb, scale, nkc_of_qt, diag_from, keys, maskcls_sb,
               tag, head_order=None, vslot_of_head=None, stage_k_fn=None,
               stage_v_fn=None):
    t, s, v, g, sy = ctx.t, ctx.s, ctx.v, ctx.g, ctx.sy
    qhead2, khead2, vhead2 = fb["qhead2"], fb["khead2"], fb["vhead2"]
    ps_sc2, ps_pv2, ps_bc = fb["ps_sc2"], fb["ps_pv2"], fb["ps_bc"]
    PTt2, densf, densb, ones2 = fb["PTt2"], fb["densf"], fb["densb"], fb["ones2"]
    if head_order is None:
        head_order = list(range(nq_heads))
    if vslot_of_head is None:
        vslot_of_head = lambda h: h
    n = len(head_order)
    jk = [0] * n
    jv = [0] * n
    for ih in range(1, n):
        jk[ih] = jk[ih - 1] + (kv_of_head(head_order[ih]) != kv_of_head(head_order[ih - 1]))
        jv[ih] = jv[ih - 1] + (vslot_of_head(head_order[ih]) != vslot_of_head(head_order[ih - 1]))
    ldf_ready = [0] * n
    vst_ready = [0] * n
    t_after = [0] * n
    pv_after = [0] * n
    kst_done = set()
    vst_done = set()
    epi = {"v_after": {}, "bcmm": {}, "vmul": {}, "pq": 0}

    def prefetch(ih):
        h = head_order[ih]
        tfree = t_after[ih - 2] if ih >= 2 else 0
        pvfree = pv_after[ih - 2] if ih >= 2 else 0
        if jk[ih] not in kst_done:
            kst_done.add(jk[ih])
            kb = khead2[jk[ih] % 2]
            sy.wait_ge(ctx.sem("t")[0], tfree)
            if stage_k_fn is None:
                i = sy.dma_start(out=kb[0:64, :],
                                 in_=kdram[64 * kv_of_head(h):64 * kv_of_head(h) + 64, :])
                ctx.inc("ldf", i, 16)
            else:
                stage_k_fn(kv_of_head(h), kb)
            i = sy.dma_start(out=kb[65:66, :], in_=inddram[0:1, :])
            ctx.inc("ldf", i, 16)
        if jv[ih] not in vst_done:
            vst_done.add(jv[ih])
            vb = vhead2[jv[ih] % 2]
            if stage_v_fn is None:
                sy.wait_ge(ctx.sem("t_pv")[0], pvfree)
                i = sy.dma_start(out=vb[:, :, :],
                                 in_=vdram[vslot_of_head(h), :, :]
                                 .rearrange("(r p) w -> p r w", p=128))
                ctx.inc("ldf", i, 16)
            else:
                stage_v_fn(vslot_of_head(h), vb, pvfree,
                           ldf_ready[ih - 2] if ih >= 2 else 0)
        qb = qhead2[ih % 2]
        sy.wait_ge(ctx.sem("t")[0], tfree)
        i = sy.dma_start(out=qb[0:64, :],
                         in_=qsrc_sb[64 * (h % 2):64 * (h % 2) + 64, h // 2, :])
        ctx.inc("ldf", i, 16)
        g.wait_ge(ctx.sem("t")[0], tfree)
        i = g.dma_start(out=qb[64:65, :], in_=m8_sb[h:h + 1, :])
        ctx.inc("ldf", i, 16)
        ldf_ready[ih] = ctx.val("ldf")
        vst_ready[ih] = ctx.val("v")

    def compute(ih):
        h = head_order[ih]
        kb = khead2[jk[ih] % 2]
        vb = vhead2[jv[ih] % 2]
        qb = qhead2[ih % 2]
        ctx.wait(t, "ldf", ldf_ready[ih])
        if vst_ready[ih]:
            ctx.wait(t, "v", vst_ready[ih])
        for qt in range(NTT):
            tsl = slice(qt * TT, (qt + 1) * TT)
            nkc = nkc_of_qt(qt)
            dfrom = diag_from(qt)
            npair = (nkc + 1) // 2
            pq = epi["pq"]
            ps_pv = ps_pv2[pq % 2]
            t_qk = [None] * npair
            v_mk = [None] * npair
            s_ex = [None] * npair
            pv_ct = [None] * npair

            def emit_qkpair(pr):
                kcs = [2 * pr] + ([2 * pr + 1] if 2 * pr + 1 < nkc else [])
                if pr >= 2:
                    ctx.wait(t, "s", s_ex[pr - 2])
                if pr == 0 and pq >= 2:
                    ctx.wait(t, "v", epi["v_after"][pq - 2])
                for kc in kcs:
                    i = t.matmul(ps_sc2[pr % 2][:, (kc % 2) * TT:(kc % 2) * TT + TT],
                                 kb[0:128, kc * 128:(kc + 1) * 128],
                                 qb[0:128, tsl], start=True, stop=True)
                t_qk[pr] = ctx.tv(i)
                nmask = 0
                for kc in kcs:
                    j = kc - dfrom
                    if 0 <= j < 4:
                        if nmask == 0:
                            ctx.wait(v, "t", t_qk[pr])
                            if pr >= 2:
                                ctx.wait(v, "s", s_ex[pr - 2])
                        i = v.tensor_add(ps_sc2[pr % 2][:, (kc % 2) * TT:(kc % 2) * TT + TT],
                                         ps_sc2[pr % 2][:, (kc % 2) * TT:(kc % 2) * TT + TT],
                                         maskcls_sb[:, j * TT:(j + 1) * TT])
                        v_mk[pr] = ctx.vv(i)
                        nmask += 1
                if v_mk[pr] is not None:
                    ctx.wait(s, "v", v_mk[pr])
                ctx.wait(s, "t", t_qk[pr])
                if pr >= 2:
                    ctx.wait(s, "t_pv", pv_ct[pr - 2])
                wid = TT * len(kcs)
                i = s.activation(PTt2[pr % 2][:, 0:wid], ps_sc2[pr % 2][:, 0:wid],
                                 AF.Exp, scale=scale)
                s_ex[pr] = ctx.sv(i)

            def emit_pvpair(pr):
                kcs = [2 * pr] + ([2 * pr + 1] if 2 * pr + 1 < nkc else [])
                ctx.wait(t, "s", s_ex[pr])
                for kc in kcs:
                    i = t.matmul(ps_pv[:, :], vb[:, kc, :],
                                 PTt2[pr % 2][:, (kc % 2) * TT:(kc % 2) * TT + TT],
                                 start=(kc == 0), stop=(kc == nkc - 1))
                pv_ct[pr] = ctx.inc("t_pv", i, 2 if len(kcs) == 2 else 1)

            emit_qkpair(0)
            for pr in range(1, npair):
                emit_qkpair(pr)
                emit_pvpair(pr - 1)
            emit_pvpair(npair - 1)
            r0 = 64 * (h % 2)
            r1 = 64 * (1 - (h % 2))
            ctx.wait(v, "t_pv", pv_ct[npair - 1])
            if pq >= 1:
                ctx.wait(v, "t", epi["bcmm"][pq - 1])
            i = v.reciprocal(densf[r1:r1 + 1, :], ps_pv[r1:r1 + 1, :]); ctx.vv(i)
            ctx.wait(v, "v")
            i = v.tensor_copy(densb[r1:r1 + 1, :], densf[r1:r1 + 1, :])
            vcp = ctx.vv(i)
            ctx.wait(t, "v", vcp)
            if pq >= 1:
                ctx.wait(t, "v", epi["vmul"][pq - 1])
            i = t.matmul(ps_bc[:, :], ones2[r1:r1 + 1, :], densb[r1:r1 + 1, :],
                         start=True, stop=True)
            epi["bcmm"][pq] = ctx.tv(i)
            ctx.wait(v, "t", epi["bcmm"][pq])
            i = v.tensor_copy(out_sb[r0:r0 + 64, h // 2, tsl], ps_pv[r0:r0 + 64, :]); ctx.vv(i)
            ctx.wait(v, "v")
            i = v.tensor_mul(out_sb[r0:r0 + 64, h // 2, tsl],
                             out_sb[r0:r0 + 64, h // 2, tsl], ps_bc[r0:r0 + 64, :])
            epi["vmul"][pq] = ctx.vv(i)
            epi["v_after"][pq] = epi["vmul"][pq]
            epi["pq"] = pq + 1
        t_after[ih] = ctx.val("t")
        pv_after[ih] = ctx.val("t_pv")

    prefetch(0)
    if n > 1:
        prefetch(1)
    for ih in range(n):
        compute(ih)
        if ih + 2 < n:
            prefetch(ih + 2)


def emit_fpatt_streamed(ctx, b_, kt_dram, v_dram, WKTs, WVs, rhs_sb, acc_sb,
                        nw, biascol, scale, tag):
    t, s, v, g, sy = ctx.t, ctx.s, ctx.v, ctx.g, ctx.sy
    nwch = nw // 128
    PT = b_["PTBIG"]
    drec, drecb, bcdb = b_["drec"], b_["drecb"], b_["bcdb"]
    ones1, ones128b = b_["ones1"], b_["ones128b"]
    nc = ctx.nc
    for tt in range(NTT):
        tsl = slice(tt * TT, (tt + 1) * TT)
        with ExitStack() as ps_scope:
            ps_scP = [ps_scope.enter_context(ctx.nc.psum_tensor(ctx.uniq(f"f_sc{tt}_{i_}"), [128, TT], F32)) for i_ in range(2)]
            ps_den = ps_scope.enter_context(ctx.nc.psum_tensor(ctx.uniq(f"f_den{tt}"), [1, TT], F32))
            ps_bc = ps_scope.enter_context(ctx.nc.psum_tensor(ctx.uniq(f"f_bc{tt}"), [128, TT], F32))
            mmct = [None] * nwch
            sct = [None] * nwch
            for nc_ in range(nwch):
                db = nc_ % 2
                if nc_ >= 2:
                    ctx.wait(g, "t", mmct[nc_ - 2])
                i = g.dma_start(out=WKTs[:, :, db * 128:db * 128 + 128],
                                in_=kt_dram[:, nc_ * 128:(nc_ + 1) * 128].rearrange(
                                    "(r p) w -> p r w", p=128))
                ld2 = ctx.inc("lds", i, 16)
                ctx.wait(t, "lds", ld2)
                if nc_ >= 2:
                    ctx.wait(t, "s", sct[nc_ - 2])
                for cc in range(NCH):
                    i = t.matmul(ps_scP[db][:, :], WKTs[:, cc, db * 128:db * 128 + 128],
                                 rhs_sb[:, cc, tsl], start=(cc == 0), stop=(cc == NCH - 1))
                mmct[nc_] = ctx.tv(i)
                ctx.wait(s, "t", mmct[nc_])
                i = s.activation(PT[:, nc_, :], ps_scP[db][:, :], AF.Exp,
                                 bias=biascol[:, 0:1], scale=scale)
                sct[nc_] = ctx.sv(i)
            ctx.wait(t, "s")
            for nc_ in range(nwch):
                i = t.matmul(ps_den[:, :], ones128b[:, 0:1], PT[:, nc_, :],
                             start=(nc_ == 0), stop=(nc_ == nwch - 1))
            ctx.tv(i)
            ctx.wait(v, "t")
            i = v.reciprocal(drec[:, :], ps_den[:, :]); ctx.vv(i)
            ctx.wait(v, "v")
            i = v.tensor_copy(drecb[:, :], drec[:, :]); ctx.vv(i)
            ctx.wait(t, "v")
            i = t.matmul(ps_bc[:, :], ones1[0:1, :], drecb[0:1, :], start=True, stop=True)
            ctx.tv(i)
            ctx.wait(v, "t")
            i = v.tensor_copy(bcdb[:, :], ps_bc[:, :]); ctx.vv(i)
            ctx.wait(v, "v")
            for nc_ in range(nwch):
                i = v.tensor_mul(PT[:, nc_, :], PT[:, nc_, :], bcdb[:, :])
            ctx.vv(i)
            ctx.wait(t, "v")
        with ExitStack() as ps_scope:
            ps8 = [ps_scope.enter_context(ctx.nc.psum_tensor(ctx.uniq(f"f_o{tt}_{o}"), [128, TT], F32))
                   for o in range(NCH)]
            vmct = [None] * nwch
            for nc_ in range(nwch):
                db = nc_ % 2
                if nc_ >= 2:
                    ctx.wait(g, "t", vmct[nc_ - 2])
                i = g.dma_start(out=WVs[:, db, :],
                                in_=v_dram[nc_ * 128:(nc_ + 1) * 128, :])
                ld2 = ctx.inc("lds", i, 16)
                ctx.wait(t, "lds", ld2)
                for oc in range(NCH):
                    i = t.matmul(ps8[oc][:, :], WVs[:, db, oc * 128:(oc + 1) * 128],
                                 PT[:, nc_, :], start=(nc_ == 0), stop=(nc_ == nwch - 1))
                vmct[nc_] = ctx.tv(i)
            ctx.wait(v, "t")
            for oc in range(NCH):
                i = v.tensor_add(acc_sb[:, oc, tsl], acc_sb[:, oc, tsl], ps8[oc][:, :])
                ctx.vv(i)
            ctx.wait(t, "v")
            ctx.wait(g, "v")


def phase_u2(ctx, d):
    nc = ctx.nc
    ind = d["ind"]
    ksrc = d.get("ksrc")
    m8in = d["m8in"]
    masks = d["masks"]
    o_o = d["o_o"]
    qin = d["qin"]
    vsrc = d.get("vsrc")
    with ctx.scope("u2"):
        g, sy, t, v, s = ctx.g, ctx.sy, ctx.t, ctx.v, ctx.s
        QT = ctx.sbuf("QT", [128, NCH, T], BF16)
        OUTF = ctx.sbuf("OUTF", [128, NCH, T], BF16)
        M8 = ctx.sbuf("M8", [16, T], F32)
        qhead2 = [ctx.sbuf(f"qhead{i_}", [128, T], BF16) for i_ in range(2)]
        khead2 = [ctx.sbuf(f"khead{i_}", [128, KEYS], BF16) for i_ in range(2)]
        vhead2 = [ctx.sbuf(f"vhead{i_}", [128, KEYS // 128, 128], BF16)
                  for i_ in range(2)]
        PTt2 = [ctx.sbuf(f"PTt{i_}", [128, 2 * TT], BF16) for i_ in range(2)]
        densf = ctx.sbuf("densf", [128, TT], F32)
        densb = ctx.sbuf("densb", [128, TT], BF16)
        MASKS = ctx.sbuf("MASKS", [128, 4 * TT], F32)
        ones2 = ctx.sbuf("ones2", [128, 128], BF16)
        ps_sc2 = [ctx.psum(f"ps_sc{i_}", [128, 2 * TT]) for i_ in range(2)]
        ps_pv2 = [ctx.psum(f"ps_pv{i_}", [128, TT]) for i_ in range(2)]
        ps_bc = ctx.psum("ps_bc", [128, TT])

        i = sy.dma_start(out=QT[:, :, :], in_=qin.rearrange("(r p) w -> p r w", p=128))
        ctx.inc("ld", i, 16)
        i = sy.dma_start(out=M8[:, :], in_=m8in[:, :]); ctx.inc("ld", i, 16)
        i = sy.dma_start(out=MASKS[:, :], in_=masks[:, :]); ctx.inc("ld", i, 16)
        BIGROW = ctx.sbuf("BIGROW", [1, T], BF16)
        for b_ in range(2):
            i = v.memset(khead2[b_][64:128, :], 0.0); ctx.vv(i)
            i = v.memset(qhead2[b_][64:128, :], 0.0); ctx.vv(i)
        ctx.wait(v, "v")
        for b_ in range(2):
            i = v.memset(khead2[b_][64:65, :], -1.0); ctx.vv(i)
        i = v.memset(BIGROW[:, :], BIGNEG); ctx.vv(i)
        i = v.memset(ones2[:, :], 1.0); ctx.vv(i)
        ctx.wait(sy, "v")
        for b_ in range(2):
            i = sy.dma_start(out=qhead2[b_][65:66, :], in_=BIGROW[0:1, :])
            ctx.inc("ld", i, 16)
        ctx.wait(t, "ld"); ctx.wait(v, "ld"); ctx.wait(s, "ld"); ctx.wait(g, "ld")
        ctx.wait(sy, "v")

        fb = dict(qhead2=qhead2, khead2=khead2, vhead2=vhead2, ps_sc2=ps_sc2,
                  ps_pv2=ps_pv2, ps_bc=ps_bc, PTt2=PTt2, densf=densf,
                  densb=densb, ones2=ones2)
        emit_flash(ctx, fb, 16, lambda h: h, ksrc, vsrc, ind, QT, M8, OUTF,
                   0.125, lambda qt: 20 + 4 * qt, lambda qt: 16 + 4 * qt,
                   KEYS, MASKS, "sdpa", stage_k_fn=d.get("stage_k"),
                   stage_v_fn=d.get("stage_v"))
        ctx.wait(sy, "v")
        i = sy.dma_start(out=o_o.rearrange("(r p) w -> p r w", p=128), in_=OUTF[:, :, :])
        ctx.inc("stx", i, 16)
        ctx.wait(sy, "stx")

def build_u2():
    nc = bass.Bass("TRN2", target_bir_lowering=False)
    qin = nc.dram_tensor("q", [C, T], BF16, kind="ExternalInput").ap()
    m8in = nc.dram_tensor("m8", [16, T], F32, kind="ExternalInput").ap()
    ksrc = nc.dram_tensor("ksrc", [C, KEYS], BF16, kind="ExternalInput").ap()
    vsrc = nc.dram_tensor("vsrc", [16, KEYS, 128], BF16, kind="ExternalInput").ap()
    ind = nc.dram_tensor("ind", [1, KEYS], BF16, kind="ExternalInput").ap()
    masks = nc.dram_tensor("masks", [128, 4 * TT], F32, kind="ExternalInput").ap()
    o_o = nc.dram_tensor("o_o", [C, T], BF16, kind="ExternalOutput").ap()

    d = {"ind": ind, "ksrc": ksrc, "m8in": m8in, "masks": masks, "o_o": o_o, "qin": qin, "vsrc": vsrc}
    with ExitStack() as st:
        ctx = Ctx(nc, st)
        phase_u2(ctx, d)
    return nc
def _sdpa_masks_host():
    m = np.zeros((128, 4 * TT), np.float32)
    for j in range(4):
        tile = m[:, j * TT:(j + 1) * TT]
        for qs in range(4):
            if qs < j:
                tile[:, qs * 128:(qs + 1) * 128] = BIGNEG
            elif qs == j:
                sub = tile[:, qs * 128:(qs + 1) * 128]
                for srel in range(128):
                    sub[srel, :srel] = BIGNEG
    return m


def emit_row_m8(ctx, src_sb, m8row_b, sq, ones128b, ps1, scale11):
    t, s, v = ctx.t, ctx.s, ctx.v
    for tt in range(NTT):
        tsl = slice(tt * TT, (tt + 1) * TT)
        for cc in range(NCH):
            ctx.wait(s, "t")
            i = s.square(sq[:, 0:TT], src_sb[:, cc, tsl]); ctx.sv(i)
            ctx.wait(t, "s")
            i = t.matmul(ps1[:, :], ones128b[:, 0:1], sq[:, 0:TT],
                         start=(cc == 0), stop=(cc == NCH - 1))
            ctx.tv(i)
        ctx.wait(s, "t")
        i = s.activation(m8row_b[0:1, tsl], ps1[:, :], AF.Sqrt, scale=scale11[0:1, 0:1])
        ctx.sv(i)
    ctx.wait(t, "s"); ctx.wait(v, "s")


def emit_streamed_maxrownorm2(ctx, kt_dram, WKTs, nw, sq, ones128b, sc128,
                              cm11, out11):
    t, s, v, g = ctx.t, ctx.s, ctx.v, ctx.g
    nc = ctx.nc
    i = v.memset(out11[0:1, 0:1], 0.0); ctx.vv(i)
    with ExitStack() as ps_scope:
        ps1 = ps_scope.enter_context(ctx.nc.psum_tensor(ctx.uniq("mrn_ps"), [1, 128], F32))
        for nc_ in range(nw // 128):
            db = nc_ % 2
            ctx.wait(g, "t")
            i = g.dma_start(out=WKTs[:, :, db * 128:db * 128 + 128],
                            in_=kt_dram[:, nc_ * 128:(nc_ + 1) * 128].rearrange(
                                "(r p) w -> p r w", p=128))
            ctx.inc("lds", i, 16)
            ctx.wait(s, "lds")
            for cc in range(NCH):
                ctx.wait(s, "t")
                i = s.square(sq[:, 0:128], WKTs[:, cc, db * 128:db * 128 + 128]); ctx.sv(i)
                ctx.wait(t, "s")
                i = t.matmul(ps1[:, :], ones128b[:, 0:1], sq[:, 0:128],
                             start=(cc == 0), stop=(cc == NCH - 1))
                ctx.tv(i)
            ctx.wait(v, "t")
            i = v.reduce_max(cm11[0:1, 0:1], ps1[0:1, :], axis=AX.X); ctx.vv(i)
            ctx.wait(v, "v")
            i = v.tensor_max(out11[0:1, 0:1], out11[0:1, 0:1], cm11[0:1, 0:1]); ctx.vv(i)
            ctx.wait(v, "v")
        ctx.wait(s, "v"); ctx.wait(t, "v")


def emit_ln_apply_residual(ctx, src_bf, rows_bf, res_dram, out_dram, ones1,
                           tmp32, x0tile):
    t, v, sy, nc = ctx.t, ctx.v, ctx.sy, ctx.nc
    with ExitStack() as ps_scope:
        ps_mu = ps_scope.enter_context(ctx.nc.psum_tensor(ctx.uniq("lnr_mu"), [128, TT], F32))
        ps_r = ps_scope.enter_context(ctx.nc.psum_tensor(ctx.uniq("lnr_r"), [128, TT], F32))
        for tt in range(NTT):
            tsl = slice(tt * TT, (tt + 1) * TT)
            ctx.wait(t, "v")
            t.matmul(ps_mu[:, :], ones1[0:1, :], rows_bf[0][0:1, tsl], start=True, stop=True)
            i = t.matmul(ps_r[:, :], ones1[0:1, :], rows_bf[1][0:1, tsl],
                         start=True, stop=True)
            ctx.tv(i)
            ctx.wait(v, "t")
            for c in range(NCH):
                ctx.wait(sy, "strx", ctx.val("strx"))
                i = sy.dma_start(out=x0tile[:, :],
                                 in_=res_dram.rearrange("(r p) w -> p r w", p=128)[:, c, tsl])
                ctx.inc("ldr", i, 16)
                ctx.wait(v, "ldr")
                ctx.wait(v, "v")
                i = v.tensor_sub(tmp32[:, 0:TT], src_bf[:, c, tsl], ps_mu[:, :]); ctx.vv(i)
                ctx.wait(v, "v")
                i = v.tensor_mul(tmp32[:, 0:TT], tmp32[:, 0:TT], ps_r[:, :]); ctx.vv(i)
                ctx.wait(v, "v")
                i = v.tensor_add(tmp32[:, 0:TT], tmp32[:, 0:TT], x0tile[:, :]); ctx.vv(i)
                ctx.wait(sy, "v")
                i = sy.dma_start(out=out_dram.rearrange("(r p) w -> p r w", p=128)[:, c, tsl],
                                 in_=tmp32[:, 0:TT])
                ctx.inc("strx", i, 16)
        ctx.wait(sy, "strx")


def phase_u3(ctx, d):
    nc = ctx.nc
    fKT = d["fKT"]
    fV = d["fV"]
    o_x2 = d["o_x2"]
    oin = d["oin"]
    pKT = d["pKT"]
    pV = d["pV"]
    x0in = d["x0in"]
    xinin = d["xinin"]
    with ctx.scope("u3"):
        g, sy, t, v, s = ctx.g, ctx.sy, ctx.t, ctx.v, ctx.s
        OT = ctx.sbuf("OT", [128, NCH, T], BF16)
        XIN = ctx.sbuf("XIN", [128, NCH, T], BF16)
        O2 = ctx.sbuf("O2", [128, NCH, T], BF16)
        WKT = ctx.sbuf("WKT", [128, NCH, 1024], BF16)
        WV = ctx.sbuf("WV", [128, NCH, 1024], BF16)
        PTSMALL = ctx.sbuf("PTSMALL", [128, NCH, TT], BF16)
        SQ = ctx.sbuf("SQ", [128, 1024], BF16)
        TMP32 = ctx.sbuf("TMP32", [128, TT], F32)
        X0TILE = ctx.sbuf("X0TILE", [128, TT], F32)
        _ssum = ctx.sbuf("ssum", [1, T], F32)
        _ssq = ctx.sbuf("ssq", [1, T], F32)
        _rs = ctx.sbuf("rs", [1, T], F32)
        rows = {"ssum": _ssum, "ssq": _ssq, "svar": _ssq, "rs": _rs}
        ROWSB = [ctx.sbuf(f"ROWSB{i_}", [1, T], BF16) for i_ in range(2)]
        M8RB = ctx.sbuf("M8RB", [1, T], BF16)
        NEGONES = ctx.sbuf("NEGONES", [1, 1024], BF16)
        DREC = ctx.sbuf("DREC", [1, TT], F32)
        DRECB = ctx.sbuf("DRECB", [1, TT], BF16)
        BCDB = ctx.sbuf("BCDB", [128, TT], BF16)
        ROWN4 = ctx.sbuf("ROWN4", [1, 1024], F32)
        CM11 = ctx.sbuf("CM11", [1, 1], F32)
        PK11 = ctx.sbuf("PK11", [1, 1], F32)
        FK11 = ctx.sbuf("FK11", [1, 1], F32)
        PK2S = ctx.sbuf("PK2S", [1, 1], F32)
        MKB = ctx.sbuf("MKB", [1, 1], BF16)
        BIAS = ctx.sbuf("BIAS", [128, 1], F32)
        ZEROC = ctx.sbuf("ZEROC", [128, 1], F32)
        EPS11 = ctx.sbuf("EPS11", [1, 1], F32)
        SC32 = ctx.sbuf("SC32", [1, 128], F32)
        COL9 = ctx.sbuf("COL9", [128, NCH + 1], F32)
        ones128b = ctx.sbuf("ones128b", [128, 1], BF16)
        ones1 = ctx.sbuf("ones1", [1, 128], BF16)

        i = sy.dma_start(out=OT[:, :, :], in_=oin.rearrange("(r p) w -> p r w", p=128))
        ctx.inc("ld", i, 16)
        i = sy.dma_start(out=XIN[:, :, :], in_=xinin.rearrange("(r p) w -> p r w", p=128))
        ctx.inc("ld", i, 16)
        load_weight(ctx, WKT, pKT)
        load_weight(ctx, WV, pV)
        i = v.memset(ones128b[:, :], 1.0); ctx.vv(i)
        i = v.memset(ones1[:, :], 1.0); ctx.vv(i)
        i = v.memset(NEGONES[:, :], -1.0); ctx.vv(i)
        i = v.memset(ZEROC[:, :], 0.0); ctx.vv(i)
        i = v.memset(EPS11[:, :], EPS); ctx.vv(i)
        ctx.wait(t, "ld"); ctx.wait(s, "ld"); ctx.wait(v, "ld")
        ctx.wait(t, "v"); ctx.wait(s, "v")

        with ExitStack() as pp:
            ps_a = pp.enter_context(ctx.nc.psum_tensor(ctx.uniq("ps_a"), [1, TT], F32))
            ps_b = pp.enter_context(ctx.nc.psum_tensor(ctx.uniq("ps_b"), [1, TT], F32))
            ps_sc2 = [pp.enter_context(ctx.nc.psum_tensor(ctx.uniq(f"ps_sc{i_}"), [128, TT], F32))
                      for i_ in range(2)]
            ps_sc = ps_sc2[0]
            ps_bc = pp.enter_context(ctx.nc.psum_tensor(ctx.uniq("ps_bc"), [128, TT], F32))
            ps_o2 = [pp.enter_context(ctx.nc.psum_tensor(ctx.uniq(f"ps_o{i_}"), [128, TT], F32))
                     for i_ in range(2)]
            ps_o = ps_o2[0]
            b_ = dict(ps_sc2=ps_sc2, ps_den=ps_a, ps_bc=ps_bc, ps_o2=ps_o2,
                      PT=PTSMALL, drec=DREC, drecb=DRECB, bcdb=BCDB, sq=SQ,
                      ones1=ones1, ones128b=ones128b)
            emit_weight_maxrownorm2(ctx, WKT, 1024, SQ, ones128b, ps_a, ROWN4, PK11)
            ctx.wait(v, "v")
            i = v.tensor_scalar_mul(PK2S[0:1, 0:1], PK11[0:1, 0:1], 1.0202 / (32.0 * 32.0))
            ctx.vv(i)
            ctx.wait(s, "v")
            i = v.tensor_scalar_mul(PK2S[0:1, 0:1], PK11[0:1, 0:1], 1.0202); ctx.vv(i)
            ctx.wait(s, "v")
            emit_row_m8(ctx, OT, M8RB, SQ, ones128b, ps_a, PK2S)
            emit_pattention(ctx, b_, WKT, WV, OT, "feat", O2, 1024,
                            ZEROC, 1.0 / 32.0, "p", mrow=(M8RB, NEGONES))
            ctx.wait(v, "t"); ctx.wait(v, "s")
            for c in range(NCH):
                i = v.tensor_add(XIN[:, c, :], XIN[:, c, :], O2[:, c, :])
            ctx.vv(i)
            ctx.wait(s, "v"); ctx.wait(t, "v")
            emit_ln_stats(ctx, XIN, rows, ROWSB + [None], SQ, ones128b,
                          (ps_a, ps_b), double_ln=False, eps11=EPS11)
            emit_ln_apply(ctx, XIN, ROWSB, 1, O2, ones1, (ps_sc, ps_bc, ps_o), TMP32)
            ctx.wait(t, "v"); ctx.wait(s, "v")

        emit_streamed_maxrownorm2(ctx, fKT, WKT[:, :, 0:256], 4096, SQ,
                                  ones128b, SC32, CM11, FK11)
        ctx.wait(s, "v")
        i = s.activation(FK11[0:1, 0:1], FK11[0:1, 0:1], AF.Sqrt, scale=1.0202); ctx.sv(i)
        ctx.wait(v, "s")
        i = v.tensor_scalar_mul(FK11[0:1, 0:1], FK11[0:1, 0:1], -1.0); ctx.vv(i)
        ctx.wait(v, "v")
        i = v.tensor_copy(MKB[0:1, 0:1], FK11[0:1, 0:1]); ctx.vv(i)
        ctx.wait(t, "v")
        with ExitStack() as pp:
            ps_bc = pp.enter_context(ctx.nc.psum_tensor(ctx.uniq("ps_bc2"), [128, 1], F32))
            i = t.matmul(ps_bc[:, :], ones1[0:1, :], MKB[0:1, 0:1], start=True, stop=True)
            ctx.tv(i)
            ctx.wait(v, "t")
            i = v.tensor_copy(BIAS[:, 0:1], ps_bc[:, :]); ctx.vv(i)
            ctx.wait(t, "v"); ctx.wait(s, "v")
        OTPT = OT[:, :, :].rearrange("p c t -> p (c t)").rearrange(
            "p (n k) -> p n k", k=TT)
        b2 = dict(PTBIG=OTPT, drec=DREC, drecb=DRECB, bcdb=BCDB,
                  ones1=ones1, ones128b=ones128b)
        emit_fpatt_streamed(ctx, b2, fKT, fV, WKT[:, :, 0:256], WV[:, 0:2, :],
                            O2, XIN, 4096, BIAS, 1.0 / 32.0, "f")
        with ExitStack() as pp:
            ps_a = pp.enter_context(ctx.nc.psum_tensor(ctx.uniq("ps_a2"), [1, TT], F32))
            ps_b = pp.enter_context(ctx.nc.psum_tensor(ctx.uniq("ps_b2"), [1, TT], F32))
            emit_ln_stats(ctx, XIN, rows, ROWSB + [None], SQ, ones128b,
                          (ps_a, ps_b), double_ln=False, eps11=EPS11)
        emit_ln_apply_residual(ctx, XIN, ROWSB, x0in, o_x2, ones1, TMP32, X0TILE)

def build_u3():
    nc = bass.Bass("TRN2", target_bir_lowering=False)
    oin = nc.dram_tensor("o", [C, T], BF16, kind="ExternalInput").ap()
    xinin = nc.dram_tensor("xin", [C, T], BF16, kind="ExternalInput").ap()
    x0in = nc.dram_tensor("x0T", [C, T], F32, kind="ExternalInput").ap()
    pKT = nc.dram_tensor("pKT", [C, 1024], F32, kind="ExternalInput").ap()
    pV = nc.dram_tensor("pV", [1024, C], F32, kind="ExternalInput").ap()
    fKT = nc.dram_tensor("fKT", [C, 4096], F32, kind="ExternalInput").ap()
    fV = nc.dram_tensor("fV", [4096, C], F32, kind="ExternalInput").ap()
    o_x2 = nc.dram_tensor("o_x2", [C, T], F32, kind="ExternalOutput").ap()

    d = {"fKT": fKT, "fV": fV, "o_x2": o_x2, "oin": oin, "pKT": pKT, "pV": pV, "x0in": x0in, "xinin": xinin}
    with ExitStack() as st:
        ctx = Ctx(nc, st)
        phase_u3(ctx, d)
    return nc
def phase_u4(ctx, d):
    nc = ctx.nc
    bselin = d["bselin"]
    halo = d["halo"]
    kscol = d["kscol"]
    o_k = d["o_k"]
    o_q = d["o_q"]
    o_v = d["o_v"]
    qscol = d["qscol"]
    rotp = d["rotp"]
    tabs = d["tabs"]
    wqkv = d["wqkv"]
    xl = d["xl"]
    with ctx.scope("u4"):
        g, sy, t, v, s = ctx.g, ctx.sy, ctx.t, ctx.v, ctx.s
        XLE = ctx.sbuf("XLE", [128, NCH, T + 1], BF16)
        WQ = ctx.sbuf("WQ", [128, NCH, 1536], BF16)
        QT = ctx.sbuf("QT", [128, 4, T], BF16)
        KT = ctx.sbuf("KT", [128, 4, T], BF16)
        VT = ctx.sbuf("VT", [128, 4, 512], BF16)
        TAB = ctx.sbuf("TAB", [128, 4, T], BF16)
        ROTP = ctx.sbuf("ROTP", [128, 128], BF16)
        QRAW = ctx.sbuf("QRAW", [128, TT], BF16)
        SQT = ctx.sbuf("SQT", [128, TT], BF16)
        ROTS = ctx.sbuf("ROTS", [128, TT], BF16)
        M1 = ctx.sbuf("M1", [128, TT], BF16)
        RN = ctx.sbuf("RN", [2, TT], F32)
        RNB = ctx.sbuf("RNB", [2, TT], BF16)
        QSC = ctx.sbuf("QSC", [128, 1], F32)
        KSC = ctx.sbuf("KSC", [128, 1], F32)
        HSEL = ctx.sbuf("HSEL", [128, 2], BF16)
        BSEL = ctx.sbuf("BSEL", [2, 128], BF16)

        i = sy.dma_start(out=XLE[:, :, 1:T + 1], in_=xl.rearrange("(r p) w -> p r w", p=128))
        ctx.inc("ld", i, 16)
        i = sy.dma_start(out=XLE[:, :, 0:1], in_=halo.rearrange("(r p) w -> p r w", p=128), allow_slow_non_contiguous=True)
        ctx.inc("ld", i, 16)
        load_weight(ctx, WQ, wqkv)
        i = g.dma_start(out=TAB[:, :, :], in_=tabs.transpose([1, 0, 2])); ctx.inc("ld", i, 16)
        i = g.dma_start(out=ROTP[:, :], in_=rotp[:, :]); ctx.inc("ld", i, 16)
        i = sy.dma_start(out=QSC[:, :], in_=qscol[:, :]); ctx.inc("ld", i, 16)
        i = sy.dma_start(out=KSC[:, :], in_=kscol[:, :]); ctx.inc("ld", i, 16)
        i = g.dma_start(out=BSEL[:, :], in_=bselin[:, :]); ctx.inc("ld", i, 16)
        i = v.memset(HSEL[:, :], 0.0); ctx.vv(i)
        ctx.wait(v, "v")
        i = v.memset(HSEL[0:64, 0:1], 1.0); ctx.vv(i)
        ctx.wait(v, "v")
        i = v.memset(HSEL[64:128, 1:2], 1.0); ctx.vv(i)
        ctx.wait(t, "ld"); ctx.wait(s, "ld"); ctx.wait(v, "ld")
        ctx.wait(t, "v"); ctx.wait(s, "v")

        def shifted_rhs(cc, tsl1):
            if cc < 4:
                return XLE[:, cc, tsl1[0] + 1:tsl1[1] + 1]
            return XLE[:, cc, tsl1[0]:tsl1[1]]

        with ExitStack() as pp:
            ps_q = pp.enter_context(ctx.nc.psum_tensor(ctx.uniq("ps_q"), [128, TT], F32))
            ps_n = pp.enter_context(ctx.nc.psum_tensor(ctx.uniq("ps_n"), [2, TT], F32))
            ps_rb = pp.enter_context(ctx.nc.psum_tensor(ctx.uniq("ps_rb"), [128, TT], F32))
            ps_rot = pp.enter_context(ctx.nc.psum_tensor(ctx.uniq("ps_rot"), [128, TT], F32))
            for tt in range(NTT):
                t0_, t1_ = tt * TT, (tt + 1) * TT
                tsl = slice(t0_, t1_)
                for part, outb, sccol, ctab in (("q", QT, QSC, 0), ("k", KT, KSC, 2),
                                                ("v", VT, None, None)):
                    if part == "v":
                        for tc4 in range(4):
                            t128 = slice(t0_ + tc4 * 128, t0_ + tc4 * 128 + 128)
                            ctx.wait(t, "v"); ctx.wait(t, "s")
                            for cc in range(NCH):
                                i = t.matmul(ps_q[:, :],
                                             shifted_rhs(cc, (t128.start, t128.stop)),
                                             WQ[:, cc, 1024:1536],
                                             start=(cc == 0), stop=(cc == NCH - 1))
                            ctx.tv(i)
                            ctx.wait(v, "t")
                            i = v.tensor_copy(VT[:, (t128.start // 128) % 4, :],
                                              ps_q[:, :]); ctx.vv(i)
                            ctx.wait(sy, "v")
                            i = sy.dma_start(out=d["o_v"][t128.start:t128.stop, :],
                                             in_=VT[:, (t128.start // 128) % 4, :])
                            ctx.inc("sttok", i, 16)
                            ctx.wait(v, "sttok", ctx.val("sttok") - 16)
                        continue
                    for oc in range(4):
                        colbase = {"q": 0, "k": 512}[part] + oc * 128
                        ctx.wait(t, "s"); ctx.wait(t, "v")
                        for cc in range(NCH):
                            i = t.matmul(ps_q[:, :], WQ[:, cc, colbase:colbase + 128],
                                         shifted_rhs(cc, (t0_, t1_)),
                                         start=(cc == 0), stop=(cc == NCH - 1))
                        ctx.tv(i)
                        ctx.wait(v, "t")
                        i = v.tensor_copy(QRAW[:, :], ps_q[:, :]); ctx.vv(i)
                        ctx.wait(s, "v")
                        i = s.square(SQT[:, :], QRAW[:, :]); ctx.sv(i)
                        ctx.wait(t, "s")
                        i = t.matmul(ps_n[:, :], HSEL[:, :], SQT[:, :], start=True, stop=True)
                        ctx.tv(i)
                        ctx.wait(s, "t")
                        i = s.activation(RN[:, :], ps_n[:, :], AF.Sqrt); ctx.sv(i)
                        ctx.wait(v, "s")
                        i = v.tensor_scalar_max(RN[:, :], RN[:, :], 1e-12); ctx.vv(i)
                        ctx.wait(v, "v")
                        i = v.reciprocal(RN[:, :], RN[:, :]); ctx.vv(i)
                        ctx.wait(v, "v")
                        i = v.tensor_copy(RNB[:, :], RN[:, :]); ctx.vv(i)
                        ctx.wait(t, "v")
                        i = t.matmul(ps_rb[:, :], BSEL[:, :], RNB[:, :], start=True, stop=True)
                        ctx.tv(i)
                        ctx.wait(v, "t")
                        i = v.tensor_mul(QRAW[:, :], QRAW[:, :], ps_rb[:, :]); ctx.vv(i)
                        ctx.wait(s, "v")
                        i = s.activation(QRAW[:, :], QRAW[:, :], AF.Copy,
                                         scale=sccol[:, 0:1])
                        ctx.sv(i)
                        ctx.wait(t, "s")
                        i = t.matmul(ps_rot[:, :], ROTP[:, :], QRAW[:, :], start=True, stop=True)
                        ctx.tv(i)
                        ctx.wait(v, "t")
                        i = v.tensor_copy(ROTS[:, :], ps_rot[:, :]); ctx.vv(i)
                        ctx.wait(v, "v")
                        i = v.tensor_mul(M1[:, :], QRAW[:, :], TAB[:, ctab, tsl]); ctx.vv(i)
                        ctx.wait(v, "v")
                        i = v.tensor_mul(ROTS[:, :], ROTS[:, :], TAB[:, ctab + 1, tsl]); ctx.vv(i)
                        ctx.wait(v, "v")
                        i = v.tensor_add(outb[:, oc, tsl], M1[:, :], ROTS[:, :]); ctx.vv(i)
                        ctx.wait(t, "v"); ctx.wait(s, "v")
        ctx.wait(sy, "v")
        for nm, buf in (("o_q", QT), ("o_k", KT)):
            i = sy.dma_start(out={"o_q": o_q, "o_k": o_k}[nm].rearrange(
                "(r p) w -> p r w", p=128), in_=buf[:, :, :])
            ctx.inc("stx", i, 16)
        ctx.wait(sy, "stx"); ctx.wait(sy, "sttok")

def build_u4():
    nc = bass.Bass("TRN2", target_bir_lowering=False)
    xl = nc.dram_tensor("xl", [C, T], BF16, kind="ExternalInput").ap()
    halo = nc.dram_tensor("halo", [C, 1], BF16, kind="ExternalInput").ap()
    wqkv = nc.dram_tensor("wqkv", [C, 1536], F32, kind="ExternalInput").ap()
    qscol = nc.dram_tensor("qscol", [128, 1], F32, kind="ExternalInput").ap()
    kscol = nc.dram_tensor("kscol", [128, 1], F32, kind="ExternalInput").ap()
    rotp = nc.dram_tensor("rotp", [128, 128], F32, kind="ExternalInput").ap()
    bselin = nc.dram_tensor("bsel", [2, 128], F32, kind="ExternalInput").ap()
    tabs = nc.dram_tensor("tabs", [4, 128, T], F32, kind="ExternalInput").ap()
    o_q = nc.dram_tensor("o_q", [512, T], BF16, kind="ExternalOutput").ap()
    o_k = nc.dram_tensor("o_k", [512, T], BF16, kind="ExternalOutput").ap()
    o_v = nc.dram_tensor("o_v", [T, 512], BF16, kind="ExternalOutput").ap()

    d = {"bselin": bselin, "halo": halo, "kscol": kscol, "o_k": o_k, "o_q": o_q, "o_v": o_v, "qscol": qscol, "rotp": rotp, "tabs": tabs, "wqkv": wqkv, "xl": xl}
    with ExitStack() as st:
        ctx = Ctx(nc, st)
        phase_u4(ctx, d)
    return nc
def _l0ffn_tables(layer_t0):
    half_idx = np.arange(0, 64, 2, dtype=np.float64)
    inv_freq = 1.0 / 10000.0 ** (half_idx / 64.0)
    tpos = np.arange(layer_t0, layer_t0 + T, dtype=np.float64)
    freqs = tpos[:, None] * inv_freq[None, :]
    base = (half_idx + 0.4 * 64) / (1.4 * 64)
    power = (tpos - TFULL // 2) / 256.0
    xs = base[None, :] ** power[:, None]
    cos, sin = np.cos(freqs), np.sin(freqs)
    def mk(a):
        p64 = np.concatenate([a, a], axis=1)
        return np.tile(p64, (1, 2)).T.astype(np.float32).copy()
    return np.stack([mk(cos * xs * 8.0), mk(sin * xs * 8.0),
                     mk(cos / xs), mk(sin / xs)])


def _rotperm_host():
    lh = np.zeros((128, 128), np.float32)
    for blk in range(2):
        o = 64 * blk
        for d in range(32):
            lh[o + d + 32, o + d] = -1.0
            lh[o + d, o + d + 32] = 1.0
    return lh


HALO = 512
BKEYS = HALO + T


def phase_u5(ctx, d):
    nc = ctx.nc
    ind = d["ind"]
    ksrc = d.get("ksrc")
    lam_a = d["lam_a"]
    lam_b = d["lam_b"]
    mcb = d["mcb"]
    o_xm = d["o_xm"]
    qin = d["qin"]
    vsrc = d.get("vsrc")
    woin = d["woin"]
    xlin = d["xlin"]
    with ctx.scope("u5"):
        g, sy, t, v, s = ctx.g, ctx.sy, ctx.t, ctx.v, ctx.s
        QT = ctx.sbuf("QT", [128, 4, T], BF16)
        XL = ctx.sbuf("XL", [128, NCH, T], BF16)
        OFF = ctx.sbuf("OFF", [128, 4, T], BF16)
        WO = ctx.sbuf("WO", [128, 4, C], BF16)
        qhead = ctx.sbuf("qhead", [66, T], BF16)
        khead = ctx.sbuf("khead", [66, BKEYS], BF16)
        vhead = ctx.sbuf("vhead", [128, BKEYS // 128, 128], BF16)
        PTtB = [ctx.sbuf(f"PTt{i_}", [128, TT], BF16) for i_ in range(2)]
        densf = ctx.sbuf("densf", [128, TT], F32)
        densb = ctx.sbuf("densb", [128, TT], BF16)
        LAMA = ctx.sbuf("LAMA", [128, 2 * 128], F32)
        LAMB = ctx.sbuf("LAMB", [128, 8 * TT], F32)
        MC = ctx.sbuf("MC", [128, 16], F32)
        MCCOL = ctx.sbuf("MCCOL", [128, 2], F32)
        BIGROW = ctx.sbuf("BIGROW", [1, T], BF16)
        ones2 = ctx.sbuf("ones2", [128, 128], BF16)

        i = sy.dma_start(out=QT[:, :, :], in_=qin.rearrange("(r p) w -> p r w", p=128))
        ctx.inc("ld", i, 16)
        i = sy.dma_start(out=XL[:, :, :], in_=xlin.rearrange("(r p) w -> p r w", p=128))
        ctx.inc("ld", i, 16)
        i = sy.dma_start(out=LAMA[:, :], in_=lam_a[:, :]); ctx.inc("ld", i, 16)
        i = sy.dma_start(out=LAMB[:, :], in_=lam_b[:, :]); ctx.inc("ld", i, 16)
        load_weight(ctx, WO, woin)
        i = v.memset(khead[64:65, :], -1.0); ctx.vv(i)
        i = v.memset(BIGROW[:, :], BIGNEG); ctx.vv(i)
        i = v.memset(ones2[:, :], 1.0); ctx.vv(i)
        ctx.wait(sy, "v")
        i = sy.dma_start(out=qhead[65:66, :], in_=BIGROW[0:1, :]); ctx.inc("ld", i, 16)
        i = sy.dma_start(out=khead[65:66, :], in_=ind[0:1, :]); ctx.inc("ld", i, 16)
        ctx.wait(t, "ld"); ctx.wait(s, "ld"); ctx.wait(v, "ld")

        with ExitStack() as pp:
            ps_scB = [pp.enter_context(ctx.nc.psum_tensor(ctx.uniq(f"ps_sc{i_}"), [128, TT], F32))
                      for i_ in range(2)]
            ps_sc = ps_scB[0]
            ps_a2 = pp.enter_context(ctx.nc.psum_tensor(ctx.uniq("ps_a2"), [128, 128], F32))
            ps_pv = pp.enter_context(ctx.nc.psum_tensor(ctx.uniq("ps_pv"), [128, TT], F32))
            ps_bc = pp.enter_context(ctx.nc.psum_tensor(ctx.uniq("ps_bc"), [128, TT], F32))
            for h in range(8):
                r0 = 64 * (h % 2)
                r1 = 64 * (1 - (h % 2))
                ctx.wait(sy, "t"); ctx.wait(sy, "v"); ctx.wait(g, "t"); ctx.wait(g, "v")
                ctx.wait(sy, "t_pv", ctx.val("t_pv"))
                ctx.wait(g, "t_pv", ctx.val("t_pv"))
                if d.get("stage_k") is None:
                    i = sy.dma_start(out=khead[0:64, :], in_=ksrc[64 * h:64 * h + 64, :])
                    ctx.inc("ldf", i, 16)
                    i = sy.dma_start(out=vhead[:, :, :],
                                     in_=vsrc[h, :, :].rearrange("(r p) w -> p r w", p=128))
                    ctx.inc("ldf", i, 16)
                else:
                    d["stage_k"](h, khead)
                    d["stage_v"](h, vhead)
                i = sy.dma_start(out=qhead[0:64, :],
                                 in_=QT[64 * (h % 2):64 * (h % 2) + 64, h // 2, :])
                ctx.inc("ldf", i, 16)
                ctx.wait(v, "ldf")
                i = v.memset(qhead[64:65, :], 0.0); ctx.vv(i)
                ctx.wait(t, "v"); ctx.wait(t, "ldf")
                for gq in range(16):
                    p0 = gq * 128
                    i = t.matmul(ps_sc[:, :], qhead[0:66, p0:p0 + 128],
                                 khead[0:66, p0:p0 + 512], start=True, stop=True)
                    i = t.matmul(ps_a2[:, :], qhead[0:66, p0:p0 + 128],
                                 khead[0:66, p0 + 512:p0 + 640], start=True, stop=True)
                    ctx.tv(i)
                    ctx.wait(v, "t")
                    i = v.tensor_add(ps_sc[:, 0:128], ps_sc[:, 0:128], LAMA[:, 0:128])
                    ctx.vv(i)
                    ctx.wait(v, "v")
                    i = v.tensor_add(ps_a2[:, :], ps_a2[:, :], LAMA[:, 128:256]); ctx.vv(i)
                    ctx.wait(v, "v")
                    i = v.reduce_max(MCCOL[:, 0:1], ps_sc[:, :], axis=AX.X); ctx.vv(i)
                    ctx.wait(v, "v")
                    i = v.reduce_max(MCCOL[:, 1:2], ps_a2[:, :], axis=AX.X); ctx.vv(i)
                    ctx.wait(v, "v")
                    i = v.tensor_max(MC[:, gq:gq + 1], MCCOL[:, 0:1], MCCOL[:, 1:2]); ctx.vv(i)
                    ctx.wait(t, "v")
                ctx.wait(g, "v"); ctx.wait(g, "t")
                i = g.dma_start(out=mcb[:, :], in_=MC[:, :]); ctx.inc("ldf", i, 16)
                ctx.wait(g, "ldf")
                i = g.dma_start(out=qhead[64:65, :],
                                in_=mcb[:, :].transpose([1, 0]), allow_slow_non_contiguous=True)
                ctx.inc("ldf", i, 16)
                ctx.wait(t, "ldf")
                for qt in range(NTT):
                    tsl = slice(qt * TT, (qt + 1) * TT)
                    s_ex = [None] * 8
                    v_mk = [None] * 8
                    pv_ct = [None] * 8

                    def emit_qk5(kc):
                        c0 = qt * TT + kc * 128
                        if kc >= 2:
                            ctx.wait(t, "s", s_ex[kc - 2])
                        i = t.matmul(ps_scB[kc % 2][:, :], khead[0:66, c0:c0 + 128],
                                     qhead[0:66, tsl], start=True, stop=True)
                        tq = ctx.tv(i)
                        ctx.wait(v, "t", tq)
                        if kc >= 2:
                            ctx.wait(v, "s", s_ex[kc - 2])
                        i = v.tensor_add(ps_scB[kc % 2][:, :], ps_scB[kc % 2][:, :],
                                         LAMB[:, kc * TT:(kc + 1) * TT])
                        v_mk[kc] = ctx.vv(i)
                        ctx.wait(s, "v", v_mk[kc])
                        if kc >= 2:
                            ctx.wait(s, "t_pv", pv_ct[kc - 2])
                        i = s.activation(PTtB[kc % 2][:, :], ps_scB[kc % 2][:, :],
                                         AF.Exp)
                        s_ex[kc] = ctx.sv(i)

                    def emit_pv5(kc):
                        c0 = qt * TT + kc * 128
                        ctx.wait(t, "s", s_ex[kc])
                        i = t.matmul(ps_pv[:, :], vhead[:, (c0 // 128), :],
                                     PTtB[kc % 2][:, :],
                                     start=(kc == 0), stop=(kc == 7))
                        pv_ct[kc] = ctx.inc("t_pv", i, 1)

                    emit_qk5(0)
                    for kc in range(1, 8):
                        emit_qk5(kc)
                        emit_pv5(kc - 1)
                    emit_pv5(7)
                    ctx.wait(v, "t_pv")
                    i = v.reciprocal(densf[r1:r1 + 1, :], ps_pv[r1:r1 + 1, :]); ctx.vv(i)
                    ctx.wait(v, "v")
                    i = v.tensor_copy(densb[r1:r1 + 1, :], densf[r1:r1 + 1, :]); ctx.vv(i)
                    ctx.wait(t, "v")
                    i = t.matmul(ps_bc[:, :], ones2[r1:r1 + 1, :], densb[r1:r1 + 1, :],
                                 start=True, stop=True)
                    ctx.tv(i)
                    ctx.wait(v, "t")
                    i = v.tensor_copy(OFF[r0:r0 + 64, h // 2, tsl], ps_pv[r0:r0 + 64, :])
                    ctx.vv(i)
                    ctx.wait(v, "v")
                    i = v.tensor_mul(OFF[r0:r0 + 64, h // 2, tsl],
                                     OFF[r0:r0 + 64, h // 2, tsl], ps_bc[r0:r0 + 64, :])
                    ctx.vv(i)
                    ctx.wait(t, "v")
        with ExitStack() as pp:
            ps_o = pp.enter_context(ctx.nc.psum_tensor(ctx.uniq("ps_wo"), [128, TT], F32))
            for tt in range(NTT):
                tsl = slice(tt * TT, (tt + 1) * TT)
                for oc in range(NCH):
                    ctx.wait(t, "v")
                    for kc4 in range(4):
                        i = t.matmul(ps_o[:, :], WO[:, kc4, oc * 128:(oc + 1) * 128],
                                     OFF[:, kc4, tsl], start=(kc4 == 0), stop=(kc4 == 3))
                    ctx.tv(i)
                    ctx.wait(v, "t")
                    i = v.tensor_add(XL[:, oc, tsl], XL[:, oc, tsl], ps_o[:, :]); ctx.vv(i)
        ctx.wait(sy, "v")
        i = sy.dma_start(out=o_xm.rearrange("(r p) w -> p r w", p=128), in_=XL[:, :, :])
        ctx.inc("stx", i, 16)
        ctx.wait(sy, "stx")

def build_u5():
    nc = bass.Bass("TRN2", target_bir_lowering=False)
    qin = nc.dram_tensor("q", [512, T], BF16, kind="ExternalInput").ap()
    ksrc = nc.dram_tensor("ksrc", [512, BKEYS], BF16, kind="ExternalInput").ap()
    vsrc = nc.dram_tensor("vsrc", [8, BKEYS, 128], BF16, kind="ExternalInput").ap()
    ind = nc.dram_tensor("ind", [1, BKEYS], BF16, kind="ExternalInput").ap()
    lam_a = nc.dram_tensor("lam_a", [128, 2 * 128], F32, kind="ExternalInput").ap()
    lam_b = nc.dram_tensor("lam_b", [128, 8 * TT], F32, kind="ExternalInput").ap()
    xlin = nc.dram_tensor("xl", [C, T], BF16, kind="ExternalInput").ap()
    woin = nc.dram_tensor("wo", [512, C], F32, kind="ExternalInput").ap()
    o_xm = nc.dram_tensor("o_xm", [C, T], BF16, kind="ExternalOutput").ap()
    mcb = nc.dram_tensor("mcb", [128, 16], F32, kind="Internal").ap()

    d = {"ind": ind, "ksrc": ksrc, "lam_a": lam_a, "lam_b": lam_b, "mcb": mcb, "o_xm": o_xm, "qin": qin, "vsrc": vsrc, "woin": woin, "xlin": xlin}
    with ExitStack() as st:
        ctx = Ctx(nc, st)
        phase_u5(ctx, d)
    return nc
def _la_masks_host():
    a = np.zeros((128, 256), np.float32)
    for r in range(128):
        a[r, 0:r] = BIGNEG
        a[r, 128 + r + 1:256] = BIGNEG
    bmask = np.zeros((128, 8 * TT), np.float32)
    for kc in range(8):
        tile = bmask[:, kc * TT:(kc + 1) * TT]
        for q in range(TT):
            qsub, qr = q // 128, q % 128
            dbase = 128 * (kc - 4 - qsub) - qr
            srel = np.arange(128)
            diff = dbase + srel
            bad = (diff < -512) | (diff > 0)
            tile[bad, q] = BIGNEG
    return a, bmask


W1CH = [(i * 128, min(128, 2730 - i * 128)) for i in range(22)]


def phase_u6(ctx, d, final=False):
    nc = ctx.nc
    b1a = d["b1a"]
    b1g = d["b1g"]
    b2c = d["b2c"]
    bcol = d["bcol"]
    gcol = d["gcol"]
    halo = d["halo"]
    o_hln = d["o_hln"]
    o_out = d["o_out"]
    w1in = d["w1in"]
    w2in = d["w2in"]
    x2in = d.get("x2in")
    xmin = d["xmin"]
    with ctx.scope("u6"):
        g, sy, t, v, s = ctx.g, ctx.sy, ctx.t, ctx.v, ctx.s
        XME = ctx.sbuf("XME", [128, NCH, T + 1], BF16)
        HLN = ctx.sbuf("HLN", [128, NCH, T], BF16)
        X1 = ctx.sbuf("X1", [128, NCH, T], BF16)
        HS = ctx.sbuf("HS", [128, 22, TT], BF16)
        WA = ctx.sbuf("WA", [128, NCH, 256], BF16)
        WG = ctx.sbuf("WG", [128, NCH, 256], BF16)
        W2B = ctx.sbuf("W2B", [128, 2, C], BF16)
        GC = ctx.sbuf("GC", [128, NCH], F32)
        BC = ctx.sbuf("BC", [128, NCH], F32)
        B1A = ctx.sbuf("B1A", [128, 22], F32)
        B1G = ctx.sbuf("B1G", [128, 22], F32)
        B2C = ctx.sbuf("B2C", [128, NCH], F32)
        SQ = ctx.sbuf("SQ", [128, TT], BF16)
        SQ2 = [SQ, ctx.sbuf("SQb", [128, TT], BF16)]
        TMP32 = ctx.sbuf("TMP32", [128, TT], F32)
        TMPB = ctx.sbuf("TMPB", [128, TT], BF16)
        TMPB2 = [TMPB, ctx.sbuf("TMPBb", [128, TT], BF16)]
        X2TILE = ctx.sbuf("X2TILE", [128, TT], F32)
        _ssum = ctx.sbuf("ssum", [1, T], F32)
        _ssq = ctx.sbuf("ssq", [1, T], F32)
        _rs = ctx.sbuf("rs", [1, T], F32)
        rows = {"ssum": _ssum, "ssq": _ssq, "svar": _ssq, "rs": _rs}
        ROWSB = [ctx.sbuf(f"ROWSB{i_}", [1, T], BF16) for i_ in range(2)]
        EPS11 = ctx.sbuf("EPS11", [1, 1], F32)
        ones128b = ctx.sbuf("ones128b", [128, 1], BF16)
        ones1 = ctx.sbuf("ones1", [1, 128], BF16)

        i = sy.dma_start(out=XME[:, :, 1:T + 1], in_=xmin.rearrange("(r p) w -> p r w", p=128))
        ctx.inc("ld", i, 16)
        i = sy.dma_start(out=XME[:, :, 0:1], in_=halo.rearrange("(r p) w -> p r w", p=128),
                         allow_slow_non_contiguous=True)
        ctx.inc("ld", i, 16)
        for dst, src in ((GC, gcol), (BC, bcol), (B1A, b1a), (B1G, b1g), (B2C, b2c)):
            i = sy.dma_start(out=dst[:, :], in_=src[:, :]); ctx.inc("ld", i, 16)
        i = v.memset(ones128b[:, :], 1.0); ctx.vv(i)
        i = v.memset(ones1[:, :], 1.0); ctx.vv(i)
        i = v.memset(EPS11[:, :], EPS); ctx.vv(i)
        ctx.wait(t, "ld"); ctx.wait(s, "ld"); ctx.wait(v, "ld")
        ctx.wait(t, "v"); ctx.wait(s, "v")

        def sh(c, tsl):
            if c < 4:
                return XME[:, c, tsl.start + 1:tsl.stop + 1]
            return XME[:, c, tsl.start:tsl.stop]

        with ExitStack() as pp:
            ps_a = pp.enter_context(ctx.nc.psum_tensor(ctx.uniq("ps_a"), [1, TT], F32))
            ps_b = pp.enter_context(ctx.nc.psum_tensor(ctx.uniq("ps_b"), [1, TT], F32))
            ps_mu = pp.enter_context(ctx.nc.psum_tensor(ctx.uniq("ps_mu"), [128, TT], F32))
            ps_r = pp.enter_context(ctx.nc.psum_tensor(ctx.uniq("ps_r"), [128, TT], F32))
            emit_ln_stats(ctx, sh, rows, ROWSB + [None], SQ, ones128b,
                          (ps_a, ps_b), double_ln=False, eps11=EPS11)
            emit_ln_apply(ctx, sh, ROWSB, 1, HLN, ones1, (ps_mu, ps_r, None),
                          TMP32, gain_bias=(GC, BC))
            ctx.wait(t, "v"); ctx.wait(t, "s")
            ctx.wait(sy, "v")
            i = sy.dma_start(out=o_hln.rearrange("(r p) w -> p r w", p=128), in_=HLN[:, :, :])
            ctx.inc("stx", i, 16)

        for tt in range(NTT):
            tsl = slice(tt * TT, (tt + 1) * TT)
            with ExitStack() as pp:
                ps_paP = [pp.enter_context(ctx.nc.psum_tensor(ctx.uniq(f"pa{tt}_{i_}"), [128, TT], F32)) for i_ in range(2)]
                ps_pgP = [pp.enter_context(ctx.nc.psum_tensor(ctx.uniq(f"pg{tt}_{i_}"), [128, TT], F32)) for i_ in range(2)]
                mm_ct = [None] * 22
                hs_ct = [None] * 22
                ge_ct = [None] * 22
                for gi, (off, M) in enumerate(W1CH):
                    db = gi % 2
                    if gi >= 2:
                        ctx.wait(g, "t", mm_ct[gi - 2])
                    i = g.dma_start(out=WA[:, :, db * 128:db * 128 + M],
                                    in_=w1in[:, off:off + M].rearrange("(r p) w -> p r w", p=128))
                    ctx.inc("lds", i, 16)
                    i = g.dma_start(out=WG[:, :, db * 128:db * 128 + M],
                                    in_=w1in[:, 2730 + off:2730 + off + M].rearrange(
                                        "(r p) w -> p r w", p=128))
                    ld2 = ctx.inc("lds", i, 16)
                    ctx.wait(t, "lds", ld2)
                    if gi >= 2:
                        ctx.wait(t, "v_hs", hs_ct[gi - 2])
                    for cc in range(NCH):
                        t.matmul(ps_paP[db][0:M, :], WA[:, cc, db * 128:db * 128 + M],
                                 HLN[:, cc, tsl], start=(cc == 0), stop=(cc == NCH - 1))
                        i = t.matmul(ps_pgP[db][0:M, :], WG[:, cc, db * 128:db * 128 + M],
                                     HLN[:, cc, tsl], start=(cc == 0), stop=(cc == NCH - 1))
                    mm_ct[gi] = ctx.tv(i)
                    ctx.wait(s, "t", mm_ct[gi])
                    if gi >= 1:
                        ctx.wait(s, "v_hs", hs_ct[gi - 1])
                    i = s.activation(TMPB2[db][:, :], ps_pgP[db][:, :], AF.Gelu,
                                     bias=B1G[:, gi:gi + 1])
                    ge_ct[gi] = ctx.sv(i)
                    ctx.wait(v, "s", ge_ct[gi])
                    i = v.tensor_scalar_add(SQ2[db][:, :], ps_paP[db][:, :], B1A[:, gi:gi + 1])
                    ctx.vv(i)
                    ctx.wait(v, "v")
                    i = v.tensor_mul(HS[:, gi, :], SQ2[db][:, :], TMPB2[db][:, :])
                    hs_ct[gi] = ctx.inc("v_hs", i, 1)
                ctx.wait(t, "v_hs")
            with ExitStack() as pp:
                ps8 = [pp.enter_context(ctx.nc.psum_tensor(ctx.uniq(f"po{tt}_{o}"), [128, TT], F32))
                       for o in range(NCH)]
                w2ct = [None] * 22
                for gi, (off, M) in enumerate(W1CH):
                    db = gi % 2
                    if gi >= 2:
                        ctx.wait(g, "t", w2ct[gi - 2])
                    i = g.dma_start(out=W2B[0:M, db, :], in_=w2in[off:off + M, :])
                    ld2 = ctx.inc("lds", i, 16)
                    ctx.wait(t, "lds", ld2)
                    for oc in range(NCH):
                        i = t.matmul(ps8[oc][:, :], W2B[0:M, db, oc * 128:(oc + 1) * 128],
                                     HS[0:M, gi, :], start=(gi == 0), stop=(gi == 21))
                    w2ct[gi] = ctx.tv(i)
                ctx.wait(v, "t")
                for oc in range(NCH):
                    i = v.tensor_scalar_add(TMPB[:, :], ps8[oc][:, :], B2C[:, oc:oc + 1])
                    ctx.vv(i)
                    ctx.wait(v, "v")
                    i = v.tensor_add(X1[:, oc, tsl], XME[:, oc, tsl.start + 1:tsl.stop + 1],
                                     TMPB[:, :]); ctx.vv(i)
                    ctx.wait(v, "v")
                ctx.wait(t, "v"); ctx.wait(g, "v")
        if not final:
            ctx.wait(sy, "v")
            i = sy.dma_start(out=o_out.rearrange("(r p) w -> p r w", p=128), in_=X1[:, :, :])
            ctx.inc("stx", i, 16)
            ctx.wait(sy, "stx")
        else:
            with ExitStack() as pp:
                ps_a = pp.enter_context(ctx.nc.psum_tensor(ctx.uniq("ps_a2"), [1, TT], F32))
                ps_b = pp.enter_context(ctx.nc.psum_tensor(ctx.uniq("ps_b2"), [1, TT], F32))
                emit_ln_stats(ctx, X1, rows, ROWSB + [None], SQ, ones128b,
                              (ps_a, ps_b), double_ln=False, eps11=EPS11)
            emit_ln_apply_residual(ctx, X1, ROWSB, x2in, o_out, ones1, TMP32, X2TILE)

def build_u6(final=False):
    nc = bass.Bass("TRN2", target_bir_lowering=False)
    xmin = nc.dram_tensor("xm", [C, T], BF16, kind="ExternalInput").ap()
    halo = nc.dram_tensor("halo", [C, 1], BF16, kind="ExternalInput").ap()
    gcol = nc.dram_tensor("gcol", [128, NCH], F32, kind="ExternalInput").ap()
    bcol = nc.dram_tensor("bcol", [128, NCH], F32, kind="ExternalInput").ap()
    w1in = nc.dram_tensor("w1", [C, 5460], F32, kind="ExternalInput").ap()
    b1a = nc.dram_tensor("b1a", [128, 22], F32, kind="ExternalInput").ap()
    b1g = nc.dram_tensor("b1g", [128, 22], F32, kind="ExternalInput").ap()
    w2in = nc.dram_tensor("w2", [2730, C], F32, kind="ExternalInput").ap()
    b2c = nc.dram_tensor("b2c", [128, NCH], F32, kind="ExternalInput").ap()
    if final:
        x2in = nc.dram_tensor("x2T", [C, T], F32, kind="ExternalInput").ap()
        o_out = nc.dram_tensor("o_x3", [C, T], F32, kind="ExternalOutput").ap()
    else:
        o_out = nc.dram_tensor("o_x1", [C, T], BF16, kind="ExternalOutput").ap()
    o_hln = nc.dram_tensor("o_hln", [C, T], BF16, kind="ExternalOutput").ap()

    d = {"b1a": b1a, "b1g": b1g, "b2c": b2c, "bcol": bcol, "gcol": gcol,
         "halo": halo, "o_hln": o_hln, "o_out": o_out, "w1in": w1in,
         "w2in": w2in, "xmin": xmin}
    if final:
        d["x2in"] = x2in
    with ExitStack() as st:
        ctx = Ctx(nc, st)
        phase_u6(ctx, d, final)
    return nc
def phase_u10(ctx, d):
    nc = ctx.nc
    o_k = d["o_k"]
    o_q = d["o_q"]
    o_qn = d["o_qn"]
    o_v = d["o_v"]
    qselin = d["qselin"]
    rotp = d["rotp"]
    tabs = d["tabs"]
    wkbp = d["wkbp"]
    wkva = d["wkva"]
    wqa = d["wqa"]
    wqbp = d["wqbp"]
    wvb = d["wvb"]
    x3in = d["x3in"]
    with ctx.scope("u10"):
        g, sy, t, v, s = ctx.g, ctx.sy, ctx.t, ctx.v, ctx.s
        X3 = ctx.sbuf("X3", [128, NCH, T], BF16)
        ZT = ctx.sbuf("ZT", [128, NCH, T], BF16)
        QTL = ctx.sbuf("QTL", [128, NCH, T], BF16)
        KTL = ctx.sbuf("KTL", [128, 2, T], BF16)
        QAT = ctx.sbuf("QAT", [64, T], BF16)
        KVAT = ctx.sbuf("KVAT", [32, T], BF16)
        WQA = ctx.sbuf("WQA", [128, NCH, 64], BF16)
        WQB = ctx.sbuf("WQB", [64, C], BF16)
        WKVA = ctx.sbuf("WKVA", [128, NCH, 32], BF16)
        WKB = ctx.sbuf("WKB", [32, 256], BF16)
        WVB = ctx.sbuf("WVB", [32, 256], BF16)
        TAB = ctx.sbuf("TAB", [128, 2, T], BF16)
        ROTP = ctx.sbuf("ROTP", [128, 128], BF16)
        QSEL = ctx.sbuf("QSEL", [128, NCH * 16], BF16)
        QRAW = ctx.sbuf("QRAW", [128, TT], BF16)
        ROTS = ctx.sbuf("ROTS", [128, TT], BF16)
        M1 = ctx.sbuf("M1", [128, TT], BF16)
        SQ = ctx.sbuf("SQ", [128, TT], BF16)
        M8 = ctx.sbuf("M8", [16, T], F32)
        VTOK = ctx.sbuf("VTOK", [128, 1, 256], BF16)
        _ssum = ctx.sbuf("ssum", [1, T], F32)
        _ssq = ctx.sbuf("ssq", [1, T], F32)
        _rs = ctx.sbuf("rs", [1, T], F32)
        rows = {"ssum": _ssum, "ssq": _ssq, "svar": _ssq, "rs": _rs}
        ROWSB = [ctx.sbuf(f"ROWSB{i_}", [1, T], BF16) for i_ in range(2)]
        TMP32 = ctx.sbuf("TMP32", [128, TT], F32)
        EPS11 = ctx.sbuf("EPS11", [1, 1], F32)
        ones128b = ctx.sbuf("ones128b", [128, 1], BF16)
        ones1 = ctx.sbuf("ones1", [1, 128], BF16)

        i = g.dma_start(out=X3[:, :, :], in_=x3in.rearrange("(r p) w -> p r w", p=128))
        ctx.inc("ld", i, 16)
        load_weight(ctx, WQA, wqa)
        load_weight(ctx, WKVA, wkva)
        i = g.dma_start(out=WQB[:, :], in_=wqbp[:, :]); ctx.inc("ld", i, 16)
        i = g.dma_start(out=WKB[:, :], in_=wkbp[:, :]); ctx.inc("ld", i, 16)
        i = g.dma_start(out=WVB[:, :], in_=wvb[:, :]); ctx.inc("ld", i, 16)
        i = g.dma_start(out=TAB[:, :, :], in_=tabs.transpose([1, 0, 2])); ctx.inc("ld", i, 16)
        i = g.dma_start(out=ROTP[:, :], in_=rotp[:, :]); ctx.inc("ld", i, 16)
        i = g.dma_start(out=QSEL[:, :], in_=qselin[:, :]); ctx.inc("ld", i, 16)
        i = v.memset(ones128b[:, :], 1.0); ctx.vv(i)
        i = v.memset(ones1[:, :], 1.0); ctx.vv(i)
        i = v.memset(EPS11[:, :], EPS); ctx.vv(i)
        ctx.wait(t, "ld"); ctx.wait(s, "ld"); ctx.wait(v, "ld")
        ctx.wait(t, "v"); ctx.wait(s, "v")

        with ExitStack() as pp:
            ps_a = pp.enter_context(ctx.nc.psum_tensor(ctx.uniq("ps_a"), [1, TT], F32))
            ps_b = pp.enter_context(ctx.nc.psum_tensor(ctx.uniq("ps_b"), [1, TT], F32))
            ps_mu = pp.enter_context(ctx.nc.psum_tensor(ctx.uniq("ps_mu"), [128, TT], F32))
            ps_r = pp.enter_context(ctx.nc.psum_tensor(ctx.uniq("ps_r"), [128, TT], F32))
            emit_ln_stats(ctx, X3, rows, ROWSB + [None], SQ, ones128b,
                          (ps_a, ps_b), double_ln=False, eps11=EPS11)
            emit_ln_apply(ctx, X3, ROWSB, 1, ZT, ones1, (ps_mu, ps_r, None), TMP32)
            ctx.wait(t, "v")
        with ExitStack() as pp:
            ps_qa = pp.enter_context(ctx.nc.psum_tensor(ctx.uniq("ps_qa"), [64, TT], F32))
            ps_kva = pp.enter_context(ctx.nc.psum_tensor(ctx.uniq("ps_kva"), [32, TT], F32))
            ps_q = pp.enter_context(ctx.nc.psum_tensor(ctx.uniq("ps_q"), [128, TT], F32))
            ps_rot = pp.enter_context(ctx.nc.psum_tensor(ctx.uniq("ps_rot"), [128, TT], F32))
            ps_qn = pp.enter_context(ctx.nc.psum_tensor(ctx.uniq("ps_qn"), [16, TT], F32))
            ps_v = pp.enter_context(ctx.nc.psum_tensor(ctx.uniq("ps_v"), [128, 256], F32))
            for tt in range(NTT):
                tsl = slice(tt * TT, (tt + 1) * TT)
                ctx.wait(t, "v"); ctx.wait(t, "s")
                for cc in range(NCH):
                    t.matmul(ps_qa[:, :], WQA[:, cc, :], ZT[:, cc, tsl],
                             start=(cc == 0), stop=(cc == NCH - 1))
                    i = t.matmul(ps_kva[:, :], WKVA[:, cc, :], ZT[:, cc, tsl],
                                 start=(cc == 0), stop=(cc == NCH - 1))
                ctx.tv(i)
                ctx.wait(v, "t")
                i = v.tensor_copy(QAT[:, tsl], ps_qa[:, :]); ctx.vv(i)
                i = v.tensor_copy(KVAT[:, tsl], ps_kva[:, :]); ctx.vv(i)
                ctx.wait(t, "v")
            for tt in range(NTT):
                tsl = slice(tt * TT, (tt + 1) * TT)
                for oc in range(NCH):
                    ctx.wait(t, "v"); ctx.wait(t, "s")
                    i = t.matmul(ps_q[:, :], WQB[:, oc * 128:(oc + 1) * 128], QAT[:, tsl],
                                 start=True, stop=True)
                    ctx.tv(i)
                    ctx.wait(v, "t")
                    i = v.tensor_copy(QRAW[:, :], ps_q[:, :]); ctx.vv(i)
                    ctx.wait(t, "v")
                    i = t.matmul(ps_rot[:, :], ROTP[:, :], QRAW[:, :], start=True, stop=True)
                    ctx.tv(i)
                    ctx.wait(v, "t")
                    i = v.tensor_copy(ROTS[:, :], ps_rot[:, :]); ctx.vv(i)
                    ctx.wait(v, "v")
                    i = v.tensor_mul(M1[:, :], QRAW[:, :], TAB[:, 0, tsl]); ctx.vv(i)
                    ctx.wait(v, "v")
                    i = v.tensor_mul(ROTS[:, :], ROTS[:, :], TAB[:, 1, tsl]); ctx.vv(i)
                    ctx.wait(v, "v")
                    i = v.tensor_add(QTL[:, oc, tsl], M1[:, :], ROTS[:, :]); ctx.vv(i)
                    ctx.wait(s, "v")
                    i = s.square(SQ[:, :], QTL[:, oc, tsl]); ctx.sv(i)
                    ctx.wait(t, "s")
                    i = t.matmul(ps_qn[:, :], QSEL[:, oc * 16:(oc + 1) * 16], SQ[:, :],
                                 start=(oc == 0), stop=(oc == NCH - 1))
                    ctx.inc("t_qn", i, 1)
                ctx.wait(s, "t_qn")
                i = s.activation(M8[0:16, tsl], ps_qn[:, :], AF.Sqrt); ctx.sv(i)
                ctx.wait(t, "s")
            for tt in range(NTT):
                tsl = slice(tt * TT, (tt + 1) * TT)
                for oc in range(2):
                    ctx.wait(t, "v"); ctx.wait(t, "s")
                    i = t.matmul(ps_q[:, :], WKB[:, oc * 128:(oc + 1) * 128], KVAT[:, tsl],
                                 start=True, stop=True)
                    ctx.tv(i)
                    ctx.wait(v, "t")
                    i = v.tensor_copy(QRAW[:, :], ps_q[:, :]); ctx.vv(i)
                    ctx.wait(t, "v")
                    i = t.matmul(ps_rot[:, :], ROTP[:, :], QRAW[:, :], start=True, stop=True)
                    ctx.tv(i)
                    ctx.wait(v, "t")
                    i = v.tensor_copy(ROTS[:, :], ps_rot[:, :]); ctx.vv(i)
                    ctx.wait(v, "v")
                    i = v.tensor_mul(M1[:, :], QRAW[:, :], TAB[:, 0, tsl]); ctx.vv(i)
                    ctx.wait(v, "v")
                    i = v.tensor_mul(ROTS[:, :], ROTS[:, :], TAB[:, 1, tsl]); ctx.vv(i)
                    ctx.wait(v, "v")
                    i = v.tensor_add(KTL[:, oc, tsl], M1[:, :], ROTS[:, :]); ctx.vv(i)
                    ctx.wait(t, "v")
            for tc in range(T // 128):
                ctx.wait(t, "v")
                ctx.wait(t, "sttok", ctx.val("sttok"))
                i = t.matmul(ps_v[:, :], KVAT[:, tc * 128:(tc + 1) * 128], WVB[:, :],
                             start=True, stop=True)
                ctx.tv(i)
                ctx.wait(v, "t")
                i = v.tensor_copy(VTOK[:, 0, :], ps_v[:, :]); ctx.vv(i)
                ctx.wait(sy, "v")
                i = sy.dma_start(out=o_v[tc * 128:(tc + 1) * 128, :], in_=VTOK[:, 0, :])
                ctx.inc("sttok", i, 16)
        ctx.wait(sy, "v"); ctx.wait(sy, "s")
        i = sy.dma_start(out=o_q.rearrange("(r p) w -> p r w", p=128), in_=QTL[:, :, :])
        ctx.inc("stx", i, 16)
        i = sy.dma_start(out=o_k.rearrange("(r p) w -> p r w", p=128), in_=KTL[:, :, :])
        ctx.inc("stx", i, 16)
        i = sy.dma_start(out=o_qn[:, :], in_=M8[0:16, :]); ctx.inc("stx", i, 16)
        ctx.wait(sy, "stx"); ctx.wait(sy, "sttok")

def build_u10():
    nc = bass.Bass("TRN2", target_bir_lowering=False)
    x3in = nc.dram_tensor("x3", [C, T], BF16, kind="ExternalInput").ap()
    wqa = nc.dram_tensor("Wqa", [C, 64], F32, kind="ExternalInput").ap()
    wqbp = nc.dram_tensor("WqbP", [64, C], F32, kind="ExternalInput").ap()
    wkva = nc.dram_tensor("Wkva", [C, 32], F32, kind="ExternalInput").ap()
    wkbp = nc.dram_tensor("WkbP", [32, 256], F32, kind="ExternalInput").ap()
    wvb = nc.dram_tensor("Wvb", [32, 256], F32, kind="ExternalInput").ap()
    rotp = nc.dram_tensor("rotp", [128, 128], F32, kind="ExternalInput").ap()
    tabs = nc.dram_tensor("tabs", [2, 128, T], F32, kind="ExternalInput").ap()
    qselin = nc.dram_tensor("qsel", [128, NCH * 16], F32, kind="ExternalInput").ap()
    o_q = nc.dram_tensor("o_q", [C, T], BF16, kind="ExternalOutput").ap()
    o_k = nc.dram_tensor("o_k", [256, T], BF16, kind="ExternalOutput").ap()
    o_v = nc.dram_tensor("o_v", [T, 256], BF16, kind="ExternalOutput").ap()
    o_qn = nc.dram_tensor("o_qn", [16, T], F32, kind="ExternalOutput").ap()

    d = {"o_k": o_k, "o_q": o_q, "o_qn": o_qn, "o_v": o_v, "qselin": qselin, "rotp": rotp, "tabs": tabs, "wkbp": wkbp, "wkva": wkva, "wqa": wqa, "wqbp": wqbp, "wvb": wvb, "x3in": x3in}
    with ExitStack() as st:
        ctx = Ctx(nc, st)
        phase_u10(ctx, d)
    return nc
def phase_u11(ctx, d):
    nc = ctx.nc
    ind = d["ind"]
    kmb = d["kmb"]
    ksrc = d.get("ksrc")
    kvsel = d["kvsel"]
    masks = d["masks"]
    o_o = d["o_o"]
    qin = d["qin"]
    qnin = d["qnin"]
    vsrc = d.get("vsrc")
    with ctx.scope("u11"):
        g, sy, t, v, s = ctx.g, ctx.sy, ctx.t, ctx.v, ctx.s
        QT = ctx.sbuf("QT", [128, NCH, T], BF16)
        OUTF = ctx.sbuf("OUTF", [128, NCH, T], BF16)
        M8 = ctx.sbuf("M8", [16, T], F32)
        qhead2 = [ctx.sbuf(f"qhead{i_}", [128, T], BF16) for i_ in range(2)]
        khead2 = [ctx.sbuf(f"khead{i_}", [128, KEYS], BF16) for i_ in range(2)]
        vhead2 = [ctx.sbuf(f"vhead{i_}", [128, KEYS // 128, 128], BF16)
                  for i_ in range(2)]
        PTt2 = [ctx.sbuf(f"PTt{i_}", [128, 2 * TT], BF16) for i_ in range(2)]
        densf = ctx.sbuf("densf", [128, TT], F32)
        densb = ctx.sbuf("densb", [128, TT], BF16)
        MASKS = ctx.sbuf("MASKS", [128, 4 * TT], F32)
        ones2 = ctx.sbuf("ones2", [128, 128], BF16)
        ones64 = ctx.sbuf("ones64", [64, 1], BF16)
        SQK = ctx.sbuf("SQK", [64, KEYS], BF16)
        KROW = ctx.sbuf("KROW", [1, KEYS], F32)
        KM11 = ctx.sbuf("KM11", [1, 1], F32)
        KMROW = ctx.sbuf("KMROW", [4, 1], BF16)
        KVSEL = ctx.sbuf("KVSEL", [4, 16], BF16)
        MK16 = ctx.sbuf("MK16", [16, 1], F32)
        BIGROW = ctx.sbuf("BIGROW", [1, T], BF16)

        i = sy.dma_start(out=QT[:, :, :], in_=qin.rearrange("(r p) w -> p r w", p=128))
        ctx.inc("ld", i, 16)
        i = sy.dma_start(out=M8[:, :], in_=qnin[:, :]); ctx.inc("ld", i, 16)
        i = sy.dma_start(out=MASKS[:, :], in_=masks[:, :]); ctx.inc("ld", i, 16)
        i = g.dma_start(out=KVSEL[:, :], in_=kvsel[:, :]); ctx.inc("ld", i, 16)
        for b_ in range(2):
            i = v.memset(khead2[b_][64:128, :], 0.0); ctx.vv(i)
            i = v.memset(qhead2[b_][64:128, :], 0.0); ctx.vv(i)
        ctx.wait(v, "v")
        for b_ in range(2):
            i = v.memset(khead2[b_][64:65, :], -1.0); ctx.vv(i)
        i = v.memset(BIGROW[:, :], BIGNEG); ctx.vv(i)
        i = v.memset(ones2[:, :], 1.0); ctx.vv(i)
        i = v.memset(ones64[:, :], 1.0); ctx.vv(i)
        ctx.wait(sy, "v")
        for b_ in range(2):
            i = sy.dma_start(out=qhead2[b_][65:66, :], in_=BIGROW[0:1, :])
            ctx.inc("ld", i, 16)
        ctx.wait(t, "ld"); ctx.wait(v, "ld"); ctx.wait(s, "ld")

        with ExitStack() as pp:
            ps_n = pp.enter_context(ctx.nc.psum_tensor(ctx.uniq("ps_n"), [1, TT], F32))
            for kv in range(4):
                ctx.wait(sy, "s"); ctx.wait(sy, "t")
                if d.get("stage_k") is None:
                    i = sy.dma_start(out=khead2[0][0:64, :], in_=ksrc[64 * kv:64 * kv + 64, :])
                    ctx.inc("ldf", i, 16)
                else:
                    d["stage_k"](kv, khead2[0])
                ctx.wait(s, "ldf")
                i = s.square(SQK[:, :], khead2[0][0:64, :]); ctx.sv(i)
                ctx.wait(t, "s")
                for kt in range(KEYS // TT):
                    i = t.matmul(ps_n[:, :], ones64[:, 0:1], SQK[:, kt * TT:(kt + 1) * TT],
                                 start=True, stop=True)
                    ctx.tv(i)
                    ctx.wait(v, "t")
                    i = v.tensor_copy(KROW[0:1, kt * TT:(kt + 1) * TT], ps_n[:, :]); ctx.vv(i)
                    ctx.wait(t, "v")
                ctx.wait(v, "v")
                i = v.reduce_max(KM11[0:1, 0:1], KROW[0:1, :], axis=AX.X); ctx.vv(i)
                ctx.wait(s, "v")
                i = s.activation(KM11[0:1, 0:1], KM11[0:1, 0:1], AF.Sqrt, scale=1.0202)
                ctx.sv(i)
                ctx.wait(v, "s")
                i = v.tensor_copy(KROW[0:1, 0:1], KM11[0:1, 0:1]); ctx.vv(i)
                ctx.wait(g, "v")
                i = g.dma_start(out=KMROW[kv:kv + 1, 0:1], in_=KROW[0:1, 0:1])
                ctx.inc("ldf", i, 16)
            ctx.wait(t, "ldf")
            ps_m = pp.enter_context(ctx.nc.psum_tensor(ctx.uniq("ps_m"), [16, 1], F32))
            i = t.matmul(ps_m[:, :], KVSEL[:, :], KMROW[:, 0:1], start=True, stop=True)
            ctx.tv(i)
            ctx.wait(v, "t")
            i = v.tensor_copy(MK16[:, 0:1], ps_m[:, :]); ctx.vv(i)
            ctx.wait(v, "v")
            i = v.tensor_scalar_mul(M8[0:16, :], M8[0:16, :], MK16[:, 0:1]); ctx.vv(i)
            ctx.wait(g, "v"); ctx.wait(t, "v"); ctx.wait(s, "v")
            ctx.wait(sy, "s")

        with ExitStack() as pp:
            ps_sc2 = [pp.enter_context(ctx.nc.psum_tensor(ctx.uniq(f"ps_sc{i_}"), [128, 2 * TT], F32))
                      for i_ in range(2)]
            ps_pv2 = [pp.enter_context(ctx.nc.psum_tensor(ctx.uniq(f"ps_pv{i_}"), [128, TT], F32))
                      for i_ in range(2)]
            ps_bc = pp.enter_context(ctx.nc.psum_tensor(ctx.uniq("ps_bc"), [128, TT], F32))
            fb = dict(qhead2=qhead2, khead2=khead2, vhead2=vhead2, ps_sc2=ps_sc2,
                      ps_pv2=ps_pv2, ps_bc=ps_bc, PTt2=PTt2, densf=densf,
                      densb=densb, ones2=ones2)
            horder = []
            for g_ in range(4):
                horder += [4 * g_, 4 * g_ + 2, 4 * g_ + 1, 4 * g_ + 3]
            emit_flash(ctx, fb, 16, lambda h: h // 4, ksrc, vsrc, ind, QT, M8, OUTF,
                       0.125, lambda qt: 20 + 4 * qt, lambda qt: 16 + 4 * qt,
                       KEYS, MASKS, "l1", head_order=horder,
                       vslot_of_head=lambda h: (h // 4) * 2 + (h % 2),
                       stage_k_fn=d.get("stage_k"), stage_v_fn=d.get("stage_v"))
        ctx.wait(sy, "v")
        i = sy.dma_start(out=o_o.rearrange("(r p) w -> p r w", p=128), in_=OUTF[:, :, :])
        ctx.inc("stx", i, 16)
        ctx.wait(sy, "stx")

def build_u11():
    nc = bass.Bass("TRN2", target_bir_lowering=False)
    qin = nc.dram_tensor("q", [C, T], BF16, kind="ExternalInput").ap()
    qnin = nc.dram_tensor("qn", [16, T], F32, kind="ExternalInput").ap()
    ksrc = nc.dram_tensor("ksrc", [256, KEYS], BF16, kind="ExternalInput").ap()
    vsrc = nc.dram_tensor("vsrc", [8, KEYS, 128], BF16, kind="ExternalInput").ap()
    ind = nc.dram_tensor("ind", [1, KEYS], BF16, kind="ExternalInput").ap()
    masks = nc.dram_tensor("masks", [128, 4 * TT], F32, kind="ExternalInput").ap()
    kvsel = nc.dram_tensor("kvsel", [4, 16], F32, kind="ExternalInput").ap()
    o_o = nc.dram_tensor("o_o", [C, T], BF16, kind="ExternalOutput").ap()
    kmb = nc.dram_tensor("kmb", [4, 1], F32, kind="Internal").ap()

    d = {"ind": ind, "kmb": kmb, "ksrc": ksrc, "kvsel": kvsel, "masks": masks, "o_o": o_o, "qin": qin, "qnin": qnin, "vsrc": vsrc}
    with ExitStack() as st:
        ctx = Ctx(nc, st)
        phase_u11(ctx, d)
    return nc
def phase_u12(ctx, d):
    nc = ctx.nc
    boutc = d["boutc"]
    o_x4 = d["o_x4"]
    oin = d["oin"]
    wout = d["wout"]
    x3in = d["x3in"]
    with ctx.scope("u12"):
        g, sy, t, v, s = ctx.g, ctx.sy, ctx.t, ctx.v, ctx.s
        OT = ctx.sbuf("OT", [128, NCH, T], BF16)
        WO = ctx.sbuf("WO", [128, NCH, C], BF16)
        YB = ctx.sbuf("YB", [128, NCH, T], BF16)
        BOC = ctx.sbuf("BOC", [128, NCH], F32)
        SQ = ctx.sbuf("SQ", [128, TT], BF16)
        TMP32 = ctx.sbuf("TMP32", [128, TT], F32)
        X0TILE = ctx.sbuf("X0TILE", [128, TT], F32)
        _ssum = ctx.sbuf("ssum", [1, T], F32)
        _ssq = ctx.sbuf("ssq", [1, T], F32)
        _rs = ctx.sbuf("rs", [1, T], F32)
        rows = {"ssum": _ssum, "ssq": _ssq, "svar": _ssq, "rs": _rs}
        ROWSB = [ctx.sbuf(f"ROWSB{i_}", [1, T], BF16) for i_ in range(2)]
        EPS11 = ctx.sbuf("EPS11", [1, 1], F32)
        ones128b = ctx.sbuf("ones128b", [128, 1], BF16)
        ones1 = ctx.sbuf("ones1", [1, 128], BF16)

        i = sy.dma_start(out=OT[:, :, :], in_=oin.rearrange("(r p) w -> p r w", p=128))
        ctx.inc("ld", i, 16)
        load_weight(ctx, WO, wout)
        i = sy.dma_start(out=BOC[:, :], in_=boutc[:, :]); ctx.inc("ld", i, 16)
        i = v.memset(ones128b[:, :], 1.0); ctx.vv(i)
        i = v.memset(ones1[:, :], 1.0); ctx.vv(i)
        i = v.memset(EPS11[:, :], EPS); ctx.vv(i)
        ctx.wait(t, "ld"); ctx.wait(s, "ld"); ctx.wait(v, "ld")
        ctx.wait(t, "v")
        with ExitStack() as pp:
            ps_o = pp.enter_context(ctx.nc.psum_tensor(ctx.uniq("ps_o"), [128, TT], F32))
            for tt in range(NTT):
                tsl = slice(tt * TT, (tt + 1) * TT)
                for oc in range(NCH):
                    ctx.wait(t, "v")
                    for cc in range(NCH):
                        i = t.matmul(ps_o[:, :], WO[:, cc, oc * 128:(oc + 1) * 128],
                                     OT[:, cc, tsl], start=(cc == 0), stop=(cc == NCH - 1))
                    ctx.tv(i)
                    ctx.wait(v, "t")
                    i = v.tensor_scalar_add(YB[:, oc, tsl], ps_o[:, :], BOC[:, oc:oc + 1])
                    ctx.vv(i)
        ctx.wait(s, "v"); ctx.wait(t, "v")
        with ExitStack() as pp:
            ps_a = pp.enter_context(ctx.nc.psum_tensor(ctx.uniq("ps_a"), [1, TT], F32))
            ps_b = pp.enter_context(ctx.nc.psum_tensor(ctx.uniq("ps_b"), [1, TT], F32))
            emit_ln_stats(ctx, YB, rows, ROWSB + [None], SQ, ones128b,
                          (ps_a, ps_b), double_ln=False, eps11=EPS11)
        emit_ln_apply_residual(ctx, YB, ROWSB, x3in, o_x4, ones1, TMP32, X0TILE)

def build_u12():
    nc = bass.Bass("TRN2", target_bir_lowering=False)
    oin = nc.dram_tensor("o", [C, T], BF16, kind="ExternalInput").ap()
    wout = nc.dram_tensor("Wout", [C, C], F32, kind="ExternalInput").ap()
    boutc = nc.dram_tensor("boutc", [128, NCH], F32, kind="ExternalInput").ap()
    x3in = nc.dram_tensor("x3T", [C, T], F32, kind="ExternalInput").ap()
    o_x4 = nc.dram_tensor("o_x4", [C, T], F32, kind="ExternalOutput").ap()

    d = {"boutc": boutc, "o_x4": o_x4, "oin": oin, "wout": wout, "x3in": x3in}
    with ExitStack() as st:
        ctx = Ctx(nc, st)
        phase_u12(ctx, d)
    return nc
def _l1_tables(layer_t0):
    inv_freq = 1.0 / 10000.0 ** (np.arange(0, 64, 2, dtype=np.float64) / 64.0)
    tpos = np.arange(layer_t0, layer_t0 + T, dtype=np.float64)
    freqs = tpos[:, None] * inv_freq[None, :]
    def mk(a):
        p64 = np.concatenate([a, a], axis=1)
        return np.tile(p64, (1, 2)).T.astype(np.float32).copy()
    return np.stack([mk(np.cos(freqs)), mk(np.sin(freqs))])


QTINY = 2.0 ** -100


def emit_quant_tile(ctx, b_, in_ap, out_ap, W):
    v = ctx.v
    nb = W // 32
    amax, ebits, scl, rcp = b_["amax"], b_["ebits"], b_["scl"], b_["rcp"]
    t1, ti = b_["t1"], b_["ti"]
    ctx.wait(v, "v")
    i = v.reduce_max(amax[:, 0:nb], in_ap.rearrange("p (g k) -> p g k", k=32),
                     axis=AX.X, apply_absolute_value=True); ctx.vv(i)
    ctx.wait(v, "v")
    i = v.tensor_scalar_max(amax[:, 0:nb], amax[:, 0:nb], QTINY); ctx.vv(i)
    ctx.wait(v, "v")
    i = v.tensor_single_scalar(ebits[:, 0:nb], amax[:, 0:nb].bitcast(I32), 23,
                               ALU.logical_shift_right); ctx.vv(i)
    ctx.wait(v, "v")
    i = v.tensor_scalar(ti[:, 0:nb], ebits[:, 0:nb], 1, -3, ALU.mult, ALU.add); ctx.vv(i)
    ctx.wait(v, "v")
    i = v.tensor_single_scalar(scl[:, 0:nb].bitcast(I32), ti[:, 0:nb], 23,
                               ALU.logical_shift_left); ctx.vv(i)
    ctx.wait(v, "v")
    i = v.tensor_scalar(ti[:, 0:nb], ebits[:, 0:nb], -1, 257, ALU.mult, ALU.add); ctx.vv(i)
    ctx.wait(v, "v")
    i = v.tensor_single_scalar(rcp[:, 0:nb].bitcast(I32), ti[:, 0:nb], 23,
                               ALU.logical_shift_left); ctx.vv(i)
    ctx.wait(v, "v")
    i = v.tensor_mul(t1[:, 0:W].rearrange("p (g k) -> p g k", k=32),
                     in_ap.rearrange("p (g k) -> p g k", k=32),
                     rcp[:, 0:nb].unsqueeze(2).broadcast_to([128, nb, 32])); ctx.vv(i)
    ctx.wait(v, "v")
    i = v.tensor_copy(ti[:, 0:W], t1[:, 0:W]); ctx.vv(i)
    ctx.wait(v, "v")
    i = v.tensor_copy(t1[:, 0:W], ti[:, 0:W]); ctx.vv(i)
    ctx.wait(v, "v")
    i = v.tensor_mul(out_ap.rearrange("p (g k) -> p g k", k=32),
                     t1[:, 0:W].rearrange("p (g k) -> p g k", k=32),
                     scl[:, 0:nb].unsqueeze(2).broadcast_to([128, nb, 32])); ctx.vv(i)
    ctx.wait(v, "v")


def _mk_quant_bufs(ctx, W):
    return dict(amax=ctx.sbuf("q_amax", [128, W // 32], F32),
                ebits=ctx.sbuf("q_eb", [128, W // 32], I32),
                scl=ctx.sbuf("q_scl", [128, W // 32], F32),
                rcp=ctx.sbuf("q_rcp", [128, W // 32], F32),
                t1=ctx.sbuf("q_t1", [128, W], F32),
                ti=ctx.sbuf("q_ti", [128, W], I32))


def phase_u13(ctx, d):
    nc = ctx.nc
    o_wfcq = d["o_wfcq"]
    o_wpjq = d["o_wpjq"]
    o_z = d["o_z"]
    wfc = d["wfc"]
    wpj = d["wpj"]
    x4in = d["x4in"]
    with ctx.scope("u13"):
        g, sy, t, v, s = ctx.g, ctx.sy, ctx.t, ctx.v, ctx.s
        WIN = ctx.sbuf("WIN", [128, 4, 1024], F32)
        WQO = ctx.sbuf("WQO", [128, 4, 1024], BF16)
        WPIN = ctx.sbuf("WPIN", [128, 4096], F32)
        WPO = ctx.sbuf("WPO", [128, 4096], BF16)
        XB = ctx.sbuf("XB", [128, NCH, T], BF16)
        qb = _mk_quant_bufs(ctx, 4096)
        SQ = ctx.sbuf("SQ", [128, TT], BF16)
        TMP32 = ctx.sbuf("TMP32", [128, TT], F32)
        _ssum = ctx.sbuf("ssum", [1, T], F32)
        _ssq = ctx.sbuf("ssq", [1, T], F32)
        _rs = ctx.sbuf("rs", [1, T], F32)
        rows = {"ssum": _ssum, "ssq": _ssq, "svar": _ssq, "rs": _rs}
        ROWSB = [ctx.sbuf(f"ROWSB{i_}", [1, T], BF16) for i_ in range(2)]
        EPS11 = ctx.sbuf("EPS11", [1, 1], F32)
        ones128b = ctx.sbuf("ones128b", [128, 1], BF16)
        ones1 = ctx.sbuf("ones1", [1, 128], BF16)

        i = sy.dma_start(out=WIN[:, :, :], in_=wfc.rearrange("(r p) w -> p r w", p=128))
        ctx.inc("ld", i, 16)
        i = sy.dma_start(out=WPIN[:, :], in_=wpj[:, :]); ctx.inc("ld", i, 16)
        i = g.dma_start(out=XB[:, :, :], in_=x4in.rearrange("(r p) w -> p r w", p=128))
        ctx.inc("ld", i, 16)
        i = v.memset(ones128b[:, :], 1.0); ctx.vv(i)
        i = v.memset(ones1[:, :], 1.0); ctx.vv(i)
        i = v.memset(EPS11[:, :], EPS); ctx.vv(i)
        ctx.wait(v, "ld"); ctx.wait(t, "v"); ctx.wait(s, "v")
        for r4 in range(4):
            emit_quant_tile(ctx, qb, WIN[:, r4, :], WQO[:, r4, :], 1024)
        emit_quant_tile(ctx, qb, WPIN[:, :], WPO[:, :], 4096)
        ctx.wait(sy, "v")
        i = sy.dma_start(out=o_wfcq.rearrange("(r p) w -> p r w", p=128), in_=WQO[:, :, :])
        ctx.inc("stx", i, 16)
        i = sy.dma_start(out=o_wpjq[:, :], in_=WPO[:, :]); ctx.inc("stx", i, 16)
        ctx.wait(s, "v"); ctx.wait(t, "v")
        with ExitStack() as pp:
            ps_a = pp.enter_context(ctx.nc.psum_tensor(ctx.uniq("ps_a"), [1, TT], F32))
            ps_b = pp.enter_context(ctx.nc.psum_tensor(ctx.uniq("ps_b"), [1, TT], F32))
            ps_mu = pp.enter_context(ctx.nc.psum_tensor(ctx.uniq("ps_mu"), [128, TT], F32))
            ps_r = pp.enter_context(ctx.nc.psum_tensor(ctx.uniq("ps_r"), [128, TT], F32))
            emit_ln_stats(ctx, XB, rows, ROWSB + [None], SQ, ones128b,
                          (ps_a, ps_b), double_ln=False, eps11=EPS11)
            for tt in range(NTT):
                tsl = slice(tt * TT, (tt + 1) * TT)
                ctx.wait(t, "v")
                t.matmul(ps_mu[:, :], ones1[0:1, :], ROWSB[0][0:1, tsl], start=True, stop=True)
                i = t.matmul(ps_r[:, :], ones1[0:1, :], ROWSB[1][0:1, tsl],
                             start=True, stop=True)
                ctx.tv(i)
                ctx.wait(v, "t")
                for c in range(NCH):
                    ctx.wait(v, "strx", ctx.val("strx"))
                    ctx.wait(v, "v")
                    i = v.tensor_sub(TMP32[:, :], XB[:, c, tsl], ps_mu[:, :]); ctx.vv(i)
                    ctx.wait(v, "v")
                    i = v.tensor_mul(TMP32[:, :], TMP32[:, :], ps_r[:, :]); ctx.vv(i)
                    ctx.wait(sy, "v")
                    i = sy.dma_start(out=o_z.rearrange("(r p) w -> p r w", p=128)[:, c, tsl],
                                     in_=TMP32[:, :])
                    ctx.inc("strx", i, 16)
        ctx.wait(sy, "strx"); ctx.wait(sy, "stx")

def build_u13():
    nc = bass.Bass("TRN2", target_bir_lowering=False)
    wfc = nc.dram_tensor("wfc", [512, 1024], F32, kind="ExternalInput").ap()
    wpj = nc.dram_tensor("wpj", [128, 4096], F32, kind="ExternalInput").ap()
    x4in = nc.dram_tensor("x4T", [C, T], F32, kind="ExternalInput").ap()
    o_wfcq = nc.dram_tensor("o_wfcq", [512, 1024], BF16, kind="ExternalOutput").ap()
    o_wpjq = nc.dram_tensor("o_wpjq", [128, 4096], BF16, kind="ExternalOutput").ap()
    o_z = nc.dram_tensor("o_z", [C, T], F32, kind="ExternalOutput").ap()

    d = {"o_wfcq": o_wfcq, "o_wpjq": o_wpjq, "o_z": o_z, "wfc": wfc, "wpj": wpj, "x4in": x4in}
    with ExitStack() as st:
        ctx = Ctx(nc, st)
        phase_u13(ctx, d)
    return nc
def phase_u14(ctx, d, W):
    nc = ctx.nc
    o_q = d["o_q"]
    zin = d["zin"]
    with ctx.scope("u14"):
        g, sy, v = ctx.g, ctx.sy, ctx.v
        ZIN = ctx.sbuf("ZIN", [128, W], F32)
        ZQ = ctx.sbuf("ZQ", [128, W], BF16)
        qb = _mk_quant_bufs(ctx, W)
        for tc in range(T // 128):
            ctx.wait(sy, "v")
            i = sy.dma_start(out=ZIN[:, :], in_=zin[tc * 128:(tc + 1) * 128, :])
            ctx.inc("ld", i, 16)
            ctx.wait(v, "ld")
            emit_quant_tile(ctx, qb, ZIN[:, :], ZQ[:, :], W)
            ctx.wait(sy, "v")
            i = sy.dma_start(out=o_q[tc * 128:(tc + 1) * 128, :], in_=ZQ[:, :])
            ctx.inc("stx", i, 16)
        ctx.wait(sy, "stx")

def build_u14(W):
    nc = bass.Bass("TRN2", target_bir_lowering=False)
    zin = nc.dram_tensor("zin", [T, W], F32, kind="ExternalInput").ap()
    o_q = nc.dram_tensor("o_q", [T, W], BF16, kind="ExternalOutput").ap()
    d = {"o_q": o_q, "zin": zin}
    with ExitStack() as st:
        ctx = Ctx(nc, st)
        phase_u14(ctx, d, W)
    return nc
def phase_u15(ctx, d):
    nc = ctx.nc
    bfcc = d["bfcc"]
    o_h = d["o_h"]
    wq = d["wq"]
    xq = d["xq"]
    with ctx.scope("u15"):
        g, sy, t, v, s = ctx.g, ctx.sy, ctx.t, ctx.v, ctx.s
        XQ = ctx.sbuf("XQ", [128, NCH, T], BF16)
        WQ = ctx.sbuf("WQ", [128, NCH, 4096], BF16)
        BF_ = ctx.sbuf("BF_", [128, 32], F32)
        HT = ctx.sbuf("HT", [128, TT], F32)
        i = sy.dma_start(out=XQ[:, :, :], in_=xq.rearrange("(r p) w -> p r w", p=128))
        ctx.inc("ld", i, 16)
        i = sy.dma_start(out=WQ[:, :, :], in_=wq.rearrange("(r p) w -> p r w", p=128))
        ctx.inc("ld", i, 16)
        i = sy.dma_start(out=BF_[:, :], in_=bfcc[:, :]); ctx.inc("ld", i, 16)
        ctx.wait(t, "ld"); ctx.wait(s, "ld")
        with ExitStack() as pp:
            ps = pp.enter_context(ctx.nc.psum_tensor(ctx.uniq("ps"), [128, TT], F32))
            for nc32 in range(32):
                for tt in range(NTT):
                    tsl = slice(tt * TT, (tt + 1) * TT)
                    ctx.wait(t, "s")
                    for cc in range(NCH):
                        i = t.matmul(ps[:, :], WQ[:, cc, nc32 * 128:(nc32 + 1) * 128],
                                     XQ[:, cc, tsl], start=(cc == 0), stop=(cc == NCH - 1))
                    ctx.tv(i)
                    ctx.wait(s, "t")
                    ctx.wait(s, "strx", ctx.val("strx"))
                    i = s.activation(HT[:, :], ps[:, :], AF.Gelu,
                                     bias=BF_[:, nc32:nc32 + 1]); ctx.sv(i)
                    ctx.wait(sy, "s")
                    i = sy.dma_start(
                        out=o_h.rearrange("(r p) w -> p r w", p=128)[:, nc32, tsl],
                        in_=HT[:, :])
                    ctx.inc("strx", i, 16)
        ctx.wait(sy, "strx")

def build_u15():
    nc = bass.Bass("TRN2", target_bir_lowering=False)
    xq = nc.dram_tensor("xqT", [C, T], BF16, kind="ExternalInput").ap()
    wq = nc.dram_tensor("wfcqT", [C, 4096], BF16, kind="ExternalInput").ap()
    bfcc = nc.dram_tensor("bfcc", [128, 32], F32, kind="ExternalInput").ap()
    o_h = nc.dram_tensor("o_h", [4096, T], F32, kind="ExternalOutput").ap()
    d = {"bfcc": bfcc, "o_h": o_h, "wq": wq, "xq": xq}
    with ExitStack() as st:
        ctx = Ctx(nc, st)
        phase_u15(ctx, d)
    return nc
def phase_u17(ctx, d):
    nc = ctx.nc
    bpc = d["bpc"]
    hq = d["hq"]
    o_x5 = d["o_x5"]
    wq = d["wq"]
    x4in = d["x4in"]
    with ctx.scope("u17"):
        g, sy, t, v, s = ctx.g, ctx.sy, ctx.t, ctx.v, ctx.s
        WQ = ctx.sbuf("WQ", [128, 32, C], BF16)
        HQT = ctx.sbuf("HQT", [128, 32, TT], BF16)
        YB = ctx.sbuf("YB", [128, NCH, T], BF16)
        BPC = ctx.sbuf("BPC", [128, NCH], F32)
        SQ = ctx.sbuf("SQ", [128, TT], BF16)
        TMP32 = ctx.sbuf("TMP32", [128, TT], F32)
        X0TILE = ctx.sbuf("X0TILE", [128, TT], F32)
        _ssum = ctx.sbuf("ssum", [1, T], F32)
        _ssq = ctx.sbuf("ssq", [1, T], F32)
        _rs = ctx.sbuf("rs", [1, T], F32)
        rows = {"ssum": _ssum, "ssq": _ssq, "svar": _ssq, "rs": _rs}
        ROWSB = [ctx.sbuf(f"ROWSB{i_}", [1, T], BF16) for i_ in range(2)]
        EPS11 = ctx.sbuf("EPS11", [1, 1], F32)
        ones128b = ctx.sbuf("ones128b", [128, 1], BF16)
        ones1 = ctx.sbuf("ones1", [1, 128], BF16)
        i = sy.dma_start(out=WQ[:, :, :], in_=wq.rearrange("(r p) w -> p r w", p=128))
        ctx.inc("ld", i, 16)
        i = sy.dma_start(out=BPC[:, :], in_=bpc[:, :]); ctx.inc("ld", i, 16)
        i = v.memset(ones128b[:, :], 1.0); ctx.vv(i)
        i = v.memset(ones1[:, :], 1.0); ctx.vv(i)
        i = v.memset(EPS11[:, :], EPS); ctx.vv(i)
        ctx.wait(t, "ld"); ctx.wait(v, "ld"); ctx.wait(s, "v")
        with ExitStack() as pp:
            ps8 = [pp.enter_context(ctx.nc.psum_tensor(ctx.uniq(f"ps{o}"), [128, TT], F32))
                   for o in range(NCH)]
            for tt in range(NTT):
                tsl = slice(tt * TT, (tt + 1) * TT)
                ctx.wait(sy, "t")
                i = sy.dma_start(out=HQT[:, :, :],
                                 in_=hq[:, tsl].rearrange("(r p) w -> p r w", p=128))
                ctx.inc("ldh", i, 16)
                ctx.wait(t, "ldh"); ctx.wait(t, "v")
                for kc in range(32):
                    for oc in range(NCH):
                        i = t.matmul(ps8[oc][:, :], WQ[:, kc, oc * 128:(oc + 1) * 128],
                                     HQT[:, kc, :], start=(kc == 0), stop=(kc == 31))
                    ctx.tv(i)
                ctx.wait(v, "t")
                for oc in range(NCH):
                    i = v.tensor_scalar_add(YB[:, oc, tsl], ps8[oc][:, :],
                                            BPC[:, oc:oc + 1]); ctx.vv(i)
        ctx.wait(s, "v"); ctx.wait(t, "v")
        with ExitStack() as pp:
            ps_a = pp.enter_context(ctx.nc.psum_tensor(ctx.uniq("ps_a"), [1, TT], F32))
            ps_b = pp.enter_context(ctx.nc.psum_tensor(ctx.uniq("ps_b"), [1, TT], F32))
            emit_ln_stats(ctx, YB, rows, ROWSB + [None], SQ, ones128b,
                          (ps_a, ps_b), double_ln=False, eps11=EPS11)
        emit_ln_apply_residual(ctx, YB, ROWSB, x4in, o_x5, ones1, TMP32, X0TILE)

def build_u17():
    nc = bass.Bass("TRN2", target_bir_lowering=False)
    hq = nc.dram_tensor("hqT", [4096, T], BF16, kind="ExternalInput").ap()
    wq = nc.dram_tensor("wpjqT", [4096, C], BF16, kind="ExternalInput").ap()
    bpc = nc.dram_tensor("bpc", [128, NCH], F32, kind="ExternalInput").ap()
    x4in = nc.dram_tensor("x4T", [C, T], F32, kind="ExternalInput").ap()
    o_x5 = nc.dram_tensor("o_x5", [C, T], F32, kind="ExternalOutput").ap()
    d = {"bpc": bpc, "hq": hq, "o_x5": o_x5, "wq": wq, "x4in": x4in}
    with ExitStack() as st:
        ctx = Ctx(nc, st)
        phase_u17(ctx, d)
    return nc
def _shard_xT(x):
    out = []
    for c in range(8):
        b, h = c // 2, c % 2
        out.append(np.ascontiguousarray(x[b, h * T:(h + 1) * T, :].T))
    return out


def _qsel_host():
    q = np.zeros((128, NCH * 16), np.float32)
    for oc in range(NCH):
        for r in range(128):
            head = 2 * oc + (1 if r >= 64 else 0)
            q[r, oc * 16 + head] = 1.0
    return q




def _xw(ctx, tag, pairs, pregs, smeta, flag, nread=None):
    nc, sy = ctx.nc, ctx.sy
    rpar = ctx._sregs[0]
    with ctx.scope("xw" + tag):
        xbs = []
        for pi, (pairt, local, R, W, dt_) in enumerate(pairs):
            xbs.append(ctx.sbuf(f"xb{pi}", [128, R // 128, W], dt_))
        ctx.wait(sy, "stx"); ctx.wait(sy, "strx", ctx.val("strx"))
        ctx.wait(sy, "sttok", ctx.val("sttok"))
        ctx.wait(sy, "stq", ctx.val("stq"))
        nc.br_cmp(rpar, 0, on_true=f"xwe_{tag}", on_false=f"xwo_{tag}",
                  comp_op="IS_EQ", engines=[sy.engine])
        for par, bodyname in ((0, f"xwe_{tag}"), (1, f"xwo_{tag}")):
            with nc.body(bodyname, valid_engines=[sy.engine]):
                base = ctx.val("ldx")
                n = 0
                for pi, (pairt, local, R, W, dt_) in enumerate(pairs):
                    xb = xbs[pi]
                    sy.dma_start(out=xb[:, :, :],
                                 in_=local.rearrange("(r p) w -> p r w", p=128),
                                 allow_slow_non_contiguous=True).then_inc(
                                     ctx.sem("ldx")[0], 16)
                    n += 16
                    sy.wait_ge(ctx.sem("ldx")[0], base + n)
                    dst = AP(pairt.tensor, par * (R * W), [[W, R], [1, W]])
                    sy.dma_start(out=dst.rearrange("(r p) w -> p r w", p=128),
                                 in_=xb[:, :, :],
                                 allow_slow_non_contiguous=True).then_inc(
                                     ctx.sem("ldx")[0], 16)
                    n += 16
                    sy.wait_ge(ctx.sem("ldx")[0], base + n)
                fdst = AP(flag.tensor, par * 16, [[1, 1], [1, 1]])
                sy.dma_start(out=fdst, in_=smeta[0:1, 1:2]).then_inc(
                    ctx.sem("ldx")[0], 16)
                n += 16
                sy.wait_ge(ctx.sem("ldx")[0], base + n)
                nc.br(f"xwj_{tag}", engines=[sy.engine])
        nc.switch_bb(f"xwj_{tag}")
        ctx.sem("ldx")[1] += len(pairs) * 32 + 16


def _assemble_pair(ctx, tag, pairt, R, W, dst, pregs):
    nc, sy = ctx.nc, ctx.sy
    rpar = ctx._sregs[0]
    with ctx.scope("asm" + tag):
        xb = ctx.sbuf("xb", [128, R // 128, W], BF16)
        nc.br_cmp(rpar, 0, on_true=f"ase_{tag}", on_false=f"aso_{tag}",
                  comp_op="IS_EQ", engines=[sy.engine])
        for par, bodyname in ((0, f"ase_{tag}"), (1, f"aso_{tag}")):
            with nc.body(bodyname, valid_engines=[sy.engine]):
                base = ctx.val("ldx")
                n = 0
                for off, sl in ((0, 1 - par), (R, par)):
                    src = AP(pairt.tensor, sl * (R * W), [[W, R], [1, W]])
                    sy.dma_start(out=xb[:, :, :],
                                 in_=src.rearrange("(r p) w -> p r w", p=128)).then_inc(
                                     ctx.sem("ldx")[0], 16)
                    n += 16
                    sy.wait_ge(ctx.sem("ldx")[0], base + n)
                    sy.dma_start(out=dst[off:off + R, :].rearrange(
                        "(r p) w -> p r w", p=128), in_=xb[:, :, :]).then_inc(
                            ctx.sem("ldx")[0], 16)
                    n += 16
                    sy.wait_ge(ctx.sem("ldx")[0], base + n)
                nc.br(f"asj_{tag}", engines=[sy.engine])
        nc.switch_bb(f"asj_{tag}")
        ctx.sem("ldx")[1] += 64


def _xpoll(ctx, tag, flag, sregs, sflag_sb, smeta_sy):
    nc, sy = ctx.nc, ctx.sy
    rpar, rno, rfv, rctr = sregs
    ps = ctx.sem("xps_" + tag)
    nc.regs_mov(rctr, 16)
    nc.br_cmp(rpar, 0, on_true=f"xe_{tag}", on_false=f"xo_{tag}",
              comp_op="IS_EQ", engines=[sy.engine])
    for par, bodyname in ((0, f"xe_{tag}"), (1, f"xo_{tag}")):
        with nc.body(bodyname, valid_engines=[sy.engine]):
            src = AP(flag.tensor, (1 - par) * 16, [[1, 1], [1, 1]])
            nc.br(f"xsp_{tag}_{par}", engines=[sy.engine])
            with nc.body(f"xsp_{tag}_{par}", valid_engines=[sy.engine]):
                sy.dma_start(out=sflag_sb[0:1, 0:1], in_=src).then_inc(ps[0], 16)
                sy.wait_ge(ps[0], rctr)
                nc.regs_load(rfv, sflag_sb[0:1, 0:1])
                nc.regs_alu(rctr, rctr, 16, op=mybir.AluOpType.add)
                nc.br_cmp(rfv, rno, on_true=f"xd_{tag}_{par}", on_false=f"xsp_{tag}_{par}",
                          comp_op="IS_EQ", engines=[sy.engine])
            nc.switch_bb(f"xd_{tag}_{par}")
            nc.br(f"xj_{tag}", engines=[sy.engine])
    nc.switch_bb(f"xj_{tag}")
    i = sy.dma_start(out=sflag_sb[0:1, 2:3], in_=smeta_sy[0:1, 1:2])
    ctx.inc("xdone", i, 16)


def _halo_read(ctx, tag, halopair, halo_local, sregs):
    nc, sy = ctx.nc, ctx.sy
    rpar = sregs[0]
    nc.br_cmp(rpar, 0, on_true=f"hj_{tag}", on_false=f"ho_{tag}",
              comp_op="IS_EQ", engines=[sy.engine])
    with nc.body(f"ho_{tag}", valid_engines=[sy.engine]):
        with ctx.scope("hr" + tag):
            hb = ctx.sbuf("hb", [128, NCH, 1], BF16)
            hs = ctx.sem("hx_" + tag)
            sy.dma_start(out=hb[:, :, :],
                         in_=halopair[0:C, 0:1].rearrange("(r p) w -> p r w", p=128),
                         allow_slow_non_contiguous=True).then_inc(hs[0], 16)
            sy.wait_ge(hs[0], 16)
            sy.dma_start(out=halo_local.rearrange("(r p) w -> p r w", p=128),
                         in_=hb[:, :, :], allow_slow_non_contiguous=True).then_inc(hs[0], 16)
            sy.wait_ge(hs[0], 32)
        nc.br(f"hj_{tag}", engines=[sy.engine])
    nc.switch_bb(f"hj_{tag}")
    i = sy.dma_start(out=ctx._sflag[0:1, 3:4], in_=ctx._smeta_sy[0:1, 0:1])
    ctx.inc("xdone", i, 16)


def _mk_stage_k_stacked(ctx, KD, NK):
    sy = ctx.sy

    def stage_k(kv, khead):
        i = sy.dma_start(out=khead[0:64, 0:T], in_=KD[64 * kv:64 * kv + 64, :])
        ctx.inc("ldf", i, 16)
        i = sy.dma_start(out=khead[0:64, T:2 * T], in_=KD[NK + 64 * kv:NK + 64 * kv + 64, :])
        ctx.inc("ldf", i, 16)
    return stage_k


def _mk_stage_v_stacked(ctx, VD, vw, vmapfn):
    g, v = ctx.g, ctx.v

    def stage_v(vs_, vhead, pvfree=None, ldffree=None):
        vh, c0, c1 = vmapfn(vs_)
        if pvfree is None:
            pvfree = ctx.val("t_pv")
        if ldffree is None:
            ldffree = ctx.val("ldf")
        v.wait_ge(ctx.sem("t_pv")[0], pvfree)
        v.wait_ge(ctx.sem("ldf")[0], ldffree)
        i = v.memset(vhead[:, :, c1:c1 + 1], 1.0); ctx.vv(i)
        ctx.wait(g, "v")
        for j in range(8):
            i = g.dma_start(out=vhead[:, 4 * j:4 * (j + 1), c0:c0 + 64],
                            in_=VD[512 * j:512 * (j + 1), 64 * vh:64 * vh + 64]
                            .rearrange("(a p) w -> p a w", p=128))
            ctx.inc("ldf", i, 16)
    return stage_v


def phase_f_wq(ctx, D, L):
    g, sy, v = ctx.g, ctx.sy, ctx.v
    with ctx.scope("fwq"):
        WIN = ctx.sbuf("WIN", [128, 4096], F32)
        WQO = ctx.sbuf("WQO", [128, 4096], BF16)
        qb = _mk_quant_bufs(ctx, 4096)
        for r in range(16):
            ctx.wait(sy, "v")
            i = sy.dma_start(out=WIN[:, 0:1024],
                             in_=D["wfc_half"][128 * r:128 * (r + 1), :])
            ctx.inc("ldq", i, 16)
            ctx.wait(v, "ldq")
            emit_quant_tile(ctx, qb, WIN[:, 0:1024], WQO[:, 0:1024], 1024)
            ctx.wait(sy, "v")
            i = sy.dma_start(out=L["wfcq_d"][128 * r:128 * (r + 1), :],
                             in_=WQO[:, 0:1024])
            ctx.inc("stq", i, 16)
        for r in range(4):
            ctx.wait(sy, "v")
            i = sy.dma_start(out=WIN[:, :], in_=D["wpj_half"][128 * r:128 * (r + 1), :])
            ctx.inc("ldq", i, 16)
            ctx.wait(v, "ldq")
            emit_quant_tile(ctx, qb, WIN[:, :], WQO[:, :], 4096)
            ctx.wait(sy, "v")
            i = sy.dma_start(out=L["wpjq_d"][128 * r:128 * (r + 1), :], in_=WQO[:, :])
            ctx.inc("stq", i, 16)
        ctx.wait(sy, "stq"); ctx.wait(g, "stq")


def phase_f_zq(ctx, D, L):
    nc = ctx.nc
    g, sy, t, v, s = ctx.g, ctx.sy, ctx.t, ctx.v, ctx.s
    with ctx.scope("fzq"):
        XB = ctx.sbuf("XB", [128, NCH, T], BF16)
        X4F = ctx.sbuf("X4F", [128, NCH, 128], F32)
        IDT = ctx.sbuf("IDT", [128, 128], F32)
        ZT = ctx.sbuf("ZT", [128, 128], F32)
        ZTOK = ctx.sbuf("ZTOK", [128, NCH, 128], F32)
        ZQ = ctx.sbuf("ZQ", [128, 1024], BF16)
        SQ = ctx.sbuf("SQ", [128, TT], BF16)
        qb = _mk_quant_bufs(ctx, 1024)
        _ssum = ctx.sbuf("ssum", [1, T], F32)
        _ssq = ctx.sbuf("ssq", [1, T], F32)
        _rs = ctx.sbuf("rs", [1, T], F32)
        rows = {"ssum": _ssum, "ssq": _ssq, "svar": _ssq, "rs": _rs}
        ROWSB = [ctx.sbuf(f"ROWSB{i_}", [1, T], BF16) for i_ in range(2)]
        EPS11 = ctx.sbuf("EPS11", [1, 1], F32)
        ones128b = ctx.sbuf("ones128b", [128, 1], BF16)
        ones1 = ctx.sbuf("ones1", [1, 128], BF16)
        i = g.dma_start(out=XB[:, :, :], in_=L["x4_d"].rearrange("(r p) w -> p r w", p=128))
        ctx.inc("ldq", i, 16)
        i = sy.dma_start(out=IDT[:, :], in_=D["identb"][:, :]); ctx.inc("ldq", i, 16)
        i = v.memset(ones128b[:, :], 1.0); ctx.vv(i)
        i = v.memset(ones1[:, :], 1.0); ctx.vv(i)
        i = v.memset(EPS11[:, :], EPS); ctx.vv(i)
        ctx.wait(t, "ldq"); ctx.wait(s, "ldq"); ctx.wait(v, "ldq")
        ctx.wait(t, "v"); ctx.wait(s, "v")
        with ExitStack() as pp:
            ps_a = pp.enter_context(nc.psum_tensor(ctx.uniq("ps_a"), [1, TT], F32))
            ps_b = pp.enter_context(nc.psum_tensor(ctx.uniq("ps_b"), [1, TT], F32))
            emit_ln_stats(ctx, XB, rows, ROWSB + [None], SQ, ones128b,
                          (ps_a, ps_b), double_ln=False, eps11=EPS11)
        with ExitStack() as pp:
            ps_mu = pp.enter_context(nc.psum_tensor(ctx.uniq("ps_mu"), [128, 128], F32))
            ps_r = pp.enter_context(nc.psum_tensor(ctx.uniq("ps_r"), [128, 128], F32))
            ps_tr = pp.enter_context(nc.psum_tensor(ctx.uniq("ps_tr"), [128, 128], F32))
            for tc in range(T // 128):
                csl = slice(tc * 128, (tc + 1) * 128)
                ctx.wait(t, "v")
                t.matmul(ps_mu[:, :], ones1[0:1, :], ROWSB[0][0:1, csl], start=True, stop=True)
                i = t.matmul(ps_r[:, :], ones1[0:1, :], ROWSB[1][0:1, csl], start=True, stop=True)
                ctx.tv(i)
                ctx.wait(sy, "v"); ctx.wait(sy, "ldq", ctx.val("ldq"))
                i = sy.dma_start(out=X4F[:, :, :],
                                 in_=L["x4_d"].rearrange("(r p) w -> p r w", p=128)[:, :, csl])
                ctx.inc("ldq", i, 16)
                ctx.wait(v, "ldq"); ctx.wait(v, "t")
                for c in range(NCH):
                    ctx.wait(v, "v")
                    i = v.tensor_sub(ZT[:, :], X4F[:, c, :], ps_mu[:, :]); ctx.vv(i)
                    ctx.wait(v, "v")
                    i = v.tensor_mul(ZT[:, :], ZT[:, :], ps_r[:, :]); ctx.vv(i)
                    ctx.wait(t, "v")
                    i = t.transpose(ps_tr[:, :], ZT[:, :], IDT[:, :]); ctx.tv(i)
                    ctx.wait(v, "t")
                    i = v.tensor_copy(ZTOK[:, c, :], ps_tr[:, :]); ctx.vv(i)
                ctx.wait(v, "v")
                emit_quant_tile(ctx, qb, ZTOK[:, :, :].rearrange("p a b -> p (a b)"),
                                ZQ[:, :], 1024)
                ctx.wait(sy, "v")
                i = sy.dma_start(out=L["xq_d"][tc * 128:(tc + 1) * 128, :], in_=ZQ[:, :])
                ctx.inc("stq", i, 16)
                ctx.wait(v, "stq", ctx.val("stq") - 16)
        ctx.wait(sy, "stq"); ctx.wait(t, "v"); ctx.wait(s, "v"); ctx.wait(g, "v")


def phase_f_fc(ctx, D, L, P):
    nc = ctx.nc
    g, sy, t, v, s = ctx.g, ctx.sy, ctx.t, ctx.v, ctx.s
    with ctx.scope("ffc"):
        XQT = ctx.sbuf("XQT", [128, NCH, T], BF16)
        WQ = ctx.sbuf("WQ", [128, NCH, 4096], BF16)
        BFCB = ctx.sbuf("BFCB", [128, 4096], BF16)
        BROW = ctx.sbuf("BROW", [1, 4096], BF16)
        HTOK = ctx.sbuf("HTOK", [128, 4096], F32)
        HQ = ctx.sbuf("HQ", [128, 4096], BF16)
        qb = _mk_quant_bufs(ctx, 4096)
        ones1 = ctx.sbuf("ones1", [1, 128], BF16)
        i = v.memset(ones1[:, :], 1.0); ctx.vv(i)
        for c in range(NCH):
            i = sy.dma_start(out=XQT[:, c, :], in_=L["xq_d"][:, 128 * c:128 * (c + 1)],
                             transpose=True)
            ctx.inc("ldq", i, 16)
        for sl in range(2):
            for kc in range(NCH):
                i = sy.dma_start(out=WQ[:, kc, 2048 * sl:2048 * (sl + 1)],
                                 in_=AP(P["wfcqp"].tensor, sl * 2048 * 1024 + 128 * kc,
                                        [[1024, 2048], [1, 128]]),
                                 transpose=True)
                ctx.inc("ldq", i, 16)
        i = g.dma_start(out=BROW[0:1, :], in_=D["bfcrow"][0:1, :]); ctx.inc("ldq", i, 16)
        ctx.wait(t, "ldq"); ctx.wait(v, "ldq"); ctx.wait(s, "ldq")
        ctx.wait(t, "v")
        with ExitStack() as pp:
            ps_bb = pp.enter_context(nc.psum_tensor(ctx.uniq("ps_bb"), [128, TT], F32))
            for nt in range(8):
                ctx.wait(t, "v")
                i = t.matmul(ps_bb[:, :], ones1[0:1, :], BROW[0:1, nt * TT:(nt + 1) * TT],
                             start=True, stop=True)
                ctx.tv(i)
                ctx.wait(v, "t")
                i = v.tensor_copy(BFCB[:, nt * TT:(nt + 1) * TT], ps_bb[:, :]); ctx.vv(i)
        ctx.wait(t, "v")
        with ExitStack() as pp:
            psP = [pp.enter_context(nc.psum_tensor(ctx.uniq(f"ps_fc{i_}"), [128, TT], F32))
                   for i_ in range(2)]
            for tc in range(T // 128):
                sct = [None] * 8
                for nt in range(8):
                    db = nt % 2
                    if nt >= 2:
                        ctx.wait(t, "s", sct[nt - 2])
                    for kc in range(NCH):
                        i = t.matmul(psP[db][:, :], XQT[:, kc, tc * 128:(tc + 1) * 128],
                                     WQ[:, kc, nt * TT:(nt + 1) * TT],
                                     start=(kc == 0), stop=(kc == NCH - 1))
                    mmc = ctx.tv(i)
                    ctx.wait(v, "t", mmc)
                    if nt >= 2:
                        ctx.wait(v, "s", sct[nt - 2])
                    i = v.tensor_add(psP[db][:, :], psP[db][:, :],
                                     BFCB[:, nt * TT:(nt + 1) * TT])
                    vvc = ctx.vv(i)
                    ctx.wait(s, "v", vvc)
                    i = s.activation(HTOK[:, nt * TT:(nt + 1) * TT], psP[db][:, :], AF.Gelu)
                    sct[nt] = ctx.sv(i)
                ctx.wait(v, "s")
                emit_quant_tile(ctx, qb, HTOK[:, :], HQ[:, :], 4096)
                ctx.wait(sy, "v")
                i = sy.dma_start(out=L["hq_d"][tc * 128:(tc + 1) * 128, :], in_=HQ[:, :])
                ctx.inc("stq", i, 16)
                ctx.wait(v, "stq", ctx.val("stq") - 16)
                ctx.wait(s, "stq", ctx.val("stq") - 16)
        ctx.wait(sy, "stq"); ctx.wait(g, "v")


def phase_f_proj(ctx, D, L, P, o_final):
    nc = ctx.nc
    g, sy, t, v, s = ctx.g, ctx.sy, ctx.t, ctx.v, ctx.s
    with ctx.scope("fpj"):
        WPJT = ctx.sbuf("WPJT", [128, 32, C], BF16)
        HQT = ctx.sbuf("HQT", [128, 32, TT], BF16)
        YB = ctx.sbuf("YB", [128, NCH, T], BF16)
        BPC = ctx.sbuf("BPC", [128, NCH], F32)
        SQ = ctx.sbuf("SQ", [128, TT], BF16)
        TMP32 = ctx.sbuf("TMP32", [128, TT], F32)
        X0TILE = ctx.sbuf("X0TILE", [128, TT], F32)
        _ssum = ctx.sbuf("ssum", [1, T], F32)
        _ssq = ctx.sbuf("ssq", [1, T], F32)
        _rs = ctx.sbuf("rs", [1, T], F32)
        rows = {"ssum": _ssum, "ssq": _ssq, "svar": _ssq, "rs": _rs}
        ROWSB = [ctx.sbuf(f"ROWSB{i_}", [1, T], BF16) for i_ in range(2)]
        EPS11 = ctx.sbuf("EPS11", [1, 1], F32)
        ones128b = ctx.sbuf("ones128b", [128, 1], BF16)
        ones1 = ctx.sbuf("ones1", [1, 128], BF16)
        for sl in range(2):
            for kc in range(32):
                i = sy.dma_start(out=WPJT[:, kc, 512 * sl:512 * (sl + 1)],
                                 in_=AP(P["wpjqp"].tensor, sl * 512 * 4096 + 128 * kc,
                                        [[4096, 512], [1, 128]]),
                                 transpose=True)
                ctx.inc("ldq", i, 16)
        i = sy.dma_start(out=BPC[:, :], in_=D["bpc"][:, :]); ctx.inc("ldq", i, 16)
        i = v.memset(ones128b[:, :], 1.0); ctx.vv(i)
        i = v.memset(ones1[:, :], 1.0); ctx.vv(i)
        i = v.memset(EPS11[:, :], EPS); ctx.vv(i)
        ctx.wait(t, "ldq"); ctx.wait(v, "ldq"); ctx.wait(s, "v"); ctx.wait(t, "v")
        with ExitStack() as pp:
            ps8 = [pp.enter_context(nc.psum_tensor(ctx.uniq(f"ps{o}"), [128, TT], F32))
                   for o in range(NCH)]
            for tt in range(NTT):
                tsl = slice(tt * TT, (tt + 1) * TT)
                ctx.wait(sy, "t")
                for kc in range(32):
                    i = sy.dma_start(out=HQT[:, kc, :],
                                     in_=L["hq_d"][tsl, 128 * kc:128 * (kc + 1)],
                                     transpose=True)
                    ctx.inc("ldh", i, 16)
                ctx.wait(t, "ldh"); ctx.wait(t, "v")
                for kc in range(32):
                    for oc in range(NCH):
                        i = t.matmul(ps8[oc][:, :], WPJT[:, kc, oc * 128:(oc + 1) * 128],
                                     HQT[:, kc, :], start=(kc == 0), stop=(kc == 31))
                    ctx.tv(i)
                ctx.wait(v, "t")
                for oc in range(NCH):
                    i = v.tensor_scalar_add(YB[:, oc, tsl], ps8[oc][:, :],
                                            BPC[:, oc:oc + 1]); ctx.vv(i)
        ctx.wait(s, "v"); ctx.wait(t, "v")
        with ExitStack() as pp:
            ps_a = pp.enter_context(nc.psum_tensor(ctx.uniq("ps_a"), [1, TT], F32))
            ps_b = pp.enter_context(nc.psum_tensor(ctx.uniq("ps_b"), [1, TT], F32))
            emit_ln_stats(ctx, YB, rows, ROWSB + [None], SQ, ones128b,
                          (ps_a, ps_b), double_ln=False, eps11=EPS11)
        emit_ln_apply_residual(ctx, YB, ROWSB, L["x4_d"], o_final, ones1, TMP32, X0TILE)


def build_fused(debug=False):
    nc = bass.Bass("TRN2", target_bir_lowering=False)
    EI, EO, IN, SH = "ExternalInput", "ExternalOutput", "Internal", "Shared"

    def din(nm, shp, dt=F32):
        return nc.dram_tensor(nm, shp, dt, kind=EI).ap()

    D = {}
    D["xT"] = din("xT", [C, T])
    for nm, shp in [("qKT", [C, 1024]), ("kKT", [C, 1024]), ("vKT", [C, 1024]),
                    ("qV", [1024, C]), ("kV", [1024, C]), ("vV", [1024, C]),
                    ("qsel", [128, NCH * 16]), ("pKT", [C, 1024]), ("pV", [1024, C]),
                    ("fKT", [C, 4096]), ("fV", [4096, C]),
                    ("masks", [128, 4 * TT]), ("lam_a", [128, 256]),
                    ("lam_b", [128, 8 * TT]), ("rotp", [128, 128]), ("bsel", [2, 128]),
                    ("tabsF", [4, 128, T]), ("tabsL1", [2, 128, T]),
                    ("Wqa", [C, 64]), ("WqbP", [64, C]), ("Wkva", [C, 32]),
                    ("WkbP", [32, 256]), ("Wvb", [32, 256]), ("kvsel", [4, 16]),
                    ("boutc", [128, NCH]), ("Wout", [C, C]), ("wfc_half", [2048, 1024]),
                    ("wpj_half", [512, 4096]), ("bfcc", [128, 32]),
                    ("bfcrow", [1, 4096]), ("bpc", [128, NCH]), ("identb", [128, 128])]:
        D[nm] = din(nm, shp)
    for lay in range(2):
        for nm, shp in [("wqkv", [C, 1536]), ("qscol", [128, 1]), ("kscol", [128, 1]),
                        ("wo", [512, C]), ("gcol", [128, NCH]), ("bcol", [128, NCH]),
                        ("w1", [C, 5460]), ("b1a", [128, 22]), ("b1g", [128, 22]),
                        ("w2", [2730, C]), ("b2c", [128, NCH])]:
            D[f"{nm}{lay}"] = din(f"{nm}{lay}", shp)
    D["ind2t"] = din("ind2t", [1, KEYS], BF16)
    D["indloc"] = din("indloc", [1, BKEYS], BF16)
    D["meta"] = din("meta", [1, 4], I32)
    o_final = nc.dram_tensor("o_final", [C, T], F32, kind=EO).ap()

    def loc(nm, shp, dt):
        return nc.dram_tensor(nm, shp, dt, kind=IN).ap()

    L = {}
    _dbgkind = "ExternalOutput" if debug else IN

    def loc_dbg(nm, shp, dt_):
        return nc.dram_tensor(nm, shp, dt_, kind=_dbgkind).ap()
    for nm, shp, dt_ in [("xin_d", [C, T], BF16), ("h_scr", [C, T], BF16),
                         ("q0_d", [C, T], BF16), ("k0_d", [C, T], BF16),
                         ("v0_d", [T, C], BF16), ("m80_d", [16, T], F32),
                         ("o0_d", [C, T], BF16), ("x2_d", [C, T], F32),
                         ("z0_d", [C, T], BF16), ("haloz_d", [C, 1], BF16),
                         ("qf_d", [512, T], BF16), ("kf_d", [512, T], BF16),
                         ("vf_d", [T, 512], BF16), ("xm_d", [C, T], BF16),
                         ("halom_d", [C, 1], BF16), ("x1_d", [C, T], BF16),
                         ("halo1_d", [C, 1], BF16), ("halom2_d", [C, 1], BF16),
                         ("x3_d", [C, T], F32), ("hln_scr", [C, T], BF16),
                         ("qL_d", [C, T], BF16), ("kL_d", [256, T], BF16),
                         ("vL_d", [T, 256], BF16), ("qn_d", [16, T], F32),
                         ("oL_d", [C, T], BF16), ("x4_d", [C, T], F32),
                         ("kmb", [4, 1], F32), ("mcb", [128, 16], F32),
                         ("wfcq_d", [2048, 1024], BF16), ("wpjq_d", [512, 4096], BF16),
                         ("xq_d", [T, C], BF16), ("hq_d", [T, 4096], BF16),
                         ("ks0_d", [2 * C, T], BF16), ("vs0_d", [2 * T, C], BF16),
                         ("ksF_d", [2 * 512, T], BF16), ("vsF_d", [2 * T, 512], BF16),
                         ("ksL_d", [2 * 256, T], BF16), ("vsL_d", [2 * T, 256], BF16)]:
        if nm in ("x2_d", "x3_d", "x4_d", "xm_d", "x1_d", "z0_d", "o0_d",
                  "q0_d", "k0_d", "xq_d", "hq_d"):
            L[nm] = loc_dbg(nm, shp, dt_)
        else:
            L[nm] = loc(nm, shp, dt_)
    P = {}
    for nm, shp, dt_ in [("kpL0", [2 * C, T], BF16), ("vpL0", [2 * T, C], BF16),
                         ("hzp", [2 * C, 1], BF16), ("kpF", [2 * 512, T], BF16),
                         ("vpF", [2 * T, 512], BF16), ("hmp", [2 * C, 1], BF16),
                         ("h1p", [2 * C, 1], BF16), ("kpF2", [2 * 512, T], BF16),
                         ("vpF2", [2 * T, 512], BF16), ("hm2p", [2 * C, 1], BF16),
                         ("kpL1", [2 * 256, T], BF16), ("vpL1", [2 * T, 256], BF16),
                         ("wfcqp", [2 * 2048, 1024], BF16), ("wpjqp", [2 * 512, 4096], BF16)]:
        P[nm] = nc.dram_tensor(nm, shp, dt_, kind=IN, addr_space=SH).ap()
    FL = {}
    for tag in ["E1", "E2a", "E2b", "E3a", "E3b", "E3c", "E4a", "E4b", "E5", "E6"]:
        FL[tag] = nc.dram_tensor("flag" + tag, [2, 16], I32, kind=IN, addr_space=SH).ap()

    with ExitStack() as st:
        ctx = Ctx(nc, st)
        g, sy, t, v, s = ctx.g, ctx.sy, ctx.t, ctx.v, ctx.s
        SMETA = ctx.sbuf("SMETA", [1, 4], I32)
        SFLAG = ctx.sbuf("SFLAG", [1, 4], I32)
        ZC = ctx.sbuf("ZC", [128, NCH, 1], BF16)
        ctx._sflag = SFLAG
        i = g.dma_start(out=SMETA[:, :], in_=D["meta"][:, :]); ctx.inc("ld", i, 16)
        ctx.wait(g, "ld"); ctx.wait(sy, "ld"); ctx.wait(v, "ld")
        ctx._smeta_sy = SMETA
        rpar = nc.alloc_registers("rpar", engines=[sy.engine])
        rno = nc.alloc_registers("rno", engines=[sy.engine])
        rfv = nc.alloc_registers("rfv", engines=[sy.engine])
        rctr = nc.alloc_registers("rctr", engines=[sy.engine])
        nc.regs_load(rpar, SMETA[0:1, 0:1])
        nc.regs_load(rno, SMETA[0:1, 1:2])
        sregs = (rpar, rno, rfv, rctr)
        rp_g = nc.alloc_registers("rp_g", engines=[g.engine])
        rpn_g = nc.alloc_registers("rpn_g", engines=[g.engine])
        nc.regs_load(rp_g, SMETA[0:1, 0:1])
        nc.regs_load(rpn_g, SMETA[0:1, 2:3])
        pregs = (rp_g, rpn_g)
        ctx._sregs = (rpar, rno, rfv, rctr)
        i = v.memset(ZC[:, :, :], 0.0); ctx.vv(i)
        ctx.wait(sy, "v")
        for nm in ("haloz_d", "halom_d", "halo1_d", "halom2_d"):
            i = sy.dma_start(out=L[nm].rearrange("(r p) w -> p r w", p=128),
                             in_=ZC[:, :, :], allow_slow_non_contiguous=True)
            ctx.inc("ld", i, 16)
        ctx.wait(sy, "ld"); ctx.wait(g, "ld")

        def barrier_all():
            for e_ in (g, sy, t, v, s):
                ctx.wait(e_, "xdone", ctx.val("xdone"))

        d1 = dict(D)
        d1.update({"o_xin": L["xin_d"], "o_h": L["h_scr"], "o_q": L["q0_d"],
                   "o_k": L["k0_d"], "o_v": L["v0_d"], "o_m8": L["m80_d"]})
        phase_u1(ctx, d1)
        ctx.hard_barrier()
        _xw(ctx, "E1", [(P["kpL0"], L["k0_d"], C, T, BF16),
                        (P["vpL0"], L["v0_d"], T, C, BF16)], pregs, SMETA, FL["E1"])
        _xpoll(ctx, "E1", FL["E1"], sregs, SFLAG, SMETA)
        barrier_all()
        _assemble_pair(ctx, "E1k", P["kpL0"], C, T, L["ks0_d"], pregs)
        _assemble_pair(ctx, "E1v", P["vpL0"], T, C, L["vs0_d"], pregs)
        sk = _mk_stage_k_stacked(ctx, L["ks0_d"], C)
        sv2 = _mk_stage_v_stacked(ctx, L["vs0_d"], C,
                                  lambda vs_: (vs_, 64 * (vs_ % 2), 64 * (1 - vs_ % 2)))
        d2 = dict(D)
        d2.update({"qin": L["q0_d"], "m8in": L["m80_d"], "ind": D["ind2t"],
                   "o_o": L["o0_d"], "stage_k": sk, "stage_v": sv2,
                   "ksrc": None, "vsrc": None})
        phase_u2(ctx, d2)
        ctx.hard_barrier()
        d3 = dict(D)
        d3.update({"oin": L["o0_d"], "xinin": L["xin_d"], "x0in": D["xT"],
                   "o_x2": L["x2_d"]})
        phase_u3(ctx, d3)
        ctx.hard_barrier()
        phase_uln(ctx, {"xin": L["x2_d"], "o_z": L["z0_d"]})
        ctx.hard_barrier()
        _xw(ctx, "E2a", [(P["hzp"], L["z0_d"][:, T - 1:T], C, 1, BF16)],
            pregs, SMETA, FL["E2a"])
        _xpoll(ctx, "E2a", FL["E2a"], sregs, SFLAG, SMETA)
        _halo_read(ctx, "E2a", P["hzp"], L["haloz_d"], sregs)
        barrier_all()
        xl_d = L["z0_d"]
        halo_d = L["haloz_d"]
        for lay in range(2):
            sfx = str(lay)
            kp = P["kpF"] if lay == 0 else P["kpF2"]
            vp = P["vpF"] if lay == 0 else P["vpF2"]
            tags = (("E2b", "E3a") if lay == 0 else ("E3c", "E4a"))
            d4 = dict(D)
            d4.update({"xl": xl_d, "halo": halo_d, "wqkv": D["wqkv" + sfx],
                       "qscol": D["qscol" + sfx], "kscol": D["kscol" + sfx],
                       "tabs": D["tabsF"], "bselin": D["bsel"],
                       "o_q": L["qf_d"], "o_k": L["kf_d"], "o_v": L["vf_d"]})
            phase_u4(ctx, d4)
            ctx.hard_barrier()
            _xw(ctx, tags[0], [(kp, L["kf_d"], 512, T, BF16),
                               (vp, L["vf_d"], T, 512, BF16)], pregs, SMETA, FL[tags[0]])
            _xpoll(ctx, tags[0], FL[tags[0]], sregs, SFLAG, SMETA)
            barrier_all()

            _assemble_pair(ctx, tags[0] + "k", kp, 512, T, L["ksF_d"], pregs)
            _assemble_pair(ctx, tags[0] + "v", vp, T, 512, L["vsF_d"], pregs)

            def stage_k5(h, khead):
                i = sy.dma_start(out=khead[0:64, 0:512],
                                 in_=L["ksF_d"][64 * h:64 * h + 64, T - 512:T])
                ctx.inc("ldf", i, 16)
                i = sy.dma_start(out=khead[0:64, 512:512 + T],
                                 in_=L["ksF_d"][512 + 64 * h:512 + 64 * h + 64, :])
                ctx.inc("ldf", i, 16)

            def stage_v5(h, vhead):
                c0 = 64 * (h % 2)
                c1 = 64 * (1 - (h % 2))
                ctx.wait(v, "t_pv", ctx.val("t_pv"))
                ctx.wait(v, "ldf", ctx.val("ldf"))
                i = v.memset(vhead[:, :, c1:c1 + 1], 1.0); ctx.vv(i)
                ctx.wait(g, "v")
                i = g.dma_start(out=vhead[:, 0:4, c0:c0 + 64],
                                in_=L["vsF_d"][T - 512:T, 64 * h:64 * h + 64]
                                .rearrange("(a p) w -> p a w", p=128))
                ctx.inc("ldf", i, 16)
                for j in range(4):
                    i = g.dma_start(out=vhead[:, 4 + 4 * j:8 + 4 * j, c0:c0 + 64],
                                    in_=L["vsF_d"][T + 512 * j:T + 512 * (j + 1),
                                                   64 * h:64 * h + 64]
                                    .rearrange("(a p) w -> p a w", p=128))
                    ctx.inc("ldf", i, 16)
                ctx.wait(g, "v")

            d5 = dict(D)
            d5.update({"qin": L["qf_d"], "ind": D["indloc"], "xlin": xl_d,
                       "woin": D["wo" + sfx], "o_xm": L["xm_d"], "mcb": L["mcb"],
                       "stage_k": stage_k5, "stage_v": stage_v5,
                       "ksrc": None, "vsrc": None})
            phase_u5(ctx, d5)
            ctx.hard_barrier()
            hm = P["hmp"] if lay == 0 else P["hm2p"]
            hm_local = L["halom_d"] if lay == 0 else L["halom2_d"]
            _xw(ctx, tags[1], [(hm, L["xm_d"][:, T - 1:T], C, 1, BF16)],
                pregs, SMETA, FL[tags[1]])
            _xpoll(ctx, tags[1], FL[tags[1]], sregs, SFLAG, SMETA)
            _halo_read(ctx, tags[1], hm, hm_local, sregs)
            barrier_all()
            final = (lay == 1)
            d6 = dict(D)
            d6.update({"xmin": L["xm_d"], "halo": hm_local,
                       "gcol": D["gcol" + sfx], "bcol": D["bcol" + sfx],
                       "w1in": D["w1" + sfx], "b1a": D["b1a" + sfx],
                       "b1g": D["b1g" + sfx], "w2in": D["w2" + sfx],
                       "b2c": D["b2c" + sfx], "o_hln": L["hln_scr"],
                       "o_out": (L["x3_d"] if final else L["x1_d"])})
            if final:
                d6["x2in"] = L["x2_d"]
            phase_u6(ctx, d6, final)
            ctx.hard_barrier()
            if not final:
                _xw(ctx, "E3b", [(P["h1p"], L["x1_d"][:, T - 1:T], C, 1, BF16)],
                    pregs, SMETA, FL["E3b"])
                _xpoll(ctx, "E3b", FL["E3b"], sregs, SFLAG, SMETA)
                _halo_read(ctx, "E3b", P["h1p"], L["halo1_d"], sregs)
                barrier_all()
                xl_d = L["x1_d"]
                halo_d = L["halo1_d"]
        d10 = dict(D)
        d10.update({"x3in": L["x3_d"], "wqa": D["Wqa"], "wqbp": D["WqbP"],
                    "wkva": D["Wkva"], "wkbp": D["WkbP"], "wvb": D["Wvb"],
                    "tabs": D["tabsL1"], "qselin": D["qsel"],
                    "o_q": L["qL_d"], "o_k": L["kL_d"], "o_v": L["vL_d"],
                    "o_qn": L["qn_d"]})
        phase_u10(ctx, d10)
        ctx.hard_barrier()
        _xw(ctx, "E5", [(P["kpL1"], L["kL_d"], 256, T, BF16),
                        (P["vpL1"], L["vL_d"], T, 256, BF16)], pregs, SMETA, FL["E5"])
        _xpoll(ctx, "E5", FL["E5"], sregs, SFLAG, SMETA)
        barrier_all()
        _assemble_pair(ctx, "E5k", P["kpL1"], 256, T, L["ksL_d"], pregs)
        _assemble_pair(ctx, "E5v", P["vpL1"], T, 256, L["vsL_d"], pregs)
        skL = _mk_stage_k_stacked(ctx, L["ksL_d"], 256)
        svL = _mk_stage_v_stacked(ctx, L["vsL_d"], 256,
                                  lambda vs_: (vs_ // 2, 64 * (vs_ % 2), 64 * (1 - vs_ % 2)))
        d11 = dict(D)
        d11.update({"qin": L["qL_d"], "qnin": L["qn_d"], "ind": D["ind2t"],
                    "o_o": L["oL_d"], "kmb": L["kmb"],
                    "stage_k": skL, "stage_v": svL, "ksrc": None, "vsrc": None})
        phase_u11(ctx, d11)
        ctx.hard_barrier()
        d12 = dict(D)
        d12.update({"oin": L["oL_d"], "wout": D["Wout"] if "Wout" in D else None,
                    "x3in": L["x3_d"], "o_x4": L["x4_d"]})
        d12["wout"] = D.get("Wout")
        phase_u12(ctx, d12)
        ctx.hard_barrier()
        phase_f_wq(ctx, D, L)
        ctx.hard_barrier()
        _xw(ctx, "E6", [(P["wfcqp"], L["wfcq_d"], 2048, 1024, BF16),
                        (P["wpjqp"], L["wpjq_d"], 512, 4096, BF16)],
            pregs, SMETA, FL["E6"])
        _xpoll(ctx, "E6", FL["E6"], sregs, SFLAG, SMETA)
        barrier_all()
        phase_f_zq(ctx, D, L)
        ctx.hard_barrier()
        phase_f_fc(ctx, D, L, P)
        ctx.hard_barrier()
        phase_f_proj(ctx, D, L, P, o_final)
    return nc


_PROGS = {}


def _prog(name, builder, *a):
    key = (name,) + a
    if key not in _PROGS:
        _PROGS[key] = builder(*a)
    return _PROGS[key]


def _perm_eo(n):
    idx = []
    for h in range(n):
        base = 64 * h
        idx += [base + d for d in range(0, 64, 2)]
        idx += [base + d for d in range(1, 64, 2)]
    return np.array(idx)


def _feat_gather(res, name, width=C):
    out = np.zeros((4, TFULL, width), np.float32)
    for c in range(8):
        b, hf = c // 2, c % 2
        out[b, hf * T:(hf + 1) * T, :] = np.asarray(res.results[c][name]).astype(np.float32).T
    return out



def kernel_fused(**inputs):
    z = inputs
    cores = list(range(8))
    qsel = _qsel_host()
    xs = _shard_xT(np.asarray(z["L0_attn_x"], np.float32))
    masks = _sdpa_masks_host()
    lama, lamb = _la_masks_host()
    rotp = _rotperm_host()
    bsel = np.vstack([np.concatenate([np.ones(64), np.zeros(64)]),
                      np.concatenate([np.zeros(64), np.ones(64)])]).astype(np.float32)
    pq = _perm_eo(16)
    pk = _perm_eo(4)
    wqbp = np.ascontiguousarray(z["L1_attn_Wqb"][:, pq])
    kvw = z["L1_attn_Wkvb"].reshape(32, 4, 64, 2)
    wkb = np.ascontiguousarray(kvw[:, :, :, 0].reshape(32, 256)[:, pk])
    wvb = np.ascontiguousarray(kvw[:, :, :, 1].reshape(32, 256))
    kvsel = np.zeros((4, 16), np.float32)
    for m in range(16):
        kvsel[m // 4, m] = 1.0
    nonce = int(np.random.randint(1, 2 ** 30))
    shared = {
        "qKT": np.ascontiguousarray(z["L0_attn_qK"].T),
        "kKT": np.ascontiguousarray(z["L0_attn_kK"].T),
        "vKT": np.ascontiguousarray(z["L0_attn_vK"].T),
        "qV": z["L0_attn_qV"], "kV": z["L0_attn_kV"], "vV": z["L0_attn_vV"],
        "qsel": qsel,
        "pKT": np.ascontiguousarray(z["L0_attn_pK"].T), "pV": z["L0_attn_pV"],
        "fKT": np.ascontiguousarray(z["L0_attn_fK"].T), "fV": z["L0_attn_fV"],
        "masks": masks, "lam_a": lama, "lam_b": lamb, "rotp": rotp, "bsel": bsel,
        "Wqa": z["L1_attn_Wqa"], "WqbP": wqbp, "Wkva": z["L1_attn_Wkva"],
        "WkbP": wkb, "Wvb": wvb, "kvsel": kvsel,
        "boutc": z["L1_attn_bout"].reshape(8, 128).T.astype(np.float32).copy(),
        "Wout": z["L1_attn_Wout"],
        "bfcc": z["L1_ffn_b_fc"].reshape(32, 128).T.astype(np.float32).copy(),
        "bfcrow": z["L1_ffn_b_fc"].reshape(1, 4096),
        "bpc": z["L1_ffn_b_proj"].reshape(8, 128).T.astype(np.float32).copy(),
        "identb": np.eye(128, dtype=np.float32),
    }
    for lay in range(2):
        b1 = z["L0_ffn_b1"][lay]
        B1A = np.zeros((128, 22), np.float32)
        B1G = np.zeros((128, 22), np.float32)
        for i_, (off, M) in enumerate(W1CH):
            B1A[0:M, i_] = b1[off:off + M]
            B1G[0:M, i_] = b1[2730 + off:2730 + off + M]
        shared.update({
            f"wqkv{lay}": z["L0_ffn_wqkv"][lay],
            f"qscol{lay}": np.tile(z["L0_ffn_q_scale"][lay], 2).reshape(128, 1).astype(np.float32),
            f"kscol{lay}": np.tile(z["L0_ffn_k_scale"][lay], 2).reshape(128, 1).astype(np.float32),
            f"wo{lay}": z["L0_ffn_wo"][lay],
            f"gcol{lay}": z["L0_ffn_ln_g"][lay].reshape(8, 128).T.astype(np.float32).copy(),
            f"bcol{lay}": z["L0_ffn_ln_b"][lay].reshape(8, 128).T.astype(np.float32).copy(),
            f"w1{lay}": z["L0_ffn_w1"][lay], f"b1a{lay}": B1A, f"b1g{lay}": B1G,
            f"w2{lay}": z["L0_ffn_w2"][lay],
            f"b2c{lay}": z["L0_ffn_b2"][lay].reshape(8, 128).T.astype(np.float32).copy(),
        })
    ins = []
    for c in cores:
        hf = c % 2
        ind2t = np.zeros((1, KEYS), np.float32)
        indloc = np.zeros((1, BKEYS), np.float32)
        if hf == 0:
            ind2t[0, 0:T] = 1.0
            indloc[0, 0:HALO] = 1.0
        d = dict(shared)
        d.update({
            "xT": xs[c],
            "tabsF": _l0ffn_tables(hf * T),
            "tabsL1": _l1_tables(hf * T),
            "ind2t": ind2t.astype(BF), "indloc": indloc.astype(BF),
            "wfc_half": np.ascontiguousarray(z["L1_ffn_w_fc"][hf * 2048:(hf + 1) * 2048, :]),
            "wpj_half": np.ascontiguousarray(z["L1_ffn_w_proj"][hf * 512:(hf + 1) * 512, :]),
            "meta": np.array([[hf, nonce, 1 - hf, 0]], np.int32),
        })
        ins.append(d)
    res = run_bass_kernel_spmd(_prog("fused", build_fused), ins, core_ids=cores)
    out = np.zeros((4, TFULL, C), np.float32)
    for c in cores:
        b, hf = c // 2, c % 2
        out[b, hf * T:(hf + 1) * T, :] = np.asarray(res.results[c]["o_final"]).T
    return out


def kernel_unfused(**inputs):
    z = inputs
    cores = list(range(8))
    run = run_bass_kernel_spmd

    qsel = _qsel_host()
    xs = _shard_xT(np.asarray(z["L0_attn_x"], np.float32))
    qKT = np.ascontiguousarray(z["L0_attn_qK"].T)
    kKT = np.ascontiguousarray(z["L0_attn_kK"].T)
    vKT = np.ascontiguousarray(z["L0_attn_vK"].T)
    ins = [{"xT": xs[c], "qKT": qKT, "kKT": kKT, "vKT": vKT,
            "qV": z["L0_attn_qV"], "kV": z["L0_attn_kV"], "vV": z["L0_attn_vV"],
            "qsel": qsel} for c in cores]
    r1 = run(_prog("u1", build_u1), ins, core_ids=cores)

    masks = _sdpa_masks_host()
    ins = []
    for c in cores:
        b, hf = c // 2, c % 2
        kown = np.asarray(r1.results[c]["o_k"])
        kpart = np.asarray(r1.results[2 * b]["o_k"])
        vown = np.asarray(r1.results[c]["o_v"])
        vpart = np.asarray(r1.results[2 * b]["o_v"])
        ks = np.zeros((1024, 2 * T), BF)
        vs = np.zeros((16, 2 * T, 128), BF)
        indv = np.zeros((1, 2 * T), np.float32)
        if hf == 1:
            ks[:, 0:T] = kpart
        else:
            indv[0, 0:T] = 1.0
        ks[:, T:] = kown
        vo = vown.reshape(T, 16, 64).transpose(1, 0, 2)
        vp = vpart.reshape(T, 16, 64).transpose(1, 0, 2)
        one = np.ones((), BF)
        for h in range(16):
            c0, c1 = 64 * (h % 2), 64 * (1 - (h % 2))
            vs[h, T:, c0:c0 + 64] = vo[h]
            vs[h, T:, c1] = one
            if hf == 1:
                vs[h, 0:T, c0:c0 + 64] = vp[h]
                vs[h, 0:T, c1] = one
        ins.append({"q": np.asarray(r1.results[c]["o_q"]),
                    "m8": np.asarray(r1.results[c]["o_m8"]),
                    "ksrc": ks, "vsrc": vs, "ind": indv.astype(BF), "masks": masks})
    r2 = run(_prog("u2", build_u2), ins, core_ids=cores)

    pKT = np.ascontiguousarray(z["L0_attn_pK"].T)
    fKT = np.ascontiguousarray(z["L0_attn_fK"].T)
    ins = []
    for c in cores:
        ins.append({"o": np.asarray(r2.results[c]["o_o"]),
                    "xin": np.asarray(r1.results[c]["o_xin"]),
                    "x0T": xs[c], "pKT": pKT, "pV": z["L0_attn_pV"],
                    "fKT": fKT, "fV": z["L0_attn_fV"]})
    r3 = run(_prog("u3", build_u3), ins, core_ids=cores)
    x2 = [np.asarray(r3.results[c]["o_x2"]) for c in cores]

    rln = run(_prog("uln", build_uln), [{"xT": x2[c]} for c in cores], core_ids=cores)
    xl = [np.asarray(rln.results[c]["o_z"]) for c in cores]

    lama, lamb = _la_masks_host()
    rotp = _rotperm_host()
    bsel = np.vstack([np.concatenate([np.ones(64), np.zeros(64)]),
                      np.concatenate([np.zeros(64), np.ones(64)])]).astype(np.float32)
    for lay in range(2):
        qs = z["L0_ffn_q_scale"][lay]
        ksc = z["L0_ffn_k_scale"][lay]
        ins = []
        for c in cores:
            b, hf = c // 2, c % 2
            haloc = np.zeros((1024, 1), BF) if hf == 0 else                 np.ascontiguousarray(np.asarray(xl[2 * b])[:, T - 1:T])
            ins.append({
                "xl": xl[c], "halo": haloc,
                "wqkv": z["L0_ffn_wqkv"][lay],
                "qscol": np.tile(qs, 2).reshape(128, 1).astype(np.float32),
                "kscol": np.tile(ksc, 2).reshape(128, 1).astype(np.float32),
                "rotp": rotp, "bsel": bsel, "tabs": _l0ffn_tables((c % 2) * T),
            })
        r4 = run(_prog("u4", build_u4), ins, core_ids=cores)
        ins = []
        for c in cores:
            b, hf = c // 2, c % 2
            kown = np.asarray(r4.results[c]["o_k"])
            kpart = np.asarray(r4.results[2 * b]["o_k"])
            vown = np.asarray(r4.results[c]["o_v"]).astype(np.float32)
            vpart = np.asarray(r4.results[2 * b]["o_v"]).astype(np.float32)
            ks = np.zeros((512, BKEYS), BF)
            vs = np.zeros((8, BKEYS, 128), np.float32)
            indv = np.zeros((1, BKEYS), np.float32)
            if hf == 1:
                ks[:, 0:HALO] = kpart[:, T - HALO:T]
            else:
                indv[0, 0:HALO] = 1.0
            ks[:, HALO:] = kown
            vto = vown.reshape(T, 8, 64).transpose(1, 0, 2)
            vtp = vpart.reshape(T, 8, 64).transpose(1, 0, 2)
            for h in range(8):
                c0, c1 = 64 * (h % 2), 64 * (1 - (h % 2))
                vs[h, HALO:, c0:c0 + 64] = vto[h]
                vs[h, HALO:, c1] = 1.0
                if hf == 1:
                    vs[h, 0:HALO, c0:c0 + 64] = vtp[h, T - HALO:T]
                    vs[h, 0:HALO, c1] = 1.0
            ins.append({"q": np.asarray(r4.results[c]["o_q"]), "ksrc": ks,
                        "vsrc": vs.astype(BF), "ind": indv.astype(BF),
                        "lam_a": lama, "lam_b": lamb, "xl": xl[c],
                        "wo": z["L0_ffn_wo"][lay]})
        r5 = run(_prog("u5", build_u5), ins, core_ids=cores)
        xm = [np.asarray(r5.results[c]["o_xm"]) for c in cores]
        b1 = z["L0_ffn_b1"][lay]
        B1A = np.zeros((128, 22), np.float32)
        B1G = np.zeros((128, 22), np.float32)
        for i_, (off, M) in enumerate(W1CH):
            B1A[0:M, i_] = b1[off:off + M]
            B1G[0:M, i_] = b1[2730 + off:2730 + off + M]
        final = (lay == 1)
        ins = []
        for c in cores:
            b, hf = c // 2, c % 2
            haloc = np.zeros((1024, 1), BF) if hf == 0 else                 np.ascontiguousarray(np.asarray(xm[2 * b])[:, T - 1:T])
            d = {"xm": xm[c], "halo": haloc,
                 "gcol": z["L0_ffn_ln_g"][lay].reshape(8, 128).T.astype(np.float32).copy(),
                 "bcol": z["L0_ffn_ln_b"][lay].reshape(8, 128).T.astype(np.float32).copy(),
                 "w1": z["L0_ffn_w1"][lay], "b1a": B1A, "b1g": B1G,
                 "w2": z["L0_ffn_w2"][lay],
                 "b2c": z["L0_ffn_b2"][lay].reshape(8, 128).T.astype(np.float32).copy()}
            if final:
                d["x2T"] = x2[c]
            ins.append(d)
        r6 = run(_prog("u6", build_u6, final), ins, core_ids=cores)
        if not final:
            xl = [np.asarray(r6.results[c]["o_x1"]) for c in cores]
    x3 = [np.asarray(r6.results[c]["o_x3"]) for c in cores]

    pq = _perm_eo(16)
    pk = _perm_eo(4)
    wqbp = np.ascontiguousarray(z["L1_attn_Wqb"][:, pq])
    kvw = z["L1_attn_Wkvb"].reshape(32, 4, 64, 2)
    wkb = np.ascontiguousarray(kvw[:, :, :, 0].reshape(32, 256)[:, pk])
    wvb = np.ascontiguousarray(kvw[:, :, :, 1].reshape(32, 256))
    ins = []
    for c in cores:
        ins.append({"x3": x3[c].astype(BF), "Wqa": z["L1_attn_Wqa"], "WqbP": wqbp,
                    "Wkva": z["L1_attn_Wkva"], "WkbP": wkb, "Wvb": wvb,
                    "rotp": rotp, "tabs": _l1_tables((c % 2) * T), "qsel": qsel})
    r10 = run(_prog("u10", build_u10), ins, core_ids=cores)
    kvsel = np.zeros((4, 16), np.float32)
    for m in range(16):
        kvsel[m // 4, m] = 1.0
    ins = []
    for c in cores:
        b, hf = c // 2, c % 2
        kown = np.asarray(r10.results[c]["o_k"])
        kpart = np.asarray(r10.results[2 * b]["o_k"])
        vown = np.asarray(r10.results[c]["o_v"]).astype(np.float32)
        vpart = np.asarray(r10.results[2 * b]["o_v"]).astype(np.float32)
        ks = np.zeros((256, 2 * T), BF)
        vs = np.zeros((8, 2 * T, 128), np.float32)
        indv = np.zeros((1, 2 * T), np.float32)
        if hf == 1:
            ks[:, 0:T] = kpart
        else:
            indv[0, 0:T] = 1.0
        ks[:, T:] = kown
        vo = vown.reshape(T, 4, 64).transpose(1, 0, 2)
        vp = vpart.reshape(T, 4, 64).transpose(1, 0, 2)
        for g_ in range(4):
            for par in range(2):
                sl = 2 * g_ + par
                c0, c1 = 64 * par, 64 * (1 - par)
                vs[sl, T:, c0:c0 + 64] = vo[g_]
                vs[sl, T:, c1] = 1.0
                if hf == 1:
                    vs[sl, 0:T, c0:c0 + 64] = vp[g_]
                    vs[sl, 0:T, c1] = 1.0
        ins.append({"q": np.asarray(r10.results[c]["o_q"]),
                    "qn": np.asarray(r10.results[c]["o_qn"]),
                    "ksrc": ks, "vsrc": vs.astype(BF), "ind": indv.astype(BF),
                    "masks": masks, "kvsel": kvsel})
    r11 = run(_prog("u11", build_u11), ins, core_ids=cores)
    boutc = z["L1_attn_bout"].reshape(8, 128).T.astype(np.float32).copy()
    ins = [{"o": np.asarray(r11.results[c]["o_o"]), "Wout": z["L1_attn_Wout"],
            "boutc": boutc, "x3T": x3[c]} for c in cores]
    r12 = run(_prog("u12", build_u12), ins, core_ids=cores)
    x4 = [np.asarray(r12.results[c]["o_x4"]) for c in cores]

    ins = [{"wfc": np.ascontiguousarray(z["L1_ffn_w_fc"][512 * c:512 * (c + 1), :]),
            "wpj": np.ascontiguousarray(z["L1_ffn_w_proj"][128 * c:128 * (c + 1), :]),
            "x4T": x4[c]} for c in cores]
    r13 = run(_prog("u13", build_u13), ins, core_ids=cores)
    wfcq = np.concatenate([np.asarray(r13.results[c]["o_wfcq"]) for c in cores])
    wpjq = np.concatenate([np.asarray(r13.results[c]["o_wpjq"]) for c in cores])
    zl = [np.asarray(r13.results[c]["o_z"]) for c in cores]
    ins = [{"zin": np.ascontiguousarray(zl[c].T)} for c in cores]
    r14 = run(_prog("u14", build_u14, 1024), ins, core_ids=cores)
    wfcqT = np.ascontiguousarray(wfcq.T)
    bfcc = z["L1_ffn_b_fc"].reshape(32, 128).T.astype(np.float32).copy()
    ins = [{"xqT": np.ascontiguousarray(np.asarray(r14.results[c]["o_q"]).T),
            "wfcqT": wfcqT, "bfcc": bfcc} for c in cores]
    r15 = run(_prog("u15", build_u15), ins, core_ids=cores)
    ins = [{"zin": np.ascontiguousarray(np.asarray(r15.results[c]["o_h"]).T)}
           for c in cores]
    r16 = run(_prog("u16", build_u14, 4096), ins, core_ids=cores)
    wpjqT = np.ascontiguousarray(wpjq.T)
    bpc = z["L1_ffn_b_proj"].reshape(8, 128).T.astype(np.float32).copy()
    ins = [{"hqT": np.ascontiguousarray(np.asarray(r16.results[c]["o_q"]).T),
            "wpjqT": wpjqT, "bpc": bpc, "x4T": x4[c]} for c in cores]
    r17 = run(_prog("u17", build_u17), ins, core_ids=cores)

    out = np.zeros((4, TFULL, C), np.float32)
    for c in cores:
        b, hf = c // 2, c % 2
        out[b, hf * T:(hf + 1) * T, :] = np.asarray(r17.results[c]["o_x5"]).T
    return out


def phase_uln(ctx, d):
    nc = ctx.nc
    o_z = d["o_z"]
    xin = d["xin"]
    with ctx.scope("uln"):
        g, sy, t, v, s = ctx.g, ctx.sy, ctx.t, ctx.v, ctx.s
        XB = ctx.sbuf("XB", [128, NCH, T], BF16)
        ZB = ctx.sbuf("ZB", [128, NCH, T], BF16)
        SQ = ctx.sbuf("SQ", [128, TT], BF16)
        TMP32 = ctx.sbuf("TMP32", [128, TT], F32)
        _ssum = ctx.sbuf("ssum", [1, T], F32)
        _ssq = ctx.sbuf("ssq", [1, T], F32)
        _rs = ctx.sbuf("rs", [1, T], F32)
        rows = {"ssum": _ssum, "ssq": _ssq, "svar": _ssq, "rs": _rs}
        ROWSB = [ctx.sbuf(f"ROWSB{i_}", [1, T], BF16) for i_ in range(2)]
        EPS11 = ctx.sbuf("EPS11", [1, 1], F32)
        ones128b = ctx.sbuf("ones128b", [128, 1], BF16)
        ones1 = ctx.sbuf("ones1", [1, 128], BF16)
        i = g.dma_start(out=XB[:, :, :], in_=xin.rearrange("(r p) w -> p r w", p=128))
        ctx.inc("ld", i, 16)
        i = v.memset(ones128b[:, :], 1.0); ctx.vv(i)
        i = v.memset(ones1[:, :], 1.0); ctx.vv(i)
        i = v.memset(EPS11[:, :], EPS); ctx.vv(i)
        ctx.wait(t, "ld"); ctx.wait(s, "ld"); ctx.wait(v, "ld")
        ctx.wait(t, "v"); ctx.wait(s, "v")
        with ExitStack() as pp:
            ps_a = pp.enter_context(ctx.nc.psum_tensor(ctx.uniq("ps_a"), [1, TT], F32))
            ps_b = pp.enter_context(ctx.nc.psum_tensor(ctx.uniq("ps_b"), [1, TT], F32))
            ps_mu = pp.enter_context(ctx.nc.psum_tensor(ctx.uniq("ps_mu"), [128, TT], F32))
            ps_r = pp.enter_context(ctx.nc.psum_tensor(ctx.uniq("ps_r"), [128, TT], F32))
            emit_ln_stats(ctx, XB, rows, ROWSB + [None], SQ, ones128b,
                          (ps_a, ps_b), double_ln=False, eps11=EPS11)
            emit_ln_apply(ctx, XB, ROWSB, 1, ZB, ones1, (ps_mu, ps_r, None), TMP32)
        ctx.wait(sy, "v")
        i = sy.dma_start(out=o_z.rearrange("(r p) w -> p r w", p=128), in_=ZB[:, :, :])
        ctx.inc("stx", i, 16)
        ctx.wait(sy, "stx")

def build_uln():
    nc = bass.Bass("TRN2", target_bir_lowering=False)
    xin = nc.dram_tensor("xT", [C, T], F32, kind="ExternalInput").ap()
    o_z = nc.dram_tensor("o_z", [C, T], BF16, kind="ExternalOutput").ap()
    d = {"o_z": o_z, "xin": xin}
    with ExitStack() as st:
        ctx = Ctx(nc, st)
        phase_uln(ctx, d)
    return nc


def kernel(**inputs):
    import os
    if os.environ.get("MK_UNFUSED"):
        return kernel_unfused(**inputs)
    return kernel_fused(**inputs)


# version 05
# speedup vs baseline: 2.0003x; 1.0306x over previous
import numpy as np
import ml_dtypes
from contextlib import ExitStack

import concourse.bass as bass
import concourse.mybir as mybir
from concourse.bass_types import AP
from concourse.bass_utils import run_bass_kernel_spmd

F32 = mybir.dt.float32
BF16 = mybir.dt.bfloat16
I32 = mybir.dt.int32
AF = mybir.ActivationFunctionType
ALU = mybir.AluOpType
AX = mybir.AxisListType

B, TFULL, C = 4, 4096, 1024
T = 2048
NCH = C // 128
TT = 512
NTT = T // TT
EPS = 1e-5
KEYS = 2 * T
BIGNEG = -1.0e30
BF = ml_dtypes.bfloat16


class Ctx:
    def __init__(self, nc, st):
        self.nc = nc
        self.st = st
        self.sems = {}
        self.g = nc.gpsimd
        self.sy = nc.sync
        self.t = nc.tensor
        self.v = nc.vector
        self.s = nc.scalar
        self.sbuf_bytes = 0

    def sem(self, name):
        if name not in self.sems:
            h = self.st.enter_context(self.nc.semaphore(name))
            self.sems[name] = [h, 0]
        return self.sems[name]

    def inc(self, name, instr, k):
        rec = self.sem(name)
        instr.then_inc(rec[0], k)
        rec[1] += k
        return rec[1]

    def val(self, name):
        return self.sem(name)[1]

    def wait(self, eng, name, val=None):
        rec = self.sem(name)
        eng.wait_ge(rec[0], rec[1] if val is None else val)

    def sbuf(self, name, shape, dtype):
        nb = int(np.prod(shape[1:])) * {F32: 4, BF16: 2, I32: 4}[dtype]
        self.sbuf_bytes += nb
        assert self.sbuf_bytes < 205 * 1024, f"SBUF over budget at {name}: {self.sbuf_bytes}"
        pre = getattr(self, "_prefix", "")
        return self.st.enter_context(self.nc.sbuf_tensor(pre + name, list(shape), dtype))

    def psum(self, name, shape, dtype=F32):
        pre = getattr(self, "_prefix", "")
        return self.st.enter_context(self.nc.psum_tensor(pre + name, list(shape), dtype))

    def uniq(self, name):
        return getattr(self, "_prefix", "") + name

    def scope(self, prefix):
        import contextlib

        self._scope_n = getattr(self, "_scope_n", 0) + 1
        _n = self._scope_n

        @contextlib.contextmanager
        def _sc():
            saved_st, saved_bytes, saved_pre = self.st, self.sbuf_bytes, getattr(self, "_prefix", "")
            with ExitStack() as st2:
                self.st = st2
                self._prefix = saved_pre + prefix + str(_n) + "_"
                try:
                    yield self
                finally:
                    self.st = saved_st
                    self.sbuf_bytes = saved_bytes
                    self._prefix = saved_pre
        return _sc()

    def tv(self, i):
        return self.inc("t", i, 1)

    def vv(self, i):
        return self.inc("v", i, 1)

    def sv(self, i):
        return self.inc("s", i, 1)

    def hard_barrier(self):
        for eng in (self.t, self.v, self.s, self.g, self.sy):
            for nm, rec in list(self.sems.items()):
                if rec[1] > 0:
                    eng.wait_ge(rec[0], rec[1])

    def wall(self, eng):
        for nm in ("t", "v", "s", "ld"):
            if nm in self.sems and self.sems[nm][1] > 0:
                self.wait(eng, nm)


def load_weight(ctx, dst, src_ap, rows=128):
    i = ctx.g.dma_start(out=dst[:, :, :],
                        in_=src_ap.rearrange("(r p) w -> p r w", p=rows))
    return ctx.inc("ld", i, 16)


def emit_ln_stats(ctx, xsrc_bf, rows, rows_bf, sq_scratch, ones128b, ps_pool,
                  double_ln=False, nch=NCH, nfeat=None, eps11=None):
    t, s, v = ctx.t, ctx.s, ctx.v
    nfeat = nfeat if nfeat is not None else nch * 128.0
    ssum, ssq, svar, rs = rows["ssum"], rows["ssq"], rows["svar"], rows["rs"]
    get = xsrc_bf if callable(xsrc_bf) else (lambda c, tsl: xsrc_bf[:, c, tsl])
    for tt in range(NTT):
        tsl = slice(tt * TT, (tt + 1) * TT)
        ps_sum, ps_sq = ps_pool
        for c in range(nch):
            ctx.wait(s, "t")
            i = s.square(sq_scratch[:, 0:TT], get(c, tsl)); ctx.sv(i)
            ctx.wait(t, "s")
            t.matmul(ps_sum[:, :], ones128b[:, 0:1], get(c, tsl),
                     start=(c == 0), stop=(c == nch - 1))
            i = t.matmul(ps_sq[:, :], ones128b[:, 0:1], sq_scratch[:, 0:TT],
                         start=(c == 0), stop=(c == nch - 1))
            ctx.tv(i)
        ctx.wait(v, "t")
        i = v.tensor_scalar_mul(ssum[0:1, tsl], ps_sum[:, :], 1.0 / nfeat); ctx.vv(i)
        i = v.tensor_scalar_mul(ssq[0:1, tsl], ps_sq[:, :], 1.0 / nfeat); ctx.vv(i)
    ctx.wait(v, "v")
    i = v.tensor_mul(rs[0:1, :], ssum[0:1, :], ssum[0:1, :]); ctx.vv(i)
    ctx.wait(v, "v")
    i = v.tensor_sub(svar[0:1, :], ssq[0:1, :], rs[0:1, :]); ctx.vv(i)
    ctx.wait(s, "v")
    i = s.activation(rs[0:1, :], svar[0:1, :], AF.Sqrt, bias=eps11[0:1, 0:1]); ctx.sv(i)
    ctx.wait(v, "s")
    i = v.reciprocal(rs[0:1, :], rs[0:1, :]); ctx.vv(i)
    ctx.wait(v, "v")
    i = v.tensor_copy(rows_bf[0][0:1, :], ssum[0:1, :]); ctx.vv(i)
    i = v.tensor_copy(rows_bf[1][0:1, :], rs[0:1, :]); ctx.vv(i)
    if double_ln:
        ctx.wait(v, "v")
        i = v.tensor_mul(ssq[0:1, :], svar[0:1, :], rs[0:1, :]); ctx.vv(i)
        ctx.wait(v, "v")
        i = v.tensor_mul(ssq[0:1, :], ssq[0:1, :], rs[0:1, :]); ctx.vv(i)
        ctx.wait(s, "v")
        i = s.activation(ssq[0:1, :], ssq[0:1, :], AF.Sqrt, bias=eps11[0:1, 0:1]); ctx.sv(i)
        ctx.wait(v, "s")
        i = v.reciprocal(ssq[0:1, :], ssq[0:1, :]); ctx.vv(i)
        ctx.wait(v, "v")
        i = v.tensor_mul(rs[0:1, :], rs[0:1, :], ssq[0:1, :]); ctx.vv(i)
        ctx.wait(v, "v")
        i = v.tensor_copy(rows_bf[2][0:1, :], rs[0:1, :]); ctx.vv(i)
    ctx.wait(t, "v")


def emit_ln_apply(ctx, xsrc_f32, rows_bf, row_idx, out_bf, ones1, ps3, tmp32,
                  nch=NCH, gain_bias=None):
    t, v, s = ctx.t, ctx.v, ctx.s
    ps_mu, ps_r, _ = ps3
    get = xsrc_f32 if callable(xsrc_f32) else (lambda c, tsl: xsrc_f32[:, c, tsl])
    for tt in range(NTT):
        tsl = slice(tt * TT, (tt + 1) * TT)
        ctx.wait(t, "v")
        t.matmul(ps_mu[:, :], ones1[0:1, :], rows_bf[0][0:1, tsl], start=True, stop=True)
        i = t.matmul(ps_r[:, :], ones1[0:1, :], rows_bf[row_idx][0:1, tsl],
                     start=True, stop=True)
        ctx.tv(i)
        ctx.wait(v, "t")
        for c in range(nch):
            ctx.wait(v, "v")
            i = v.tensor_sub(tmp32[:, 0:TT], get(c, tsl), ps_mu[:, :]); ctx.vv(i)
            ctx.wait(v, "v")
            if gain_bias is None:
                i = v.tensor_mul(out_bf[:, c, tsl], tmp32[:, 0:TT], ps_r[:, :]); ctx.vv(i)
            else:
                i = v.tensor_mul(tmp32[:, 0:TT], tmp32[:, 0:TT], ps_r[:, :]); ctx.vv(i)
                gcols, bcols = gain_bias
                ctx.wait(v, "v")
                i = v.tensor_scalar(out_bf[:, c, tsl], tmp32[:, 0:TT],
                                    gcols[:, c:c + 1], bcols[:, c:c + 1],
                                    ALU.mult, ALU.add)
                ctx.vv(i)


def fold_max128_to_scalar(ctx, col128, row128, out11):
    v, sy = ctx.v, ctx.sy
    ctx.wait(sy, "v")
    i = sy.dma_start(out=row128[0:1, :], in_=col128[:, 0:1])
    ctx.inc("foldst", i, 16)
    ctx.wait(v, "foldst")
    i = v.reduce_max(out11[0:1, 0:1], row128[0:1, :], axis=AX.X); ctx.vv(i)
    ctx.wait(v, "v")


def emit_weight_maxrownorm2(ctx, WT_sb, nw, sqw, ones128b, ps1, rown, out11,
                            nch=NCH):
    t, s, v = ctx.t, ctx.s, ctx.v
    for nt in range(nw // TT):
        for cc in range(nch):
            ctx.wait(s, "t")
            i = s.square(sqw[:, 0:TT], WT_sb[:, cc, nt * TT:(nt + 1) * TT]); ctx.sv(i)
            ctx.wait(t, "s")
            i = t.matmul(ps1[:, :], ones128b[:, 0:1], sqw[:, 0:TT],
                         start=(cc == 0), stop=(cc == nch - 1))
            ctx.tv(i)
        ctx.wait(v, "t")
        i = v.tensor_copy(rown[0:1, nt * TT:(nt + 1) * TT], ps1[:, :]); ctx.vv(i)
    ctx.wait(v, "v")
    i = v.reduce_max(out11[0:1, 0:1], rown[0:1, 0:nw], axis=AX.X); ctx.vv(i)
    ctx.wait(v, "v")


def emit_natural_maxrownorm2(ctx, W_sb, nch_w, width, sq, col128, scratch32,
                             out11):
    t, s, v = ctx.t, ctx.s, ctx.v
    for ncw in range(nch_w):
        ctx.wait(s, "v")
        i = s.square(sq[:, 0:width], W_sb[:, ncw, 0:width]); ctx.sv(i)
        ctx.wait(v, "s")
        i = v.reduce_sum(col128[:, ncw:ncw + 1], sq[:, 0:width], axis=AX.X); ctx.vv(i)
    ctx.wait(v, "v")
    i = v.reduce_max(col128[:, nch_w:nch_w + 1], col128[:, 0:nch_w], axis=AX.X); ctx.vv(i)
    fold_max128_to_scalar(ctx, col128[:, nch_w:nch_w + 1], scratch32, out11)


def emit_pattention(ctx, b_, KT_sb, V_sb, rhs_sb, out_mode, out_sb, nw,
                    biascol, scale, tag, qnorm=None, mrow=None, stream=None):
    t, s, v, g = ctx.t, ctx.s, ctx.v, ctx.g
    nwch = nw // 128
    ps_sc2, ps_den, ps_bc, ps_o2 = b_["ps_sc2"], b_["ps_den"], b_["ps_bc"], b_["ps_o2"]
    PT, drec, drecb, bcdb = b_["PT"], b_["drec"], b_["drecb"], b_["bcdb"]
    ones1, ones128b, sq = b_["ones1"], b_["ones128b"], b_["sq"]
    if stream is not None:
        kt_dram, v_dram = stream

    for tt in range(NTT):
        tsl = slice(tt * TT, (tt + 1) * TT)
        s_ex = [None] * nwch
        ld_ct = [None] * nwch
        t_qk = [None] * nwch

        def emit_sc(nc_):
            if stream is not None:
                db = nc_ % 2
                if nc_ >= 2 and t_qk[nc_ - 2] is not None:
                    ctx.wait(g, "t", t_qk[nc_ - 2])
                i = g.dma_start(
                    out=KT_sb[:, :, db * 128:db * 128 + 128],
                    in_=kt_dram[:, nc_ * 128:(nc_ + 1) * 128].rearrange(
                        "(r p) w -> p r w", p=128))
                ld_ct[nc_] = ctx.inc("lds", i, 16)
                ctx.wait(t, "lds", ld_ct[nc_])
                ktv = KT_sb[:, :, db * 128:db * 128 + 128]
            if nc_ >= 2 and s_ex[nc_ - 2] is not None:
                ctx.wait(t, "s", s_ex[nc_ - 2])
            for cc in range(NCH):
                lhsT = (KT_sb[:, cc, nc_ * 128:(nc_ + 1) * 128] if stream is None
                        else ktv[:, cc, :])
                i = t.matmul(ps_sc2[nc_ % 2][:, :], lhsT,
                             rhs_sb[:, cc, tsl], start=(cc == 0),
                             stop=(cc == NCH - 1 and mrow is None))
            if mrow is not None:
                i = t.matmul(ps_sc2[nc_ % 2][:, :],
                             mrow[1][0:1, nc_ * 128:(nc_ + 1) * 128],
                             mrow[0][0:1, tsl], start=False, stop=True)
            t_qk[nc_] = ctx.tv(i)
            ctx.wait(s, "t", t_qk[nc_])
            i = s.activation(PT[:, nc_, :], ps_sc2[nc_ % 2][:, :], AF.Exp,
                             bias=biascol[:, 0:1], scale=scale)
            s_ex[nc_] = ctx.sv(i)

        for nc_ in range(nwch):
            emit_sc(nc_)
        ctx.wait(t, "s")
        ctx.wait(t, "v")
        for nc_ in range(nwch):
            i = t.matmul(ps_den[:, :], ones128b[:, 0:1], PT[:, nc_, :],
                         start=(nc_ == 0), stop=(nc_ == nwch - 1))
        ctx.tv(i)
        ctx.wait(v, "t")
        i = v.reciprocal(drec[:, :], ps_den[:, :]); ctx.vv(i)
        ctx.wait(v, "v")
        i = v.tensor_copy(drecb[:, :], drec[:, :]); ctx.vv(i)
        ctx.wait(t, "v")
        i = t.matmul(ps_bc[:, :], ones1[0:1, :], drecb[0:1, :], start=True, stop=True)
        ctx.tv(i)
        ctx.wait(v, "t")
        i = v.tensor_copy(bcdb[:, :], ps_bc[:, :]); ctx.vv(i)
        ctx.wait(v, "v")
        for nc_ in range(nwch):
            i = v.tensor_mul(PT[:, nc_, :], PT[:, nc_, :], bcdb[:, :])
        ctx.vv(i)
        ctx.wait(t, "v")
        if out_mode == "feat":
            ev_ct = [None] * NCH
            mm_ct = [None] * NCH
            for oc in range(NCH):
                if oc >= 2:
                    ctx.wait(t, "v", ev_ct[oc - 2])
                ctx.wait(t, "t_qn", ctx.val("t_qn"))
                for nc_ in range(nwch):
                    i = t.matmul(ps_o2[oc % 2][:, :],
                                 V_sb[:, nc_, oc * 128:(oc + 1) * 128],
                                 PT[:, nc_, :], start=(nc_ == 0), stop=(nc_ == nwch - 1))
                mm_ct[oc] = ctx.tv(i)
                ctx.wait(v, "t", mm_ct[oc])
                i = v.tensor_copy(out_sb[:, oc, tsl], ps_o2[oc % 2][:, :])
                ev_ct[oc] = ctx.vv(i)
                if qnorm is not None:
                    qsel_sb, ps_qn, m8_sb, mk2col = qnorm
                    ctx.wait(s, "v", ev_ct[oc])
                    i = s.square(sq[:, 0:TT], out_sb[:, oc, tsl]); ctx.sv(i)
                    ctx.wait(t, "s")
                    i = t.matmul(ps_qn[:, :], qsel_sb[:, oc * 16:(oc + 1) * 16],
                                 sq[:, 0:TT], start=(oc == 0), stop=(oc == NCH - 1))
                    ctx.inc("t_qn", i, 1)
            if qnorm is not None:
                qsel_sb, ps_qn, m8_sb, mk2col = qnorm
                ctx.wait(s, "t_qn")
                i = s.activation(m8_sb[0:16, tsl], ps_qn[:, :], AF.Sqrt,
                                 scale=mk2col[:, 0:1])
                ctx.sv(i)
        else:
            tokdram = b_.get("tok_dram")
            for tc in range(TT // 128):
                tgl = tt * (TT // 128) + tc
                for ch in range(2):
                    ctx.wait(t, "v")
                    for nc_ in range(nwch):
                        i = t.matmul(ps_o2[0][:, :], PT[:, nc_, tc * 128:(tc + 1) * 128],
                                     V_sb[:, nc_, ch * 512:(ch + 1) * 512],
                                     start=(nc_ == 0), stop=(nc_ == nwch - 1))
                    ctx.tv(i)
                    ctx.wait(v, "t")
                    ctx.wait(v, "sttok", ctx.val("sttok"))
                    i = v.tensor_copy(out_sb[:, 0, ch * 512:(ch + 1) * 512], ps_o2[0][:, :])
                    ctx.vv(i)
                if tokdram is not None:
                    ctx.wait(ctx.sy, "v")
                    i = ctx.sy.dma_start(out=tokdram[tgl * 128:(tgl + 1) * 128, :],
                                         in_=out_sb[:, 0, :])
                    ctx.inc("sttok", i, 16)


def phase_u1(ctx, d):
    nc = ctx.nc
    xT = d["xT"]
    outs = d
    w_in = d
    with ctx.scope("u1"):
        g, sy, t, v, s = ctx.g, ctx.sy, ctx.t, ctx.v, ctx.s

        XB = ctx.sbuf("XB", [128, NCH, T], BF16)
        HT = ctx.sbuf("HT", [128, NCH, T], BF16)
        OUTF = ctx.sbuf("OUTF", [128, NCH, T], BF16)
        VTOK = ctx.sbuf("VTOK", [128, 1, C], BF16)
        WKT = ctx.sbuf("WKT", [128, NCH, 1024], BF16)
        WV = ctx.sbuf("WV", [128, NCH, 1024], BF16)
        PT = ctx.sbuf("PT", [128, NCH, TT], BF16)
        SQ = ctx.sbuf("SQ", [128, 1024], BF16)
        TMP32 = ctx.sbuf("TMP32", [128, TT], F32)
        M8 = ctx.sbuf("M8", [16, T], F32)
        QSEL = ctx.sbuf("QSEL", [128, NCH * 16], BF16)
        ones128b = ctx.sbuf("ones128b", [128, 1], BF16)
        ones1 = ctx.sbuf("ones1", [1, 128], BF16)
        rows = {k: ctx.sbuf(k, [1, T], F32) for k in ("ssum", "ssq", "svar", "rs")}
        ROWSB = [ctx.sbuf(f"ROWSB{i_}", [1, T], BF16) for i_ in range(3)]
        DREC = ctx.sbuf("DREC", [1, TT], F32)
        DRECB = ctx.sbuf("DRECB", [1, TT], BF16)
        BCDB = ctx.sbuf("BCDB", [128, TT], BF16)
        ROWN = ctx.sbuf("ROWN", [1, 1024], F32)
        MK11 = ctx.sbuf("MK11", [1, 1], F32)
        MK2B = ctx.sbuf("MK2B", [1, 1], BF16)
        MK2C = ctx.sbuf("MK2C", [16, 1], F32)
        BIAS = ctx.sbuf("BIAS", [128, 1], F32)
        SC32 = ctx.sbuf("SC32", [1, 128], F32)
        COL9 = ctx.sbuf("COL9", [128, NCH + 1], F32)
        KV11 = ctx.sbuf("KV11", [1, 1], F32)
        EPS11 = ctx.sbuf("EPS11", [1, 1], F32)

        ps_a = ctx.psum("ps_a", [1, TT])
        ps_b = ctx.psum("ps_b", [1, TT])
        ps_sc2 = [ctx.psum(f"ps_sc{i_}", [128, TT]) for i_ in range(2)]
        ps_sc = ps_sc2[0]
        ps_bc = ctx.psum("ps_bc", [128, TT])
        ps_o2 = [ctx.psum(f"ps_o{i_}", [128, TT]) for i_ in range(2)]
        ps_o = ps_o2[0]
        ps_qn = ctx.psum("ps_qn", [16, TT])
        ps_c16 = ps_qn[0:16, 0:1]

        b_ = dict(ps_sc2=ps_sc2, ps_den=ps_a, ps_bc=ps_bc, ps_o2=ps_o2, PT=PT,
                  drec=DREC, drecb=DRECB, bcdb=BCDB, sq=SQ, ones1=ones1,
                  ones128b=ones128b)

        i = g.dma_start(out=XB[:, :, :], in_=xT.rearrange("(r p) w -> p r w", p=128))
        ctx.inc("ld", i, 16)
        i = g.dma_start(out=QSEL[:, :], in_=w_in["qsel"][:, :]); ctx.inc("ld", i, 16)
        i = v.memset(ones128b[:, :], 1.0); ctx.vv(i)
        i = v.memset(ones1[:, :], 1.0); ctx.vv(i)
        i = v.memset(EPS11[:, :], EPS); ctx.vv(i)

        ctx.wait(s, "ld"); ctx.wait(t, "ld"); ctx.wait(v, "ld")
        emit_ln_stats(ctx, XB, rows, ROWSB, SQ, ones128b, (ps_a, ps_b), double_ln=True,
                      eps11=EPS11)
        emit_ln_apply(ctx, XB, ROWSB, 2, HT, ones1, (ps_sc, ps_bc, ps_o), TMP32)
        emit_ln_apply(ctx, XB, ROWSB, 1, XB, ones1, (ps_sc, ps_bc, ps_o), TMP32)
        ctx.wait(sy, "v")
        i = sy.dma_start(out=outs["o_xin"].rearrange("(r p) w -> p r w", p=128), in_=XB[:, :, :])
        ctx.inc("stx", i, 16)
        i = sy.dma_start(out=outs["o_h"].rearrange("(r p) w -> p r w", p=128), in_=HT[:, :, :])
        ctx.inc("stx", i, 16)

        def patt(kt_name, v_name, mode, out_sb, out_dram, qn, kvnorm_out=None):
            ctx.wait(g, "t"); ctx.wait(g, "v"); ctx.wait(g, "s")
            load_weight(ctx, WKT, w_in[kt_name])
            load_weight(ctx, WV, w_in[v_name])
            ctx.wait(t, "ld"); ctx.wait(s, "ld"); ctx.wait(v, "ld")
            if kvnorm_out is not None:
                emit_natural_maxrownorm2(ctx, WV, NCH, 1024, SQ, COL9, SC32,
                                         kvnorm_out)
            emit_weight_maxrownorm2(ctx, WKT, 1024, SQ, ones128b, ps_a, ROWN, MK11)
            ctx.wait(s, "v")
            i = s.activation(MK11[0:1, 0:1], MK11[0:1, 0:1], AF.Sqrt, scale=1.0202)
            ctx.sv(i)
            ctx.wait(v, "s")
            i = v.tensor_scalar_mul(MK11[0:1, 0:1], MK11[0:1, 0:1], -1.0); ctx.vv(i)
            ctx.wait(v, "v")
            i = v.tensor_copy(MK2B[0:1, 0:1], MK11[0:1, 0:1]); ctx.vv(i)
            ctx.wait(t, "v")
            i = t.matmul(ps_bc[0:128, 0:1], ones1[0:1, :], MK2B[0:1, 0:1],
                         start=True, stop=True)
            ctx.tv(i)
            ctx.wait(v, "t")
            i = v.tensor_copy(BIAS[:, 0:1], ps_bc[0:128, 0:1]); ctx.vv(i)
            if qn:
                ctx.wait(v, "v")
                i = v.tensor_scalar_mul(KV11[0:1, 0:1], KV11[0:1, 0:1], 1.0202); ctx.vv(i)
                ctx.wait(v, "v")
                i = v.tensor_copy(MK2B[0:1, 0:1], KV11[0:1, 0:1]); ctx.vv(i)
                ctx.wait(t, "v")
                i = t.matmul(ps_c16, ones1[0:1, 0:16], MK2B[0:1, 0:1],
                             start=True, stop=True)
                ctx.tv(i)
                ctx.wait(v, "t")
                i = v.tensor_copy(MK2C[:, 0:1], ps_c16); ctx.vv(i)
            ctx.wait(t, "v"); ctx.wait(s, "v")
            qnorm = (QSEL, ps_qn, M8, MK2C) if qn else None
            b_["tok_dram"] = out_dram if mode == "tok" else None
            emit_pattention(ctx, b_, WKT, WV, HT, mode, out_sb, 1024,
                            BIAS, 1.0 / 32.0, kt_name, qnorm=qnorm)
            ctx.wait(sy, "v"); ctx.wait(sy, "s")
            if mode == "feat":
                i = sy.dma_start(out=out_dram.rearrange("(r p) w -> p r w", p=128),
                                 in_=out_sb[:, :, :])
                ctx.inc("stx", i, 16)
            if qn:
                i = sy.dma_start(out=outs["o_m8"][:, :], in_=M8[0:16, :])
                ctx.inc("stx", i, 16)

        patt("kKT", "kV", "feat", OUTF, outs["o_k"], False, kvnorm_out=KV11)
        patt("vKT", "vV", "tok", VTOK, outs["o_v"], False)
        patt("qKT", "qV", "feat", OUTF, outs["o_q"], True)

        ctx.wait(sy, "stx")

def build_u1():
    nc = bass.Bass("TRN2", target_bir_lowering=False)
    xT = nc.dram_tensor("xT", [C, T], F32, kind="ExternalInput").ap()
    w_in = {}
    for nm, shp in [("qKT", [C, 1024]), ("kKT", [C, 1024]), ("vKT", [C, 1024]),
                    ("qV", [1024, C]), ("kV", [1024, C]), ("vV", [1024, C]),
                    ("qsel", [128, NCH * 16])]:
        w_in[nm] = nc.dram_tensor(nm, shp, F32, kind="ExternalInput").ap()
    outs = {}
    for nm, shp, dt in [("o_xin", [C, T], BF16), ("o_h", [C, T], BF16),
                        ("o_q", [C, T], BF16), ("o_k", [C, T], BF16),
                        ("o_v", [T, C], BF16), ("o_m8", [16, T], F32)]:
        outs[nm] = nc.dram_tensor(nm, shp, dt, kind="ExternalOutput").ap()

    d = {"xT": xT}
    d.update(outs)
    d.update(w_in)
    with ExitStack() as st:
        ctx = Ctx(nc, st)
        phase_u1(ctx, d)
    return nc
def emit_flash(ctx, fb, nq_heads, kv_of_head, kdram, vdram, inddram, qsrc_sb,
               m8_sb, out_sb, scale, nkc_of_qt, diag_from, keys, maskcls_sb,
               tag, head_order=None, vslot_of_head=None, stage_k_fn=None,
               stage_v_fn=None):
    t, s, v, g, sy = ctx.t, ctx.s, ctx.v, ctx.g, ctx.sy
    qhead2, khead2, vhead2 = fb["qhead2"], fb["khead2"], fb["vhead2"]
    ps_sc2, ps_pv2, ps_bc = fb["ps_sc2"], fb["ps_pv2"], fb["ps_bc"]
    PTt2, densf, densb, ones2 = fb["PTt2"], fb["densf"], fb["densb"], fb["ones2"]
    if head_order is None:
        head_order = list(range(nq_heads))
    if vslot_of_head is None:
        vslot_of_head = lambda h: h
    n = len(head_order)
    jk = [0] * n
    jv = [0] * n
    for ih in range(1, n):
        jk[ih] = jk[ih - 1] + (kv_of_head(head_order[ih]) != kv_of_head(head_order[ih - 1]))
        jv[ih] = jv[ih - 1] + (vslot_of_head(head_order[ih]) != vslot_of_head(head_order[ih - 1]))
    ldf_ready = [0] * n
    vst_ready = [0] * n
    t_after = [0] * n
    pv_after = [0] * n
    kst_done = set()
    vst_done = set()
    epi = {"v_after": {}, "bcmm": {}, "vmul": {}, "pq": 0}

    def prefetch(ih):
        h = head_order[ih]
        tfree = t_after[ih - 2] if ih >= 2 else 0
        pvfree = pv_after[ih - 2] if ih >= 2 else 0
        if jk[ih] not in kst_done:
            kst_done.add(jk[ih])
            kb = khead2[jk[ih] % 2]
            sy.wait_ge(ctx.sem("t")[0], tfree)
            if stage_k_fn is None:
                i = sy.dma_start(out=kb[0:64, :],
                                 in_=kdram[64 * kv_of_head(h):64 * kv_of_head(h) + 64, :])
                ctx.inc("ldf", i, 16)
            else:
                stage_k_fn(kv_of_head(h), kb)
            i = sy.dma_start(out=kb[65:66, :], in_=inddram[0:1, :])
            ctx.inc("ldf", i, 16)
        if jv[ih] not in vst_done:
            vst_done.add(jv[ih])
            vb = vhead2[jv[ih] % 2]
            if stage_v_fn is None:
                sy.wait_ge(ctx.sem("t_pv")[0], pvfree)
                i = sy.dma_start(out=vb[:, :, :],
                                 in_=vdram[vslot_of_head(h), :, :]
                                 .rearrange("(r p) w -> p r w", p=128))
                ctx.inc("ldf", i, 16)
            else:
                stage_v_fn(vslot_of_head(h), vb, pvfree,
                           ldf_ready[ih - 2] if ih >= 2 else 0)
        qb = qhead2[ih % 2]
        sy.wait_ge(ctx.sem("t")[0], tfree)
        i = sy.dma_start(out=qb[0:64, :],
                         in_=qsrc_sb[64 * (h % 2):64 * (h % 2) + 64, h // 2, :])
        ctx.inc("ldf", i, 16)
        g.wait_ge(ctx.sem("t")[0], tfree)
        i = g.dma_start(out=qb[64:65, :], in_=m8_sb[h:h + 1, :])
        ctx.inc("ldf", i, 16)
        ldf_ready[ih] = ctx.val("ldf")
        vst_ready[ih] = ctx.val("v")

    def compute(ih):
        h = head_order[ih]
        kb = khead2[jk[ih] % 2]
        vb = vhead2[jv[ih] % 2]
        qb = qhead2[ih % 2]
        ctx.wait(t, "ldf", ldf_ready[ih])
        if vst_ready[ih]:
            ctx.wait(t, "v", vst_ready[ih])
        for qt in range(NTT):
            tsl = slice(qt * TT, (qt + 1) * TT)
            nkc = nkc_of_qt(qt)
            dfrom = diag_from(qt)
            npair = (nkc + 1) // 2
            pq = epi["pq"]
            ps_pv = ps_pv2[pq % 2]
            t_qk = [None] * npair
            v_mk = [None] * npair
            s_ex = [None] * npair
            pv_ct = [None] * npair

            def emit_qkpair(pr):
                kcs = [2 * pr] + ([2 * pr + 1] if 2 * pr + 1 < nkc else [])
                if pr >= 2:
                    ctx.wait(t, "s", s_ex[pr - 2])
                if pr == 0 and pq >= 2:
                    ctx.wait(t, "v", epi["v_after"][pq - 2])
                for kc in kcs:
                    i = t.matmul(ps_sc2[pr % 2][:, (kc % 2) * TT:(kc % 2) * TT + TT],
                                 kb[0:128, kc * 128:(kc + 1) * 128],
                                 qb[0:128, tsl], start=True, stop=True)
                t_qk[pr] = ctx.tv(i)
                nmask = 0
                for kc in kcs:
                    j = kc - dfrom
                    if 0 <= j < 4:
                        if nmask == 0:
                            ctx.wait(v, "t", t_qk[pr])
                            if pr >= 2:
                                ctx.wait(v, "s", s_ex[pr - 2])
                        i = v.tensor_add(ps_sc2[pr % 2][:, (kc % 2) * TT:(kc % 2) * TT + TT],
                                         ps_sc2[pr % 2][:, (kc % 2) * TT:(kc % 2) * TT + TT],
                                         maskcls_sb[:, j * TT:(j + 1) * TT])
                        v_mk[pr] = ctx.vv(i)
                        nmask += 1
                if v_mk[pr] is not None:
                    ctx.wait(s, "v", v_mk[pr])
                ctx.wait(s, "t", t_qk[pr])
                if pr >= 2:
                    ctx.wait(s, "t_pv", pv_ct[pr - 2])
                wid = TT * len(kcs)
                i = s.activation(PTt2[pr % 2][:, 0:wid], ps_sc2[pr % 2][:, 0:wid],
                                 AF.Exp, scale=scale)
                s_ex[pr] = ctx.sv(i)

            def emit_pvpair(pr):
                kcs = [2 * pr] + ([2 * pr + 1] if 2 * pr + 1 < nkc else [])
                ctx.wait(t, "s", s_ex[pr])
                for kc in kcs:
                    i = t.matmul(ps_pv[:, :], vb[:, kc, :],
                                 PTt2[pr % 2][:, (kc % 2) * TT:(kc % 2) * TT + TT],
                                 start=(kc == 0), stop=(kc == nkc - 1))
                pv_ct[pr] = ctx.inc("t_pv", i, 2 if len(kcs) == 2 else 1)

            emit_qkpair(0)
            for pr in range(1, npair):
                emit_qkpair(pr)
                emit_pvpair(pr - 1)
            emit_pvpair(npair - 1)
            r0 = 64 * (h % 2)
            r1 = 64 * (1 - (h % 2))
            ctx.wait(v, "t_pv", pv_ct[npair - 1])
            if pq >= 1:
                ctx.wait(v, "t", epi["bcmm"][pq - 1])
            i = v.reciprocal(densf[r1:r1 + 1, :], ps_pv[r1:r1 + 1, :]); ctx.vv(i)
            ctx.wait(v, "v")
            i = v.tensor_copy(densb[r1:r1 + 1, :], densf[r1:r1 + 1, :])
            vcp = ctx.vv(i)
            ctx.wait(t, "v", vcp)
            if pq >= 1:
                ctx.wait(t, "v", epi["vmul"][pq - 1])
            i = t.matmul(ps_bc[:, :], ones2[r1:r1 + 1, :], densb[r1:r1 + 1, :],
                         start=True, stop=True)
            epi["bcmm"][pq] = ctx.tv(i)
            ctx.wait(v, "t", epi["bcmm"][pq])
            i = v.tensor_copy(out_sb[r0:r0 + 64, h // 2, tsl], ps_pv[r0:r0 + 64, :]); ctx.vv(i)
            ctx.wait(v, "v")
            i = v.tensor_mul(out_sb[r0:r0 + 64, h // 2, tsl],
                             out_sb[r0:r0 + 64, h // 2, tsl], ps_bc[r0:r0 + 64, :])
            epi["vmul"][pq] = ctx.vv(i)
            epi["v_after"][pq] = epi["vmul"][pq]
            epi["pq"] = pq + 1
        t_after[ih] = ctx.val("t")
        pv_after[ih] = ctx.val("t_pv")

    prefetch(0)
    if n > 1:
        prefetch(1)
    for ih in range(n):
        compute(ih)
        if ih + 2 < n:
            prefetch(ih + 2)


def emit_fpatt_norm_cast(ctx, fKT, fkt3_d, WKP2, SQ, ones128b, sc11, out11):
    t, s, v, g, sy = ctx.t, ctx.s, ctx.v, ctx.g, ctx.sy
    i = v.memset(out11[0:1, 0:1], 0.0); ctx.vv(i)
    wbct = [None] * 8
    with ExitStack() as pp:
        ps1 = pp.enter_context(ctx.nc.psum_tensor(ctx.uniq("fnorm_ps"), [1, TT], F32))
        sqct = [None] * 8
        for tl in range(8):
            db = tl % 2
            if tl >= 2:
                ctx.wait(g, "s", sqct[tl - 2])
                ctx.wait(g, "fk3w", wbct[tl - 2])
            i = g.dma_start(out=WKP2[db][:, :, :],
                            in_=fKT[:, tl * 512:(tl + 1) * 512]
                            .rearrange("(cc p) w -> p cc w", p=128))
            ld = ctx.inc("lds", i, 16)
            ctx.wait(sy, "lds", ld)
            i = sy.dma_start(out=fkt3_d[:, tl, :, :], in_=WKP2[db][:, :, :])
            wbct[tl] = ctx.inc("fk3w", i, 16)
            ctx.wait(s, "lds", ld)
            for cc in range(NCH):
                ctx.wait(s, "t")
                i = s.square(SQ[:, (cc % 2) * TT:(cc % 2 + 1) * TT], WKP2[db][:, cc, :])
                sqct[tl] = ctx.sv(i)
                ctx.wait(t, "s", sqct[tl])
                i = t.matmul(ps1[:, :], ones128b[:, 0:1],
                             SQ[:, (cc % 2) * TT:(cc % 2 + 1) * TT],
                             start=(cc == 0), stop=(cc == NCH - 1))
                ctx.tv(i)
            ctx.wait(v, "t")
            i = v.reduce_max(sc11[0:1, 0:1], ps1[0:1, :], axis=AX.X); ctx.vv(i)
            ctx.wait(v, "v")
            i = v.tensor_max(out11[0:1, 0:1], out11[0:1, 0:1], sc11[0:1, 0:1]); ctx.vv(i)
            ctx.wait(t, "v")
        ctx.wait(s, "v")
    return wbct


def emit_fpatt_v2(ctx, b_, fV, fkt3_d, fv3_d, WKP2, WVP2, rhs_sb, acc_sb,
                  biascol, scale, wbct):
    t, s, v, g, sy = ctx.t, ctx.s, ctx.v, ctx.g, ctx.sy
    PT = b_["PTBIG"]
    drec, drecb, bcdb = b_["drec"], b_["drecb"], b_["bcdb"]
    ones1, ones128b = b_["ones1"], b_["ones128b"]
    nc = ctx.nc
    fv3_ready = [None] * 8
    av_done_prev = [0]
    prev_sc = [0, 0]
    norm_s = ctx.val("s")
    for tt in range(NTT):
        tsl = slice(tt * TT, (tt + 1) * TT)
        av_prev = av_done_prev[0]
        with ExitStack() as pp:
            ps_scP = [pp.enter_context(nc.psum_tensor(ctx.uniq(f"f_sc{tt}_{i_}"), [128, 2 * TT], F32))
                      for i_ in range(2)]
            ps_den = pp.enter_context(nc.psum_tensor(ctx.uniq(f"f_den{tt}"), [1, TT], F32))
            ps_bc = pp.enter_context(nc.psum_tensor(ctx.uniq(f"f_bc{tt}"), [128, TT], F32))
            ld = [None] * 8
            mmP = [None] * 16
            exP = [None] * 16
            dnP = [None] * 16

            def load_kpanel(tl):
                db = tl % 2
                if tl >= 2:
                    ctx.wait(sy, "t", mmP[2 * (tl - 2) + 1])
                elif tt == 0:
                    ctx.wait(sy, "fk3w", wbct[tl])
                    ctx.wait(sy, "s", norm_s)
                else:
                    ctx.wait(sy, "t", prev_sc[tl])
                i = sy.dma_start(out=WKP2[db][:, :, :], in_=fkt3_d[:, tl, :, :])
                ld[tl] = ctx.inc("lds", i, 16)

            def emit_qk(pr):
                tl = pr // 2
                ctx.wait(t, "lds", ld[tl])
                if pr >= 2:
                    ctx.wait(t, "s", exP[pr - 2])
                if pr == 0 and av_done_prev[0]:
                    ctx.wait(s, "t_pv", av_done_prev[0])
                for half in range(2):
                    nc_ = 2 * pr + half
                    for cc in range(NCH):
                        i = t.matmul(ps_scP[pr % 2][:, half * TT:half * TT + TT],
                                     WKP2[tl % 2][:, cc, (nc_ % 4) * 128:(nc_ % 4) * 128 + 128],
                                     rhs_sb[:, cc, tsl],
                                     start=(cc == 0), stop=(cc == NCH - 1))
                mmP[pr] = ctx.tv(i)
                ctx.wait(s, "t", mmP[pr])
                i = s.activation(PT[:, 2 * pr:2 * pr + 2, :], ps_scP[pr % 2][:, :],
                                 AF.Exp, bias=biascol[:, 0:1], scale=scale)
                exP[pr] = ctx.sv(i)

            def emit_den(pr):
                ctx.wait(t, "s", exP[pr])
                for half in range(2):
                    nc_ = 2 * pr + half
                    i = t.matmul(ps_den[:, :], ones128b[:, 0:1], PT[:, nc_, :],
                                 start=(nc_ == 0), stop=(nc_ == 31))
                dnP[pr] = ctx.tv(i)

            load_kpanel(0)
            load_kpanel(1)
            emit_qk(0)
            for pr in range(1, 16):
                if pr % 2 == 1 and pr // 2 + 2 < 8:
                    load_kpanel(pr // 2 + 2)
                emit_qk(pr)
                emit_den(pr - 1)
            emit_den(15)
            ctx.wait(v, "t", dnP[15])
            i = v.reciprocal(drec[:, :], ps_den[:, :]); ctx.vv(i)
            ctx.wait(v, "v")
            i = v.tensor_copy(drecb[:, :], drec[:, :]); ctx.vv(i)
            ctx.wait(t, "v")
            i = t.matmul(ps_bc[:, :], ones1[0:1, :], drecb[0:1, :], start=True, stop=True)
            ctx.tv(i)
            ctx.wait(v, "t")
            i = v.tensor_copy(bcdb[:, :], ps_bc[:, :])
            bccp = ctx.vv(i)
            last_ex = exP[15]
            prev_sc[0] = mmP[13]
            prev_sc[1] = mmP[15]
        with ExitStack() as pp:
            ps8 = [pp.enter_context(nc.psum_tensor(ctx.uniq(f"f_o{tt}_{o}"), [128, TT], F32))
                   for o in range(NCH)]
            vld = [None] * 8
            vmm = [None] * 8

            def load_vpanel(pv):
                db = pv % 2
                if pv >= 2:
                    ctx.wait(g, "t_pv", vmm[pv - 2])
                elif av_prev:
                    ctx.wait(g, "t_pv", av_prev)
                if tt == 0:
                    i = g.dma_start(out=WVP2[db][:, :, :],
                                    in_=fV[512 * pv:512 * (pv + 1), :]
                                    .rearrange("(ch p) w -> p ch w", p=128))
                    vld[pv] = ctx.inc("lds", i, 16)
                    ctx.wait(sy, "lds", vld[pv])
                    i = sy.dma_start(out=fv3_d[:, 4 * pv:4 * (pv + 1), :], in_=WVP2[db][:, :, :])
                    fv3_ready[pv] = ctx.inc("fv3w", i, 16)
                else:
                    if pv < 2:
                        ctx.wait(g, "fv3w", fv3_ready[pv])
                    i = g.dma_start(out=WVP2[db][:, :, :], in_=fv3_d[:, 4 * pv:4 * (pv + 1), :])
                    vld[pv] = ctx.inc("lds", i, 16)

            def emit_av(pv):
                ctx.wait(t, "lds", vld[pv])
                if pv == 0:
                    ctx.wait(t, "s", last_ex)
                    ctx.wait(t, "v", bccp)
                for j in range(4):
                    nc_ = 4 * pv + j
                    for oc in range(NCH):
                        i = t.matmul(ps8[oc][:, :], WVP2[pv % 2][:, j, oc * 128:(oc + 1) * 128],
                                     PT[:, nc_, :], start=(nc_ == 0), stop=(nc_ == 31))
                vmm[pv] = ctx.inc("t_pv", i, 1)

            load_vpanel(0)
            load_vpanel(1)
            for pv in range(8):
                if pv + 2 < 8:
                    load_vpanel(pv + 2)
                emit_av(pv)
            av_done_prev[0] = ctx.val("t_pv")
            ctx.wait(v, "t_pv", av_done_prev[0])
            for oc in range(NCH):
                ctx.wait(v, "v")
                i = v.tensor_mul(ps8[oc][:, :], ps8[oc][:, :], bcdb[:, :]); ctx.vv(i)
                ctx.wait(v, "v")
                i = v.tensor_add(acc_sb[:, oc, tsl], acc_sb[:, oc, tsl], ps8[oc][:, :])
                ctx.vv(i)
            ctx.wait(t, "v")
            ctx.wait(g, "v")


def emit_fpatt_streamed(ctx, b_, kt_dram, v_dram, WKTs, WVs, rhs_sb, acc_sb,
                        nw, biascol, scale, tag):
    t, s, v, g, sy = ctx.t, ctx.s, ctx.v, ctx.g, ctx.sy
    nwch = nw // 128
    PT = b_["PTBIG"]
    drec, drecb, bcdb = b_["drec"], b_["drecb"], b_["bcdb"]
    ones1, ones128b = b_["ones1"], b_["ones128b"]
    nc = ctx.nc
    for tt in range(NTT):
        tsl = slice(tt * TT, (tt + 1) * TT)
        with ExitStack() as ps_scope:
            ps_scP = [ps_scope.enter_context(ctx.nc.psum_tensor(ctx.uniq(f"f_sc{tt}_{i_}"), [128, TT], F32)) for i_ in range(2)]
            ps_den = ps_scope.enter_context(ctx.nc.psum_tensor(ctx.uniq(f"f_den{tt}"), [1, TT], F32))
            ps_bc = ps_scope.enter_context(ctx.nc.psum_tensor(ctx.uniq(f"f_bc{tt}"), [128, TT], F32))
            mmct = [None] * nwch
            sct = [None] * nwch
            for nc_ in range(nwch):
                db = nc_ % 2
                if nc_ >= 2:
                    ctx.wait(g, "t", mmct[nc_ - 2])
                i = g.dma_start(out=WKTs[:, :, db * 128:db * 128 + 128],
                                in_=kt_dram[:, nc_ * 128:(nc_ + 1) * 128].rearrange(
                                    "(r p) w -> p r w", p=128))
                ld2 = ctx.inc("lds", i, 16)
                ctx.wait(t, "lds", ld2)
                if nc_ >= 2:
                    ctx.wait(t, "s", sct[nc_ - 2])
                for cc in range(NCH):
                    i = t.matmul(ps_scP[db][:, :], WKTs[:, cc, db * 128:db * 128 + 128],
                                 rhs_sb[:, cc, tsl], start=(cc == 0), stop=(cc == NCH - 1))
                mmct[nc_] = ctx.tv(i)
                ctx.wait(s, "t", mmct[nc_])
                i = s.activation(PT[:, nc_, :], ps_scP[db][:, :], AF.Exp,
                                 bias=biascol[:, 0:1], scale=scale)
                sct[nc_] = ctx.sv(i)
            ctx.wait(t, "s")
            for nc_ in range(nwch):
                i = t.matmul(ps_den[:, :], ones128b[:, 0:1], PT[:, nc_, :],
                             start=(nc_ == 0), stop=(nc_ == nwch - 1))
            ctx.tv(i)
            ctx.wait(v, "t")
            i = v.reciprocal(drec[:, :], ps_den[:, :]); ctx.vv(i)
            ctx.wait(v, "v")
            i = v.tensor_copy(drecb[:, :], drec[:, :]); ctx.vv(i)
            ctx.wait(t, "v")
            i = t.matmul(ps_bc[:, :], ones1[0:1, :], drecb[0:1, :], start=True, stop=True)
            ctx.tv(i)
            ctx.wait(v, "t")
            i = v.tensor_copy(bcdb[:, :], ps_bc[:, :]); ctx.vv(i)
            ctx.wait(v, "v")
            for nc_ in range(nwch):
                i = v.tensor_mul(PT[:, nc_, :], PT[:, nc_, :], bcdb[:, :])
            ctx.vv(i)
            ctx.wait(t, "v")
        with ExitStack() as ps_scope:
            ps8 = [ps_scope.enter_context(ctx.nc.psum_tensor(ctx.uniq(f"f_o{tt}_{o}"), [128, TT], F32))
                   for o in range(NCH)]
            vmct = [None] * nwch
            for nc_ in range(nwch):
                db = nc_ % 2
                if nc_ >= 2:
                    ctx.wait(g, "t", vmct[nc_ - 2])
                i = g.dma_start(out=WVs[:, db, :],
                                in_=v_dram[nc_ * 128:(nc_ + 1) * 128, :])
                ld2 = ctx.inc("lds", i, 16)
                ctx.wait(t, "lds", ld2)
                for oc in range(NCH):
                    i = t.matmul(ps8[oc][:, :], WVs[:, db, oc * 128:(oc + 1) * 128],
                                 PT[:, nc_, :], start=(nc_ == 0), stop=(nc_ == nwch - 1))
                vmct[nc_] = ctx.tv(i)
            ctx.wait(v, "t")
            for oc in range(NCH):
                i = v.tensor_add(acc_sb[:, oc, tsl], acc_sb[:, oc, tsl], ps8[oc][:, :])
                ctx.vv(i)
            ctx.wait(t, "v")
            ctx.wait(g, "v")


def phase_u2(ctx, d):
    nc = ctx.nc
    ind = d["ind"]
    ksrc = d.get("ksrc")
    m8in = d["m8in"]
    masks = d["masks"]
    o_o = d["o_o"]
    qin = d["qin"]
    vsrc = d.get("vsrc")
    with ctx.scope("u2"):
        g, sy, t, v, s = ctx.g, ctx.sy, ctx.t, ctx.v, ctx.s
        QT = ctx.sbuf("QT", [128, NCH, T], BF16)
        OUTF = ctx.sbuf("OUTF", [128, NCH, T], BF16)
        M8 = ctx.sbuf("M8", [16, T], F32)
        qhead2 = [ctx.sbuf(f"qhead{i_}", [128, T], BF16) for i_ in range(2)]
        khead2 = [ctx.sbuf(f"khead{i_}", [128, KEYS], BF16) for i_ in range(2)]
        vhead2 = [ctx.sbuf(f"vhead{i_}", [128, KEYS // 128, 128], BF16)
                  for i_ in range(2)]
        PTt2 = [ctx.sbuf(f"PTt{i_}", [128, 2 * TT], BF16) for i_ in range(2)]
        densf = ctx.sbuf("densf", [128, TT], F32)
        densb = ctx.sbuf("densb", [128, TT], BF16)
        MASKS = ctx.sbuf("MASKS", [128, 4 * TT], F32)
        ones2 = ctx.sbuf("ones2", [128, 128], BF16)
        ps_sc2 = [ctx.psum(f"ps_sc{i_}", [128, 2 * TT]) for i_ in range(2)]
        ps_pv2 = [ctx.psum(f"ps_pv{i_}", [128, TT]) for i_ in range(2)]
        ps_bc = ctx.psum("ps_bc", [128, TT])

        i = sy.dma_start(out=QT[:, :, :], in_=qin.rearrange("(r p) w -> p r w", p=128))
        ctx.inc("ld", i, 16)
        i = sy.dma_start(out=M8[:, :], in_=m8in[:, :]); ctx.inc("ld", i, 16)
        i = sy.dma_start(out=MASKS[:, :], in_=masks[:, :]); ctx.inc("ld", i, 16)
        BIGROW = ctx.sbuf("BIGROW", [1, T], BF16)
        for b_ in range(2):
            i = v.memset(khead2[b_][64:128, :], 0.0); ctx.vv(i)
            i = v.memset(qhead2[b_][64:128, :], 0.0); ctx.vv(i)
        ctx.wait(v, "v")
        for b_ in range(2):
            i = v.memset(khead2[b_][64:65, :], -1.0); ctx.vv(i)
        i = v.memset(BIGROW[:, :], BIGNEG); ctx.vv(i)
        i = v.memset(ones2[:, :], 1.0); ctx.vv(i)
        ctx.wait(sy, "v")
        for b_ in range(2):
            i = sy.dma_start(out=qhead2[b_][65:66, :], in_=BIGROW[0:1, :])
            ctx.inc("ld", i, 16)
        ctx.wait(t, "ld"); ctx.wait(v, "ld"); ctx.wait(s, "ld"); ctx.wait(g, "ld")
        ctx.wait(sy, "v")

        fb = dict(qhead2=qhead2, khead2=khead2, vhead2=vhead2, ps_sc2=ps_sc2,
                  ps_pv2=ps_pv2, ps_bc=ps_bc, PTt2=PTt2, densf=densf,
                  densb=densb, ones2=ones2)
        emit_flash(ctx, fb, 16, lambda h: h, ksrc, vsrc, ind, QT, M8, OUTF,
                   0.125, lambda qt: 20 + 4 * qt, lambda qt: 16 + 4 * qt,
                   KEYS, MASKS, "sdpa", stage_k_fn=d.get("stage_k"),
                   stage_v_fn=d.get("stage_v"))
        ctx.wait(sy, "v")
        i = sy.dma_start(out=o_o.rearrange("(r p) w -> p r w", p=128), in_=OUTF[:, :, :])
        ctx.inc("stx", i, 16)
        ctx.wait(sy, "stx")

def build_u2():
    nc = bass.Bass("TRN2", target_bir_lowering=False)
    qin = nc.dram_tensor("q", [C, T], BF16, kind="ExternalInput").ap()
    m8in = nc.dram_tensor("m8", [16, T], F32, kind="ExternalInput").ap()
    ksrc = nc.dram_tensor("ksrc", [C, KEYS], BF16, kind="ExternalInput").ap()
    vsrc = nc.dram_tensor("vsrc", [16, KEYS, 128], BF16, kind="ExternalInput").ap()
    ind = nc.dram_tensor("ind", [1, KEYS], BF16, kind="ExternalInput").ap()
    masks = nc.dram_tensor("masks", [128, 4 * TT], F32, kind="ExternalInput").ap()
    o_o = nc.dram_tensor("o_o", [C, T], BF16, kind="ExternalOutput").ap()

    d = {"ind": ind, "ksrc": ksrc, "m8in": m8in, "masks": masks, "o_o": o_o, "qin": qin, "vsrc": vsrc}
    with ExitStack() as st:
        ctx = Ctx(nc, st)
        phase_u2(ctx, d)
    return nc
def _sdpa_masks_host():
    m = np.zeros((128, 4 * TT), np.float32)
    for j in range(4):
        tile = m[:, j * TT:(j + 1) * TT]
        for qs in range(4):
            if qs < j:
                tile[:, qs * 128:(qs + 1) * 128] = BIGNEG
            elif qs == j:
                sub = tile[:, qs * 128:(qs + 1) * 128]
                for srel in range(128):
                    sub[srel, :srel] = BIGNEG
    return m


def emit_row_m8(ctx, src_sb, m8row_b, sq, ones128b, ps1, scale11):
    t, s, v = ctx.t, ctx.s, ctx.v
    for tt in range(NTT):
        tsl = slice(tt * TT, (tt + 1) * TT)
        for cc in range(NCH):
            ctx.wait(s, "t")
            i = s.square(sq[:, 0:TT], src_sb[:, cc, tsl]); ctx.sv(i)
            ctx.wait(t, "s")
            i = t.matmul(ps1[:, :], ones128b[:, 0:1], sq[:, 0:TT],
                         start=(cc == 0), stop=(cc == NCH - 1))
            ctx.tv(i)
        ctx.wait(s, "t")
        i = s.activation(m8row_b[0:1, tsl], ps1[:, :], AF.Sqrt, scale=scale11[0:1, 0:1])
        ctx.sv(i)
    ctx.wait(t, "s"); ctx.wait(v, "s")


def emit_streamed_maxrownorm2(ctx, kt_dram, WKTs, nw, sq, ones128b, sc128,
                              cm11, out11):
    t, s, v, g = ctx.t, ctx.s, ctx.v, ctx.g
    nc = ctx.nc
    i = v.memset(out11[0:1, 0:1], 0.0); ctx.vv(i)
    with ExitStack() as ps_scope:
        ps1 = ps_scope.enter_context(ctx.nc.psum_tensor(ctx.uniq("mrn_ps"), [1, 128], F32))
        for nc_ in range(nw // 128):
            db = nc_ % 2
            ctx.wait(g, "t")
            i = g.dma_start(out=WKTs[:, :, db * 128:db * 128 + 128],
                            in_=kt_dram[:, nc_ * 128:(nc_ + 1) * 128].rearrange(
                                "(r p) w -> p r w", p=128))
            ctx.inc("lds", i, 16)
            ctx.wait(s, "lds")
            for cc in range(NCH):
                ctx.wait(s, "t")
                i = s.square(sq[:, 0:128], WKTs[:, cc, db * 128:db * 128 + 128]); ctx.sv(i)
                ctx.wait(t, "s")
                i = t.matmul(ps1[:, :], ones128b[:, 0:1], sq[:, 0:128],
                             start=(cc == 0), stop=(cc == NCH - 1))
                ctx.tv(i)
            ctx.wait(v, "t")
            i = v.reduce_max(cm11[0:1, 0:1], ps1[0:1, :], axis=AX.X); ctx.vv(i)
            ctx.wait(v, "v")
            i = v.tensor_max(out11[0:1, 0:1], out11[0:1, 0:1], cm11[0:1, 0:1]); ctx.vv(i)
            ctx.wait(v, "v")
        ctx.wait(s, "v"); ctx.wait(t, "v")


def emit_ln_apply_residual(ctx, src_bf, rows_bf, res_dram, out_dram, ones1,
                           tmp32, x0tile):
    t, v, sy, nc = ctx.t, ctx.v, ctx.sy, ctx.nc
    with ExitStack() as ps_scope:
        ps_mu = ps_scope.enter_context(ctx.nc.psum_tensor(ctx.uniq("lnr_mu"), [128, TT], F32))
        ps_r = ps_scope.enter_context(ctx.nc.psum_tensor(ctx.uniq("lnr_r"), [128, TT], F32))
        for tt in range(NTT):
            tsl = slice(tt * TT, (tt + 1) * TT)
            ctx.wait(t, "v")
            t.matmul(ps_mu[:, :], ones1[0:1, :], rows_bf[0][0:1, tsl], start=True, stop=True)
            i = t.matmul(ps_r[:, :], ones1[0:1, :], rows_bf[1][0:1, tsl],
                         start=True, stop=True)
            ctx.tv(i)
            ctx.wait(v, "t")
            for c in range(NCH):
                ctx.wait(sy, "strx", ctx.val("strx"))
                i = sy.dma_start(out=x0tile[:, :],
                                 in_=res_dram.rearrange("(r p) w -> p r w", p=128)[:, c, tsl])
                ctx.inc("ldr", i, 16)
                ctx.wait(v, "ldr")
                ctx.wait(v, "v")
                i = v.tensor_sub(tmp32[:, 0:TT], src_bf[:, c, tsl], ps_mu[:, :]); ctx.vv(i)
                ctx.wait(v, "v")
                i = v.tensor_mul(tmp32[:, 0:TT], tmp32[:, 0:TT], ps_r[:, :]); ctx.vv(i)
                ctx.wait(v, "v")
                i = v.tensor_add(tmp32[:, 0:TT], tmp32[:, 0:TT], x0tile[:, :]); ctx.vv(i)
                ctx.wait(sy, "v")
                i = sy.dma_start(out=out_dram.rearrange("(r p) w -> p r w", p=128)[:, c, tsl],
                                 in_=tmp32[:, 0:TT])
                ctx.inc("strx", i, 16)
        ctx.wait(sy, "strx")


def phase_u3(ctx, d):
    nc = ctx.nc
    fKT = d["fKT"]
    fV = d["fV"]
    o_x2 = d["o_x2"]
    oin = d["oin"]
    pKT = d["pKT"]
    pV = d["pV"]
    x0in = d["x0in"]
    xinin = d["xinin"]
    with ctx.scope("u3"):
        g, sy, t, v, s = ctx.g, ctx.sy, ctx.t, ctx.v, ctx.s
        OT = ctx.sbuf("OT", [128, NCH, T], BF16)
        XIN = ctx.sbuf("XIN", [128, NCH, T], BF16)
        O2 = ctx.sbuf("O2", [128, NCH, T], BF16)
        SQ = ctx.sbuf("SQ", [128, 1024], BF16)
        TMP32 = ctx.sbuf("TMP32", [128, TT], F32)
        X0TILE = ctx.sbuf("X0TILE", [128, TT], F32)
        _ssum = ctx.sbuf("ssum", [1, T], F32)
        _ssq = ctx.sbuf("ssq", [1, T], F32)
        _rs = ctx.sbuf("rs", [1, T], F32)
        rows = {"ssum": _ssum, "ssq": _ssq, "svar": _ssq, "rs": _rs}
        ROWSB = [ctx.sbuf(f"ROWSB{i_}", [1, T], BF16) for i_ in range(2)]
        M8RB = ctx.sbuf("M8RB", [1, T], BF16)
        NEGONES = ctx.sbuf("NEGONES", [1, 1024], BF16)
        DREC = ctx.sbuf("DREC", [1, TT], F32)
        DRECB = ctx.sbuf("DRECB", [1, TT], BF16)
        BCDB = ctx.sbuf("BCDB", [128, TT], BF16)
        CM11 = ctx.sbuf("CM11", [1, 1], F32)
        FK11 = ctx.sbuf("FK11", [1, 1], F32)
        MKB = ctx.sbuf("MKB", [1, 1], BF16)
        BIAS = ctx.sbuf("BIAS", [128, 1], F32)
        ZEROC = ctx.sbuf("ZEROC", [128, 1], F32)
        EPS11 = ctx.sbuf("EPS11", [1, 1], F32)
        SC32 = ctx.sbuf("SC32", [1, 128], F32)
        COL9 = ctx.sbuf("COL9", [128, NCH + 1], F32)
        ones128b = ctx.sbuf("ones128b", [128, 1], BF16)
        ones1 = ctx.sbuf("ones1", [1, 128], BF16)

        i = sy.dma_start(out=OT[:, :, :], in_=oin.rearrange("(r p) w -> p r w", p=128))
        ctx.inc("ld", i, 16)
        i = sy.dma_start(out=XIN[:, :, :], in_=xinin.rearrange("(r p) w -> p r w", p=128))
        ctx.inc("ld", i, 16)
        pscope = ctx.scope("u3p")
        pscope.__enter__()
        WKT = ctx.sbuf("WKT", [128, NCH, 1024], BF16)
        WV = ctx.sbuf("WV", [128, NCH, 1024], BF16)
        PTSMALL = ctx.sbuf("PTSMALL", [128, NCH, TT], BF16)
        ROWN4 = ctx.sbuf("ROWN4", [1, 1024], F32)
        PK11 = ctx.sbuf("PK11", [1, 1], F32)
        PK2S = ctx.sbuf("PK2S", [1, 1], F32)
        load_weight(ctx, WKT, pKT)
        load_weight(ctx, WV, pV)
        i = v.memset(ones128b[:, :], 1.0); ctx.vv(i)
        i = v.memset(ones1[:, :], 1.0); ctx.vv(i)
        i = v.memset(NEGONES[:, :], -1.0); ctx.vv(i)
        i = v.memset(ZEROC[:, :], 0.0); ctx.vv(i)
        i = v.memset(EPS11[:, :], EPS); ctx.vv(i)
        ctx.wait(t, "ld"); ctx.wait(s, "ld"); ctx.wait(v, "ld")
        ctx.wait(t, "v"); ctx.wait(s, "v")

        with ExitStack() as pp:
            ps_a = pp.enter_context(ctx.nc.psum_tensor(ctx.uniq("ps_a"), [1, TT], F32))
            ps_b = pp.enter_context(ctx.nc.psum_tensor(ctx.uniq("ps_b"), [1, TT], F32))
            ps_sc2 = [pp.enter_context(ctx.nc.psum_tensor(ctx.uniq(f"ps_sc{i_}"), [128, TT], F32))
                      for i_ in range(2)]
            ps_sc = ps_sc2[0]
            ps_bc = pp.enter_context(ctx.nc.psum_tensor(ctx.uniq("ps_bc"), [128, TT], F32))
            ps_o2 = [pp.enter_context(ctx.nc.psum_tensor(ctx.uniq(f"ps_o{i_}"), [128, TT], F32))
                     for i_ in range(2)]
            ps_o = ps_o2[0]
            b_ = dict(ps_sc2=ps_sc2, ps_den=ps_a, ps_bc=ps_bc, ps_o2=ps_o2,
                      PT=PTSMALL, drec=DREC, drecb=DRECB, bcdb=BCDB, sq=SQ,
                      ones1=ones1, ones128b=ones128b)
            emit_weight_maxrownorm2(ctx, WKT, 1024, SQ, ones128b, ps_a, ROWN4, PK11)
            ctx.wait(v, "v")
            i = v.tensor_scalar_mul(PK2S[0:1, 0:1], PK11[0:1, 0:1], 1.0202 / (32.0 * 32.0))
            ctx.vv(i)
            ctx.wait(s, "v")
            i = v.tensor_scalar_mul(PK2S[0:1, 0:1], PK11[0:1, 0:1], 1.0202); ctx.vv(i)
            ctx.wait(s, "v")
            emit_row_m8(ctx, OT, M8RB, SQ, ones128b, ps_a, PK2S)
            emit_pattention(ctx, b_, WKT, WV, OT, "feat", O2, 1024,
                            ZEROC, 1.0 / 32.0, "p", mrow=(M8RB, NEGONES))
            ctx.wait(v, "t"); ctx.wait(v, "s")
            for c in range(NCH):
                i = v.tensor_add(XIN[:, c, :], XIN[:, c, :], O2[:, c, :])
            ctx.vv(i)
            ctx.wait(s, "v"); ctx.wait(t, "v")
            emit_ln_stats(ctx, XIN, rows, ROWSB + [None], SQ, ones128b,
                          (ps_a, ps_b), double_ln=False, eps11=EPS11)
            emit_ln_apply(ctx, XIN, ROWSB, 1, O2, ones1, (ps_sc, ps_bc, ps_o), TMP32)
            ctx.wait(t, "v"); ctx.wait(s, "v")
        pscope.__exit__(None, None, None)

        fkt3_d = nc.dram_tensor(ctx.uniq("fkt3"), [128, 8, NCH, 512], BF16,
                                kind="Internal").ap()
        fv3_d = nc.dram_tensor(ctx.uniq("fv3"), [128, 32, 1024], BF16,
                               kind="Internal").ap()
        WKP2 = [ctx.sbuf(f"WKP{i_}", [128, NCH, 512], BF16) for i_ in range(2)]
        WVP2 = [ctx.sbuf(f"WVP{i_}", [128, 4, 1024], BF16) for i_ in range(2)]
        ctx.wait(g, "t"); ctx.wait(g, "s"); ctx.wait(g, "v")
        ctx.wait(sy, "t"); ctx.wait(sy, "s"); ctx.wait(sy, "v")
        wbct = emit_fpatt_norm_cast(ctx, fKT, fkt3_d, WKP2, SQ, ones128b,
                                    CM11, FK11)
        ctx.wait(s, "v")
        i = s.activation(FK11[0:1, 0:1], FK11[0:1, 0:1], AF.Sqrt, scale=1.0202); ctx.sv(i)
        ctx.wait(v, "s")
        i = v.tensor_scalar_mul(FK11[0:1, 0:1], FK11[0:1, 0:1], -1.0); ctx.vv(i)
        ctx.wait(v, "v")
        i = v.tensor_copy(MKB[0:1, 0:1], FK11[0:1, 0:1]); ctx.vv(i)
        ctx.wait(t, "v")
        with ExitStack() as pp:
            ps_bc = pp.enter_context(ctx.nc.psum_tensor(ctx.uniq("ps_bc2"), [128, 1], F32))
            i = t.matmul(ps_bc[:, :], ones1[0:1, :], MKB[0:1, 0:1], start=True, stop=True)
            ctx.tv(i)
            ctx.wait(v, "t")
            i = v.tensor_copy(BIAS[:, 0:1], ps_bc[:, :]); ctx.vv(i)
            ctx.wait(t, "v"); ctx.wait(s, "v")
        OTPT = OT[:, :, :].rearrange("p c t -> p (c t)").rearrange(
            "p (n k) -> p n k", k=TT)
        b2 = dict(PTBIG=OTPT, drec=DREC, drecb=DRECB, bcdb=BCDB,
                  ones1=ones1, ones128b=ones128b)
        emit_fpatt_v2(ctx, b2, fV, fkt3_d, fv3_d, WKP2, WVP2,
                      O2, XIN, BIAS, 1.0 / 32.0, wbct)
        with ExitStack() as pp:
            ps_a = pp.enter_context(ctx.nc.psum_tensor(ctx.uniq("ps_a2"), [1, TT], F32))
            ps_b = pp.enter_context(ctx.nc.psum_tensor(ctx.uniq("ps_b2"), [1, TT], F32))
            emit_ln_stats(ctx, XIN, rows, ROWSB + [None], SQ, ones128b,
                          (ps_a, ps_b), double_ln=False, eps11=EPS11)
        emit_ln_apply_residual(ctx, XIN, ROWSB, x0in, o_x2, ones1, TMP32, X0TILE)

def build_u3():
    nc = bass.Bass("TRN2", target_bir_lowering=False)
    oin = nc.dram_tensor("o", [C, T], BF16, kind="ExternalInput").ap()
    xinin = nc.dram_tensor("xin", [C, T], BF16, kind="ExternalInput").ap()
    x0in = nc.dram_tensor("x0T", [C, T], F32, kind="ExternalInput").ap()
    pKT = nc.dram_tensor("pKT", [C, 1024], F32, kind="ExternalInput").ap()
    pV = nc.dram_tensor("pV", [1024, C], F32, kind="ExternalInput").ap()
    fKT = nc.dram_tensor("fKT", [C, 4096], F32, kind="ExternalInput").ap()
    fV = nc.dram_tensor("fV", [4096, C], F32, kind="ExternalInput").ap()
    o_x2 = nc.dram_tensor("o_x2", [C, T], F32, kind="ExternalOutput").ap()

    d = {"fKT": fKT, "fV": fV, "o_x2": o_x2, "oin": oin, "pKT": pKT, "pV": pV, "x0in": x0in, "xinin": xinin}
    with ExitStack() as st:
        ctx = Ctx(nc, st)
        phase_u3(ctx, d)
    return nc
def phase_u4(ctx, d):
    nc = ctx.nc
    bselin = d["bselin"]
    halo = d["halo"]
    kscol = d["kscol"]
    o_k = d["o_k"]
    o_q = d["o_q"]
    o_v = d["o_v"]
    qscol = d["qscol"]
    rotp = d["rotp"]
    tabs = d["tabs"]
    wqkv = d["wqkv"]
    xl = d["xl"]
    with ctx.scope("u4"):
        g, sy, t, v, s = ctx.g, ctx.sy, ctx.t, ctx.v, ctx.s
        XLE = ctx.sbuf("XLE", [128, NCH, T + 1], BF16)
        WQ = ctx.sbuf("WQ", [128, NCH, 1536], BF16)
        QT = ctx.sbuf("QT", [128, 4, T], BF16)
        KT = ctx.sbuf("KT", [128, 4, T], BF16)
        VT = ctx.sbuf("VT", [128, 4, 512], BF16)
        TAB = ctx.sbuf("TAB", [128, 4, T], BF16)
        ROTP = ctx.sbuf("ROTP", [128, 128], BF16)
        QRAW = ctx.sbuf("QRAW", [128, TT], BF16)
        SQT = ctx.sbuf("SQT", [128, TT], BF16)
        ROTS = ctx.sbuf("ROTS", [128, TT], BF16)
        M1 = ctx.sbuf("M1", [128, TT], BF16)
        RN = ctx.sbuf("RN", [2, TT], F32)
        RNB = ctx.sbuf("RNB", [2, TT], BF16)
        QSC = ctx.sbuf("QSC", [128, 1], F32)
        KSC = ctx.sbuf("KSC", [128, 1], F32)
        HSEL = ctx.sbuf("HSEL", [128, 2], BF16)
        BSEL = ctx.sbuf("BSEL", [2, 128], BF16)

        i = sy.dma_start(out=XLE[:, :, 1:T + 1], in_=xl.rearrange("(r p) w -> p r w", p=128))
        ctx.inc("ld", i, 16)
        i = sy.dma_start(out=XLE[:, :, 0:1], in_=halo.rearrange("(r p) w -> p r w", p=128), allow_slow_non_contiguous=True)
        ctx.inc("ld", i, 16)
        load_weight(ctx, WQ, wqkv)
        i = g.dma_start(out=TAB[:, :, :], in_=tabs.transpose([1, 0, 2])); ctx.inc("ld", i, 16)
        i = g.dma_start(out=ROTP[:, :], in_=rotp[:, :]); ctx.inc("ld", i, 16)
        i = sy.dma_start(out=QSC[:, :], in_=qscol[:, :]); ctx.inc("ld", i, 16)
        i = sy.dma_start(out=KSC[:, :], in_=kscol[:, :]); ctx.inc("ld", i, 16)
        i = g.dma_start(out=BSEL[:, :], in_=bselin[:, :]); ctx.inc("ld", i, 16)
        i = v.memset(HSEL[:, :], 0.0); ctx.vv(i)
        ctx.wait(v, "v")
        i = v.memset(HSEL[0:64, 0:1], 1.0); ctx.vv(i)
        ctx.wait(v, "v")
        i = v.memset(HSEL[64:128, 1:2], 1.0); ctx.vv(i)
        ctx.wait(t, "ld"); ctx.wait(s, "ld"); ctx.wait(v, "ld")
        ctx.wait(t, "v"); ctx.wait(s, "v")

        def shifted_rhs(cc, tsl1):
            if cc < 4:
                return XLE[:, cc, tsl1[0] + 1:tsl1[1] + 1]
            return XLE[:, cc, tsl1[0]:tsl1[1]]

        with ExitStack() as pp:
            ps_q = pp.enter_context(ctx.nc.psum_tensor(ctx.uniq("ps_q"), [128, TT], F32))
            ps_n = pp.enter_context(ctx.nc.psum_tensor(ctx.uniq("ps_n"), [2, TT], F32))
            ps_rb = pp.enter_context(ctx.nc.psum_tensor(ctx.uniq("ps_rb"), [128, TT], F32))
            ps_rot = pp.enter_context(ctx.nc.psum_tensor(ctx.uniq("ps_rot"), [128, TT], F32))
            for tt in range(NTT):
                t0_, t1_ = tt * TT, (tt + 1) * TT
                tsl = slice(t0_, t1_)
                for part, outb, sccol, ctab in (("q", QT, QSC, 0), ("k", KT, KSC, 2),
                                                ("v", VT, None, None)):
                    if part == "v":
                        for tc4 in range(4):
                            t128 = slice(t0_ + tc4 * 128, t0_ + tc4 * 128 + 128)
                            ctx.wait(t, "v"); ctx.wait(t, "s")
                            for cc in range(NCH):
                                i = t.matmul(ps_q[:, :],
                                             shifted_rhs(cc, (t128.start, t128.stop)),
                                             WQ[:, cc, 1024:1536],
                                             start=(cc == 0), stop=(cc == NCH - 1))
                            ctx.tv(i)
                            ctx.wait(v, "t")
                            i = v.tensor_copy(VT[:, (t128.start // 128) % 4, :],
                                              ps_q[:, :]); ctx.vv(i)
                            ctx.wait(sy, "v")
                            i = sy.dma_start(out=d["o_v"][t128.start:t128.stop, :],
                                             in_=VT[:, (t128.start // 128) % 4, :])
                            ctx.inc("sttok", i, 16)
                            ctx.wait(v, "sttok", ctx.val("sttok") - 16)
                        continue
                    for oc in range(4):
                        colbase = {"q": 0, "k": 512}[part] + oc * 128
                        ctx.wait(t, "s"); ctx.wait(t, "v")
                        for cc in range(NCH):
                            i = t.matmul(ps_q[:, :], WQ[:, cc, colbase:colbase + 128],
                                         shifted_rhs(cc, (t0_, t1_)),
                                         start=(cc == 0), stop=(cc == NCH - 1))
                        ctx.tv(i)
                        ctx.wait(v, "t")
                        i = v.tensor_copy(QRAW[:, :], ps_q[:, :]); ctx.vv(i)
                        ctx.wait(s, "v")
                        i = s.square(SQT[:, :], QRAW[:, :]); ctx.sv(i)
                        ctx.wait(t, "s")
                        i = t.matmul(ps_n[:, :], HSEL[:, :], SQT[:, :], start=True, stop=True)
                        ctx.tv(i)
                        ctx.wait(s, "t")
                        i = s.activation(RN[:, :], ps_n[:, :], AF.Sqrt); ctx.sv(i)
                        ctx.wait(v, "s")
                        i = v.tensor_scalar_max(RN[:, :], RN[:, :], 1e-12); ctx.vv(i)
                        ctx.wait(v, "v")
                        i = v.reciprocal(RN[:, :], RN[:, :]); ctx.vv(i)
                        ctx.wait(v, "v")
                        i = v.tensor_copy(RNB[:, :], RN[:, :]); ctx.vv(i)
                        ctx.wait(t, "v")
                        i = t.matmul(ps_rb[:, :], BSEL[:, :], RNB[:, :], start=True, stop=True)
                        ctx.tv(i)
                        ctx.wait(v, "t")
                        i = v.tensor_mul(QRAW[:, :], QRAW[:, :], ps_rb[:, :]); ctx.vv(i)
                        ctx.wait(s, "v")
                        i = s.activation(QRAW[:, :], QRAW[:, :], AF.Copy,
                                         scale=sccol[:, 0:1])
                        ctx.sv(i)
                        ctx.wait(t, "s")
                        i = t.matmul(ps_rot[:, :], ROTP[:, :], QRAW[:, :], start=True, stop=True)
                        ctx.tv(i)
                        ctx.wait(v, "t")
                        i = v.tensor_copy(ROTS[:, :], ps_rot[:, :]); ctx.vv(i)
                        ctx.wait(v, "v")
                        i = v.tensor_mul(M1[:, :], QRAW[:, :], TAB[:, ctab, tsl]); ctx.vv(i)
                        ctx.wait(v, "v")
                        i = v.tensor_mul(ROTS[:, :], ROTS[:, :], TAB[:, ctab + 1, tsl]); ctx.vv(i)
                        ctx.wait(v, "v")
                        i = v.tensor_add(outb[:, oc, tsl], M1[:, :], ROTS[:, :]); ctx.vv(i)
                        ctx.wait(t, "v"); ctx.wait(s, "v")
        ctx.wait(sy, "v")
        for nm, buf in (("o_q", QT), ("o_k", KT)):
            i = sy.dma_start(out={"o_q": o_q, "o_k": o_k}[nm].rearrange(
                "(r p) w -> p r w", p=128), in_=buf[:, :, :])
            ctx.inc("stx", i, 16)
        ctx.wait(sy, "stx"); ctx.wait(sy, "sttok")

def build_u4():
    nc = bass.Bass("TRN2", target_bir_lowering=False)
    xl = nc.dram_tensor("xl", [C, T], BF16, kind="ExternalInput").ap()
    halo = nc.dram_tensor("halo", [C, 1], BF16, kind="ExternalInput").ap()
    wqkv = nc.dram_tensor("wqkv", [C, 1536], F32, kind="ExternalInput").ap()
    qscol = nc.dram_tensor("qscol", [128, 1], F32, kind="ExternalInput").ap()
    kscol = nc.dram_tensor("kscol", [128, 1], F32, kind="ExternalInput").ap()
    rotp = nc.dram_tensor("rotp", [128, 128], F32, kind="ExternalInput").ap()
    bselin = nc.dram_tensor("bsel", [2, 128], F32, kind="ExternalInput").ap()
    tabs = nc.dram_tensor("tabs", [4, 128, T], F32, kind="ExternalInput").ap()
    o_q = nc.dram_tensor("o_q", [512, T], BF16, kind="ExternalOutput").ap()
    o_k = nc.dram_tensor("o_k", [512, T], BF16, kind="ExternalOutput").ap()
    o_v = nc.dram_tensor("o_v", [T, 512], BF16, kind="ExternalOutput").ap()

    d = {"bselin": bselin, "halo": halo, "kscol": kscol, "o_k": o_k, "o_q": o_q, "o_v": o_v, "qscol": qscol, "rotp": rotp, "tabs": tabs, "wqkv": wqkv, "xl": xl}
    with ExitStack() as st:
        ctx = Ctx(nc, st)
        phase_u4(ctx, d)
    return nc
def _l0ffn_tables(layer_t0):
    half_idx = np.arange(0, 64, 2, dtype=np.float64)
    inv_freq = 1.0 / 10000.0 ** (half_idx / 64.0)
    tpos = np.arange(layer_t0, layer_t0 + T, dtype=np.float64)
    freqs = tpos[:, None] * inv_freq[None, :]
    base = (half_idx + 0.4 * 64) / (1.4 * 64)
    power = (tpos - TFULL // 2) / 256.0
    xs = base[None, :] ** power[:, None]
    cos, sin = np.cos(freqs), np.sin(freqs)
    def mk(a):
        p64 = np.concatenate([a, a], axis=1)
        return np.tile(p64, (1, 2)).T.astype(np.float32).copy()
    return np.stack([mk(cos * xs * 8.0), mk(sin * xs * 8.0),
                     mk(cos / xs), mk(sin / xs)])


def _rotperm_host():
    lh = np.zeros((128, 128), np.float32)
    for blk in range(2):
        o = 64 * blk
        for d in range(32):
            lh[o + d + 32, o + d] = -1.0
            lh[o + d, o + d + 32] = 1.0
    return lh


HALO = 512
BKEYS = HALO + T


def phase_u5(ctx, d):
    nc = ctx.nc
    ind = d["ind"]
    ksrc = d.get("ksrc")
    lam_a = d["lam_a"]
    lam_b = d["lam_b"]
    mcb = d["mcb"]
    o_xm = d["o_xm"]
    qin = d["qin"]
    vsrc = d.get("vsrc")
    woin = d["woin"]
    xlin = d["xlin"]
    with ctx.scope("u5"):
        g, sy, t, v, s = ctx.g, ctx.sy, ctx.t, ctx.v, ctx.s
        QT = ctx.sbuf("QT", [128, 4, T], BF16)
        XL = ctx.sbuf("XL", [128, NCH, T], BF16)
        OFF = ctx.sbuf("OFF", [128, 4, T], BF16)
        WO = ctx.sbuf("WO", [128, 4, C], BF16)
        qhead = ctx.sbuf("qhead", [66, T], BF16)
        khead = ctx.sbuf("khead", [66, BKEYS], BF16)
        vhead = ctx.sbuf("vhead", [128, BKEYS // 128, 128], BF16)
        PTtB = [ctx.sbuf(f"PTt{i_}", [128, TT], BF16) for i_ in range(2)]
        densf = ctx.sbuf("densf", [128, TT], F32)
        densb = ctx.sbuf("densb", [128, TT], BF16)
        LAMA = ctx.sbuf("LAMA", [128, 2 * 128], F32)
        LAMB = ctx.sbuf("LAMB", [128, 8 * TT], F32)
        MC = ctx.sbuf("MC", [128, 16], F32)
        MCCOL = ctx.sbuf("MCCOL", [128, 2], F32)
        BIGROW = ctx.sbuf("BIGROW", [1, T], BF16)
        ones2 = ctx.sbuf("ones2", [128, 128], BF16)

        i = sy.dma_start(out=QT[:, :, :], in_=qin.rearrange("(r p) w -> p r w", p=128))
        ctx.inc("ld", i, 16)
        i = sy.dma_start(out=XL[:, :, :], in_=xlin.rearrange("(r p) w -> p r w", p=128))
        ctx.inc("ld", i, 16)
        i = sy.dma_start(out=LAMA[:, :], in_=lam_a[:, :]); ctx.inc("ld", i, 16)
        i = sy.dma_start(out=LAMB[:, :], in_=lam_b[:, :]); ctx.inc("ld", i, 16)
        load_weight(ctx, WO, woin)
        i = v.memset(khead[64:65, :], -1.0); ctx.vv(i)
        i = v.memset(BIGROW[:, :], BIGNEG); ctx.vv(i)
        i = v.memset(ones2[:, :], 1.0); ctx.vv(i)
        ctx.wait(sy, "v")
        i = sy.dma_start(out=qhead[65:66, :], in_=BIGROW[0:1, :]); ctx.inc("ld", i, 16)
        i = sy.dma_start(out=khead[65:66, :], in_=ind[0:1, :]); ctx.inc("ld", i, 16)
        ctx.wait(t, "ld"); ctx.wait(s, "ld"); ctx.wait(v, "ld")

        with ExitStack() as pp:
            ps_scB = [pp.enter_context(ctx.nc.psum_tensor(ctx.uniq(f"ps_sc{i_}"), [128, TT], F32))
                      for i_ in range(2)]
            ps_sc = ps_scB[0]
            ps_a2 = pp.enter_context(ctx.nc.psum_tensor(ctx.uniq("ps_a2"), [128, 128], F32))
            ps_pv = pp.enter_context(ctx.nc.psum_tensor(ctx.uniq("ps_pv"), [128, TT], F32))
            ps_bc = pp.enter_context(ctx.nc.psum_tensor(ctx.uniq("ps_bc"), [128, TT], F32))
            for h in range(8):
                r0 = 64 * (h % 2)
                r1 = 64 * (1 - (h % 2))
                ctx.wait(sy, "t"); ctx.wait(sy, "v"); ctx.wait(g, "t"); ctx.wait(g, "v")
                ctx.wait(sy, "t_pv", ctx.val("t_pv"))
                ctx.wait(g, "t_pv", ctx.val("t_pv"))
                if d.get("stage_k") is None:
                    i = sy.dma_start(out=khead[0:64, :], in_=ksrc[64 * h:64 * h + 64, :])
                    ctx.inc("ldf", i, 16)
                    i = sy.dma_start(out=vhead[:, :, :],
                                     in_=vsrc[h, :, :].rearrange("(r p) w -> p r w", p=128))
                    ctx.inc("ldf", i, 16)
                else:
                    d["stage_k"](h, khead)
                    d["stage_v"](h, vhead)
                i = sy.dma_start(out=qhead[0:64, :],
                                 in_=QT[64 * (h % 2):64 * (h % 2) + 64, h // 2, :])
                ctx.inc("ldf", i, 16)
                ctx.wait(v, "ldf")
                i = v.memset(qhead[64:65, :], 0.0); ctx.vv(i)
                ctx.wait(t, "v"); ctx.wait(t, "ldf")
                for gq in range(16):
                    p0 = gq * 128
                    i = t.matmul(ps_sc[:, :], qhead[0:66, p0:p0 + 128],
                                 khead[0:66, p0:p0 + 512], start=True, stop=True)
                    i = t.matmul(ps_a2[:, :], qhead[0:66, p0:p0 + 128],
                                 khead[0:66, p0 + 512:p0 + 640], start=True, stop=True)
                    ctx.tv(i)
                    ctx.wait(v, "t")
                    i = v.tensor_add(ps_sc[:, 0:128], ps_sc[:, 0:128], LAMA[:, 0:128])
                    ctx.vv(i)
                    ctx.wait(v, "v")
                    i = v.tensor_add(ps_a2[:, :], ps_a2[:, :], LAMA[:, 128:256]); ctx.vv(i)
                    ctx.wait(v, "v")
                    i = v.reduce_max(MCCOL[:, 0:1], ps_sc[:, :], axis=AX.X); ctx.vv(i)
                    ctx.wait(v, "v")
                    i = v.reduce_max(MCCOL[:, 1:2], ps_a2[:, :], axis=AX.X); ctx.vv(i)
                    ctx.wait(v, "v")
                    i = v.tensor_max(MC[:, gq:gq + 1], MCCOL[:, 0:1], MCCOL[:, 1:2]); ctx.vv(i)
                    ctx.wait(t, "v")
                ctx.wait(g, "v"); ctx.wait(g, "t")
                i = g.dma_start(out=mcb[:, :], in_=MC[:, :]); ctx.inc("ldf", i, 16)
                ctx.wait(g, "ldf")
                i = g.dma_start(out=qhead[64:65, :],
                                in_=mcb[:, :].transpose([1, 0]), allow_slow_non_contiguous=True)
                ctx.inc("ldf", i, 16)
                ctx.wait(t, "ldf")
                for qt in range(NTT):
                    tsl = slice(qt * TT, (qt + 1) * TT)
                    s_ex = [None] * 8
                    v_mk = [None] * 8
                    pv_ct = [None] * 8

                    def emit_qk5(kc):
                        c0 = qt * TT + kc * 128
                        if kc >= 2:
                            ctx.wait(t, "s", s_ex[kc - 2])
                        i = t.matmul(ps_scB[kc % 2][:, :], khead[0:66, c0:c0 + 128],
                                     qhead[0:66, tsl], start=True, stop=True)
                        tq = ctx.tv(i)
                        ctx.wait(v, "t", tq)
                        if kc >= 2:
                            ctx.wait(v, "s", s_ex[kc - 2])
                        i = v.tensor_add(ps_scB[kc % 2][:, :], ps_scB[kc % 2][:, :],
                                         LAMB[:, kc * TT:(kc + 1) * TT])
                        v_mk[kc] = ctx.vv(i)
                        ctx.wait(s, "v", v_mk[kc])
                        if kc >= 2:
                            ctx.wait(s, "t_pv", pv_ct[kc - 2])
                        i = s.activation(PTtB[kc % 2][:, :], ps_scB[kc % 2][:, :],
                                         AF.Exp)
                        s_ex[kc] = ctx.sv(i)

                    def emit_pv5(kc):
                        c0 = qt * TT + kc * 128
                        ctx.wait(t, "s", s_ex[kc])
                        i = t.matmul(ps_pv[:, :], vhead[:, (c0 // 128), :],
                                     PTtB[kc % 2][:, :],
                                     start=(kc == 0), stop=(kc == 7))
                        pv_ct[kc] = ctx.inc("t_pv", i, 1)

                    emit_qk5(0)
                    for kc in range(1, 8):
                        emit_qk5(kc)
                        emit_pv5(kc - 1)
                    emit_pv5(7)
                    ctx.wait(v, "t_pv")
                    i = v.reciprocal(densf[r1:r1 + 1, :], ps_pv[r1:r1 + 1, :]); ctx.vv(i)
                    ctx.wait(v, "v")
                    i = v.tensor_copy(densb[r1:r1 + 1, :], densf[r1:r1 + 1, :]); ctx.vv(i)
                    ctx.wait(t, "v")
                    i = t.matmul(ps_bc[:, :], ones2[r1:r1 + 1, :], densb[r1:r1 + 1, :],
                                 start=True, stop=True)
                    ctx.tv(i)
                    ctx.wait(v, "t")
                    i = v.tensor_copy(OFF[r0:r0 + 64, h // 2, tsl], ps_pv[r0:r0 + 64, :])
                    ctx.vv(i)
                    ctx.wait(v, "v")
                    i = v.tensor_mul(OFF[r0:r0 + 64, h // 2, tsl],
                                     OFF[r0:r0 + 64, h // 2, tsl], ps_bc[r0:r0 + 64, :])
                    ctx.vv(i)
                    ctx.wait(t, "v")
        with ExitStack() as pp:
            ps_o = pp.enter_context(ctx.nc.psum_tensor(ctx.uniq("ps_wo"), [128, TT], F32))
            for tt in range(NTT):
                tsl = slice(tt * TT, (tt + 1) * TT)
                for oc in range(NCH):
                    ctx.wait(t, "v")
                    for kc4 in range(4):
                        i = t.matmul(ps_o[:, :], WO[:, kc4, oc * 128:(oc + 1) * 128],
                                     OFF[:, kc4, tsl], start=(kc4 == 0), stop=(kc4 == 3))
                    ctx.tv(i)
                    ctx.wait(v, "t")
                    i = v.tensor_add(XL[:, oc, tsl], XL[:, oc, tsl], ps_o[:, :]); ctx.vv(i)
        ctx.wait(sy, "v")
        i = sy.dma_start(out=o_xm.rearrange("(r p) w -> p r w", p=128), in_=XL[:, :, :])
        ctx.inc("stx", i, 16)
        ctx.wait(sy, "stx")

def build_u5():
    nc = bass.Bass("TRN2", target_bir_lowering=False)
    qin = nc.dram_tensor("q", [512, T], BF16, kind="ExternalInput").ap()
    ksrc = nc.dram_tensor("ksrc", [512, BKEYS], BF16, kind="ExternalInput").ap()
    vsrc = nc.dram_tensor("vsrc", [8, BKEYS, 128], BF16, kind="ExternalInput").ap()
    ind = nc.dram_tensor("ind", [1, BKEYS], BF16, kind="ExternalInput").ap()
    lam_a = nc.dram_tensor("lam_a", [128, 2 * 128], F32, kind="ExternalInput").ap()
    lam_b = nc.dram_tensor("lam_b", [128, 8 * TT], F32, kind="ExternalInput").ap()
    xlin = nc.dram_tensor("xl", [C, T], BF16, kind="ExternalInput").ap()
    woin = nc.dram_tensor("wo", [512, C], F32, kind="ExternalInput").ap()
    o_xm = nc.dram_tensor("o_xm", [C, T], BF16, kind="ExternalOutput").ap()
    mcb = nc.dram_tensor("mcb", [128, 16], F32, kind="Internal").ap()

    d = {"ind": ind, "ksrc": ksrc, "lam_a": lam_a, "lam_b": lam_b, "mcb": mcb, "o_xm": o_xm, "qin": qin, "vsrc": vsrc, "woin": woin, "xlin": xlin}
    with ExitStack() as st:
        ctx = Ctx(nc, st)
        phase_u5(ctx, d)
    return nc
def _la_masks_host():
    a = np.zeros((128, 256), np.float32)
    for r in range(128):
        a[r, 0:r] = BIGNEG
        a[r, 128 + r + 1:256] = BIGNEG
    bmask = np.zeros((128, 8 * TT), np.float32)
    for kc in range(8):
        tile = bmask[:, kc * TT:(kc + 1) * TT]
        for q in range(TT):
            qsub, qr = q // 128, q % 128
            dbase = 128 * (kc - 4 - qsub) - qr
            srel = np.arange(128)
            diff = dbase + srel
            bad = (diff < -512) | (diff > 0)
            tile[bad, q] = BIGNEG
    return a, bmask


W1CH = [(i * 128, min(128, 2730 - i * 128)) for i in range(22)]


def phase_u6(ctx, d, final=False):
    nc = ctx.nc
    b1a = d["b1a"]
    b1g = d["b1g"]
    b2c = d["b2c"]
    bcol = d["bcol"]
    gcol = d["gcol"]
    halo = d["halo"]
    o_hln = d["o_hln"]
    o_out = d["o_out"]
    w1in = d["w1in"]
    w2in = d["w2in"]
    x2in = d.get("x2in")
    xmin = d["xmin"]
    with ctx.scope("u6"):
        g, sy, t, v, s = ctx.g, ctx.sy, ctx.t, ctx.v, ctx.s
        XME = ctx.sbuf("XME", [128, NCH, T + 1], BF16)
        HLN = ctx.sbuf("HLN", [128, NCH, T], BF16)
        X1 = ctx.sbuf("X1", [128, NCH, T], BF16)
        HS = ctx.sbuf("HS", [128, 22, TT], BF16)
        WA = ctx.sbuf("WA", [128, NCH, 256], BF16)
        WG = ctx.sbuf("WG", [128, NCH, 256], BF16)
        W2B = ctx.sbuf("W2B", [128, 2, C], BF16)
        GC = ctx.sbuf("GC", [128, NCH], F32)
        BC = ctx.sbuf("BC", [128, NCH], F32)
        B1A = ctx.sbuf("B1A", [128, 22], F32)
        B1G = ctx.sbuf("B1G", [128, 22], F32)
        B2C = ctx.sbuf("B2C", [128, NCH], F32)
        SQ = ctx.sbuf("SQ", [128, TT], BF16)
        SQ2 = [SQ, ctx.sbuf("SQb", [128, TT], BF16)]
        TMP32 = ctx.sbuf("TMP32", [128, TT], F32)
        TMPB = ctx.sbuf("TMPB", [128, TT], BF16)
        TMPB2 = [TMPB, ctx.sbuf("TMPBb", [128, TT], BF16)]
        X2TILE = ctx.sbuf("X2TILE", [128, TT], F32)
        _ssum = ctx.sbuf("ssum", [1, T], F32)
        _ssq = ctx.sbuf("ssq", [1, T], F32)
        _rs = ctx.sbuf("rs", [1, T], F32)
        rows = {"ssum": _ssum, "ssq": _ssq, "svar": _ssq, "rs": _rs}
        ROWSB = [ctx.sbuf(f"ROWSB{i_}", [1, T], BF16) for i_ in range(2)]
        EPS11 = ctx.sbuf("EPS11", [1, 1], F32)
        ones128b = ctx.sbuf("ones128b", [128, 1], BF16)
        ones1 = ctx.sbuf("ones1", [1, 128], BF16)

        i = sy.dma_start(out=XME[:, :, 1:T + 1], in_=xmin.rearrange("(r p) w -> p r w", p=128))
        ctx.inc("ld", i, 16)
        i = sy.dma_start(out=XME[:, :, 0:1], in_=halo.rearrange("(r p) w -> p r w", p=128),
                         allow_slow_non_contiguous=True)
        ctx.inc("ld", i, 16)
        for dst, src in ((GC, gcol), (BC, bcol), (B1A, b1a), (B1G, b1g), (B2C, b2c)):
            i = sy.dma_start(out=dst[:, :], in_=src[:, :]); ctx.inc("ld", i, 16)
        i = v.memset(ones128b[:, :], 1.0); ctx.vv(i)
        i = v.memset(ones1[:, :], 1.0); ctx.vv(i)
        i = v.memset(EPS11[:, :], EPS); ctx.vv(i)
        ctx.wait(t, "ld"); ctx.wait(s, "ld"); ctx.wait(v, "ld")
        ctx.wait(t, "v"); ctx.wait(s, "v")

        def sh(c, tsl):
            if c < 4:
                return XME[:, c, tsl.start + 1:tsl.stop + 1]
            return XME[:, c, tsl.start:tsl.stop]

        with ExitStack() as pp:
            ps_a = pp.enter_context(ctx.nc.psum_tensor(ctx.uniq("ps_a"), [1, TT], F32))
            ps_b = pp.enter_context(ctx.nc.psum_tensor(ctx.uniq("ps_b"), [1, TT], F32))
            ps_mu = pp.enter_context(ctx.nc.psum_tensor(ctx.uniq("ps_mu"), [128, TT], F32))
            ps_r = pp.enter_context(ctx.nc.psum_tensor(ctx.uniq("ps_r"), [128, TT], F32))
            emit_ln_stats(ctx, sh, rows, ROWSB + [None], SQ, ones128b,
                          (ps_a, ps_b), double_ln=False, eps11=EPS11)
            emit_ln_apply(ctx, sh, ROWSB, 1, HLN, ones1, (ps_mu, ps_r, None),
                          TMP32, gain_bias=(GC, BC))
            ctx.wait(t, "v"); ctx.wait(t, "s")
            ctx.wait(sy, "v")
            i = sy.dma_start(out=o_hln.rearrange("(r p) w -> p r w", p=128), in_=HLN[:, :, :])
            ctx.inc("stx", i, 16)

        for tt in range(NTT):
            tsl = slice(tt * TT, (tt + 1) * TT)
            with ExitStack() as pp:
                ps_paP = [pp.enter_context(ctx.nc.psum_tensor(ctx.uniq(f"pa{tt}_{i_}"), [128, TT], F32)) for i_ in range(2)]
                ps_pgP = [pp.enter_context(ctx.nc.psum_tensor(ctx.uniq(f"pg{tt}_{i_}"), [128, TT], F32)) for i_ in range(2)]
                mm_ct = [None] * 22
                hs_ct = [None] * 22
                ge_ct = [None] * 22
                for gi, (off, M) in enumerate(W1CH):
                    db = gi % 2
                    if gi >= 2:
                        ctx.wait(g, "t", mm_ct[gi - 2])
                    i = g.dma_start(out=WA[:, :, db * 128:db * 128 + M],
                                    in_=w1in[:, off:off + M].rearrange("(r p) w -> p r w", p=128))
                    ctx.inc("lds", i, 16)
                    i = g.dma_start(out=WG[:, :, db * 128:db * 128 + M],
                                    in_=w1in[:, 2730 + off:2730 + off + M].rearrange(
                                        "(r p) w -> p r w", p=128))
                    ld2 = ctx.inc("lds", i, 16)
                    ctx.wait(t, "lds", ld2)
                    if gi >= 2:
                        ctx.wait(t, "v_hs", hs_ct[gi - 2])
                    for cc in range(NCH):
                        t.matmul(ps_paP[db][0:M, :], WA[:, cc, db * 128:db * 128 + M],
                                 HLN[:, cc, tsl], start=(cc == 0), stop=(cc == NCH - 1))
                        i = t.matmul(ps_pgP[db][0:M, :], WG[:, cc, db * 128:db * 128 + M],
                                     HLN[:, cc, tsl], start=(cc == 0), stop=(cc == NCH - 1))
                    mm_ct[gi] = ctx.tv(i)
                    ctx.wait(s, "t", mm_ct[gi])
                    if gi >= 1:
                        ctx.wait(s, "v_hs", hs_ct[gi - 1])
                    i = s.activation(TMPB2[db][:, :], ps_pgP[db][:, :], AF.Gelu,
                                     bias=B1G[:, gi:gi + 1])
                    ge_ct[gi] = ctx.sv(i)
                    ctx.wait(v, "s", ge_ct[gi])
                    i = v.tensor_scalar_add(SQ2[db][:, :], ps_paP[db][:, :], B1A[:, gi:gi + 1])
                    ctx.vv(i)
                    ctx.wait(v, "v")
                    i = v.tensor_mul(HS[:, gi, :], SQ2[db][:, :], TMPB2[db][:, :])
                    hs_ct[gi] = ctx.inc("v_hs", i, 1)
                ctx.wait(t, "v_hs")
            with ExitStack() as pp:
                ps8 = [pp.enter_context(ctx.nc.psum_tensor(ctx.uniq(f"po{tt}_{o}"), [128, TT], F32))
                       for o in range(NCH)]
                w2ct = [None] * 22
                for gi, (off, M) in enumerate(W1CH):
                    db = gi % 2
                    if gi >= 2:
                        ctx.wait(g, "t", w2ct[gi - 2])
                    i = g.dma_start(out=W2B[0:M, db, :], in_=w2in[off:off + M, :])
                    ld2 = ctx.inc("lds", i, 16)
                    ctx.wait(t, "lds", ld2)
                    for oc in range(NCH):
                        i = t.matmul(ps8[oc][:, :], W2B[0:M, db, oc * 128:(oc + 1) * 128],
                                     HS[0:M, gi, :], start=(gi == 0), stop=(gi == 21))
                    w2ct[gi] = ctx.tv(i)
                ctx.wait(v, "t")
                for oc in range(NCH):
                    i = v.tensor_scalar_add(TMPB[:, :], ps8[oc][:, :], B2C[:, oc:oc + 1])
                    ctx.vv(i)
                    ctx.wait(v, "v")
                    i = v.tensor_add(X1[:, oc, tsl], XME[:, oc, tsl.start + 1:tsl.stop + 1],
                                     TMPB[:, :]); ctx.vv(i)
                    ctx.wait(v, "v")
                ctx.wait(t, "v"); ctx.wait(g, "v")
        if not final:
            ctx.wait(sy, "v")
            i = sy.dma_start(out=o_out.rearrange("(r p) w -> p r w", p=128), in_=X1[:, :, :])
            ctx.inc("stx", i, 16)
            ctx.wait(sy, "stx")
        else:
            with ExitStack() as pp:
                ps_a = pp.enter_context(ctx.nc.psum_tensor(ctx.uniq("ps_a2"), [1, TT], F32))
                ps_b = pp.enter_context(ctx.nc.psum_tensor(ctx.uniq("ps_b2"), [1, TT], F32))
                emit_ln_stats(ctx, X1, rows, ROWSB + [None], SQ, ones128b,
                              (ps_a, ps_b), double_ln=False, eps11=EPS11)
            emit_ln_apply_residual(ctx, X1, ROWSB, x2in, o_out, ones1, TMP32, X2TILE)

def build_u6(final=False):
    nc = bass.Bass("TRN2", target_bir_lowering=False)
    xmin = nc.dram_tensor("xm", [C, T], BF16, kind="ExternalInput").ap()
    halo = nc.dram_tensor("halo", [C, 1], BF16, kind="ExternalInput").ap()
    gcol = nc.dram_tensor("gcol", [128, NCH], F32, kind="ExternalInput").ap()
    bcol = nc.dram_tensor("bcol", [128, NCH], F32, kind="ExternalInput").ap()
    w1in = nc.dram_tensor("w1", [C, 5460], F32, kind="ExternalInput").ap()
    b1a = nc.dram_tensor("b1a", [128, 22], F32, kind="ExternalInput").ap()
    b1g = nc.dram_tensor("b1g", [128, 22], F32, kind="ExternalInput").ap()
    w2in = nc.dram_tensor("w2", [2730, C], F32, kind="ExternalInput").ap()
    b2c = nc.dram_tensor("b2c", [128, NCH], F32, kind="ExternalInput").ap()
    if final:
        x2in = nc.dram_tensor("x2T", [C, T], F32, kind="ExternalInput").ap()
        o_out = nc.dram_tensor("o_x3", [C, T], F32, kind="ExternalOutput").ap()
    else:
        o_out = nc.dram_tensor("o_x1", [C, T], BF16, kind="ExternalOutput").ap()
    o_hln = nc.dram_tensor("o_hln", [C, T], BF16, kind="ExternalOutput").ap()

    d = {"b1a": b1a, "b1g": b1g, "b2c": b2c, "bcol": bcol, "gcol": gcol,
         "halo": halo, "o_hln": o_hln, "o_out": o_out, "w1in": w1in,
         "w2in": w2in, "xmin": xmin}
    if final:
        d["x2in"] = x2in
    with ExitStack() as st:
        ctx = Ctx(nc, st)
        phase_u6(ctx, d, final)
    return nc
def phase_u10(ctx, d):
    nc = ctx.nc
    o_k = d["o_k"]
    o_q = d["o_q"]
    o_qn = d["o_qn"]
    o_v = d["o_v"]
    qselin = d["qselin"]
    rotp = d["rotp"]
    tabs = d["tabs"]
    wkbp = d["wkbp"]
    wkva = d["wkva"]
    wqa = d["wqa"]
    wqbp = d["wqbp"]
    wvb = d["wvb"]
    x3in = d["x3in"]
    with ctx.scope("u10"):
        g, sy, t, v, s = ctx.g, ctx.sy, ctx.t, ctx.v, ctx.s
        X3 = ctx.sbuf("X3", [128, NCH, T], BF16)
        ZT = ctx.sbuf("ZT", [128, NCH, T], BF16)
        QTL = ctx.sbuf("QTL", [128, NCH, T], BF16)
        KTL = ctx.sbuf("KTL", [128, 2, T], BF16)
        QAT = ctx.sbuf("QAT", [64, T], BF16)
        KVAT = ctx.sbuf("KVAT", [32, T], BF16)
        WQA = ctx.sbuf("WQA", [128, NCH, 64], BF16)
        WQB = ctx.sbuf("WQB", [64, C], BF16)
        WKVA = ctx.sbuf("WKVA", [128, NCH, 32], BF16)
        WKB = ctx.sbuf("WKB", [32, 256], BF16)
        WVB = ctx.sbuf("WVB", [32, 256], BF16)
        TAB = ctx.sbuf("TAB", [128, 2, T], BF16)
        ROTP = ctx.sbuf("ROTP", [128, 128], BF16)
        QSEL = ctx.sbuf("QSEL", [128, NCH * 16], BF16)
        QRAW = ctx.sbuf("QRAW", [128, TT], BF16)
        ROTS = ctx.sbuf("ROTS", [128, TT], BF16)
        M1 = ctx.sbuf("M1", [128, TT], BF16)
        SQ = ctx.sbuf("SQ", [128, TT], BF16)
        M8 = ctx.sbuf("M8", [16, T], F32)
        VTOK = ctx.sbuf("VTOK", [128, 1, 256], BF16)
        _ssum = ctx.sbuf("ssum", [1, T], F32)
        _ssq = ctx.sbuf("ssq", [1, T], F32)
        _rs = ctx.sbuf("rs", [1, T], F32)
        rows = {"ssum": _ssum, "ssq": _ssq, "svar": _ssq, "rs": _rs}
        ROWSB = [ctx.sbuf(f"ROWSB{i_}", [1, T], BF16) for i_ in range(2)]
        TMP32 = ctx.sbuf("TMP32", [128, TT], F32)
        EPS11 = ctx.sbuf("EPS11", [1, 1], F32)
        ones128b = ctx.sbuf("ones128b", [128, 1], BF16)
        ones1 = ctx.sbuf("ones1", [1, 128], BF16)

        i = g.dma_start(out=X3[:, :, :], in_=x3in.rearrange("(r p) w -> p r w", p=128))
        ctx.inc("ld", i, 16)
        load_weight(ctx, WQA, wqa)
        load_weight(ctx, WKVA, wkva)
        i = g.dma_start(out=WQB[:, :], in_=wqbp[:, :]); ctx.inc("ld", i, 16)
        i = g.dma_start(out=WKB[:, :], in_=wkbp[:, :]); ctx.inc("ld", i, 16)
        i = g.dma_start(out=WVB[:, :], in_=wvb[:, :]); ctx.inc("ld", i, 16)
        i = g.dma_start(out=TAB[:, :, :], in_=tabs.transpose([1, 0, 2])); ctx.inc("ld", i, 16)
        i = g.dma_start(out=ROTP[:, :], in_=rotp[:, :]); ctx.inc("ld", i, 16)
        i = g.dma_start(out=QSEL[:, :], in_=qselin[:, :]); ctx.inc("ld", i, 16)
        i = v.memset(ones128b[:, :], 1.0); ctx.vv(i)
        i = v.memset(ones1[:, :], 1.0); ctx.vv(i)
        i = v.memset(EPS11[:, :], EPS); ctx.vv(i)
        ctx.wait(t, "ld"); ctx.wait(s, "ld"); ctx.wait(v, "ld")
        ctx.wait(t, "v"); ctx.wait(s, "v")

        with ExitStack() as pp:
            ps_a = pp.enter_context(ctx.nc.psum_tensor(ctx.uniq("ps_a"), [1, TT], F32))
            ps_b = pp.enter_context(ctx.nc.psum_tensor(ctx.uniq("ps_b"), [1, TT], F32))
            ps_mu = pp.enter_context(ctx.nc.psum_tensor(ctx.uniq("ps_mu"), [128, TT], F32))
            ps_r = pp.enter_context(ctx.nc.psum_tensor(ctx.uniq("ps_r"), [128, TT], F32))
            emit_ln_stats(ctx, X3, rows, ROWSB + [None], SQ, ones128b,
                          (ps_a, ps_b), double_ln=False, eps11=EPS11)
            emit_ln_apply(ctx, X3, ROWSB, 1, ZT, ones1, (ps_mu, ps_r, None), TMP32)
            ctx.wait(t, "v")
        with ExitStack() as pp:
            ps_qa = pp.enter_context(ctx.nc.psum_tensor(ctx.uniq("ps_qa"), [64, TT], F32))
            ps_kva = pp.enter_context(ctx.nc.psum_tensor(ctx.uniq("ps_kva"), [32, TT], F32))
            ps_q = pp.enter_context(ctx.nc.psum_tensor(ctx.uniq("ps_q"), [128, TT], F32))
            ps_rot = pp.enter_context(ctx.nc.psum_tensor(ctx.uniq("ps_rot"), [128, TT], F32))
            ps_qn = pp.enter_context(ctx.nc.psum_tensor(ctx.uniq("ps_qn"), [16, TT], F32))
            ps_v = pp.enter_context(ctx.nc.psum_tensor(ctx.uniq("ps_v"), [128, 256], F32))
            for tt in range(NTT):
                tsl = slice(tt * TT, (tt + 1) * TT)
                ctx.wait(t, "v"); ctx.wait(t, "s")
                for cc in range(NCH):
                    t.matmul(ps_qa[:, :], WQA[:, cc, :], ZT[:, cc, tsl],
                             start=(cc == 0), stop=(cc == NCH - 1))
                    i = t.matmul(ps_kva[:, :], WKVA[:, cc, :], ZT[:, cc, tsl],
                                 start=(cc == 0), stop=(cc == NCH - 1))
                ctx.tv(i)
                ctx.wait(v, "t")
                i = v.tensor_copy(QAT[:, tsl], ps_qa[:, :]); ctx.vv(i)
                i = v.tensor_copy(KVAT[:, tsl], ps_kva[:, :]); ctx.vv(i)
                ctx.wait(t, "v")
            for tt in range(NTT):
                tsl = slice(tt * TT, (tt + 1) * TT)
                for oc in range(NCH):
                    ctx.wait(t, "v"); ctx.wait(t, "s")
                    i = t.matmul(ps_q[:, :], WQB[:, oc * 128:(oc + 1) * 128], QAT[:, tsl],
                                 start=True, stop=True)
                    ctx.tv(i)
                    ctx.wait(v, "t")
                    i = v.tensor_copy(QRAW[:, :], ps_q[:, :]); ctx.vv(i)
                    ctx.wait(t, "v")
                    i = t.matmul(ps_rot[:, :], ROTP[:, :], QRAW[:, :], start=True, stop=True)
                    ctx.tv(i)
                    ctx.wait(v, "t")
                    i = v.tensor_copy(ROTS[:, :], ps_rot[:, :]); ctx.vv(i)
                    ctx.wait(v, "v")
                    i = v.tensor_mul(M1[:, :], QRAW[:, :], TAB[:, 0, tsl]); ctx.vv(i)
                    ctx.wait(v, "v")
                    i = v.tensor_mul(ROTS[:, :], ROTS[:, :], TAB[:, 1, tsl]); ctx.vv(i)
                    ctx.wait(v, "v")
                    i = v.tensor_add(QTL[:, oc, tsl], M1[:, :], ROTS[:, :]); ctx.vv(i)
                    ctx.wait(s, "v")
                    i = s.square(SQ[:, :], QTL[:, oc, tsl]); ctx.sv(i)
                    ctx.wait(t, "s")
                    i = t.matmul(ps_qn[:, :], QSEL[:, oc * 16:(oc + 1) * 16], SQ[:, :],
                                 start=(oc == 0), stop=(oc == NCH - 1))
                    ctx.inc("t_qn", i, 1)
                ctx.wait(s, "t_qn")
                i = s.activation(M8[0:16, tsl], ps_qn[:, :], AF.Sqrt); ctx.sv(i)
                ctx.wait(t, "s")
            for tt in range(NTT):
                tsl = slice(tt * TT, (tt + 1) * TT)
                for oc in range(2):
                    ctx.wait(t, "v"); ctx.wait(t, "s")
                    i = t.matmul(ps_q[:, :], WKB[:, oc * 128:(oc + 1) * 128], KVAT[:, tsl],
                                 start=True, stop=True)
                    ctx.tv(i)
                    ctx.wait(v, "t")
                    i = v.tensor_copy(QRAW[:, :], ps_q[:, :]); ctx.vv(i)
                    ctx.wait(t, "v")
                    i = t.matmul(ps_rot[:, :], ROTP[:, :], QRAW[:, :], start=True, stop=True)
                    ctx.tv(i)
                    ctx.wait(v, "t")
                    i = v.tensor_copy(ROTS[:, :], ps_rot[:, :]); ctx.vv(i)
                    ctx.wait(v, "v")
                    i = v.tensor_mul(M1[:, :], QRAW[:, :], TAB[:, 0, tsl]); ctx.vv(i)
                    ctx.wait(v, "v")
                    i = v.tensor_mul(ROTS[:, :], ROTS[:, :], TAB[:, 1, tsl]); ctx.vv(i)
                    ctx.wait(v, "v")
                    i = v.tensor_add(KTL[:, oc, tsl], M1[:, :], ROTS[:, :]); ctx.vv(i)
                    ctx.wait(t, "v")
            for tc in range(T // 128):
                ctx.wait(t, "v")
                ctx.wait(t, "sttok", ctx.val("sttok"))
                i = t.matmul(ps_v[:, :], KVAT[:, tc * 128:(tc + 1) * 128], WVB[:, :],
                             start=True, stop=True)
                ctx.tv(i)
                ctx.wait(v, "t")
                i = v.tensor_copy(VTOK[:, 0, :], ps_v[:, :]); ctx.vv(i)
                ctx.wait(sy, "v")
                i = sy.dma_start(out=o_v[tc * 128:(tc + 1) * 128, :], in_=VTOK[:, 0, :])
                ctx.inc("sttok", i, 16)
        ctx.wait(sy, "v"); ctx.wait(sy, "s")
        i = sy.dma_start(out=o_q.rearrange("(r p) w -> p r w", p=128), in_=QTL[:, :, :])
        ctx.inc("stx", i, 16)
        i = sy.dma_start(out=o_k.rearrange("(r p) w -> p r w", p=128), in_=KTL[:, :, :])
        ctx.inc("stx", i, 16)
        i = sy.dma_start(out=o_qn[:, :], in_=M8[0:16, :]); ctx.inc("stx", i, 16)
        ctx.wait(sy, "stx"); ctx.wait(sy, "sttok")

def build_u10():
    nc = bass.Bass("TRN2", target_bir_lowering=False)
    x3in = nc.dram_tensor("x3", [C, T], BF16, kind="ExternalInput").ap()
    wqa = nc.dram_tensor("Wqa", [C, 64], F32, kind="ExternalInput").ap()
    wqbp = nc.dram_tensor("WqbP", [64, C], F32, kind="ExternalInput").ap()
    wkva = nc.dram_tensor("Wkva", [C, 32], F32, kind="ExternalInput").ap()
    wkbp = nc.dram_tensor("WkbP", [32, 256], F32, kind="ExternalInput").ap()
    wvb = nc.dram_tensor("Wvb", [32, 256], F32, kind="ExternalInput").ap()
    rotp = nc.dram_tensor("rotp", [128, 128], F32, kind="ExternalInput").ap()
    tabs = nc.dram_tensor("tabs", [2, 128, T], F32, kind="ExternalInput").ap()
    qselin = nc.dram_tensor("qsel", [128, NCH * 16], F32, kind="ExternalInput").ap()
    o_q = nc.dram_tensor("o_q", [C, T], BF16, kind="ExternalOutput").ap()
    o_k = nc.dram_tensor("o_k", [256, T], BF16, kind="ExternalOutput").ap()
    o_v = nc.dram_tensor("o_v", [T, 256], BF16, kind="ExternalOutput").ap()
    o_qn = nc.dram_tensor("o_qn", [16, T], F32, kind="ExternalOutput").ap()

    d = {"o_k": o_k, "o_q": o_q, "o_qn": o_qn, "o_v": o_v, "qselin": qselin, "rotp": rotp, "tabs": tabs, "wkbp": wkbp, "wkva": wkva, "wqa": wqa, "wqbp": wqbp, "wvb": wvb, "x3in": x3in}
    with ExitStack() as st:
        ctx = Ctx(nc, st)
        phase_u10(ctx, d)
    return nc
def phase_u11(ctx, d):
    nc = ctx.nc
    ind = d["ind"]
    kmb = d["kmb"]
    ksrc = d.get("ksrc")
    kvsel = d["kvsel"]
    masks = d["masks"]
    o_o = d["o_o"]
    qin = d["qin"]
    qnin = d["qnin"]
    vsrc = d.get("vsrc")
    with ctx.scope("u11"):
        g, sy, t, v, s = ctx.g, ctx.sy, ctx.t, ctx.v, ctx.s
        QT = ctx.sbuf("QT", [128, NCH, T], BF16)
        OUTF = ctx.sbuf("OUTF", [128, NCH, T], BF16)
        M8 = ctx.sbuf("M8", [16, T], F32)
        qhead2 = [ctx.sbuf(f"qhead{i_}", [128, T], BF16) for i_ in range(2)]
        khead2 = [ctx.sbuf(f"khead{i_}", [128, KEYS], BF16) for i_ in range(2)]
        vhead2 = [ctx.sbuf(f"vhead{i_}", [128, KEYS // 128, 128], BF16)
                  for i_ in range(2)]
        PTt2 = [ctx.sbuf(f"PTt{i_}", [128, 2 * TT], BF16) for i_ in range(2)]
        densf = ctx.sbuf("densf", [128, TT], F32)
        densb = ctx.sbuf("densb", [128, TT], BF16)
        MASKS = ctx.sbuf("MASKS", [128, 4 * TT], F32)
        ones2 = ctx.sbuf("ones2", [128, 128], BF16)
        ones64 = ctx.sbuf("ones64", [64, 1], BF16)
        SQK = ctx.sbuf("SQK", [64, KEYS], BF16)
        KROW = ctx.sbuf("KROW", [1, KEYS], F32)
        KM11 = ctx.sbuf("KM11", [1, 1], F32)
        KMROW = ctx.sbuf("KMROW", [4, 1], BF16)
        KVSEL = ctx.sbuf("KVSEL", [4, 16], BF16)
        MK16 = ctx.sbuf("MK16", [16, 1], F32)
        BIGROW = ctx.sbuf("BIGROW", [1, T], BF16)

        i = sy.dma_start(out=QT[:, :, :], in_=qin.rearrange("(r p) w -> p r w", p=128))
        ctx.inc("ld", i, 16)
        i = sy.dma_start(out=M8[:, :], in_=qnin[:, :]); ctx.inc("ld", i, 16)
        i = sy.dma_start(out=MASKS[:, :], in_=masks[:, :]); ctx.inc("ld", i, 16)
        i = g.dma_start(out=KVSEL[:, :], in_=kvsel[:, :]); ctx.inc("ld", i, 16)
        for b_ in range(2):
            i = v.memset(khead2[b_][64:128, :], 0.0); ctx.vv(i)
            i = v.memset(qhead2[b_][64:128, :], 0.0); ctx.vv(i)
        ctx.wait(v, "v")
        for b_ in range(2):
            i = v.memset(khead2[b_][64:65, :], -1.0); ctx.vv(i)
        i = v.memset(BIGROW[:, :], BIGNEG); ctx.vv(i)
        i = v.memset(ones2[:, :], 1.0); ctx.vv(i)
        i = v.memset(ones64[:, :], 1.0); ctx.vv(i)
        ctx.wait(sy, "v")
        for b_ in range(2):
            i = sy.dma_start(out=qhead2[b_][65:66, :], in_=BIGROW[0:1, :])
            ctx.inc("ld", i, 16)
        ctx.wait(t, "ld"); ctx.wait(v, "ld"); ctx.wait(s, "ld")

        with ExitStack() as pp:
            ps_n = pp.enter_context(ctx.nc.psum_tensor(ctx.uniq("ps_n"), [1, TT], F32))
            for kv in range(4):
                ctx.wait(sy, "s"); ctx.wait(sy, "t")
                if d.get("stage_k") is None:
                    i = sy.dma_start(out=khead2[0][0:64, :], in_=ksrc[64 * kv:64 * kv + 64, :])
                    ctx.inc("ldf", i, 16)
                else:
                    d["stage_k"](kv, khead2[0])
                ctx.wait(s, "ldf")
                i = s.square(SQK[:, :], khead2[0][0:64, :]); ctx.sv(i)
                ctx.wait(t, "s")
                for kt in range(KEYS // TT):
                    i = t.matmul(ps_n[:, :], ones64[:, 0:1], SQK[:, kt * TT:(kt + 1) * TT],
                                 start=True, stop=True)
                    ctx.tv(i)
                    ctx.wait(v, "t")
                    i = v.tensor_copy(KROW[0:1, kt * TT:(kt + 1) * TT], ps_n[:, :]); ctx.vv(i)
                    ctx.wait(t, "v")
                ctx.wait(v, "v")
                i = v.reduce_max(KM11[0:1, 0:1], KROW[0:1, :], axis=AX.X); ctx.vv(i)
                ctx.wait(s, "v")
                i = s.activation(KM11[0:1, 0:1], KM11[0:1, 0:1], AF.Sqrt, scale=1.0202)
                ctx.sv(i)
                ctx.wait(v, "s")
                i = v.tensor_copy(KROW[0:1, 0:1], KM11[0:1, 0:1]); ctx.vv(i)
                ctx.wait(g, "v")
                i = g.dma_start(out=KMROW[kv:kv + 1, 0:1], in_=KROW[0:1, 0:1])
                ctx.inc("ldf", i, 16)
            ctx.wait(t, "ldf")
            ps_m = pp.enter_context(ctx.nc.psum_tensor(ctx.uniq("ps_m"), [16, 1], F32))
            i = t.matmul(ps_m[:, :], KVSEL[:, :], KMROW[:, 0:1], start=True, stop=True)
            ctx.tv(i)
            ctx.wait(v, "t")
            i = v.tensor_copy(MK16[:, 0:1], ps_m[:, :]); ctx.vv(i)
            ctx.wait(v, "v")
            i = v.tensor_scalar_mul(M8[0:16, :], M8[0:16, :], MK16[:, 0:1]); ctx.vv(i)
            ctx.wait(g, "v"); ctx.wait(t, "v"); ctx.wait(s, "v")
            ctx.wait(sy, "s")

        with ExitStack() as pp:
            ps_sc2 = [pp.enter_context(ctx.nc.psum_tensor(ctx.uniq(f"ps_sc{i_}"), [128, 2 * TT], F32))
                      for i_ in range(2)]
            ps_pv2 = [pp.enter_context(ctx.nc.psum_tensor(ctx.uniq(f"ps_pv{i_}"), [128, TT], F32))
                      for i_ in range(2)]
            ps_bc = pp.enter_context(ctx.nc.psum_tensor(ctx.uniq("ps_bc"), [128, TT], F32))
            fb = dict(qhead2=qhead2, khead2=khead2, vhead2=vhead2, ps_sc2=ps_sc2,
                      ps_pv2=ps_pv2, ps_bc=ps_bc, PTt2=PTt2, densf=densf,
                      densb=densb, ones2=ones2)
            horder = []
            for g_ in range(4):
                horder += [4 * g_, 4 * g_ + 2, 4 * g_ + 1, 4 * g_ + 3]
            emit_flash(ctx, fb, 16, lambda h: h // 4, ksrc, vsrc, ind, QT, M8, OUTF,
                       0.125, lambda qt: 20 + 4 * qt, lambda qt: 16 + 4 * qt,
                       KEYS, MASKS, "l1", head_order=horder,
                       vslot_of_head=lambda h: (h // 4) * 2 + (h % 2),
                       stage_k_fn=d.get("stage_k"), stage_v_fn=d.get("stage_v"))
        ctx.wait(sy, "v")
        i = sy.dma_start(out=o_o.rearrange("(r p) w -> p r w", p=128), in_=OUTF[:, :, :])
        ctx.inc("stx", i, 16)
        ctx.wait(sy, "stx")

def build_u11():
    nc = bass.Bass("TRN2", target_bir_lowering=False)
    qin = nc.dram_tensor("q", [C, T], BF16, kind="ExternalInput").ap()
    qnin = nc.dram_tensor("qn", [16, T], F32, kind="ExternalInput").ap()
    ksrc = nc.dram_tensor("ksrc", [256, KEYS], BF16, kind="ExternalInput").ap()
    vsrc = nc.dram_tensor("vsrc", [8, KEYS, 128], BF16, kind="ExternalInput").ap()
    ind = nc.dram_tensor("ind", [1, KEYS], BF16, kind="ExternalInput").ap()
    masks = nc.dram_tensor("masks", [128, 4 * TT], F32, kind="ExternalInput").ap()
    kvsel = nc.dram_tensor("kvsel", [4, 16], F32, kind="ExternalInput").ap()
    o_o = nc.dram_tensor("o_o", [C, T], BF16, kind="ExternalOutput").ap()
    kmb = nc.dram_tensor("kmb", [4, 1], F32, kind="Internal").ap()

    d = {"ind": ind, "kmb": kmb, "ksrc": ksrc, "kvsel": kvsel, "masks": masks, "o_o": o_o, "qin": qin, "qnin": qnin, "vsrc": vsrc}
    with ExitStack() as st:
        ctx = Ctx(nc, st)
        phase_u11(ctx, d)
    return nc
def phase_u12(ctx, d):
    nc = ctx.nc
    boutc = d["boutc"]
    o_x4 = d["o_x4"]
    oin = d["oin"]
    wout = d["wout"]
    x3in = d["x3in"]
    with ctx.scope("u12"):
        g, sy, t, v, s = ctx.g, ctx.sy, ctx.t, ctx.v, ctx.s
        OT = ctx.sbuf("OT", [128, NCH, T], BF16)
        WO = ctx.sbuf("WO", [128, NCH, C], BF16)
        YB = ctx.sbuf("YB", [128, NCH, T], BF16)
        BOC = ctx.sbuf("BOC", [128, NCH], F32)
        SQ = ctx.sbuf("SQ", [128, TT], BF16)
        TMP32 = ctx.sbuf("TMP32", [128, TT], F32)
        X0TILE = ctx.sbuf("X0TILE", [128, TT], F32)
        _ssum = ctx.sbuf("ssum", [1, T], F32)
        _ssq = ctx.sbuf("ssq", [1, T], F32)
        _rs = ctx.sbuf("rs", [1, T], F32)
        rows = {"ssum": _ssum, "ssq": _ssq, "svar": _ssq, "rs": _rs}
        ROWSB = [ctx.sbuf(f"ROWSB{i_}", [1, T], BF16) for i_ in range(2)]
        EPS11 = ctx.sbuf("EPS11", [1, 1], F32)
        ones128b = ctx.sbuf("ones128b", [128, 1], BF16)
        ones1 = ctx.sbuf("ones1", [1, 128], BF16)

        i = sy.dma_start(out=OT[:, :, :], in_=oin.rearrange("(r p) w -> p r w", p=128))
        ctx.inc("ld", i, 16)
        load_weight(ctx, WO, wout)
        i = sy.dma_start(out=BOC[:, :], in_=boutc[:, :]); ctx.inc("ld", i, 16)
        i = v.memset(ones128b[:, :], 1.0); ctx.vv(i)
        i = v.memset(ones1[:, :], 1.0); ctx.vv(i)
        i = v.memset(EPS11[:, :], EPS); ctx.vv(i)
        ctx.wait(t, "ld"); ctx.wait(s, "ld"); ctx.wait(v, "ld")
        ctx.wait(t, "v")
        with ExitStack() as pp:
            ps_o = pp.enter_context(ctx.nc.psum_tensor(ctx.uniq("ps_o"), [128, TT], F32))
            for tt in range(NTT):
                tsl = slice(tt * TT, (tt + 1) * TT)
                for oc in range(NCH):
                    ctx.wait(t, "v")
                    for cc in range(NCH):
                        i = t.matmul(ps_o[:, :], WO[:, cc, oc * 128:(oc + 1) * 128],
                                     OT[:, cc, tsl], start=(cc == 0), stop=(cc == NCH - 1))
                    ctx.tv(i)
                    ctx.wait(v, "t")
                    i = v.tensor_scalar_add(YB[:, oc, tsl], ps_o[:, :], BOC[:, oc:oc + 1])
                    ctx.vv(i)
        ctx.wait(s, "v"); ctx.wait(t, "v")
        with ExitStack() as pp:
            ps_a = pp.enter_context(ctx.nc.psum_tensor(ctx.uniq("ps_a"), [1, TT], F32))
            ps_b = pp.enter_context(ctx.nc.psum_tensor(ctx.uniq("ps_b"), [1, TT], F32))
            emit_ln_stats(ctx, YB, rows, ROWSB + [None], SQ, ones128b,
                          (ps_a, ps_b), double_ln=False, eps11=EPS11)
        emit_ln_apply_residual(ctx, YB, ROWSB, x3in, o_x4, ones1, TMP32, X0TILE)

def build_u12():
    nc = bass.Bass("TRN2", target_bir_lowering=False)
    oin = nc.dram_tensor("o", [C, T], BF16, kind="ExternalInput").ap()
    wout = nc.dram_tensor("Wout", [C, C], F32, kind="ExternalInput").ap()
    boutc = nc.dram_tensor("boutc", [128, NCH], F32, kind="ExternalInput").ap()
    x3in = nc.dram_tensor("x3T", [C, T], F32, kind="ExternalInput").ap()
    o_x4 = nc.dram_tensor("o_x4", [C, T], F32, kind="ExternalOutput").ap()

    d = {"boutc": boutc, "o_x4": o_x4, "oin": oin, "wout": wout, "x3in": x3in}
    with ExitStack() as st:
        ctx = Ctx(nc, st)
        phase_u12(ctx, d)
    return nc
def _l1_tables(layer_t0):
    inv_freq = 1.0 / 10000.0 ** (np.arange(0, 64, 2, dtype=np.float64) / 64.0)
    tpos = np.arange(layer_t0, layer_t0 + T, dtype=np.float64)
    freqs = tpos[:, None] * inv_freq[None, :]
    def mk(a):
        p64 = np.concatenate([a, a], axis=1)
        return np.tile(p64, (1, 2)).T.astype(np.float32).copy()
    return np.stack([mk(np.cos(freqs)), mk(np.sin(freqs))])


QTINY = 2.0 ** -100


def emit_quant_tile(ctx, b_, in_ap, out_ap, W):
    v = ctx.v
    nb = W // 32
    amax, ebits, scl, rcp = b_["amax"], b_["ebits"], b_["scl"], b_["rcp"]
    t1, ti = b_["t1"], b_["ti"]
    ctx.wait(v, "v")
    i = v.reduce_max(amax[:, 0:nb], in_ap.rearrange("p (g k) -> p g k", k=32),
                     axis=AX.X, apply_absolute_value=True); ctx.vv(i)
    ctx.wait(v, "v")
    i = v.tensor_scalar_max(amax[:, 0:nb], amax[:, 0:nb], QTINY); ctx.vv(i)
    ctx.wait(v, "v")
    i = v.tensor_single_scalar(ebits[:, 0:nb], amax[:, 0:nb].bitcast(I32), 23,
                               ALU.logical_shift_right); ctx.vv(i)
    ctx.wait(v, "v")
    i = v.tensor_scalar(ti[:, 0:nb], ebits[:, 0:nb], 1, -3, ALU.mult, ALU.add); ctx.vv(i)
    ctx.wait(v, "v")
    i = v.tensor_single_scalar(scl[:, 0:nb].bitcast(I32), ti[:, 0:nb], 23,
                               ALU.logical_shift_left); ctx.vv(i)
    ctx.wait(v, "v")
    i = v.tensor_scalar(ti[:, 0:nb], ebits[:, 0:nb], -1, 257, ALU.mult, ALU.add); ctx.vv(i)
    ctx.wait(v, "v")
    i = v.tensor_single_scalar(rcp[:, 0:nb].bitcast(I32), ti[:, 0:nb], 23,
                               ALU.logical_shift_left); ctx.vv(i)
    ctx.wait(v, "v")
    i = v.tensor_mul(t1[:, 0:W].rearrange("p (g k) -> p g k", k=32),
                     in_ap.rearrange("p (g k) -> p g k", k=32),
                     rcp[:, 0:nb].unsqueeze(2).broadcast_to([128, nb, 32])); ctx.vv(i)
    ctx.wait(v, "v")
    i = v.tensor_copy(ti[:, 0:W], t1[:, 0:W]); ctx.vv(i)
    ctx.wait(v, "v")
    i = v.tensor_copy(t1[:, 0:W], ti[:, 0:W]); ctx.vv(i)
    ctx.wait(v, "v")
    i = v.tensor_mul(out_ap.rearrange("p (g k) -> p g k", k=32),
                     t1[:, 0:W].rearrange("p (g k) -> p g k", k=32),
                     scl[:, 0:nb].unsqueeze(2).broadcast_to([128, nb, 32])); ctx.vv(i)
    ctx.wait(v, "v")


def _mk_quant_bufs(ctx, W):
    return dict(amax=ctx.sbuf("q_amax", [128, W // 32], F32),
                ebits=ctx.sbuf("q_eb", [128, W // 32], I32),
                scl=ctx.sbuf("q_scl", [128, W // 32], F32),
                rcp=ctx.sbuf("q_rcp", [128, W // 32], F32),
                t1=ctx.sbuf("q_t1", [128, W], F32),
                ti=ctx.sbuf("q_ti", [128, W], I32))


def phase_u13(ctx, d):
    nc = ctx.nc
    o_wfcq = d["o_wfcq"]
    o_wpjq = d["o_wpjq"]
    o_z = d["o_z"]
    wfc = d["wfc"]
    wpj = d["wpj"]
    x4in = d["x4in"]
    with ctx.scope("u13"):
        g, sy, t, v, s = ctx.g, ctx.sy, ctx.t, ctx.v, ctx.s
        WIN = ctx.sbuf("WIN", [128, 4, 1024], F32)
        WQO = ctx.sbuf("WQO", [128, 4, 1024], BF16)
        WPIN = ctx.sbuf("WPIN", [128, 4096], F32)
        WPO = ctx.sbuf("WPO", [128, 4096], BF16)
        XB = ctx.sbuf("XB", [128, NCH, T], BF16)
        qb = _mk_quant_bufs(ctx, 4096)
        SQ = ctx.sbuf("SQ", [128, TT], BF16)
        TMP32 = ctx.sbuf("TMP32", [128, TT], F32)
        _ssum = ctx.sbuf("ssum", [1, T], F32)
        _ssq = ctx.sbuf("ssq", [1, T], F32)
        _rs = ctx.sbuf("rs", [1, T], F32)
        rows = {"ssum": _ssum, "ssq": _ssq, "svar": _ssq, "rs": _rs}
        ROWSB = [ctx.sbuf(f"ROWSB{i_}", [1, T], BF16) for i_ in range(2)]
        EPS11 = ctx.sbuf("EPS11", [1, 1], F32)
        ones128b = ctx.sbuf("ones128b", [128, 1], BF16)
        ones1 = ctx.sbuf("ones1", [1, 128], BF16)

        i = sy.dma_start(out=WIN[:, :, :], in_=wfc.rearrange("(r p) w -> p r w", p=128))
        ctx.inc("ld", i, 16)
        i = sy.dma_start(out=WPIN[:, :], in_=wpj[:, :]); ctx.inc("ld", i, 16)
        i = g.dma_start(out=XB[:, :, :], in_=x4in.rearrange("(r p) w -> p r w", p=128))
        ctx.inc("ld", i, 16)
        i = v.memset(ones128b[:, :], 1.0); ctx.vv(i)
        i = v.memset(ones1[:, :], 1.0); ctx.vv(i)
        i = v.memset(EPS11[:, :], EPS); ctx.vv(i)
        ctx.wait(v, "ld"); ctx.wait(t, "v"); ctx.wait(s, "v")
        for r4 in range(4):
            emit_quant_tile(ctx, qb, WIN[:, r4, :], WQO[:, r4, :], 1024)
        emit_quant_tile(ctx, qb, WPIN[:, :], WPO[:, :], 4096)
        ctx.wait(sy, "v")
        i = sy.dma_start(out=o_wfcq.rearrange("(r p) w -> p r w", p=128), in_=WQO[:, :, :])
        ctx.inc("stx", i, 16)
        i = sy.dma_start(out=o_wpjq[:, :], in_=WPO[:, :]); ctx.inc("stx", i, 16)
        ctx.wait(s, "v"); ctx.wait(t, "v")
        with ExitStack() as pp:
            ps_a = pp.enter_context(ctx.nc.psum_tensor(ctx.uniq("ps_a"), [1, TT], F32))
            ps_b = pp.enter_context(ctx.nc.psum_tensor(ctx.uniq("ps_b"), [1, TT], F32))
            ps_mu = pp.enter_context(ctx.nc.psum_tensor(ctx.uniq("ps_mu"), [128, TT], F32))
            ps_r = pp.enter_context(ctx.nc.psum_tensor(ctx.uniq("ps_r"), [128, TT], F32))
            emit_ln_stats(ctx, XB, rows, ROWSB + [None], SQ, ones128b,
                          (ps_a, ps_b), double_ln=False, eps11=EPS11)
            for tt in range(NTT):
                tsl = slice(tt * TT, (tt + 1) * TT)
                ctx.wait(t, "v")
                t.matmul(ps_mu[:, :], ones1[0:1, :], ROWSB[0][0:1, tsl], start=True, stop=True)
                i = t.matmul(ps_r[:, :], ones1[0:1, :], ROWSB[1][0:1, tsl],
                             start=True, stop=True)
                ctx.tv(i)
                ctx.wait(v, "t")
                for c in range(NCH):
                    ctx.wait(v, "strx", ctx.val("strx"))
                    ctx.wait(v, "v")
                    i = v.tensor_sub(TMP32[:, :], XB[:, c, tsl], ps_mu[:, :]); ctx.vv(i)
                    ctx.wait(v, "v")
                    i = v.tensor_mul(TMP32[:, :], TMP32[:, :], ps_r[:, :]); ctx.vv(i)
                    ctx.wait(sy, "v")
                    i = sy.dma_start(out=o_z.rearrange("(r p) w -> p r w", p=128)[:, c, tsl],
                                     in_=TMP32[:, :])
                    ctx.inc("strx", i, 16)
        ctx.wait(sy, "strx"); ctx.wait(sy, "stx")

def build_u13():
    nc = bass.Bass("TRN2", target_bir_lowering=False)
    wfc = nc.dram_tensor("wfc", [512, 1024], F32, kind="ExternalInput").ap()
    wpj = nc.dram_tensor("wpj", [128, 4096], F32, kind="ExternalInput").ap()
    x4in = nc.dram_tensor("x4T", [C, T], F32, kind="ExternalInput").ap()
    o_wfcq = nc.dram_tensor("o_wfcq", [512, 1024], BF16, kind="ExternalOutput").ap()
    o_wpjq = nc.dram_tensor("o_wpjq", [128, 4096], BF16, kind="ExternalOutput").ap()
    o_z = nc.dram_tensor("o_z", [C, T], F32, kind="ExternalOutput").ap()

    d = {"o_wfcq": o_wfcq, "o_wpjq": o_wpjq, "o_z": o_z, "wfc": wfc, "wpj": wpj, "x4in": x4in}
    with ExitStack() as st:
        ctx = Ctx(nc, st)
        phase_u13(ctx, d)
    return nc
def phase_u14(ctx, d, W):
    nc = ctx.nc
    o_q = d["o_q"]
    zin = d["zin"]
    with ctx.scope("u14"):
        g, sy, v = ctx.g, ctx.sy, ctx.v
        ZIN = ctx.sbuf("ZIN", [128, W], F32)
        ZQ = ctx.sbuf("ZQ", [128, W], BF16)
        qb = _mk_quant_bufs(ctx, W)
        for tc in range(T // 128):
            ctx.wait(sy, "v")
            i = sy.dma_start(out=ZIN[:, :], in_=zin[tc * 128:(tc + 1) * 128, :])
            ctx.inc("ld", i, 16)
            ctx.wait(v, "ld")
            emit_quant_tile(ctx, qb, ZIN[:, :], ZQ[:, :], W)
            ctx.wait(sy, "v")
            i = sy.dma_start(out=o_q[tc * 128:(tc + 1) * 128, :], in_=ZQ[:, :])
            ctx.inc("stx", i, 16)
        ctx.wait(sy, "stx")

def build_u14(W):
    nc = bass.Bass("TRN2", target_bir_lowering=False)
    zin = nc.dram_tensor("zin", [T, W], F32, kind="ExternalInput").ap()
    o_q = nc.dram_tensor("o_q", [T, W], BF16, kind="ExternalOutput").ap()
    d = {"o_q": o_q, "zin": zin}
    with ExitStack() as st:
        ctx = Ctx(nc, st)
        phase_u14(ctx, d, W)
    return nc
def phase_u15(ctx, d):
    nc = ctx.nc
    bfcc = d["bfcc"]
    o_h = d["o_h"]
    wq = d["wq"]
    xq = d["xq"]
    with ctx.scope("u15"):
        g, sy, t, v, s = ctx.g, ctx.sy, ctx.t, ctx.v, ctx.s
        XQ = ctx.sbuf("XQ", [128, NCH, T], BF16)
        WQ = ctx.sbuf("WQ", [128, NCH, 4096], BF16)
        BF_ = ctx.sbuf("BF_", [128, 32], F32)
        HT = ctx.sbuf("HT", [128, TT], F32)
        i = sy.dma_start(out=XQ[:, :, :], in_=xq.rearrange("(r p) w -> p r w", p=128))
        ctx.inc("ld", i, 16)
        i = sy.dma_start(out=WQ[:, :, :], in_=wq.rearrange("(r p) w -> p r w", p=128))
        ctx.inc("ld", i, 16)
        i = sy.dma_start(out=BF_[:, :], in_=bfcc[:, :]); ctx.inc("ld", i, 16)
        ctx.wait(t, "ld"); ctx.wait(s, "ld")
        with ExitStack() as pp:
            ps = pp.enter_context(ctx.nc.psum_tensor(ctx.uniq("ps"), [128, TT], F32))
            for nc32 in range(32):
                for tt in range(NTT):
                    tsl = slice(tt * TT, (tt + 1) * TT)
                    ctx.wait(t, "s")
                    for cc in range(NCH):
                        i = t.matmul(ps[:, :], WQ[:, cc, nc32 * 128:(nc32 + 1) * 128],
                                     XQ[:, cc, tsl], start=(cc == 0), stop=(cc == NCH - 1))
                    ctx.tv(i)
                    ctx.wait(s, "t")
                    ctx.wait(s, "strx", ctx.val("strx"))
                    i = s.activation(HT[:, :], ps[:, :], AF.Gelu,
                                     bias=BF_[:, nc32:nc32 + 1]); ctx.sv(i)
                    ctx.wait(sy, "s")
                    i = sy.dma_start(
                        out=o_h.rearrange("(r p) w -> p r w", p=128)[:, nc32, tsl],
                        in_=HT[:, :])
                    ctx.inc("strx", i, 16)
        ctx.wait(sy, "strx")

def build_u15():
    nc = bass.Bass("TRN2", target_bir_lowering=False)
    xq = nc.dram_tensor("xqT", [C, T], BF16, kind="ExternalInput").ap()
    wq = nc.dram_tensor("wfcqT", [C, 4096], BF16, kind="ExternalInput").ap()
    bfcc = nc.dram_tensor("bfcc", [128, 32], F32, kind="ExternalInput").ap()
    o_h = nc.dram_tensor("o_h", [4096, T], F32, kind="ExternalOutput").ap()
    d = {"bfcc": bfcc, "o_h": o_h, "wq": wq, "xq": xq}
    with ExitStack() as st:
        ctx = Ctx(nc, st)
        phase_u15(ctx, d)
    return nc
def phase_u17(ctx, d):
    nc = ctx.nc
    bpc = d["bpc"]
    hq = d["hq"]
    o_x5 = d["o_x5"]
    wq = d["wq"]
    x4in = d["x4in"]
    with ctx.scope("u17"):
        g, sy, t, v, s = ctx.g, ctx.sy, ctx.t, ctx.v, ctx.s
        WQ = ctx.sbuf("WQ", [128, 32, C], BF16)
        HQT = ctx.sbuf("HQT", [128, 32, TT], BF16)
        YB = ctx.sbuf("YB", [128, NCH, T], BF16)
        BPC = ctx.sbuf("BPC", [128, NCH], F32)
        SQ = ctx.sbuf("SQ", [128, TT], BF16)
        TMP32 = ctx.sbuf("TMP32", [128, TT], F32)
        X0TILE = ctx.sbuf("X0TILE", [128, TT], F32)
        _ssum = ctx.sbuf("ssum", [1, T], F32)
        _ssq = ctx.sbuf("ssq", [1, T], F32)
        _rs = ctx.sbuf("rs", [1, T], F32)
        rows = {"ssum": _ssum, "ssq": _ssq, "svar": _ssq, "rs": _rs}
        ROWSB = [ctx.sbuf(f"ROWSB{i_}", [1, T], BF16) for i_ in range(2)]
        EPS11 = ctx.sbuf("EPS11", [1, 1], F32)
        ones128b = ctx.sbuf("ones128b", [128, 1], BF16)
        ones1 = ctx.sbuf("ones1", [1, 128], BF16)
        i = sy.dma_start(out=WQ[:, :, :], in_=wq.rearrange("(r p) w -> p r w", p=128))
        ctx.inc("ld", i, 16)
        i = sy.dma_start(out=BPC[:, :], in_=bpc[:, :]); ctx.inc("ld", i, 16)
        i = v.memset(ones128b[:, :], 1.0); ctx.vv(i)
        i = v.memset(ones1[:, :], 1.0); ctx.vv(i)
        i = v.memset(EPS11[:, :], EPS); ctx.vv(i)
        ctx.wait(t, "ld"); ctx.wait(v, "ld"); ctx.wait(s, "v")
        with ExitStack() as pp:
            ps8 = [pp.enter_context(ctx.nc.psum_tensor(ctx.uniq(f"ps{o}"), [128, TT], F32))
                   for o in range(NCH)]
            for tt in range(NTT):
                tsl = slice(tt * TT, (tt + 1) * TT)
                ctx.wait(sy, "t")
                i = sy.dma_start(out=HQT[:, :, :],
                                 in_=hq[:, tsl].rearrange("(r p) w -> p r w", p=128))
                ctx.inc("ldh", i, 16)
                ctx.wait(t, "ldh"); ctx.wait(t, "v")
                for kc in range(32):
                    for oc in range(NCH):
                        i = t.matmul(ps8[oc][:, :], WQ[:, kc, oc * 128:(oc + 1) * 128],
                                     HQT[:, kc, :], start=(kc == 0), stop=(kc == 31))
                    ctx.tv(i)
                ctx.wait(v, "t")
                for oc in range(NCH):
                    i = v.tensor_scalar_add(YB[:, oc, tsl], ps8[oc][:, :],
                                            BPC[:, oc:oc + 1]); ctx.vv(i)
        ctx.wait(s, "v"); ctx.wait(t, "v")
        with ExitStack() as pp:
            ps_a = pp.enter_context(ctx.nc.psum_tensor(ctx.uniq("ps_a"), [1, TT], F32))
            ps_b = pp.enter_context(ctx.nc.psum_tensor(ctx.uniq("ps_b"), [1, TT], F32))
            emit_ln_stats(ctx, YB, rows, ROWSB + [None], SQ, ones128b,
                          (ps_a, ps_b), double_ln=False, eps11=EPS11)
        emit_ln_apply_residual(ctx, YB, ROWSB, x4in, o_x5, ones1, TMP32, X0TILE)

def build_u17():
    nc = bass.Bass("TRN2", target_bir_lowering=False)
    hq = nc.dram_tensor("hqT", [4096, T], BF16, kind="ExternalInput").ap()
    wq = nc.dram_tensor("wpjqT", [4096, C], BF16, kind="ExternalInput").ap()
    bpc = nc.dram_tensor("bpc", [128, NCH], F32, kind="ExternalInput").ap()
    x4in = nc.dram_tensor("x4T", [C, T], F32, kind="ExternalInput").ap()
    o_x5 = nc.dram_tensor("o_x5", [C, T], F32, kind="ExternalOutput").ap()
    d = {"bpc": bpc, "hq": hq, "o_x5": o_x5, "wq": wq, "x4in": x4in}
    with ExitStack() as st:
        ctx = Ctx(nc, st)
        phase_u17(ctx, d)
    return nc
def _shard_xT(x):
    out = []
    for c in range(8):
        b, h = c // 2, c % 2
        out.append(np.ascontiguousarray(x[b, h * T:(h + 1) * T, :].T))
    return out


def _qsel_host():
    q = np.zeros((128, NCH * 16), np.float32)
    for oc in range(NCH):
        for r in range(128):
            head = 2 * oc + (1 if r >= 64 else 0)
            q[r, oc * 16 + head] = 1.0
    return q




def _xw(ctx, tag, pairs, pregs, smeta, flag, nread=None):
    nc, sy = ctx.nc, ctx.sy
    rpar = ctx._sregs[0]
    with ctx.scope("xw" + tag):
        xbs = []
        for pi, (pairt, local, R, W, dt_) in enumerate(pairs):
            xbs.append(ctx.sbuf(f"xb{pi}", [128, R // 128, W], dt_))
        ctx.wait(sy, "stx"); ctx.wait(sy, "strx", ctx.val("strx"))
        ctx.wait(sy, "sttok", ctx.val("sttok"))
        ctx.wait(sy, "stq", ctx.val("stq"))
        nc.br_cmp(rpar, 0, on_true=f"xwe_{tag}", on_false=f"xwo_{tag}",
                  comp_op="IS_EQ", engines=[sy.engine])
        for par, bodyname in ((0, f"xwe_{tag}"), (1, f"xwo_{tag}")):
            with nc.body(bodyname, valid_engines=[sy.engine]):
                base = ctx.val("ldx")
                n = 0
                for pi, (pairt, local, R, W, dt_) in enumerate(pairs):
                    xb = xbs[pi]
                    sy.dma_start(out=xb[:, :, :],
                                 in_=local.rearrange("(r p) w -> p r w", p=128),
                                 allow_slow_non_contiguous=True).then_inc(
                                     ctx.sem("ldx")[0], 16)
                    n += 16
                    sy.wait_ge(ctx.sem("ldx")[0], base + n)
                    dst = AP(pairt.tensor, par * (R * W), [[W, R], [1, W]])
                    sy.dma_start(out=dst.rearrange("(r p) w -> p r w", p=128),
                                 in_=xb[:, :, :],
                                 allow_slow_non_contiguous=True).then_inc(
                                     ctx.sem("ldx")[0], 16)
                    n += 16
                    sy.wait_ge(ctx.sem("ldx")[0], base + n)
                fdst = AP(flag.tensor, par * 16, [[1, 1], [1, 1]])
                sy.dma_start(out=fdst, in_=smeta[0:1, 1:2]).then_inc(
                    ctx.sem("ldx")[0], 16)
                n += 16
                sy.wait_ge(ctx.sem("ldx")[0], base + n)
                nc.br(f"xwj_{tag}", engines=[sy.engine])
        nc.switch_bb(f"xwj_{tag}")
        ctx.sem("ldx")[1] += len(pairs) * 32 + 16


def _assemble_pair(ctx, tag, pairt, R, W, dst, pregs):
    nc, sy = ctx.nc, ctx.sy
    rpar = ctx._sregs[0]
    with ctx.scope("asm" + tag):
        xb = ctx.sbuf("xb", [128, R // 128, W], BF16)
        nc.br_cmp(rpar, 0, on_true=f"ase_{tag}", on_false=f"aso_{tag}",
                  comp_op="IS_EQ", engines=[sy.engine])
        for par, bodyname in ((0, f"ase_{tag}"), (1, f"aso_{tag}")):
            with nc.body(bodyname, valid_engines=[sy.engine]):
                base = ctx.val("ldx")
                n = 0
                for off, sl in ((0, 1 - par), (R, par)):
                    src = AP(pairt.tensor, sl * (R * W), [[W, R], [1, W]])
                    sy.dma_start(out=xb[:, :, :],
                                 in_=src.rearrange("(r p) w -> p r w", p=128)).then_inc(
                                     ctx.sem("ldx")[0], 16)
                    n += 16
                    sy.wait_ge(ctx.sem("ldx")[0], base + n)
                    sy.dma_start(out=dst[off:off + R, :].rearrange(
                        "(r p) w -> p r w", p=128), in_=xb[:, :, :]).then_inc(
                            ctx.sem("ldx")[0], 16)
                    n += 16
                    sy.wait_ge(ctx.sem("ldx")[0], base + n)
                nc.br(f"asj_{tag}", engines=[sy.engine])
        nc.switch_bb(f"asj_{tag}")
        ctx.sem("ldx")[1] += 64


def _xpoll(ctx, tag, flag, sregs, sflag_sb, smeta_sy):
    nc, sy = ctx.nc, ctx.sy
    rpar, rno, rfv, rctr = sregs
    ps = ctx.sem("xps_" + tag)
    nc.regs_mov(rctr, 16)
    nc.br_cmp(rpar, 0, on_true=f"xe_{tag}", on_false=f"xo_{tag}",
              comp_op="IS_EQ", engines=[sy.engine])
    for par, bodyname in ((0, f"xe_{tag}"), (1, f"xo_{tag}")):
        with nc.body(bodyname, valid_engines=[sy.engine]):
            src = AP(flag.tensor, (1 - par) * 16, [[1, 1], [1, 1]])
            nc.br(f"xsp_{tag}_{par}", engines=[sy.engine])
            with nc.body(f"xsp_{tag}_{par}", valid_engines=[sy.engine]):
                sy.dma_start(out=sflag_sb[0:1, 0:1], in_=src).then_inc(ps[0], 16)
                sy.wait_ge(ps[0], rctr)
                nc.regs_load(rfv, sflag_sb[0:1, 0:1])
                nc.regs_alu(rctr, rctr, 16, op=mybir.AluOpType.add)
                nc.br_cmp(rfv, rno, on_true=f"xd_{tag}_{par}", on_false=f"xsp_{tag}_{par}",
                          comp_op="IS_EQ", engines=[sy.engine])
            nc.switch_bb(f"xd_{tag}_{par}")
            nc.br(f"xj_{tag}", engines=[sy.engine])
    nc.switch_bb(f"xj_{tag}")
    i = sy.dma_start(out=sflag_sb[0:1, 2:3], in_=smeta_sy[0:1, 1:2])
    ctx.inc("xdone", i, 16)


def _halo_read(ctx, tag, halopair, halo_local, sregs):
    nc, sy = ctx.nc, ctx.sy
    rpar = sregs[0]
    nc.br_cmp(rpar, 0, on_true=f"hj_{tag}", on_false=f"ho_{tag}",
              comp_op="IS_EQ", engines=[sy.engine])
    with nc.body(f"ho_{tag}", valid_engines=[sy.engine]):
        with ctx.scope("hr" + tag):
            hb = ctx.sbuf("hb", [128, NCH, 1], BF16)
            hs = ctx.sem("hx_" + tag)
            sy.dma_start(out=hb[:, :, :],
                         in_=halopair[0:C, 0:1].rearrange("(r p) w -> p r w", p=128),
                         allow_slow_non_contiguous=True).then_inc(hs[0], 16)
            sy.wait_ge(hs[0], 16)
            sy.dma_start(out=halo_local.rearrange("(r p) w -> p r w", p=128),
                         in_=hb[:, :, :], allow_slow_non_contiguous=True).then_inc(hs[0], 16)
            sy.wait_ge(hs[0], 32)
        nc.br(f"hj_{tag}", engines=[sy.engine])
    nc.switch_bb(f"hj_{tag}")
    i = sy.dma_start(out=ctx._sflag[0:1, 3:4], in_=ctx._smeta_sy[0:1, 0:1])
    ctx.inc("xdone", i, 16)


def _mk_stage_k_stacked(ctx, KD, NK):
    sy = ctx.sy

    def stage_k(kv, khead):
        i = sy.dma_start(out=khead[0:64, 0:T], in_=KD[64 * kv:64 * kv + 64, :])
        ctx.inc("ldf", i, 16)
        i = sy.dma_start(out=khead[0:64, T:2 * T], in_=KD[NK + 64 * kv:NK + 64 * kv + 64, :])
        ctx.inc("ldf", i, 16)
    return stage_k


def _mk_stage_v_stacked(ctx, VD, vw, vmapfn):
    g, v = ctx.g, ctx.v

    def stage_v(vs_, vhead, pvfree=None, ldffree=None):
        vh, c0, c1 = vmapfn(vs_)
        if pvfree is None:
            pvfree = ctx.val("t_pv")
        if ldffree is None:
            ldffree = ctx.val("ldf")
        v.wait_ge(ctx.sem("t_pv")[0], pvfree)
        v.wait_ge(ctx.sem("ldf")[0], ldffree)
        i = v.memset(vhead[:, :, c1:c1 + 1], 1.0); ctx.vv(i)
        ctx.wait(g, "v")
        for j in range(8):
            i = g.dma_start(out=vhead[:, 4 * j:4 * (j + 1), c0:c0 + 64],
                            in_=VD[512 * j:512 * (j + 1), 64 * vh:64 * vh + 64]
                            .rearrange("(a p) w -> p a w", p=128))
            ctx.inc("ldf", i, 16)
    return stage_v


def phase_f_wq(ctx, D, L):
    g, sy, v = ctx.g, ctx.sy, ctx.v
    with ctx.scope("fwq"):
        WIN = ctx.sbuf("WIN", [128, 4096], F32)
        WQO = ctx.sbuf("WQO", [128, 4096], BF16)
        qb = _mk_quant_bufs(ctx, 4096)
        for r in range(16):
            ctx.wait(sy, "v")
            i = sy.dma_start(out=WIN[:, 0:1024],
                             in_=D["wfc_half"][128 * r:128 * (r + 1), :])
            ctx.inc("ldq", i, 16)
            ctx.wait(v, "ldq")
            emit_quant_tile(ctx, qb, WIN[:, 0:1024], WQO[:, 0:1024], 1024)
            ctx.wait(sy, "v")
            i = sy.dma_start(out=L["wfcq_d"][128 * r:128 * (r + 1), :],
                             in_=WQO[:, 0:1024])
            ctx.inc("stq", i, 16)
        for r in range(4):
            ctx.wait(sy, "v")
            i = sy.dma_start(out=WIN[:, :], in_=D["wpj_half"][128 * r:128 * (r + 1), :])
            ctx.inc("ldq", i, 16)
            ctx.wait(v, "ldq")
            emit_quant_tile(ctx, qb, WIN[:, :], WQO[:, :], 4096)
            ctx.wait(sy, "v")
            i = sy.dma_start(out=L["wpjq_d"][128 * r:128 * (r + 1), :], in_=WQO[:, :])
            ctx.inc("stq", i, 16)
        ctx.wait(sy, "stq"); ctx.wait(g, "stq")


def phase_f_zq(ctx, D, L):
    nc = ctx.nc
    g, sy, t, v, s = ctx.g, ctx.sy, ctx.t, ctx.v, ctx.s
    with ctx.scope("fzq"):
        XB = ctx.sbuf("XB", [128, NCH, T], BF16)
        X4F = ctx.sbuf("X4F", [128, NCH, 128], F32)
        IDT = ctx.sbuf("IDT", [128, 128], F32)
        ZT = ctx.sbuf("ZT", [128, 128], F32)
        ZTOK = ctx.sbuf("ZTOK", [128, NCH, 128], F32)
        ZQ = ctx.sbuf("ZQ", [128, 1024], BF16)
        SQ = ctx.sbuf("SQ", [128, TT], BF16)
        qb = _mk_quant_bufs(ctx, 1024)
        _ssum = ctx.sbuf("ssum", [1, T], F32)
        _ssq = ctx.sbuf("ssq", [1, T], F32)
        _rs = ctx.sbuf("rs", [1, T], F32)
        rows = {"ssum": _ssum, "ssq": _ssq, "svar": _ssq, "rs": _rs}
        ROWSB = [ctx.sbuf(f"ROWSB{i_}", [1, T], BF16) for i_ in range(2)]
        EPS11 = ctx.sbuf("EPS11", [1, 1], F32)
        ones128b = ctx.sbuf("ones128b", [128, 1], BF16)
        ones1 = ctx.sbuf("ones1", [1, 128], BF16)
        i = g.dma_start(out=XB[:, :, :], in_=L["x4_d"].rearrange("(r p) w -> p r w", p=128))
        ctx.inc("ldq", i, 16)
        i = sy.dma_start(out=IDT[:, :], in_=D["identb"][:, :]); ctx.inc("ldq", i, 16)
        i = v.memset(ones128b[:, :], 1.0); ctx.vv(i)
        i = v.memset(ones1[:, :], 1.0); ctx.vv(i)
        i = v.memset(EPS11[:, :], EPS); ctx.vv(i)
        ctx.wait(t, "ldq"); ctx.wait(s, "ldq"); ctx.wait(v, "ldq")
        ctx.wait(t, "v"); ctx.wait(s, "v")
        with ExitStack() as pp:
            ps_a = pp.enter_context(nc.psum_tensor(ctx.uniq("ps_a"), [1, TT], F32))
            ps_b = pp.enter_context(nc.psum_tensor(ctx.uniq("ps_b"), [1, TT], F32))
            emit_ln_stats(ctx, XB, rows, ROWSB + [None], SQ, ones128b,
                          (ps_a, ps_b), double_ln=False, eps11=EPS11)
        with ExitStack() as pp:
            ps_mu = pp.enter_context(nc.psum_tensor(ctx.uniq("ps_mu"), [128, 128], F32))
            ps_r = pp.enter_context(nc.psum_tensor(ctx.uniq("ps_r"), [128, 128], F32))
            ps_tr = pp.enter_context(nc.psum_tensor(ctx.uniq("ps_tr"), [128, 128], F32))
            for tc in range(T // 128):
                csl = slice(tc * 128, (tc + 1) * 128)
                ctx.wait(t, "v")
                t.matmul(ps_mu[:, :], ones1[0:1, :], ROWSB[0][0:1, csl], start=True, stop=True)
                i = t.matmul(ps_r[:, :], ones1[0:1, :], ROWSB[1][0:1, csl], start=True, stop=True)
                ctx.tv(i)
                ctx.wait(sy, "v"); ctx.wait(sy, "ldq", ctx.val("ldq"))
                i = sy.dma_start(out=X4F[:, :, :],
                                 in_=L["x4_d"].rearrange("(r p) w -> p r w", p=128)[:, :, csl])
                ctx.inc("ldq", i, 16)
                ctx.wait(v, "ldq"); ctx.wait(v, "t")
                for c in range(NCH):
                    ctx.wait(v, "v")
                    i = v.tensor_sub(ZT[:, :], X4F[:, c, :], ps_mu[:, :]); ctx.vv(i)
                    ctx.wait(v, "v")
                    i = v.tensor_mul(ZT[:, :], ZT[:, :], ps_r[:, :]); ctx.vv(i)
                    ctx.wait(t, "v")
                    i = t.transpose(ps_tr[:, :], ZT[:, :], IDT[:, :]); ctx.tv(i)
                    ctx.wait(v, "t")
                    i = v.tensor_copy(ZTOK[:, c, :], ps_tr[:, :]); ctx.vv(i)
                ctx.wait(v, "v")
                emit_quant_tile(ctx, qb, ZTOK[:, :, :].rearrange("p a b -> p (a b)"),
                                ZQ[:, :], 1024)
                ctx.wait(sy, "v")
                i = sy.dma_start(out=L["xq_d"][tc * 128:(tc + 1) * 128, :], in_=ZQ[:, :])
                ctx.inc("stq", i, 16)
                ctx.wait(v, "stq", ctx.val("stq") - 16)
        ctx.wait(sy, "stq"); ctx.wait(t, "v"); ctx.wait(s, "v"); ctx.wait(g, "v")


def phase_f_fc(ctx, D, L, P):
    nc = ctx.nc
    g, sy, t, v, s = ctx.g, ctx.sy, ctx.t, ctx.v, ctx.s
    with ctx.scope("ffc"):
        XQT = ctx.sbuf("XQT", [128, NCH, T], BF16)
        WQ = ctx.sbuf("WQ", [128, NCH, 4096], BF16)
        BFCB = ctx.sbuf("BFCB", [128, 4096], BF16)
        BROW = ctx.sbuf("BROW", [1, 4096], BF16)
        HTOK = ctx.sbuf("HTOK", [128, 4096], F32)
        HQ = ctx.sbuf("HQ", [128, 4096], BF16)
        qb = _mk_quant_bufs(ctx, 4096)
        ones1 = ctx.sbuf("ones1", [1, 128], BF16)
        i = v.memset(ones1[:, :], 1.0); ctx.vv(i)
        for c in range(NCH):
            i = sy.dma_start(out=XQT[:, c, :], in_=L["xq_d"][:, 128 * c:128 * (c + 1)],
                             transpose=True)
            ctx.inc("ldq", i, 16)
        for sl in range(2):
            for kc in range(NCH):
                i = sy.dma_start(out=WQ[:, kc, 2048 * sl:2048 * (sl + 1)],
                                 in_=AP(P["wfcqp"].tensor, sl * 2048 * 1024 + 128 * kc,
                                        [[1024, 2048], [1, 128]]),
                                 transpose=True)
                ctx.inc("ldq", i, 16)
        i = g.dma_start(out=BROW[0:1, :], in_=D["bfcrow"][0:1, :]); ctx.inc("ldq", i, 16)
        ctx.wait(t, "ldq"); ctx.wait(v, "ldq"); ctx.wait(s, "ldq")
        ctx.wait(t, "v")
        with ExitStack() as pp:
            ps_bb = pp.enter_context(nc.psum_tensor(ctx.uniq("ps_bb"), [128, TT], F32))
            for nt in range(8):
                ctx.wait(t, "v")
                i = t.matmul(ps_bb[:, :], ones1[0:1, :], BROW[0:1, nt * TT:(nt + 1) * TT],
                             start=True, stop=True)
                ctx.tv(i)
                ctx.wait(v, "t")
                i = v.tensor_copy(BFCB[:, nt * TT:(nt + 1) * TT], ps_bb[:, :]); ctx.vv(i)
        ctx.wait(t, "v")
        with ExitStack() as pp:
            psP = [pp.enter_context(nc.psum_tensor(ctx.uniq(f"ps_fc{i_}"), [128, TT], F32))
                   for i_ in range(2)]
            for tc in range(T // 128):
                sct = [None] * 8
                for nt in range(8):
                    db = nt % 2
                    if nt >= 2:
                        ctx.wait(t, "s", sct[nt - 2])
                    for kc in range(NCH):
                        i = t.matmul(psP[db][:, :], XQT[:, kc, tc * 128:(tc + 1) * 128],
                                     WQ[:, kc, nt * TT:(nt + 1) * TT],
                                     start=(kc == 0), stop=(kc == NCH - 1))
                    mmc = ctx.tv(i)
                    ctx.wait(v, "t", mmc)
                    if nt >= 2:
                        ctx.wait(v, "s", sct[nt - 2])
                    i = v.tensor_add(psP[db][:, :], psP[db][:, :],
                                     BFCB[:, nt * TT:(nt + 1) * TT])
                    vvc = ctx.vv(i)
                    ctx.wait(s, "v", vvc)
                    i = s.activation(HTOK[:, nt * TT:(nt + 1) * TT], psP[db][:, :], AF.Gelu)
                    sct[nt] = ctx.sv(i)
                ctx.wait(v, "s")
                emit_quant_tile(ctx, qb, HTOK[:, :], HQ[:, :], 4096)
                ctx.wait(sy, "v")
                i = sy.dma_start(out=L["hq_d"][tc * 128:(tc + 1) * 128, :], in_=HQ[:, :])
                ctx.inc("stq", i, 16)
                ctx.wait(v, "stq", ctx.val("stq") - 16)
                ctx.wait(s, "stq", ctx.val("stq") - 16)
        ctx.wait(sy, "stq"); ctx.wait(g, "v")


def phase_f_proj(ctx, D, L, P, o_final):
    nc = ctx.nc
    g, sy, t, v, s = ctx.g, ctx.sy, ctx.t, ctx.v, ctx.s
    with ctx.scope("fpj"):
        WPJT = ctx.sbuf("WPJT", [128, 32, C], BF16)
        HQT = ctx.sbuf("HQT", [128, 32, TT], BF16)
        YB = ctx.sbuf("YB", [128, NCH, T], BF16)
        BPC = ctx.sbuf("BPC", [128, NCH], F32)
        SQ = ctx.sbuf("SQ", [128, TT], BF16)
        TMP32 = ctx.sbuf("TMP32", [128, TT], F32)
        X0TILE = ctx.sbuf("X0TILE", [128, TT], F32)
        _ssum = ctx.sbuf("ssum", [1, T], F32)
        _ssq = ctx.sbuf("ssq", [1, T], F32)
        _rs = ctx.sbuf("rs", [1, T], F32)
        rows = {"ssum": _ssum, "ssq": _ssq, "svar": _ssq, "rs": _rs}
        ROWSB = [ctx.sbuf(f"ROWSB{i_}", [1, T], BF16) for i_ in range(2)]
        EPS11 = ctx.sbuf("EPS11", [1, 1], F32)
        ones128b = ctx.sbuf("ones128b", [128, 1], BF16)
        ones1 = ctx.sbuf("ones1", [1, 128], BF16)
        for sl in range(2):
            for kc in range(32):
                i = sy.dma_start(out=WPJT[:, kc, 512 * sl:512 * (sl + 1)],
                                 in_=AP(P["wpjqp"].tensor, sl * 512 * 4096 + 128 * kc,
                                        [[4096, 512], [1, 128]]),
                                 transpose=True)
                ctx.inc("ldq", i, 16)
        i = sy.dma_start(out=BPC[:, :], in_=D["bpc"][:, :]); ctx.inc("ldq", i, 16)
        i = v.memset(ones128b[:, :], 1.0); ctx.vv(i)
        i = v.memset(ones1[:, :], 1.0); ctx.vv(i)
        i = v.memset(EPS11[:, :], EPS); ctx.vv(i)
        ctx.wait(t, "ldq"); ctx.wait(v, "ldq"); ctx.wait(s, "v"); ctx.wait(t, "v")
        with ExitStack() as pp:
            ps8 = [pp.enter_context(nc.psum_tensor(ctx.uniq(f"ps{o}"), [128, TT], F32))
                   for o in range(NCH)]
            for tt in range(NTT):
                tsl = slice(tt * TT, (tt + 1) * TT)
                ctx.wait(sy, "t")
                for kc in range(32):
                    i = sy.dma_start(out=HQT[:, kc, :],
                                     in_=L["hq_d"][tsl, 128 * kc:128 * (kc + 1)],
                                     transpose=True)
                    ctx.inc("ldh", i, 16)
                ctx.wait(t, "ldh"); ctx.wait(t, "v")
                for kc in range(32):
                    for oc in range(NCH):
                        i = t.matmul(ps8[oc][:, :], WPJT[:, kc, oc * 128:(oc + 1) * 128],
                                     HQT[:, kc, :], start=(kc == 0), stop=(kc == 31))
                    ctx.tv(i)
                ctx.wait(v, "t")
                for oc in range(NCH):
                    i = v.tensor_scalar_add(YB[:, oc, tsl], ps8[oc][:, :],
                                            BPC[:, oc:oc + 1]); ctx.vv(i)
        ctx.wait(s, "v"); ctx.wait(t, "v")
        with ExitStack() as pp:
            ps_a = pp.enter_context(nc.psum_tensor(ctx.uniq("ps_a"), [1, TT], F32))
            ps_b = pp.enter_context(nc.psum_tensor(ctx.uniq("ps_b"), [1, TT], F32))
            emit_ln_stats(ctx, YB, rows, ROWSB + [None], SQ, ones128b,
                          (ps_a, ps_b), double_ln=False, eps11=EPS11)
        emit_ln_apply_residual(ctx, YB, ROWSB, L["x4_d"], o_final, ones1, TMP32, X0TILE)


def build_fused(debug=False):
    nc = bass.Bass("TRN2", target_bir_lowering=False)
    EI, EO, IN, SH = "ExternalInput", "ExternalOutput", "Internal", "Shared"

    def din(nm, shp, dt=F32):
        return nc.dram_tensor(nm, shp, dt, kind=EI).ap()

    D = {}
    D["xT"] = din("xT", [C, T])
    for nm, shp in [("qKT", [C, 1024]), ("kKT", [C, 1024]), ("vKT", [C, 1024]),
                    ("qV", [1024, C]), ("kV", [1024, C]), ("vV", [1024, C]),
                    ("qsel", [128, NCH * 16]), ("pKT", [C, 1024]), ("pV", [1024, C]),
                    ("fKT", [C, 4096]), ("fV", [4096, C]),
                    ("masks", [128, 4 * TT]), ("lam_a", [128, 256]),
                    ("lam_b", [128, 8 * TT]), ("rotp", [128, 128]), ("bsel", [2, 128]),
                    ("tabsF", [4, 128, T]), ("tabsL1", [2, 128, T]),
                    ("Wqa", [C, 64]), ("WqbP", [64, C]), ("Wkva", [C, 32]),
                    ("WkbP", [32, 256]), ("Wvb", [32, 256]), ("kvsel", [4, 16]),
                    ("boutc", [128, NCH]), ("Wout", [C, C]), ("wfc_half", [2048, 1024]),
                    ("wpj_half", [512, 4096]), ("bfcc", [128, 32]),
                    ("bfcrow", [1, 4096]), ("bpc", [128, NCH]), ("identb", [128, 128])]:
        D[nm] = din(nm, shp)
    for lay in range(2):
        for nm, shp in [("wqkv", [C, 1536]), ("qscol", [128, 1]), ("kscol", [128, 1]),
                        ("wo", [512, C]), ("gcol", [128, NCH]), ("bcol", [128, NCH]),
                        ("w1", [C, 5460]), ("b1a", [128, 22]), ("b1g", [128, 22]),
                        ("w2", [2730, C]), ("b2c", [128, NCH])]:
            D[f"{nm}{lay}"] = din(f"{nm}{lay}", shp)
    D["ind2t"] = din("ind2t", [1, KEYS], BF16)
    D["indloc"] = din("indloc", [1, BKEYS], BF16)
    D["meta"] = din("meta", [1, 4], I32)
    o_final = nc.dram_tensor("o_final", [C, T], F32, kind=EO).ap()

    def loc(nm, shp, dt):
        return nc.dram_tensor(nm, shp, dt, kind=IN).ap()

    L = {}
    _dbgkind = "ExternalOutput" if debug else IN

    def loc_dbg(nm, shp, dt_):
        return nc.dram_tensor(nm, shp, dt_, kind=_dbgkind).ap()
    for nm, shp, dt_ in [("xin_d", [C, T], BF16), ("h_scr", [C, T], BF16),
                         ("q0_d", [C, T], BF16), ("k0_d", [C, T], BF16),
                         ("v0_d", [T, C], BF16), ("m80_d", [16, T], F32),
                         ("o0_d", [C, T], BF16), ("x2_d", [C, T], F32),
                         ("z0_d", [C, T], BF16), ("haloz_d", [C, 1], BF16),
                         ("qf_d", [512, T], BF16), ("kf_d", [512, T], BF16),
                         ("vf_d", [T, 512], BF16), ("xm_d", [C, T], BF16),
                         ("halom_d", [C, 1], BF16), ("x1_d", [C, T], BF16),
                         ("halo1_d", [C, 1], BF16), ("halom2_d", [C, 1], BF16),
                         ("x3_d", [C, T], F32), ("hln_scr", [C, T], BF16),
                         ("qL_d", [C, T], BF16), ("kL_d", [256, T], BF16),
                         ("vL_d", [T, 256], BF16), ("qn_d", [16, T], F32),
                         ("oL_d", [C, T], BF16), ("x4_d", [C, T], F32),
                         ("kmb", [4, 1], F32), ("mcb", [128, 16], F32),
                         ("wfcq_d", [2048, 1024], BF16), ("wpjq_d", [512, 4096], BF16),
                         ("xq_d", [T, C], BF16), ("hq_d", [T, 4096], BF16),
                         ("ks0_d", [2 * C, T], BF16), ("vs0_d", [2 * T, C], BF16),
                         ("ksF_d", [2 * 512, T], BF16), ("vsF_d", [2 * T, 512], BF16),
                         ("ksL_d", [2 * 256, T], BF16), ("vsL_d", [2 * T, 256], BF16)]:
        if nm in ("x2_d", "x3_d", "x4_d", "xm_d", "x1_d", "z0_d", "o0_d",
                  "q0_d", "k0_d", "xq_d", "hq_d"):
            L[nm] = loc_dbg(nm, shp, dt_)
        else:
            L[nm] = loc(nm, shp, dt_)
    P = {}
    for nm, shp, dt_ in [("kpL0", [2 * C, T], BF16), ("vpL0", [2 * T, C], BF16),
                         ("hzp", [2 * C, 1], BF16), ("kpF", [2 * 512, T], BF16),
                         ("vpF", [2 * T, 512], BF16), ("hmp", [2 * C, 1], BF16),
                         ("h1p", [2 * C, 1], BF16), ("kpF2", [2 * 512, T], BF16),
                         ("vpF2", [2 * T, 512], BF16), ("hm2p", [2 * C, 1], BF16),
                         ("kpL1", [2 * 256, T], BF16), ("vpL1", [2 * T, 256], BF16),
                         ("wfcqp", [2 * 2048, 1024], BF16), ("wpjqp", [2 * 512, 4096], BF16)]:
        P[nm] = nc.dram_tensor(nm, shp, dt_, kind=IN, addr_space=SH).ap()
    FL = {}
    for tag in ["E1", "E2a", "E2b", "E3a", "E3b", "E3c", "E4a", "E4b", "E5", "E6"]:
        FL[tag] = nc.dram_tensor("flag" + tag, [2, 16], I32, kind=IN, addr_space=SH).ap()

    with ExitStack() as st:
        ctx = Ctx(nc, st)
        g, sy, t, v, s = ctx.g, ctx.sy, ctx.t, ctx.v, ctx.s
        SMETA = ctx.sbuf("SMETA", [1, 4], I32)
        SFLAG = ctx.sbuf("SFLAG", [1, 4], I32)
        ZC = ctx.sbuf("ZC", [128, NCH, 1], BF16)
        ctx._sflag = SFLAG
        i = g.dma_start(out=SMETA[:, :], in_=D["meta"][:, :]); ctx.inc("ld", i, 16)
        ctx.wait(g, "ld"); ctx.wait(sy, "ld"); ctx.wait(v, "ld")
        ctx._smeta_sy = SMETA
        rpar = nc.alloc_registers("rpar", engines=[sy.engine])
        rno = nc.alloc_registers("rno", engines=[sy.engine])
        rfv = nc.alloc_registers("rfv", engines=[sy.engine])
        rctr = nc.alloc_registers("rctr", engines=[sy.engine])
        nc.regs_load(rpar, SMETA[0:1, 0:1])
        nc.regs_load(rno, SMETA[0:1, 1:2])
        sregs = (rpar, rno, rfv, rctr)
        rp_g = nc.alloc_registers("rp_g", engines=[g.engine])
        rpn_g = nc.alloc_registers("rpn_g", engines=[g.engine])
        nc.regs_load(rp_g, SMETA[0:1, 0:1])
        nc.regs_load(rpn_g, SMETA[0:1, 2:3])
        pregs = (rp_g, rpn_g)
        ctx._sregs = (rpar, rno, rfv, rctr)
        i = v.memset(ZC[:, :, :], 0.0); ctx.vv(i)
        ctx.wait(sy, "v")
        for nm in ("haloz_d", "halom_d", "halo1_d", "halom2_d"):
            i = sy.dma_start(out=L[nm].rearrange("(r p) w -> p r w", p=128),
                             in_=ZC[:, :, :], allow_slow_non_contiguous=True)
            ctx.inc("ld", i, 16)
        ctx.wait(sy, "ld"); ctx.wait(g, "ld")

        def barrier_all():
            for e_ in (g, sy, t, v, s):
                ctx.wait(e_, "xdone", ctx.val("xdone"))

        d1 = dict(D)
        d1.update({"o_xin": L["xin_d"], "o_h": L["h_scr"], "o_q": L["q0_d"],
                   "o_k": L["k0_d"], "o_v": L["v0_d"], "o_m8": L["m80_d"]})
        phase_u1(ctx, d1)
        ctx.hard_barrier()
        _xw(ctx, "E1", [(P["kpL0"], L["k0_d"], C, T, BF16),
                        (P["vpL0"], L["v0_d"], T, C, BF16)], pregs, SMETA, FL["E1"])
        _xpoll(ctx, "E1", FL["E1"], sregs, SFLAG, SMETA)
        barrier_all()
        _assemble_pair(ctx, "E1k", P["kpL0"], C, T, L["ks0_d"], pregs)
        _assemble_pair(ctx, "E1v", P["vpL0"], T, C, L["vs0_d"], pregs)
        sk = _mk_stage_k_stacked(ctx, L["ks0_d"], C)
        sv2 = _mk_stage_v_stacked(ctx, L["vs0_d"], C,
                                  lambda vs_: (vs_, 64 * (vs_ % 2), 64 * (1 - vs_ % 2)))
        d2 = dict(D)
        d2.update({"qin": L["q0_d"], "m8in": L["m80_d"], "ind": D["ind2t"],
                   "o_o": L["o0_d"], "stage_k": sk, "stage_v": sv2,
                   "ksrc": None, "vsrc": None})
        phase_u2(ctx, d2)
        ctx.hard_barrier()
        d3 = dict(D)
        d3.update({"oin": L["o0_d"], "xinin": L["xin_d"], "x0in": D["xT"],
                   "o_x2": L["x2_d"]})
        phase_u3(ctx, d3)
        ctx.hard_barrier()
        phase_uln(ctx, {"xin": L["x2_d"], "o_z": L["z0_d"]})
        ctx.hard_barrier()
        _xw(ctx, "E2a", [(P["hzp"], L["z0_d"][:, T - 1:T], C, 1, BF16)],
            pregs, SMETA, FL["E2a"])
        _xpoll(ctx, "E2a", FL["E2a"], sregs, SFLAG, SMETA)
        _halo_read(ctx, "E2a", P["hzp"], L["haloz_d"], sregs)
        barrier_all()
        xl_d = L["z0_d"]
        halo_d = L["haloz_d"]
        for lay in range(2):
            sfx = str(lay)
            kp = P["kpF"] if lay == 0 else P["kpF2"]
            vp = P["vpF"] if lay == 0 else P["vpF2"]
            tags = (("E2b", "E3a") if lay == 0 else ("E3c", "E4a"))
            d4 = dict(D)
            d4.update({"xl": xl_d, "halo": halo_d, "wqkv": D["wqkv" + sfx],
                       "qscol": D["qscol" + sfx], "kscol": D["kscol" + sfx],
                       "tabs": D["tabsF"], "bselin": D["bsel"],
                       "o_q": L["qf_d"], "o_k": L["kf_d"], "o_v": L["vf_d"]})
            phase_u4(ctx, d4)
            ctx.hard_barrier()
            _xw(ctx, tags[0], [(kp, L["kf_d"], 512, T, BF16),
                               (vp, L["vf_d"], T, 512, BF16)], pregs, SMETA, FL[tags[0]])
            _xpoll(ctx, tags[0], FL[tags[0]], sregs, SFLAG, SMETA)
            barrier_all()

            _assemble_pair(ctx, tags[0] + "k", kp, 512, T, L["ksF_d"], pregs)
            _assemble_pair(ctx, tags[0] + "v", vp, T, 512, L["vsF_d"], pregs)

            def stage_k5(h, khead):
                i = sy.dma_start(out=khead[0:64, 0:512],
                                 in_=L["ksF_d"][64 * h:64 * h + 64, T - 512:T])
                ctx.inc("ldf", i, 16)
                i = sy.dma_start(out=khead[0:64, 512:512 + T],
                                 in_=L["ksF_d"][512 + 64 * h:512 + 64 * h + 64, :])
                ctx.inc("ldf", i, 16)

            def stage_v5(h, vhead):
                c0 = 64 * (h % 2)
                c1 = 64 * (1 - (h % 2))
                ctx.wait(v, "t_pv", ctx.val("t_pv"))
                ctx.wait(v, "ldf", ctx.val("ldf"))
                i = v.memset(vhead[:, :, c1:c1 + 1], 1.0); ctx.vv(i)
                ctx.wait(g, "v")
                i = g.dma_start(out=vhead[:, 0:4, c0:c0 + 64],
                                in_=L["vsF_d"][T - 512:T, 64 * h:64 * h + 64]
                                .rearrange("(a p) w -> p a w", p=128))
                ctx.inc("ldf", i, 16)
                for j in range(4):
                    i = g.dma_start(out=vhead[:, 4 + 4 * j:8 + 4 * j, c0:c0 + 64],
                                    in_=L["vsF_d"][T + 512 * j:T + 512 * (j + 1),
                                                   64 * h:64 * h + 64]
                                    .rearrange("(a p) w -> p a w", p=128))
                    ctx.inc("ldf", i, 16)
                ctx.wait(g, "v")

            d5 = dict(D)
            d5.update({"qin": L["qf_d"], "ind": D["indloc"], "xlin": xl_d,
                       "woin": D["wo" + sfx], "o_xm": L["xm_d"], "mcb": L["mcb"],
                       "stage_k": stage_k5, "stage_v": stage_v5,
                       "ksrc": None, "vsrc": None})
            phase_u5(ctx, d5)
            ctx.hard_barrier()
            hm = P["hmp"] if lay == 0 else P["hm2p"]
            hm_local = L["halom_d"] if lay == 0 else L["halom2_d"]
            _xw(ctx, tags[1], [(hm, L["xm_d"][:, T - 1:T], C, 1, BF16)],
                pregs, SMETA, FL[tags[1]])
            _xpoll(ctx, tags[1], FL[tags[1]], sregs, SFLAG, SMETA)
            _halo_read(ctx, tags[1], hm, hm_local, sregs)
            barrier_all()
            final = (lay == 1)
            d6 = dict(D)
            d6.update({"xmin": L["xm_d"], "halo": hm_local,
                       "gcol": D["gcol" + sfx], "bcol": D["bcol" + sfx],
                       "w1in": D["w1" + sfx], "b1a": D["b1a" + sfx],
                       "b1g": D["b1g" + sfx], "w2in": D["w2" + sfx],
                       "b2c": D["b2c" + sfx], "o_hln": L["hln_scr"],
                       "o_out": (L["x3_d"] if final else L["x1_d"])})
            if final:
                d6["x2in"] = L["x2_d"]
            phase_u6(ctx, d6, final)
            ctx.hard_barrier()
            if not final:
                _xw(ctx, "E3b", [(P["h1p"], L["x1_d"][:, T - 1:T], C, 1, BF16)],
                    pregs, SMETA, FL["E3b"])
                _xpoll(ctx, "E3b", FL["E3b"], sregs, SFLAG, SMETA)
                _halo_read(ctx, "E3b", P["h1p"], L["halo1_d"], sregs)
                barrier_all()
                xl_d = L["x1_d"]
                halo_d = L["halo1_d"]
        d10 = dict(D)
        d10.update({"x3in": L["x3_d"], "wqa": D["Wqa"], "wqbp": D["WqbP"],
                    "wkva": D["Wkva"], "wkbp": D["WkbP"], "wvb": D["Wvb"],
                    "tabs": D["tabsL1"], "qselin": D["qsel"],
                    "o_q": L["qL_d"], "o_k": L["kL_d"], "o_v": L["vL_d"],
                    "o_qn": L["qn_d"]})
        phase_u10(ctx, d10)
        ctx.hard_barrier()
        _xw(ctx, "E5", [(P["kpL1"], L["kL_d"], 256, T, BF16),
                        (P["vpL1"], L["vL_d"], T, 256, BF16)], pregs, SMETA, FL["E5"])
        _xpoll(ctx, "E5", FL["E5"], sregs, SFLAG, SMETA)
        barrier_all()
        _assemble_pair(ctx, "E5k", P["kpL1"], 256, T, L["ksL_d"], pregs)
        _assemble_pair(ctx, "E5v", P["vpL1"], T, 256, L["vsL_d"], pregs)
        skL = _mk_stage_k_stacked(ctx, L["ksL_d"], 256)
        svL = _mk_stage_v_stacked(ctx, L["vsL_d"], 256,
                                  lambda vs_: (vs_ // 2, 64 * (vs_ % 2), 64 * (1 - vs_ % 2)))
        d11 = dict(D)
        d11.update({"qin": L["qL_d"], "qnin": L["qn_d"], "ind": D["ind2t"],
                    "o_o": L["oL_d"], "kmb": L["kmb"],
                    "stage_k": skL, "stage_v": svL, "ksrc": None, "vsrc": None})
        phase_u11(ctx, d11)
        ctx.hard_barrier()
        d12 = dict(D)
        d12.update({"oin": L["oL_d"], "wout": D["Wout"] if "Wout" in D else None,
                    "x3in": L["x3_d"], "o_x4": L["x4_d"]})
        d12["wout"] = D.get("Wout")
        phase_u12(ctx, d12)
        ctx.hard_barrier()
        phase_f_wq(ctx, D, L)
        ctx.hard_barrier()
        _xw(ctx, "E6", [(P["wfcqp"], L["wfcq_d"], 2048, 1024, BF16),
                        (P["wpjqp"], L["wpjq_d"], 512, 4096, BF16)],
            pregs, SMETA, FL["E6"])
        _xpoll(ctx, "E6", FL["E6"], sregs, SFLAG, SMETA)
        barrier_all()
        phase_f_zq(ctx, D, L)
        ctx.hard_barrier()
        phase_f_fc(ctx, D, L, P)
        ctx.hard_barrier()
        phase_f_proj(ctx, D, L, P, o_final)
    return nc


_PROGS = {}


def _prog(name, builder, *a):
    key = (name,) + a
    if key not in _PROGS:
        _PROGS[key] = builder(*a)
    return _PROGS[key]


def _perm_eo(n):
    idx = []
    for h in range(n):
        base = 64 * h
        idx += [base + d for d in range(0, 64, 2)]
        idx += [base + d for d in range(1, 64, 2)]
    return np.array(idx)


def _feat_gather(res, name, width=C):
    out = np.zeros((4, TFULL, width), np.float32)
    for c in range(8):
        b, hf = c // 2, c % 2
        out[b, hf * T:(hf + 1) * T, :] = np.asarray(res.results[c][name]).astype(np.float32).T
    return out



def kernel_fused(**inputs):
    z = inputs
    cores = list(range(8))
    qsel = _qsel_host()
    xs = _shard_xT(np.asarray(z["L0_attn_x"], np.float32))
    masks = _sdpa_masks_host()
    lama, lamb = _la_masks_host()
    rotp = _rotperm_host()
    bsel = np.vstack([np.concatenate([np.ones(64), np.zeros(64)]),
                      np.concatenate([np.zeros(64), np.ones(64)])]).astype(np.float32)
    pq = _perm_eo(16)
    pk = _perm_eo(4)
    wqbp = np.ascontiguousarray(z["L1_attn_Wqb"][:, pq])
    kvw = z["L1_attn_Wkvb"].reshape(32, 4, 64, 2)
    wkb = np.ascontiguousarray(kvw[:, :, :, 0].reshape(32, 256)[:, pk])
    wvb = np.ascontiguousarray(kvw[:, :, :, 1].reshape(32, 256))
    kvsel = np.zeros((4, 16), np.float32)
    for m in range(16):
        kvsel[m // 4, m] = 1.0
    nonce = int(np.random.randint(1, 2 ** 30))
    shared = {
        "qKT": np.ascontiguousarray(z["L0_attn_qK"].T),
        "kKT": np.ascontiguousarray(z["L0_attn_kK"].T),
        "vKT": np.ascontiguousarray(z["L0_attn_vK"].T),
        "qV": z["L0_attn_qV"], "kV": z["L0_attn_kV"], "vV": z["L0_attn_vV"],
        "qsel": qsel,
        "pKT": np.ascontiguousarray(z["L0_attn_pK"].T), "pV": z["L0_attn_pV"],
        "fKT": np.ascontiguousarray(z["L0_attn_fK"].T), "fV": z["L0_attn_fV"],
        "masks": masks, "lam_a": lama, "lam_b": lamb, "rotp": rotp, "bsel": bsel,
        "Wqa": z["L1_attn_Wqa"], "WqbP": wqbp, "Wkva": z["L1_attn_Wkva"],
        "WkbP": wkb, "Wvb": wvb, "kvsel": kvsel,
        "boutc": z["L1_attn_bout"].reshape(8, 128).T.astype(np.float32).copy(),
        "Wout": z["L1_attn_Wout"],
        "bfcc": z["L1_ffn_b_fc"].reshape(32, 128).T.astype(np.float32).copy(),
        "bfcrow": z["L1_ffn_b_fc"].reshape(1, 4096),
        "bpc": z["L1_ffn_b_proj"].reshape(8, 128).T.astype(np.float32).copy(),
        "identb": np.eye(128, dtype=np.float32),
    }
    for lay in range(2):
        b1 = z["L0_ffn_b1"][lay]
        B1A = np.zeros((128, 22), np.float32)
        B1G = np.zeros((128, 22), np.float32)
        for i_, (off, M) in enumerate(W1CH):
            B1A[0:M, i_] = b1[off:off + M]
            B1G[0:M, i_] = b1[2730 + off:2730 + off + M]
        shared.update({
            f"wqkv{lay}": z["L0_ffn_wqkv"][lay],
            f"qscol{lay}": np.tile(z["L0_ffn_q_scale"][lay], 2).reshape(128, 1).astype(np.float32),
            f"kscol{lay}": np.tile(z["L0_ffn_k_scale"][lay], 2).reshape(128, 1).astype(np.float32),
            f"wo{lay}": z["L0_ffn_wo"][lay],
            f"gcol{lay}": z["L0_ffn_ln_g"][lay].reshape(8, 128).T.astype(np.float32).copy(),
            f"bcol{lay}": z["L0_ffn_ln_b"][lay].reshape(8, 128).T.astype(np.float32).copy(),
            f"w1{lay}": z["L0_ffn_w1"][lay], f"b1a{lay}": B1A, f"b1g{lay}": B1G,
            f"w2{lay}": z["L0_ffn_w2"][lay],
            f"b2c{lay}": z["L0_ffn_b2"][lay].reshape(8, 128).T.astype(np.float32).copy(),
        })
    ins = []
    for c in cores:
        hf = c % 2
        ind2t = np.zeros((1, KEYS), np.float32)
        indloc = np.zeros((1, BKEYS), np.float32)
        if hf == 0:
            ind2t[0, 0:T] = 1.0
            indloc[0, 0:HALO] = 1.0
        d = dict(shared)
        d.update({
            "xT": xs[c],
            "tabsF": _l0ffn_tables(hf * T),
            "tabsL1": _l1_tables(hf * T),
            "ind2t": ind2t.astype(BF), "indloc": indloc.astype(BF),
            "wfc_half": np.ascontiguousarray(z["L1_ffn_w_fc"][hf * 2048:(hf + 1) * 2048, :]),
            "wpj_half": np.ascontiguousarray(z["L1_ffn_w_proj"][hf * 512:(hf + 1) * 512, :]),
            "meta": np.array([[hf, nonce, 1 - hf, 0]], np.int32),
        })
        ins.append(d)
    res = run_bass_kernel_spmd(_prog("fused", build_fused), ins, core_ids=cores)
    out = np.zeros((4, TFULL, C), np.float32)
    for c in cores:
        b, hf = c // 2, c % 2
        out[b, hf * T:(hf + 1) * T, :] = np.asarray(res.results[c]["o_final"]).T
    return out


def kernel_unfused(**inputs):
    z = inputs
    cores = list(range(8))
    run = run_bass_kernel_spmd

    qsel = _qsel_host()
    xs = _shard_xT(np.asarray(z["L0_attn_x"], np.float32))
    qKT = np.ascontiguousarray(z["L0_attn_qK"].T)
    kKT = np.ascontiguousarray(z["L0_attn_kK"].T)
    vKT = np.ascontiguousarray(z["L0_attn_vK"].T)
    ins = [{"xT": xs[c], "qKT": qKT, "kKT": kKT, "vKT": vKT,
            "qV": z["L0_attn_qV"], "kV": z["L0_attn_kV"], "vV": z["L0_attn_vV"],
            "qsel": qsel} for c in cores]
    r1 = run(_prog("u1", build_u1), ins, core_ids=cores)

    masks = _sdpa_masks_host()
    ins = []
    for c in cores:
        b, hf = c // 2, c % 2
        kown = np.asarray(r1.results[c]["o_k"])
        kpart = np.asarray(r1.results[2 * b]["o_k"])
        vown = np.asarray(r1.results[c]["o_v"])
        vpart = np.asarray(r1.results[2 * b]["o_v"])
        ks = np.zeros((1024, 2 * T), BF)
        vs = np.zeros((16, 2 * T, 128), BF)
        indv = np.zeros((1, 2 * T), np.float32)
        if hf == 1:
            ks[:, 0:T] = kpart
        else:
            indv[0, 0:T] = 1.0
        ks[:, T:] = kown
        vo = vown.reshape(T, 16, 64).transpose(1, 0, 2)
        vp = vpart.reshape(T, 16, 64).transpose(1, 0, 2)
        one = np.ones((), BF)
        for h in range(16):
            c0, c1 = 64 * (h % 2), 64 * (1 - (h % 2))
            vs[h, T:, c0:c0 + 64] = vo[h]
            vs[h, T:, c1] = one
            if hf == 1:
                vs[h, 0:T, c0:c0 + 64] = vp[h]
                vs[h, 0:T, c1] = one
        ins.append({"q": np.asarray(r1.results[c]["o_q"]),
                    "m8": np.asarray(r1.results[c]["o_m8"]),
                    "ksrc": ks, "vsrc": vs, "ind": indv.astype(BF), "masks": masks})
    r2 = run(_prog("u2", build_u2), ins, core_ids=cores)

    pKT = np.ascontiguousarray(z["L0_attn_pK"].T)
    fKT = np.ascontiguousarray(z["L0_attn_fK"].T)
    ins = []
    for c in cores:
        ins.append({"o": np.asarray(r2.results[c]["o_o"]),
                    "xin": np.asarray(r1.results[c]["o_xin"]),
                    "x0T": xs[c], "pKT": pKT, "pV": z["L0_attn_pV"],
                    "fKT": fKT, "fV": z["L0_attn_fV"]})
    r3 = run(_prog("u3", build_u3), ins, core_ids=cores)
    x2 = [np.asarray(r3.results[c]["o_x2"]) for c in cores]

    rln = run(_prog("uln", build_uln), [{"xT": x2[c]} for c in cores], core_ids=cores)
    xl = [np.asarray(rln.results[c]["o_z"]) for c in cores]

    lama, lamb = _la_masks_host()
    rotp = _rotperm_host()
    bsel = np.vstack([np.concatenate([np.ones(64), np.zeros(64)]),
                      np.concatenate([np.zeros(64), np.ones(64)])]).astype(np.float32)
    for lay in range(2):
        qs = z["L0_ffn_q_scale"][lay]
        ksc = z["L0_ffn_k_scale"][lay]
        ins = []
        for c in cores:
            b, hf = c // 2, c % 2
            haloc = np.zeros((1024, 1), BF) if hf == 0 else                 np.ascontiguousarray(np.asarray(xl[2 * b])[:, T - 1:T])
            ins.append({
                "xl": xl[c], "halo": haloc,
                "wqkv": z["L0_ffn_wqkv"][lay],
                "qscol": np.tile(qs, 2).reshape(128, 1).astype(np.float32),
                "kscol": np.tile(ksc, 2).reshape(128, 1).astype(np.float32),
                "rotp": rotp, "bsel": bsel, "tabs": _l0ffn_tables((c % 2) * T),
            })
        r4 = run(_prog("u4", build_u4), ins, core_ids=cores)
        ins = []
        for c in cores:
            b, hf = c // 2, c % 2
            kown = np.asarray(r4.results[c]["o_k"])
            kpart = np.asarray(r4.results[2 * b]["o_k"])
            vown = np.asarray(r4.results[c]["o_v"]).astype(np.float32)
            vpart = np.asarray(r4.results[2 * b]["o_v"]).astype(np.float32)
            ks = np.zeros((512, BKEYS), BF)
            vs = np.zeros((8, BKEYS, 128), np.float32)
            indv = np.zeros((1, BKEYS), np.float32)
            if hf == 1:
                ks[:, 0:HALO] = kpart[:, T - HALO:T]
            else:
                indv[0, 0:HALO] = 1.0
            ks[:, HALO:] = kown
            vto = vown.reshape(T, 8, 64).transpose(1, 0, 2)
            vtp = vpart.reshape(T, 8, 64).transpose(1, 0, 2)
            for h in range(8):
                c0, c1 = 64 * (h % 2), 64 * (1 - (h % 2))
                vs[h, HALO:, c0:c0 + 64] = vto[h]
                vs[h, HALO:, c1] = 1.0
                if hf == 1:
                    vs[h, 0:HALO, c0:c0 + 64] = vtp[h, T - HALO:T]
                    vs[h, 0:HALO, c1] = 1.0
            ins.append({"q": np.asarray(r4.results[c]["o_q"]), "ksrc": ks,
                        "vsrc": vs.astype(BF), "ind": indv.astype(BF),
                        "lam_a": lama, "lam_b": lamb, "xl": xl[c],
                        "wo": z["L0_ffn_wo"][lay]})
        r5 = run(_prog("u5", build_u5), ins, core_ids=cores)
        xm = [np.asarray(r5.results[c]["o_xm"]) for c in cores]
        b1 = z["L0_ffn_b1"][lay]
        B1A = np.zeros((128, 22), np.float32)
        B1G = np.zeros((128, 22), np.float32)
        for i_, (off, M) in enumerate(W1CH):
            B1A[0:M, i_] = b1[off:off + M]
            B1G[0:M, i_] = b1[2730 + off:2730 + off + M]
        final = (lay == 1)
        ins = []
        for c in cores:
            b, hf = c // 2, c % 2
            haloc = np.zeros((1024, 1), BF) if hf == 0 else                 np.ascontiguousarray(np.asarray(xm[2 * b])[:, T - 1:T])
            d = {"xm": xm[c], "halo": haloc,
                 "gcol": z["L0_ffn_ln_g"][lay].reshape(8, 128).T.astype(np.float32).copy(),
                 "bcol": z["L0_ffn_ln_b"][lay].reshape(8, 128).T.astype(np.float32).copy(),
                 "w1": z["L0_ffn_w1"][lay], "b1a": B1A, "b1g": B1G,
                 "w2": z["L0_ffn_w2"][lay],
                 "b2c": z["L0_ffn_b2"][lay].reshape(8, 128).T.astype(np.float32).copy()}
            if final:
                d["x2T"] = x2[c]
            ins.append(d)
        r6 = run(_prog("u6", build_u6, final), ins, core_ids=cores)
        if not final:
            xl = [np.asarray(r6.results[c]["o_x1"]) for c in cores]
    x3 = [np.asarray(r6.results[c]["o_x3"]) for c in cores]

    pq = _perm_eo(16)
    pk = _perm_eo(4)
    wqbp = np.ascontiguousarray(z["L1_attn_Wqb"][:, pq])
    kvw = z["L1_attn_Wkvb"].reshape(32, 4, 64, 2)
    wkb = np.ascontiguousarray(kvw[:, :, :, 0].reshape(32, 256)[:, pk])
    wvb = np.ascontiguousarray(kvw[:, :, :, 1].reshape(32, 256))
    ins = []
    for c in cores:
        ins.append({"x3": x3[c].astype(BF), "Wqa": z["L1_attn_Wqa"], "WqbP": wqbp,
                    "Wkva": z["L1_attn_Wkva"], "WkbP": wkb, "Wvb": wvb,
                    "rotp": rotp, "tabs": _l1_tables((c % 2) * T), "qsel": qsel})
    r10 = run(_prog("u10", build_u10), ins, core_ids=cores)
    kvsel = np.zeros((4, 16), np.float32)
    for m in range(16):
        kvsel[m // 4, m] = 1.0
    ins = []
    for c in cores:
        b, hf = c // 2, c % 2
        kown = np.asarray(r10.results[c]["o_k"])
        kpart = np.asarray(r10.results[2 * b]["o_k"])
        vown = np.asarray(r10.results[c]["o_v"]).astype(np.float32)
        vpart = np.asarray(r10.results[2 * b]["o_v"]).astype(np.float32)
        ks = np.zeros((256, 2 * T), BF)
        vs = np.zeros((8, 2 * T, 128), np.float32)
        indv = np.zeros((1, 2 * T), np.float32)
        if hf == 1:
            ks[:, 0:T] = kpart
        else:
            indv[0, 0:T] = 1.0
        ks[:, T:] = kown
        vo = vown.reshape(T, 4, 64).transpose(1, 0, 2)
        vp = vpart.reshape(T, 4, 64).transpose(1, 0, 2)
        for g_ in range(4):
            for par in range(2):
                sl = 2 * g_ + par
                c0, c1 = 64 * par, 64 * (1 - par)
                vs[sl, T:, c0:c0 + 64] = vo[g_]
                vs[sl, T:, c1] = 1.0
                if hf == 1:
                    vs[sl, 0:T, c0:c0 + 64] = vp[g_]
                    vs[sl, 0:T, c1] = 1.0
        ins.append({"q": np.asarray(r10.results[c]["o_q"]),
                    "qn": np.asarray(r10.results[c]["o_qn"]),
                    "ksrc": ks, "vsrc": vs.astype(BF), "ind": indv.astype(BF),
                    "masks": masks, "kvsel": kvsel})
    r11 = run(_prog("u11", build_u11), ins, core_ids=cores)
    boutc = z["L1_attn_bout"].reshape(8, 128).T.astype(np.float32).copy()
    ins = [{"o": np.asarray(r11.results[c]["o_o"]), "Wout": z["L1_attn_Wout"],
            "boutc": boutc, "x3T": x3[c]} for c in cores]
    r12 = run(_prog("u12", build_u12), ins, core_ids=cores)
    x4 = [np.asarray(r12.results[c]["o_x4"]) for c in cores]

    ins = [{"wfc": np.ascontiguousarray(z["L1_ffn_w_fc"][512 * c:512 * (c + 1), :]),
            "wpj": np.ascontiguousarray(z["L1_ffn_w_proj"][128 * c:128 * (c + 1), :]),
            "x4T": x4[c]} for c in cores]
    r13 = run(_prog("u13", build_u13), ins, core_ids=cores)
    wfcq = np.concatenate([np.asarray(r13.results[c]["o_wfcq"]) for c in cores])
    wpjq = np.concatenate([np.asarray(r13.results[c]["o_wpjq"]) for c in cores])
    zl = [np.asarray(r13.results[c]["o_z"]) for c in cores]
    ins = [{"zin": np.ascontiguousarray(zl[c].T)} for c in cores]
    r14 = run(_prog("u14", build_u14, 1024), ins, core_ids=cores)
    wfcqT = np.ascontiguousarray(wfcq.T)
    bfcc = z["L1_ffn_b_fc"].reshape(32, 128).T.astype(np.float32).copy()
    ins = [{"xqT": np.ascontiguousarray(np.asarray(r14.results[c]["o_q"]).T),
            "wfcqT": wfcqT, "bfcc": bfcc} for c in cores]
    r15 = run(_prog("u15", build_u15), ins, core_ids=cores)
    ins = [{"zin": np.ascontiguousarray(np.asarray(r15.results[c]["o_h"]).T)}
           for c in cores]
    r16 = run(_prog("u16", build_u14, 4096), ins, core_ids=cores)
    wpjqT = np.ascontiguousarray(wpjq.T)
    bpc = z["L1_ffn_b_proj"].reshape(8, 128).T.astype(np.float32).copy()
    ins = [{"hqT": np.ascontiguousarray(np.asarray(r16.results[c]["o_q"]).T),
            "wpjqT": wpjqT, "bpc": bpc, "x4T": x4[c]} for c in cores]
    r17 = run(_prog("u17", build_u17), ins, core_ids=cores)

    out = np.zeros((4, TFULL, C), np.float32)
    for c in cores:
        b, hf = c // 2, c % 2
        out[b, hf * T:(hf + 1) * T, :] = np.asarray(r17.results[c]["o_x5"]).T
    return out


def phase_uln(ctx, d):
    nc = ctx.nc
    o_z = d["o_z"]
    xin = d["xin"]
    with ctx.scope("uln"):
        g, sy, t, v, s = ctx.g, ctx.sy, ctx.t, ctx.v, ctx.s
        XB = ctx.sbuf("XB", [128, NCH, T], BF16)
        ZB = ctx.sbuf("ZB", [128, NCH, T], BF16)
        SQ = ctx.sbuf("SQ", [128, TT], BF16)
        TMP32 = ctx.sbuf("TMP32", [128, TT], F32)
        _ssum = ctx.sbuf("ssum", [1, T], F32)
        _ssq = ctx.sbuf("ssq", [1, T], F32)
        _rs = ctx.sbuf("rs", [1, T], F32)
        rows = {"ssum": _ssum, "ssq": _ssq, "svar": _ssq, "rs": _rs}
        ROWSB = [ctx.sbuf(f"ROWSB{i_}", [1, T], BF16) for i_ in range(2)]
        EPS11 = ctx.sbuf("EPS11", [1, 1], F32)
        ones128b = ctx.sbuf("ones128b", [128, 1], BF16)
        ones1 = ctx.sbuf("ones1", [1, 128], BF16)
        i = g.dma_start(out=XB[:, :, :], in_=xin.rearrange("(r p) w -> p r w", p=128))
        ctx.inc("ld", i, 16)
        i = v.memset(ones128b[:, :], 1.0); ctx.vv(i)
        i = v.memset(ones1[:, :], 1.0); ctx.vv(i)
        i = v.memset(EPS11[:, :], EPS); ctx.vv(i)
        ctx.wait(t, "ld"); ctx.wait(s, "ld"); ctx.wait(v, "ld")
        ctx.wait(t, "v"); ctx.wait(s, "v")
        with ExitStack() as pp:
            ps_a = pp.enter_context(ctx.nc.psum_tensor(ctx.uniq("ps_a"), [1, TT], F32))
            ps_b = pp.enter_context(ctx.nc.psum_tensor(ctx.uniq("ps_b"), [1, TT], F32))
            ps_mu = pp.enter_context(ctx.nc.psum_tensor(ctx.uniq("ps_mu"), [128, TT], F32))
            ps_r = pp.enter_context(ctx.nc.psum_tensor(ctx.uniq("ps_r"), [128, TT], F32))
            emit_ln_stats(ctx, XB, rows, ROWSB + [None], SQ, ones128b,
                          (ps_a, ps_b), double_ln=False, eps11=EPS11)
            emit_ln_apply(ctx, XB, ROWSB, 1, ZB, ones1, (ps_mu, ps_r, None), TMP32)
        ctx.wait(sy, "v")
        i = sy.dma_start(out=o_z.rearrange("(r p) w -> p r w", p=128), in_=ZB[:, :, :])
        ctx.inc("stx", i, 16)
        ctx.wait(sy, "stx")

def build_uln():
    nc = bass.Bass("TRN2", target_bir_lowering=False)
    xin = nc.dram_tensor("xT", [C, T], F32, kind="ExternalInput").ap()
    o_z = nc.dram_tensor("o_z", [C, T], BF16, kind="ExternalOutput").ap()
    d = {"o_z": o_z, "xin": xin}
    with ExitStack() as st:
        ctx = Ctx(nc, st)
        phase_uln(ctx, d)
    return nc


def kernel(**inputs):
    import os
    if os.environ.get("MK_UNFUSED"):
        return kernel_unfused(**inputs)
    return kernel_fused(**inputs)
